# version 08
# baseline (speedup 1.0000x reference)
import numpy as np
import ml_dtypes
from contextlib import ExitStack

import concourse.bass as bass
import concourse.mybir as mybir
from concourse.bass_utils import run_bass_kernel_spmd

F32 = mybir.dt.float32
BF16 = mybir.dt.bfloat16
I32 = mybir.dt.int32
AF = mybir.ActivationFunctionType
ALU = mybir.AluOpType
AX = mybir.AxisListType
BT = ml_dtypes.bfloat16

D = 1024
T = 2048
JB = 8
NH = 8
DFF = 4096
HID = 3072
EPS_LN = 1e-5
EPS_ATTN = 1e-6
TW = 512
TC = T // TW
NST = 16
TCH = 64
NCH = T // TCH
INTER = 512
NEX = 16
K8 = 0.5 / (8.0 + 1e-8)
K5 = 0.5 / (5.0 + 1e-8)
KDT = 0.5 / (0.0495 + 1e-8)


class Prog:
    def __init__(self, nc, ctx):
        self.nc = nc
        self.ctx = ctx
        self.block = ctx.enter_context(nc.Block())
        self._sems = {}

    def sem(self, name):
        if name not in self._sems:
            h = self.ctx.enter_context(self.nc.semaphore(name))
            self._sems[name] = [h, 0]
        return self._sems[name]

    def inc(self, name, inst, k=1):
        if name.startswith("d"):
            k *= 16
        s = self.sem(name)
        s[1] += k
        inst.then_inc(s[0], k)
        return s[1]

    def val(self, name):
        return self.sem(name)[1]

    def wait(self, eng, name, v=None):
        s = self.sem(name)
        v = s[1] if v is None else v
        if v > 0:
            eng.wait_ge(s[0], v)

    _scope = None

    def mark(self, name):
        if self._scope is not None:
            self.nc.leave_named_scope(self._scope[0], self._scope[1], False)
            self._scope = None
        if name is not None:
            sid, _ = self.nc.enter_named_scope(name, False)
            self._scope = (name, sid)


def fsl(kb, tc, w=TW):
    return slice(kb * T + tc * w, kb * T + (tc + 1) * w)


def emit_ln(p, r, src_bf, val_src, dst_bf, gcol=None, bcol=None):
    block = p.block
    ps = r["ps"]
    for tc in range(TC):
        for kb in range(JB):
            @block.scalar
            def _(s, kb=kb, tc=tc):
                p.wait(s, "pe")
                if kb == 0:
                    p.wait(s, "dw")
                    p.wait(s, "ds")
                    p.wait(s, "ve")
                p.inc("se", s.activation(r["sqb"][:, 0:TW], src_bf[:, fsl(kb, tc)], AF.Square))

            @block.tensor
            def _(te, kb=kb, tc=tc):
                p.wait(te, "se")
                if kb == 0:
                    p.wait(te, "ve")
                p.inc("pe", te.matmul(ps[7][0:1, 0:TW], r["ones_cb"][:], r["sqb"][:, 0:TW],
                                      start=(kb == 0), stop=(kb == JB - 1)))
                p.inc("pe", te.matmul(ps[6][0:1, 0:TW], r["ones_cb"][:], src_bf[:, fsl(kb, tc)],
                                      start=(kb == 0), stop=(kb == JB - 1)))

        @block.vector
        def _(v, tc=tc):
            p.wait(v, "pe")
            p.inc("ve", v.tensor_scalar_mul(r["row0"], ps[6][0:1, 0:TW], 1.0 / D))
            p.inc("ve", v.tensor_scalar_mul(r["row1"], ps[7][0:1, 0:TW], 1.0 / D))
            p.inc("ve", v.tensor_mul(r["row2"], r["row0"], r["row0"]))
            p.inc("ve", v.tensor_sub(r["row1"], r["row1"], r["row2"]))

        @block.scalar
        def _(s, tc=tc):
            p.wait(s, "ve")
            p.inc("se", s.activation(r["row1"], r["row1"], AF.Ln, bias=r["epsln"]))
            p.inc("se", s.activation(r["row1"], r["row1"], AF.Exp, scale=-0.5))

        @block.tensor
        def _(te, tc=tc):
            p.wait(te, "se")
            p.wait(te, "ve")
            p.inc("pe", te.matmul(ps[4][:, 0:TW], r["ones_r32"][:], r["row0"],
                                  start=True, stop=True))
            p.inc("pe", te.matmul(ps[5][:, 0:TW], r["ones_r32"][:], r["row1"],
                                  start=True, stop=True))

        @block.vector
        def _(v, tc=tc):
            p.wait(v, "pe")
            for kb in range(JB):
                vs = val_src(kb, tc)
                p.inc("ve", v.tensor_sub(r["sc0"][:, 0:TW], vs, ps[4][:, 0:TW]))
                if gcol is None:
                    p.inc("ve", v.tensor_mul(dst_bf[:, fsl(kb, tc)], r["sc0"][:, 0:TW], ps[5][:, 0:TW]))
                else:
                    p.inc("ve", v.tensor_mul(r["sc0"][:, 0:TW], r["sc0"][:, 0:TW], ps[5][:, 0:TW]))
                    p.inc("ve", v.tensor_scalar(dst_bf[:, fsl(kb, tc)], r["sc0"][:, 0:TW],
                                                gcol[:, kb:kb + 1], bcol[:, kb:kb + 1],
                                                ALU.mult, ALU.add))


def emit_elu_epilogue(p, ps_ap, dst_ap, r, w=TW):
    block = p.block

    @block.vector
    def _(v):
        p.wait(v, "pe")
        p.inc("ve", v.tensor_scalar_min(r["sc0"][:, 0:w], ps_ap, 0.0))
        p.inc("ve", v.tensor_sub(r["sc1"][:, 0:w], ps_ap, r["sc0"][:, 0:w]))

    @block.scalar
    def _(s):
        p.wait(s, "ve")
        p.inc("se", s.activation(r["sc2"][:, 0:w], r["sc0"][:, 0:w], AF.Exp))

    @block.vector
    def _(v):
        p.wait(v, "se")
        p.inc("ve", v.tensor_add(dst_ap, r["sc1"][:, 0:w], r["sc2"][:, 0:w]))


def _alloc(nc, e):
    sb = {}
    t = lambda name, shp, dt=F32: e(nc.sbuf_tensor(name, shp, dt))
    sb["xres"] = t("xres", [128, JB * T], F32)
    sb["xnb"] = t("xnb", [128, JB * T], BF16)
    sb["A1"] = t("A1", [128, JB * T], BF16)
    sb["A2"] = t("A2", [128, JB * T], BF16)
    sb["A3"] = t("A3", [128, JB * T], BF16)
    sb["sc0"] = t("sc0", [128, TW], F32)
    sb["sc1"] = t("sc1", [128, TW], F32)
    sb["sc2"] = t("sc2", [128, TW], F32)
    _w1 = t("w1buf", [128, JB * 128], BF16)
    _w2 = t("w2buf", [128, 1024], BF16)
    sb["w1buf"] = [_w1, _w1]
    sb["w2buf"] = [_w2, _w2]
    for n, w in (("b1s", DFF // 128), ("b2s", JB), ("g1s", JB), ("be1s", JB),
                 ("g2s", JB), ("be2s", JB), ("bdt_sb", JB), ("dp_sb", JB)):
        sb[n] = t(n, [128, w], F32)
    sb["ones_r32"] = t("ones_r32", [1, 128], F32)
    sb["ones_cb"] = t("ones_cb", [128, 1], BF16)
    sb["ones_cb2"] = t("ones_cb2", [128, 2], BF16)
    sb["epsc"] = t("epsc", [1, 1], F32)
    sb["idb"] = t("idb", [128, 128], F32)
    sb["ones_sq"] = sb["idb"]
    sb["pcol"] = t("pcol", [128, 1], F32)
    sb["a_sb"] = t("a_sb", [128, 128], F32)
    sb["a_sbb"] = t("a_sbb", [128, 128], BF16)
    sb["h_st"] = t("h_st", [128, 128], F32)
    sb["iof"] = sb["h_st"]
    sb["srow"] = t("srow", [128, 34], F32)
    sb["pm2"] = t("pm2", [128, 128], BF16)
    sb["abq"] = t("abq", [128, 4], BF16)
    sb["tq1"] = t("tq1", [128, 1], F32)
    sb["tcomb"] = t("tcomb", [128, 1], F32)
    sb["ones_sqb"] = t("ones_sqb", [128, 128], BF16)
    sb["wb_sb"] = t("wb_sb", [128, JB * NST], BF16)
    sb["wc_sb"] = t("wc_sb", [128, JB * NST], BF16)
    sb["wg_sb"] = t("wg_sb", [128, JB * NEX], BF16)
    sb["gb_rep"] = t("gb_rep", [128, NEX], F32)
    sb["gbr_sb"] = t("gbr_sb", [1, NEX], F32)
    for n in ("rcol", "mcol", "wcol"):
        sb[n] = t(n, [128, 1], F32)
    for n in ("rsc", "bia", "bwk", "selw", "comb_sc"):
        sb[n] = t(n, [128, NEX], F32)
    sb["ps"] = [e(nc.psum_tensor(f"ps{i}", [128, TW], F32)) for i in range(8)]
    return sb


def _emit_l0(p, io, sb):
    block = p.block
    x_t, wqT, wkT, wvT, w1T, w2T = io["x_t"], io["wqT"], io["wkT"], io["wvT"], io["w1T"], io["w2T"]
    b1c, b2c, g1c, be1c, g2c, be2c = io["b1c"], io["b2c"], io["g1c"], io["be1c"], io["g2c"], io["be2c"]
    wgT, wuT, wdT = io["wgT"], io["wuT"], io["wdT"]
    xres, xnb, A1, A2, A3 = sb["xres"], sb["xnb"], sb["A1"], sb["A2"], sb["A3"]
    sc0, sc1, sc2 = sb["sc0"], sb["sc1"], sb["sc2"]
    w1buf, w2buf = sb["w1buf"], sb["w2buf"]
    b1s, b2s, g1s, be1s, g2s, be2s = sb["b1s"], sb["b2s"], sb["g1s"], sb["be1s"], sb["g2s"], sb["be2s"]
    ones_r32, ones_cb, ones_cb2, epsc = sb["ones_r32"], sb["ones_cb"], sb["ones_cb2"], sb["epsc"]
    ps = sb["ps"]

    xrv = xres[:].bitcast(BF16)
    ktile = xrv[:, 0:1024]
    vtile = xrv[:, 1024:2048]
    kvb = xrv[:, 2048:3072]
    ksb = xrv[:, 3072:3072 + NH]
    wqbuf = xrv[:, 4096:4096 + 1024]
    xrvf = xres[:]
    row0 = xrvf[0:1, 4096:4096 + TW]
    row1 = xrvf[0:1, 4608:4608 + TW]
    row2 = xrvf[0:1, 5120:5120 + TW]
    zrow = xrvf[0:1, 5632:5632 + TW]
    sqb = xrv[:, 12288:12288 + TW]
    wk_sb = A2[:, 0:JB * D]
    wv_sb = A2[:, JB * D:2 * JB * D]

    r = dict(ps=ps, sqb=sqb, sc0=sc0, sc1=sc1, sc2=sc2, row0=row0, row1=row1,
             row2=row2, ones_r32=ones_r32, ones_cb=ones_cb, epsln=epsc[0:1, 0:1])

    @block.gpsimd
    def _(g):
        p.inc("dw", g.dma_start(
            out=A3[:].rearrange("p (j t) -> p j t", j=JB),
            in_=x_t.rearrange("(j p) t -> p j t", p=128)))
        for w, dst in ((wkT, wk_sb), (wvT, wv_sb)):
            p.inc("dw", g.dma_start(
                out=dst.rearrange("p (k o) -> p k o", k=JB),
                in_=w.rearrange("(k p) o -> p k o", p=128)))

    @block.sync
    def _(sync):
        for src, dst in ((b1c, b1s), (b2c, b2s), (g1c, g1s), (be1c, be1s),
                         (g2c, g2s), (be2c, be2s)):
            p.inc("ds", sync.dma_start(out=dst[:], in_=src))

    @block.vector
    def _(v):
        p.inc("ve", v.memset(ones_r32[:], 1.0))
        p.inc("ve", v.memset(ones_cb[:], 1.0))
        p.inc("ve", v.memset(ones_cb2[:], 1.0))
        p.inc("ve", v.memset(epsc[:], EPS_LN))

    p.mark("l0_ln0")
    emit_ln(p, r, A3, lambda kb, tc: A3[:, fsl(kb, tc)], xnb)

    p.mark("l0_ln1")
    emit_ln(p, r, xnb, lambda kb, tc: xnb[:, fsl(kb, tc)], A1, gcol=g1s, bcol=be1s)

    p.mark("l0_kv")
    NT = T // 128
    for tt in range(NT):
        for half in range(2):
            @block.tensor
            def _(te, tt=tt, half=half):
                p.wait(te, "ve")
                p.wait(te, "se")
                for kb in range(JB):
                    p.inc("pe", te.matmul(
                        ps[half][:, 0:TW],
                        A1[:, kb * T + tt * 128: kb * T + (tt + 1) * 128],
                        wk_sb[:, kb * D + half * TW: kb * D + (half + 1) * TW],
                        start=(kb == 0), stop=(kb == JB - 1)))
            emit_elu_epilogue(p, ps[half][:, 0:TW], ktile[:, half * TW:(half + 1) * TW], r)

        for half in range(2):
            @block.tensor
            def _(te, tt=tt, half=half):
                p.wait(te, "ve")
                p.wait(te, "se")
                for kb in range(JB):
                    p.inc("pe", te.matmul(
                        ps[half][:, 0:TW],
                        A1[:, kb * T + tt * 128: kb * T + (tt + 1) * 128],
                        wv_sb[:, kb * D + half * TW: kb * D + (half + 1) * TW],
                        start=(kb == 0), stop=(kb == JB - 1)))

            @block.scalar
            def _(s, tt=tt, half=half):
                p.wait(s, "pe")
                p.inc("se", s.activation(vtile[:, half * TW:(half + 1) * TW],
                                         ps[half][:, 0:TW], AF.Copy))

        @block.tensor
        def _(te, tt=tt):
            p.wait(te, "se")
            p.wait(te, "ve")
            for h in range(NH):
                pbank = ps[2] if h < 4 else ps[3]
                col = (h % 4) * 128
                p.inc("pe", te.matmul(
                    pbank[:, col:col + 128],
                    ktile[:, h * 128:(h + 1) * 128],
                    vtile[:, h * 128:(h + 1) * 128],
                    start=(tt == 0 and (h % 4) == 0),
                    stop=(tt == NT - 1 and (h % 4) == 3)))
            for h in range(NH):
                p.inc("pe", te.matmul(
                    ps[6][:, 2 * h:2 * h + 2],
                    ktile[:, h * 128:(h + 1) * 128],
                    ones_cb2[:],
                    start=(tt == 0 and h == 0), stop=(tt == NT - 1 and h == NH - 1)))

    @block.scalar
    def _(s):
        p.wait(s, "pe")
        p.inc("se", s.activation(kvb[:, 0:TW], ps[2][:, 0:TW], AF.Copy))
        p.inc("se", s.activation(kvb[:, TW:2 * TW], ps[3][:, 0:TW], AF.Copy))
        p.inc("se", s.activation(ksb, ps[6][:, 0:2 * NH:2], AF.Copy))

    p.mark("l0_q")
    for h in range(NH):
        @block.gpsimd
        def _(g, h=h):
            p.wait(g, "pe")
            for kq in range(JB):
                p.inc("dw", g.dma_start(
                    out=wqbuf[:, kq * 128:(kq + 1) * 128],
                    in_=wqT.rearrange("(k p) o -> k p o", p=128)[kq, :, h * 128:(h + 1) * 128]))

        for tc in range(TC):
            bank = tc % 2
            @block.tensor
            def _(te, h=h, tc=tc, bank=bank):
                p.wait(te, "dw")
                p.wait(te, "ve")
                for kb in range(JB):
                    p.inc("pe", te.matmul(
                        ps[bank][:, 0:TW],
                        wqbuf[:, kb * 128:(kb + 1) * 128],
                        A1[:, fsl(kb, tc)],
                        start=(kb == 0), stop=(kb == JB - 1)))
            emit_elu_epilogue(p, ps[bank][:, 0:TW], A2[:, fsl(h, tc)], r)

    p.mark("l0_att")
    for h in range(NH):
        for tc in range(TC):
            @block.tensor
            def _(te, h=h, tc=tc):
                p.wait(te, "se")
                p.wait(te, "ve")
                p.inc("pe", te.matmul(ps[6][0:1, 0:TW], ksb[:, h:h + 1],
                                      A2[:, fsl(h, tc)], start=True, stop=True))

            @block.vector
            def _(v, h=h, tc=tc):
                p.wait(v, "pe")
                p.inc("ve", v.tensor_scalar_add(zrow, ps[6][0:1, 0:TW], EPS_ATTN))
                p.inc("ve", v.reciprocal(zrow, zrow))

            @block.tensor
            def _(te, h=h, tc=tc):
                p.wait(te, "ve")
                p.inc("pe", te.matmul(ps[4][:, 0:TW], ones_r32[:], zrow,
                                      start=True, stop=True))
                p.inc("pe", te.matmul(ps[5][:, 0:TW],
                                      kvb[:, h * 128:(h + 1) * 128],
                                      A2[:, fsl(h, tc)], start=True, stop=True))

            @block.scalar
            def _(s, h=h, tc=tc):
                p.wait(s, "pe")
                p.inc("se", s.activation(sc0[:, 0:TW], ps[4][:, 0:TW], AF.Copy))

            @block.vector
            def _(v, h=h, tc=tc):
                p.wait(v, "se")
                p.inc("ve", v.tensor_mul(sc1[:, 0:TW], ps[5][:, 0:TW], sc0[:, 0:TW]))
                p.inc("ve", v.tensor_add(A3[:, fsl(h, tc)], sc1[:, 0:TW], xnb[:, fsl(h, tc)]))

    p.mark("l0_ln2")
    emit_ln(p, r, A3, lambda kb, tc: A3[:, fsl(kb, tc)], A1, gcol=g2s, bcol=be2s)

    @block.sync
    def _(sync):
        p.wait(sync, "pe")
        p.wait(sync, "ve")
        p.wait(sync, "se")
        p.inc("dx", sync.dma_start(
            out=xres[:].rearrange("p (j t) -> p j t", j=JB),
            in_=x_t.rearrange("(j p) t -> p j t", p=128)))

    p.mark("l0_ffnA")
    NFB = DFF // 128
    pe_fb = {}
    for tc in range(TC):
        for fb in range(NFB):
            buf = fb % 2
            wsel = [w1buf[0], w2buf[0]]
            @block.gpsimd
            def _(g, tc=tc, fb=fb, buf=buf, wsel=wsel):
                if fb < 2:
                    if tc == 0:
                        p.wait(g, "pe")
                    else:
                        p.wait(g, "pe", pe_fb[("a2", tc * NFB - 1)])
                else:
                    p.wait(g, "pe", pe_fb[tc * NFB + fb - 2])
                for kq in range(JB):
                    p.inc("dw", g.dma_start(
                        out=wsel[buf][:, kq * 128:(kq + 1) * 128],
                        in_=w1T.rearrange("(k p) o -> k p o", p=128)[kq, :, fb * 128:(fb + 1) * 128]))

            @block.tensor
            def _(te, tc=tc, fb=fb, buf=buf, wsel=wsel):
                p.wait(te, "dw")
                p.wait(te, "se")
                if fb == 0:
                    p.wait(te, "ve")
                for kb in range(JB):
                    p.inc("pe", te.matmul(
                        ps[7][:, 0:TW], wsel[buf][:, kb * 128:(kb + 1) * 128],
                        A1[:, fsl(kb, tc)],
                        start=(kb == 0), stop=(kb == JB - 1)))
                pe_fb[tc * NFB + fb] = p.val("pe")

            @block.scalar
            def _(s, tc=tc, fb=fb):
                p.wait(s, "pe")
                p.inc("se", s.activation(A2[:, fb * TW:(fb + 1) * TW], ps[7][:, 0:TW],
                                         AF.Gelu, bias=b1s[:, fb:fb + 1], scale=1.0))

        for fb in range(NFB):
            buf = fb % 2
            wsel2 = [w2buf[0], w1buf[0]]
            @block.gpsimd
            def _(g, tc=tc, fb=fb, buf=buf, wsel2=wsel2):
                if fb < 2:
                    p.wait(g, "pe", pe_fb[tc * NFB + NFB - 1])
                else:
                    p.wait(g, "pe", pe_fb[("a2", tc * NFB + fb - 2)])
                p.inc("dw", g.dma_start(
                    out=wsel2[buf][:],
                    in_=w2T.rearrange("(f p) o -> f p o", p=128)[fb]))

            @block.tensor
            def _(te, tc=tc, fb=fb, buf=buf, wsel2=wsel2):
                p.wait(te, "dw")
                if fb == 0:
                    p.wait(te, "ve")
                    p.wait(te, "se")
                for ob in range(JB):
                    p.inc("pe", te.matmul(
                        ps[ob][:, 0:TW], wsel2[buf][:, ob * 128:(ob + 1) * 128],
                        A2[:, fb * TW:(fb + 1) * TW],
                        start=(fb == 0), stop=(fb == NFB - 1)))
                pe_fb[("a2", tc * NFB + fb)] = p.val("pe")

        @block.vector
        def _(v, tc=tc):
            p.wait(v, "pe")
            p.wait(v, "dx")
            for ob in range(JB):
                p.inc("ve", v.scalar_tensor_tensor(
                    sc0[:, 0:TW], ps[ob][:, 0:TW], b2s[:, ob:ob + 1],
                    A3[:, fsl(ob, tc)], ALU.add, ALU.add))
                p.inc("ve", v.tensor_add(xres[:, fsl(ob, tc)], xres[:, fsl(ob, tc)],
                                         sc0[:, 0:TW]))

    p.mark("l0_ffnB")
    NHB = HID // 128
    wgB = [w1buf[0], A2[:, 12288:13312]]
    wuB = [w2buf[0], A2[:, 13312:14336]]
    for tc in range(TC):
        for fb in range(NHB):
            buf = fb % 2
            @block.gpsimd
            def _(g, tc=tc, fb=fb, buf=buf):
                if fb < 2:
                    if tc == 0:
                        p.wait(g, "pe")
                        p.wait(g, "ve")
                    else:
                        p.wait(g, "pe", pe_fb[("b2", tc * NHB - 1)])
                else:
                    p.wait(g, "pe", pe_fb[("b1", tc * NHB + fb - 2)])
                for kq in range(JB):
                    p.inc("dw", g.dma_start(
                        out=wgB[buf][:, kq * 128:(kq + 1) * 128],
                        in_=wgT.rearrange("(k p) o -> k p o", p=128)[kq, :, fb * 128:(fb + 1) * 128]))
                for kq in range(JB):
                    p.inc("dw", g.dma_start(
                        out=wuB[buf][:, kq * 128:(kq + 1) * 128],
                        in_=wuT.rearrange("(k p) o -> k p o", p=128)[kq, :, fb * 128:(fb + 1) * 128]))

            @block.tensor
            def _(te, tc=tc, fb=fb, buf=buf):
                p.wait(te, "dw")
                p.wait(te, "se")
                p.wait(te, "ve")
                for kb in range(JB):
                    p.inc("pe", te.matmul(
                        ps[6][:, 0:TW], wgB[buf][:, kb * 128:(kb + 1) * 128],
                        xnb[:, fsl(kb, tc)],
                        start=(kb == 0), stop=(kb == JB - 1)))
                for kb in range(JB):
                    p.inc("pe", te.matmul(
                        ps[7][:, 0:TW], wuB[buf][:, kb * 128:(kb + 1) * 128],
                        xnb[:, fsl(kb, tc)],
                        start=(kb == 0), stop=(kb == JB - 1)))
                pe_fb[("b1", tc * NHB + fb)] = p.val("pe")

            @block.scalar
            def _(s, tc=tc, fb=fb):
                p.wait(s, "pe")
                p.inc("se", s.activation(sc0[:, 0:TW], ps[6][:, 0:TW], AF.Silu,
                                         scale=1.0 / 127.0))

            @block.vector
            def _(v, tc=tc, fb=fb):
                p.wait(v, "se")
                p.inc("ve", v.scalar_tensor_tensor(
                    A2[:, fb * TW:(fb + 1) * TW], ps[7][:, 0:TW], 1.0 / 127.0,
                    sc0[:, 0:TW], ALU.mult, ALU.mult))

        for fb in range(NHB):
            buf = fb % 2
            wselD = [w2buf[0], w1buf[0]]
            @block.gpsimd
            def _(g, tc=tc, fb=fb, buf=buf, wselD=wselD):
                if fb < 2:
                    p.wait(g, "pe", pe_fb[("b1", tc * NHB + NHB - 1)])
                else:
                    p.wait(g, "pe", pe_fb[("b2", tc * NHB + fb - 2)])
                p.inc("dw", g.dma_start(
                    out=wselD[buf][:],
                    in_=wdT.rearrange("(f p) o -> f p o", p=128)[fb]))

            @block.tensor
            def _(te, tc=tc, fb=fb, buf=buf, wselD=wselD):
                p.wait(te, "dw")
                if fb == 0:
                    p.wait(te, "ve")
                    p.wait(te, "se")
                for ob in range(JB):
                    p.inc("pe", te.matmul(
                        ps[ob][:, 0:TW], wselD[buf][:, ob * 128:(ob + 1) * 128],
                        A2[:, fb * TW:(fb + 1) * TW],
                        start=(fb == 0), stop=(fb == NHB - 1)))
                pe_fb[("b2", tc * NHB + fb)] = p.val("pe")

        @block.vector
        def _(v, tc=tc):
            p.wait(v, "pe")
            for ob in range(JB):
                p.inc("ve", v.scalar_tensor_tensor(
                    xres[:, fsl(ob, tc)], ps[ob][:, 0:TW], 1.0 / 127.0,
                    xres[:, fsl(ob, tc)], ALU.mult, ALU.add))


def _emit_l1(p, io, sb, fused, skip=()):
    block = p.block
    WdtT, bdtc, WBT, WCT = io["WdtT"], io["bdtc"], io["WBT"], io["WCT"]
    A_logc, Dpc, gwT, gbr = io["A_logc"], io["Dpc"], io["gwT"], io["gbr"]
    w13T, w2mT, w13sT, w2sT, out_t = io["w13T"], io["w2mT"], io["w13sT"], io["w2sT"], io["out_t"]
    x1res, xn1b, A1, A2, A3 = sb["xres"], sb["xnb"], sb["A1"], sb["A2"], sb["A3"]
    sc0, sc1, sc2 = sb["sc0"], sb["sc1"], sb["sc2"]
    ones_r32, ones_cb, ones_sq, epsc = sb["ones_r32"], sb["ones_cb"], sb["ones_sq"], sb["epsc"]
    idb, iof, pcol = sb["idb"], sb["iof"], sb["pcol"]
    a_sb, a_sbb, h_st = sb["a_sb"], sb["a_sbb"], sb["h_st"]
    srow, abq, tq1, tcomb = sb["srow"], sb["abq"], sb["tq1"], sb["tcomb"]
    pm2 = sb["pm2"]
    ones_sqb = sb["ones_sqb"]
    wb_sb, wc_sb, wg_sb = sb["wb_sb"], sb["wc_sb"], sb["wg_sb"]
    gb_rep, gbr_sb, bdt_sb, dp_sb = sb["gb_rep"], sb["gbr_sb"], sb["bdt_sb"], sb["dp_sb"]
    rcol = sb["rcol"]
    rsc, bia, bwk, mcol, selw, comb_sc, wcol = sb["rsc"], sb["bia"], sb["bwk"], sb["mcol"], sb["selw"], sb["comb_sc"], sb["wcol"]
    ps = sb["ps"]

    A1f = A1[:].bitcast(F32)
    A2f = A2[:].bitcast(F32)
    x1bf = A3[:, 0:JB * T]
    wdt_sb = A3[:, 0:JB * D]
    bT_sb = A3[0:NST, 8192:8192 + T]
    cT_sb = A3[0:NST, 10240:10240 + T]
    combT = A3[0:NST, 12288:12288 + T]
    Esel = A3[0:NST, 14336:14336 + NST * 128]
    io_sc = A2[:, 0:NST * 128]
    m13buf = [A1[:, 0:8192], A1[:, 8192:16384]]
    m2buf = [A2[:, 8192:12288], A2[:, 12288:16384]]
    hb_moe = A2[:, 0:2048]
    crep_moe = A2f[:, 1024:1024 + TW]
    row0 = A1f[0:1, 0:TW]
    row1 = A1f[0:1, TW:2 * TW]
    row2 = A1f[0:1, 2 * TW:3 * TW]
    sqb = sc2[:].bitcast(BF16)[:, 0:TW]

    r = dict(ps=ps, sqb=sqb, sc0=sc0, sc1=sc1, sc2=sc2, row0=row0, row1=row1,
             row2=row2, ones_r32=ones_r32, ones_cb=ones_cb, epsln=epsc[0:1, 0:1])

    p.mark("l1_setup")
    @block.sync
    def _(sync):
        if not fused:
            p.inc("ds", sync.dma_start(
                out=x1res[:].rearrange("p (j t) -> p j t", j=JB),
                in_=io["x1_t"].rearrange("(j p) t -> p j t", p=128)))
        p.inc("ds", sync.dma_start(out=a_sb[:], in_=A_logc))
        p.inc("ds", sync.dma_start(out=bdt_sb[:], in_=bdtc))
        p.inc("ds", sync.dma_start(out=dp_sb[:], in_=Dpc))
        p.inc("ds", sync.dma_start(out=gbr_sb[:], in_=gbr))

    @block.gpsimd
    def _(g):
        p.wait(g, "pe")
        p.wait(g, "ve")
        if not fused:
            p.inc("dw", g.dma_start(
                out=x1bf.rearrange("p (j t) -> p j t", j=JB),
                in_=io["x1_t"].rearrange("(j p) t -> p j t", p=128)))
        p.inc("dw", g.dma_start(
            out=wb_sb[:].rearrange("p (k o) -> p k o", k=JB),
            in_=WBT.rearrange("(k p) o -> p k o", p=128)))
        p.inc("dw", g.dma_start(
            out=wc_sb[:].rearrange("p (k o) -> p k o", k=JB),
            in_=WCT.rearrange("(k p) o -> p k o", p=128)))
        p.inc("dw", g.dma_start(
            out=wg_sb[:].rearrange("p (k o) -> p k o", k=JB),
            in_=gwT.rearrange("(k p) o -> p k o", p=128)))
        p.inc("go", g.iota(iof[:], [[1, 128]], base=0, channel_multiplier=0,
                           allow_small_or_imprecise_dtypes=True))
        p.inc("go", g.iota(pcol[:], [[0, 1]], base=0, channel_multiplier=1,
                           allow_small_or_imprecise_dtypes=True))
        p.inc("go", g.iota(io_sc, [[1, NST], [0, 128]], base=0, channel_multiplier=0,
                           allow_small_or_imprecise_dtypes=True))

    if fused:
        for kb in range(JB):
            @block.scalar
            def _(s, kb=kb):
                if kb == 0:
                    p.wait(s, "ve")
                    p.wait(s, "pe")
                p.inc("se", s.activation(x1bf[:, kb * T:(kb + 1) * T],
                                         x1res[:, kb * T:(kb + 1) * T], AF.Copy))

    @block.vector
    def _(v):
        p.wait(v, "go")
        p.inc("ve", v.memset(h_st[:], 0.0))
        p.inc("ve", v.tensor_single_scalar(idb[:], iof[:], pcol[:, 0:1], ALU.is_equal))
        if not fused:
            p.inc("ve", v.memset(ones_r32[:], 1.0))
            p.inc("ve", v.memset(ones_cb[:], 1.0))
            p.inc("ve", v.memset(epsc[:], EPS_LN))

    @block.tensor
    def _(te):
        p.wait(te, "dw")
        p.wait(te, "ds")
        p.wait(te, "ve")
        p.wait(te, "se")
        p.inc("pe", te.matmul(ps[0][:, 0:NEX], ones_r32[:], gbr_sb[:], start=True, stop=True))

    @block.scalar
    def _(s):
        p.wait(s, "pe")
        p.inc("se", s.activation(gb_rep[:], ps[0][:, 0:NEX], AF.Copy))

    @block.vector
    def _(v):
        p.wait(v, "dw")
        p.wait(v, "ds")
        p.inc("ve", v.tensor_scalar_mul(a_sb[:], a_sb[:], K5))

    @block.scalar
    def _(s):
        p.wait(s, "ve")
        p.inc("se", s.activation(a_sb[:], a_sb[:], AF.Tanh))

    @block.scalar
    def _(s):
        p.wait(s, "se")
        p.inc("se", s.activation(a_sb[:], a_sb[:], AF.Exp, scale=10.0))

    @block.vector
    def _(v):
        p.wait(v, "se")
        p.inc("ve", v.tensor_scalar_mul(a_sb[:], a_sb[:], -1.0))

    p.mark("l1_ln")
    emit_ln(p, r, x1bf, lambda kb, tc: x1res[:, fsl(kb, tc)], xn1b)

    @block.gpsimd
    def _(g):
        p.wait(g, "se")
        p.wait(g, "pe")
        p.wait(g, "ve")
        p.inc("dw", g.dma_start(
            out=wdt_sb.rearrange("p (k o) -> p k o", k=JB),
            in_=WdtT.rearrange("(k p) o -> p k o", p=128)))

    @block.vector
    def _(v):
        p.wait(v, "go")
        p.wait(v, "se")
        p.wait(v, "pe")
        p.inc("ve", v.tensor_single_scalar(Esel, io_sc[0:16, :], pcol[0:16, 0:1], ALU.is_equal))

    p.mark("l1_bc")
    for tc in range(TC):
        @block.tensor
        def _(te, tc=tc):
            p.wait(te, "ve")
            p.wait(te, "se")
            for kb in range(JB):
                p.inc("pe", te.matmul(ps[0][0:NST, 0:TW],
                                      wb_sb[:, kb * NST:(kb + 1) * NST],
                                      xn1b[:, fsl(kb, tc)],
                                      start=(kb == 0), stop=(kb == JB - 1)))
            for kb in range(JB):
                p.inc("pe", te.matmul(ps[1][0:NST, 0:TW],
                                      wc_sb[:, kb * NST:(kb + 1) * NST],
                                      xn1b[:, fsl(kb, tc)],
                                      start=(kb == 0), stop=(kb == JB - 1)))

        @block.scalar
        def _(s, tc=tc):
            p.wait(s, "pe")
            p.inc("se", s.activation(bT_sb[:, tc * TW:(tc + 1) * TW], ps[0][0:NST, 0:TW], AF.Copy))
            p.inc("se", s.activation(cT_sb[:, tc * TW:(tc + 1) * TW], ps[1][0:NST, 0:TW], AF.Copy))

    p.mark("l1_route")
    NT = T // 128
    for tt in range(NT):
        @block.tensor
        def _(te, tt=tt):
            p.wait(te, "se")
            p.wait(te, "ve")
            for kb in range(JB):
                p.inc("pe", te.matmul(ps[2][:, 0:NEX],
                                      xn1b[:, kb * T + tt * 128: kb * T + (tt + 1) * 128],
                                      wg_sb[:, kb * NEX:(kb + 1) * NEX],
                                      start=(kb == 0), stop=(kb == JB - 1)))

        @block.scalar
        def _(s, tt=tt):
            p.wait(s, "pe")
            p.inc("se", s.activation(rsc[:], ps[2][:, 0:NEX], AF.Sigmoid))

        @block.vector
        def _(v, tt=tt):
            p.wait(v, "se")
            p.inc("ve", v.tensor_add(bia[:], rsc[:], gb_rep[:]))
            p.inc("ve", v.tensor_copy(bwk[:], bia[:]))
            for rnd in range(4):
                p.inc("ve", v.reduce_max(mcol[:], bwk[:], AX.X))
                if rnd < 3:
                    p.inc("ve", v.tensor_single_scalar(selw[:], bwk[:], mcol[:, 0:1], ALU.is_equal))
                    p.inc("ve", v.scalar_tensor_tensor(bwk[:], selw[:], -1e30, bwk[:],
                                                       ALU.mult, ALU.add))
            p.inc("ve", v.tensor_single_scalar(selw[:], bia[:], mcol[:, 0:1], ALU.is_ge))
            p.inc("ve", v.tensor_mul(selw[:], rsc[:], selw[:]))
            p.inc("ve", v.tensor_reduce(wcol[:], selw[:], AX.X, ALU.add))
            p.inc("ve", v.reciprocal(wcol[:], wcol[:]))
            p.inc("ve", v.tensor_scalar(comb_sc[:], selw[:], wcol[:, 0:1], 2.5,
                                        ALU.mult, ALU.mult))

        @block.tensor
        def _(te, tt=tt):
            p.wait(te, "ve")
            p.inc("pe", te.transpose(ps[3][0:NEX, 0:128], comb_sc[:], idb[:]))

        @block.scalar
        def _(s, tt=tt):
            p.wait(s, "pe")
            p.inc("se", s.activation(combT[:, tt * 128:(tt + 1) * 128],
                                     ps[3][0:NEX, 0:128], AF.Copy))

    @block.vector
    def _(v):
        p.wait(v, "pe")
        p.inc("ve", v.memset(ones_sq[:], 1.0))

    p.mark("l1_scan")
    SCH = 32
    ESC = SCH + 1
    NCHS = T // SCH
    SCS = 128
    ESS = SCS + 1
    A1b = A1[:]
    A1f = A1[:].bitcast(F32)
    A2b = A2[:]
    A2f2 = A2[:].bitcast(F32)
    dAx = A1b[:, 0:ESC * 128]
    qbuf = A1b[:, 4224:4224 + ESC * 128]
    t1p = A1b[:, 8448:8448 + SCH * 128]
    pcs = A1b[:, 0:SCH * 128]
    dtraw = A1f[:, 0:1040]
    dtrw2 = A1f[:, 1040:2080]
    wx = A2b[:, 0:ESC * 128]
    udA = A2b[:, 4224:4224 + SCH * 128]
    dt_sc = A2b[:, 8320:8320 + JB * ESS]
    u_sc = A2b[:, 9352:9352 + JB * ESS]
    brep_sc = A2b[:, 10384:10384 + NST * ESS]
    crep_sc = A2b[:, 12448:12448 + NST * ESS]
    ccv = A2f2[:, 7256:7256 + SCH]
    cuv = A2f2[:, 7296:7296 + SCH]
    cbv = A2f2[:, 7336:7336 + SCH]
    sc0b = sc0[:].bitcast(BF16)
    sc1b = sc1[:].bitcast(BF16)
    rbuf = sc0b[:, 0:128]

    if "scan" not in skip:
        @block.scalar
        def _(s):
            p.wait(s, "ve")
            p.wait(s, "pe")
            p.inc("se", s.activation(a_sbb[:], a_sb[:], AF.Copy))

        @block.vector
        def _(v):
            p.wait(v, "se")
            p.inc("ve", v.memset(qbuf[:, 0:128].bitcast(F32), 0.0))
            p.inc("ve", v.memset(srow[:], 1.0))
            p.inc("ve", v.memset(abq[:].bitcast(F32), 0.0))
            p.inc("ve", v.memset(ones_sqb[:], 1.0))

        @block.tensor
        def _(te):
            p.wait(te, "ve")
            p.inc("pe", te.matmul(ps[5][:, 0:2], ones_sqb[:], abq[:, 0:2],
                                  start=True, stop=True))
            p.inc("pe", te.matmul(ps[6][:, 0:2], ones_sqb[:], abq[:, 2:4],
                                  start=True, stop=True))

    for ch in range(NCHS if "scan" not in skip else 0):
        t0 = ch * SCH
        screl = (ch % (SCS // SCH)) * SCH
        if screl == 0:
            st0 = (ch // (SCS // SCH)) * SCS
            NE = ESS if st0 + SCS < T else SCS

            @block.tensor
            def _(te, st0=st0, NE=NE):
                p.wait(te, "ve")
                p.wait(te, "se")
                p.wait(te, "dw")
                for ob in range(JB):
                    bank, off = divmod(ob, 3)
                    for kb in range(JB):
                        p.inc("pe", te.matmul(
                            ps[bank][:, off * 130:off * 130 + NE],
                            wdt_sb[:, kb * D + ob * 128: kb * D + ob * 128 + 128],
                            xn1b[:, kb * T + st0: kb * T + st0 + NE],
                            start=(off == 0 and kb == 0),
                            stop=(kb == JB - 1)))

            @block.vector
            def _(v, NE=NE):
                p.wait(v, "pe")
                for ob in range(JB):
                    bank, off = divmod(ob, 3)
                    p.inc("ve", v.tensor_single_scalar(
                        dtraw[:, ob * 130:ob * 130 + NE],
                        ps[bank][:, off * 130:off * 130 + NE],
                        bdt_sb[:, ob:ob + 1], ALU.add))

            dra = dtraw.rearrange("p (o i) -> p o i", i=130)[:, :, 0:NE]
            dr2 = dtrw2.rearrange("p (o i) -> p o i", i=130)[:, :, 0:NE]

            @block.scalar
            def _(s, dra=dra, dr2=dr2):
                p.wait(s, "ve")
                p.inc("se", s.activation(dr2, dra, AF.Abs))
                p.inc("se", s.activation(dr2, dr2, AF.Exp, scale=-1.0))
                p.inc("se", s.activation(dr2, dr2, AF.Ln, bias=1.0))

            @block.vector
            def _(v, dra=dra, dr2=dr2):
                p.wait(v, "se")
                p.inc("ve", v.tensor_scalar_max(dra, dra, 0.0))
                p.inc("ve", v.tensor_add(dra, dra, dr2))
                p.inc("ve", v.tensor_scalar(dra, dra, 0.0505, KDT,
                                            ALU.subtract, ALU.mult))

            @block.scalar
            def _(s, dra=dra):
                p.wait(s, "ve")
                p.inc("se", s.activation(dra, dra, AF.Tanh))

            @block.vector
            def _(v, st0=st0, NE=NE, dra=dra):
                p.wait(v, "se")
                dts = dt_sc.rearrange("p (o i) -> p o i", i=ESS)[:, :, 0:NE]
                p.inc("ve", v.tensor_scalar(dts, dra, 0.099, 0.0505,
                                            ALU.mult, ALU.add))
                p.inc("ve", v.tensor_tensor(
                    u_sc.rearrange("p (o i) -> p o i", i=ESS)[:, :, 0:NE],
                    dts,
                    xn1b[:].rearrange("p (o t) -> p o t", o=JB)[:, :, st0:st0 + NE],
                    ALU.mult))

            for n in range(NST):
                @block.tensor
                def _(te, st0=st0, NE=NE, n=n):
                    p.wait(te, "se")
                    if n == 0:
                        p.wait(te, "ve")
                    p.inc("pe", te.matmul(ps[3][:, 0:NE],
                                          Esel[:, n * 128:(n + 1) * 128],
                                          bT_sb[:, st0:st0 + NE], start=True, stop=True))
                    p.inc("pe", te.matmul(ps[4][:, 0:NE],
                                          Esel[:, n * 128:(n + 1) * 128],
                                          cT_sb[:, st0:st0 + NE], start=True, stop=True))

                @block.scalar
                def _(s, NE=NE, n=n):
                    p.wait(s, "pe")
                    p.inc("se", s.activation(brep_sc[:, n * ESS:n * ESS + NE],
                                             ps[3][:, 0:NE], AF.Copy))
                    p.inc("se", s.activation(crep_sc[:, n * ESS:n * ESS + NE],
                                             ps[4][:, 0:NE], AF.Copy))

        NE1 = ESC if t0 + SCH < T else SCH
        dts_c = dt_sc.rearrange("p (o i) -> p o i", i=ESS)
        us_c = u_sc.rearrange("p (o i) -> p o i", i=ESS)
        br_c = brep_sc.rearrange("p (n i) -> p n i", i=ESS)
        cr_c = crep_sc.rearrange("p (n i) -> p n i", i=ESS)
        dA4 = dAx.rearrange("p (t o n) -> p t o n", o=JB, n=NST)
        w4 = wx.rearrange("p (t o n) -> p t o n", o=JB, n=NST)
        u4 = udA.rearrange("p (t o n) -> p t o n", o=JB, n=NST)

        @block.vector
        def _(v, screl=screl, NE1=NE1, dA4=dA4, dts_c=dts_c):
            p.wait(v, "se")
            p.wait(v, "pe")
            dtt = dts_c[:, :, screl:screl + NE1].transpose([0, 2, 1])
            p.inc("ve", v.tensor_tensor(
                dA4[:, 0:NE1], dtt.unsqueeze(3).broadcast_to((128, NE1, JB, NST)),
                a_sbb[:].rearrange("p (o n) -> p o n", o=JB).unsqueeze(1)
                .broadcast_to((128, NE1, JB, NST)), ALU.mult))

        @block.scalar
        def _(s, NE1=NE1):
            p.wait(s, "ve")
            p.inc("se", s.activation(dAx[:, 0:NE1 * 128], dAx[:, 0:NE1 * 128],
                                     AF.Tanh, scale=K8))
            p.inc("se", s.activation(dAx[:, 0:NE1 * 128], dAx[:, 0:NE1 * 128],
                                     AF.Exp, scale=16.0))

        @block.vector
        def _(v, screl=screl, NE1=NE1, w4=w4, u4=u4, us_c=us_c, br_c=br_c, dA4=dA4):
            p.wait(v, "se")
            utt = us_c[:, :, screl:screl + NE1].transpose([0, 2, 1])
            p.inc("ve", v.tensor_tensor(
                w4[:, 0:NE1], utt.unsqueeze(3).broadcast_to((128, NE1, JB, NST)),
                br_c[:, :, screl:screl + NE1].transpose([0, 2, 1]).unsqueeze(2)
                .broadcast_to((128, NE1, JB, NST)), ALU.mult))
            p.inc("ve", v.tensor_tensor(u4, dA4[:, 1:SCH + 1], w4[:, 0:SCH], ALU.mult))
            p.inc("ve", v.tensor_tensor(
                sc0b[:, 0:JB * SCH].rearrange("p (o i) -> p o i", i=SCH),
                us_c[:, :, screl:screl + SCH],
                us_c[:, :, screl:screl + SCH],
                ALU.mult))

        @block.tensor
        def _(te, ch=ch):
            p.wait(te, "ve")
            p.inc("pe", te.matmul(ps[7][:, 0:JB * SCH], ones_sqb[:],
                                  sc0b[:, 0:JB * SCH], start=True, stop=True))

        @block.vector
        def _(v, screl=screl, br_c=br_c):
            p.wait(v, "pe")
            p.inc("ve", v.tensor_reduce(
                cuv[:],
                ps[7][:, 0:JB * SCH].rearrange("p (o i) -> p i o", o=JB),
                AX.X, ALU.add))
            p.inc("ve", v.tensor_tensor(
                sc1b[:, 0:NST * SCH].rearrange("p (n i) -> p n i", i=SCH),
                br_c[:, :, screl:screl + SCH],
                br_c[:, :, screl:screl + SCH],
                ALU.mult))
            p.inc("ve", v.tensor_reduce(
                cbv[:],
                sc1b[:, 0:NST * SCH].rearrange("p (n i) -> p i n", n=NST),
                AX.X, ALU.add))
            p.inc("ve", v.tensor_mul(ccv[:], cuv[:], cbv[:]))

        for tl in range(SCH):
            t = t0 + tl
            par = t % 2
            npar = (t + 1) % 2
            last = (t == T - 1)
            se_sqrt = p.val("se") + 1
            marks = {}

            @block.vector
            def _(v, tl=tl, par=par, npar=npar, last=last, se_sqrt=se_sqrt, marks=marks):
                if not last:
                    p.inc("ve", v.tensor_mul(rbuf, dAx[:, (tl + 1) * 128:(tl + 2) * 128],
                                             qbuf[:, tl * 128:(tl + 1) * 128]))
                    p.inc("ve", v.scalar_tensor_tensor(
                        qbuf[:, (tl + 1) * 128:(tl + 2) * 128], rbuf,
                        srow[:, tl:tl + 1], udA[:, tl * 128:(tl + 1) * 128],
                        ALU.mult, ALU.add))
                    marks["q"] = p.val("ve")
                    with p.nc.allow_low_precision(reason="a/b partial sums bf16"):
                        p.inc("ve", v.scalar_tensor_tensor(
                            h_st[:], qbuf[:, (tl + 1) * 128:(tl + 2) * 128], 2.0,
                            wx[:, (tl + 1) * 128:(tl + 2) * 128], ALU.mult, ALU.mult,
                            accum_out=abq[:, 2 * npar + 1:2 * npar + 2]))
                        p.inc("ve", v.scalar_tensor_tensor(
                            h_st[:], qbuf[:, (tl + 1) * 128:(tl + 2) * 128], 1.0,
                            qbuf[:, (tl + 1) * 128:(tl + 2) * 128], ALU.mult, ALU.mult,
                            accum_out=abq[:, 2 * npar:2 * npar + 1]))
                p.wait(v, "pe")
                p.inc("ve", v.tensor_single_scalar(tq1[:], ps[5 + par][:, 0:1],
                                                   srow[:, tl:tl + 1], ALU.mult))
                p.inc("ve", v.tensor_tensor(tcomb[:], tq1[:], ps[5 + par][:, 1:2],
                                            ALU.add))
                p.inc("ve", v.tensor_scalar(tq1[:], tcomb[:], srow[:, tl:tl + 1],
                                            ccv[:, tl:tl + 1], ALU.mult, ALU.add))
                marks["x"] = p.val("ve")
                p.wait(v, "se", se_sqrt)
                p.inc("ve", v.tensor_scalar_add(rcol[:], rcol[:], 1e-8))
                p.inc("ve", v.reciprocal(rcol[:], rcol[:]))
                p.inc("ve", v.tensor_scalar(srow[:, tl + 1:tl + 2], rcol[:],
                                            20.0, 1.0, ALU.mult, ALU.min))

            @block.scalar
            def _(s, tl=tl, npar=npar, last=last, marks=marks):
                p.wait(s, "ve", marks["x"])
                p.inc("se", s.activation(rcol[:], tq1[:], AF.Sqrt))

            if not last:
                @block.tensor
                def _(te, npar=npar):
                    p.wait(te, "se")
                    p.wait(te, "ve")
                    p.inc("pe", te.matmul(ps[5 + npar][:, 0:2], ones_sqb[:],
                                          abq[:, 2 * npar:2 * npar + 2],
                                          start=True, stop=True))

        @block.vector
        def _(v, t0=t0, screl=screl, ch=ch, cr_c=cr_c):
            p.wait(v, "se")
            qb4 = qbuf.rearrange("p (t o n) -> p t o n", o=JB, n=NST)[:, 0:SCH]
            sb4 = srow[:, 0:SCH].unsqueeze(2).unsqueeze(3) \
                .broadcast_to((128, SCH, JB, NST))
            p.inc("ve", v.tensor_tensor(
                t1p.rearrange("p (t o n) -> p t o n", o=JB, n=NST),
                qb4, sb4, ALU.mult))
            p.inc("ve", v.tensor_tensor(t1p, t1p, wx[:, 0:SCH * 128], ALU.add))
            p.inc("ve", v.tensor_tensor(
                pcs.rearrange("p (t o n) -> p t o n", o=JB, n=NST),
                t1p.rearrange("p (t o n) -> p t o n", o=JB, n=NST),
                cr_c[:, :, screl:screl + SCH].transpose([0, 2, 1]).unsqueeze(2)
                .broadcast_to((128, SCH, JB, NST)), ALU.mult))
            p4 = pcs.rearrange("p (t o n) -> p t o n", o=JB, n=NST)
            for hw in (8, 4, 2, 1):
                p.inc("ve", v.tensor_tensor(p4[:, :, :, 0:hw], p4[:, :, :, 0:hw],
                                            p4[:, :, :, hw:2 * hw], ALU.add))
            ymb = sc1[:, 0:JB * SCH]
            p.inc("ve", v.tensor_tensor(
                ymb.rearrange("p (o t) -> p o t", o=JB),
                p4[:, :, :, 0].transpose([0, 2, 1]),
                srow[:, 1:SCH + 1].unsqueeze(1).broadcast_to((128, JB, SCH)),
                ALU.mult))
            xsl = x1res[:].rearrange("p (o t) -> p o t", o=JB)[:, :, t0:t0 + SCH]
            p.inc("ve", v.tensor_tensor(
                xsl, xsl, ymb.rearrange("p (o t) -> p o t", o=JB), ALU.add))
            if ch + 1 < NCHS:
                p.inc("ve", v.tensor_copy(qbuf[:, 0:128],
                                          qbuf[:, SCH * 128:(SCH + 1) * 128]))
                p.inc("ve", v.tensor_copy(srow[:, 0:1], srow[:, SCH:SCH + 1]))


    p.mark("l1_dp")
    for tc in range(TC if "dp" not in skip else 0):
        @block.vector
        def _(v, tc=tc):
            p.wait(v, "dw")
            p.wait(v, "ds")
            for j in range(JB):
                p.inc("ve", v.scalar_tensor_tensor(
                    x1res[:, fsl(j, tc)], xn1b[:, fsl(j, tc)], dp_sb[:, j:j + 1],
                    x1res[:, fsl(j, tc)], ALU.mult, ALU.add))

    p.mark("l1_moe")
    pe_after_expert = {}
    for ei in range((NEX + 1) if "moe" not in skip else 0):
        shared = (ei == NEX)
        buf = ei % 2

        @block.gpsimd
        def _(g, ei=ei, buf=buf, shared=shared):
            if ei < 2:
                p.wait(g, "pe")
            else:
                p.wait(g, "pe", pe_after_expert[ei - 2])
            for kb in range(JB):
                src = (w13sT.rearrange("(k p) o -> k p o", p=128)[kb] if shared
                       else w13T.rearrange("e (k p) o -> e k p o", p=128)[ei, kb])
                p.inc("dw", g.dma_start(out=m13buf[buf][:, kb * 1024:(kb + 1) * 1024], in_=src))
            for ib in range(4):
                src = (w2sT.rearrange("(k p) o -> k p o", p=128)[ib] if shared
                       else w2mT.rearrange("e (k p) o -> e k p o", p=128)[ei, ib])
                p.inc("dw", g.dma_start(out=m2buf[buf][:, ib * 1024:(ib + 1) * 1024], in_=src))

        for tc in range(TC):
            if not shared:
                @block.tensor
                def _(te, ei=ei, tc=tc):
                    p.wait(te, "ve")
                    p.wait(te, "se")
                    p.inc("pe", te.matmul(ps[0][:, 0:TW], Esel[:, ei * 128:(ei + 1) * 128],
                                          combT[:, tc * TW:(tc + 1) * TW],
                                          start=True, stop=True))

                @block.scalar
                def _(s, ei=ei, tc=tc):
                    p.wait(s, "pe")
                    p.inc("se", s.activation(crep_moe, ps[0][:, 0:TW], AF.Copy))

            @block.tensor
            def _(te, ei=ei, tc=tc, buf=buf):
                p.wait(te, "dw")
                p.wait(te, "se")
                p.wait(te, "ve")
                for ib in range(4):
                    for kb in range(JB):
                        p.inc("pe", te.matmul(
                            ps[ib][:, 0:TW],
                            m13buf[buf][:, kb * 1024 + ib * 128: kb * 1024 + ib * 128 + 128],
                            xn1b[:, fsl(kb, tc)], start=(kb == 0), stop=(kb == JB - 1)))
                for ib in range(4):
                    for kb in range(JB):
                        p.inc("pe", te.matmul(
                            ps[4 + ib][:, 0:TW],
                            m13buf[buf][:, kb * 1024 + 512 + ib * 128: kb * 1024 + 512 + ib * 128 + 128],
                            xn1b[:, fsl(kb, tc)], start=(kb == 0), stop=(kb == JB - 1)))

            for ib in range(4):
                @block.scalar
                def _(s, ei=ei, tc=tc, ib=ib):
                    p.wait(s, "pe")
                    p.inc("se", s.activation(sc0[:, 0:TW], ps[ib][:, 0:TW], AF.Silu,
                                             scale=1.0 / 127.0))

                @block.vector
                def _(v, ei=ei, tc=tc, ib=ib, shared=shared):
                    p.wait(v, "se")
                    if shared:
                        p.inc("ve", v.scalar_tensor_tensor(
                            hb_moe[:, ib * TW:(ib + 1) * TW], ps[4 + ib][:, 0:TW],
                            1.0 / 127.0, sc0[:, 0:TW], ALU.mult, ALU.mult))
                    else:
                        p.inc("ve", v.scalar_tensor_tensor(
                            sc1[:, 0:TW], ps[4 + ib][:, 0:TW],
                            1.0 / 127.0, sc0[:, 0:TW], ALU.mult, ALU.mult))
                        p.inc("ve", v.tensor_mul(hb_moe[:, ib * TW:(ib + 1) * TW],
                                                 sc1[:, 0:TW], crep_moe))

            @block.tensor
            def _(te, ei=ei, tc=tc, buf=buf):
                p.wait(te, "ve")
                for ob in range(JB):
                    for ib in range(4):
                        p.inc("pe", te.matmul(
                            ps[ob][:, 0:TW],
                            m2buf[buf][:, ib * 1024 + ob * 128: ib * 1024 + ob * 128 + 128],
                            hb_moe[:, ib * TW:(ib + 1) * TW],
                            start=(ib == 0), stop=(ib == 3)))

            @block.vector
            def _(v, ei=ei, tc=tc):
                p.wait(v, "pe")
                for ob in range(JB):
                    p.inc("ve", v.scalar_tensor_tensor(
                        x1res[:, fsl(ob, tc)], ps[ob][:, 0:TW], 1.0 / 127.0,
                        x1res[:, fsl(ob, tc)], ALU.mult, ALU.add))

        pe_after_expert[ei] = p.val("pe")

    p.mark(None)
    @block.sync
    def _(sync):
        p.wait(sync, "ve")
        p.inc("dx", sync.dma_start(
            out=out_t.rearrange("(j p) t -> p j t", p=128),
            in_=x1res[:].rearrange("p (j t) -> p j t", j=JB)))
        p.wait(sync, "dx")


def _mk_io_l0(nc, out_kind="Internal"):
    ap = lambda n, shp, dt=F32, kind="ExternalInput": nc.dram_tensor(n, shp, dt, kind=kind).ap()
    io = {
        "x_t": ap("x_t", [D, T]),
        "wqT": ap("wqT", [D, D]),
        "wkT": ap("wkT", [D, D]),
        "wvT": ap("wvT", [D, D]),
        "w1T": ap("w1T", [D, DFF]),
        "w2T": ap("w2T", [DFF, D]),
        "b1c": ap("b1c", [128, DFF // 128]),
        "b2c": ap("b2c", [128, JB]),
        "g1c": ap("g1c", [128, JB]),
        "be1c": ap("be1c", [128, JB]),
        "g2c": ap("g2c", [128, JB]),
        "be2c": ap("be2c", [128, JB]),
        "wgT": ap("wgT", [D, HID], I32),
        "wuT": ap("wuT", [D, HID], I32),
        "wdT": ap("wdT", [HID, D], I32),
    }
    return io


def _mk_io_l1(nc, with_x1):
    ap = lambda n, shp, dt=F32, kind="ExternalInput": nc.dram_tensor(n, shp, dt, kind=kind).ap()
    io = {
        "WdtT": ap("WdtT", [D, D]),
        "bdtc": ap("bdtc", [128, JB]),
        "WBT": ap("WBT", [D, NST]),
        "WCT": ap("WCT", [D, NST]),
        "A_logc": ap("A_logc", [128, 128]),
        "Dpc": ap("Dpc", [128, JB]),
        "gwT": ap("gwT", [D, NEX]),
        "gbr": ap("gbr", [1, NEX]),
        "w13T": ap("w13T", [NEX, D, D]),
        "w2mT": ap("w2mT", [NEX, INTER, D]),
        "w13sT": ap("w13sT", [D, D]),
        "w2sT": ap("w2sT", [INTER, D]),
        "out_t": ap("out_t", [D, T], kind="ExternalOutput"),
    }
    if with_x1:
        io["x1_t"] = ap("x1_t", [D, T])
    return io


def build_full(skip=(), debug_taps=False):
    nc = bass.Bass("TRN2", target_bir_lowering=False)
    io0 = _mk_io_l0(nc)
    io1 = _mk_io_l1(nc, with_x1=False)
    dbg = None
    if debug_taps:
        dbg = nc.dram_tensor("dbg_t", [D, T], F32, kind="ExternalOutput").ap()
    with ExitStack() as ctx:
        e = ctx.enter_context
        p = Prog(nc, ctx)
        sb = _alloc(nc, e)
        _emit_l0(p, io0, sb)
        if debug_taps:
            block = p.block
            @block.sync
            def _(sync):
                p.wait(sync, "ve")
                p.wait(sync, "se")
                p.wait(sync, "pe")
                p.inc("dx", sync.dma_start(
                    out=dbg.rearrange("(j p) t -> p j t", p=128),
                    in_=sb["xres"][:].rearrange("p (j t) -> p j t", j=JB)))
                p.wait(sync, "dx")
        _emit_l1(p, io1, sb, fused=True, skip=skip)
    return nc


def build_l0():
    nc = bass.Bass("TRN2", target_bir_lowering=False)
    io0 = _mk_io_l0(nc)
    x1_t = nc.dram_tensor("x1_t", [D, T], F32, kind="ExternalOutput").ap()
    with ExitStack() as ctx:
        e = ctx.enter_context
        p = Prog(nc, ctx)
        sb = _alloc(nc, e)
        _emit_l0(p, io0, sb)
        block = p.block

        @block.sync
        def _(sync):
            p.wait(sync, "ve")
            p.inc("dx", sync.dma_start(
                out=x1_t.rearrange("(j p) t -> p j t", p=128),
                in_=sb["xres"][:].rearrange("p (j t) -> p j t", j=JB)))
            p.wait(sync, "dx")
    return nc


def build_l1():
    nc = bass.Bass("TRN2", target_bir_lowering=False)
    io1 = _mk_io_l1(nc, with_x1=True)
    with ExitStack() as ctx:
        e = ctx.enter_context
        p = Prog(nc, ctx)
        sb = _alloc(nc, e)
        _emit_l1(p, io1, sb, fused=False)
    return nc



def _host_weights(inputs):
    f = np.ascontiguousarray
    return {
        "wqT": f(inputs["L0_attn_wq"].T),
        "wkT": f(inputs["L0_attn_wk"].T),
        "wvT": f(inputs["L0_attn_wv"].T),
        "w1T": f(inputs["L0_attn_w1"].T),
        "w2T": f(inputs["L0_attn_w2"].T),
        "b1c": f(inputs["L0_attn_b1"].reshape(DFF // 128, 128).T),
        "b2c": f(inputs["L0_attn_b2"].reshape(JB, 128).T),
        "g1c": f(inputs["L0_attn_g1"].reshape(JB, 128).T),
        "be1c": f(inputs["L0_attn_be1"].reshape(JB, 128).T),
        "g2c": f(inputs["L0_attn_g2"].reshape(JB, 128).T),
        "be2c": f(inputs["L0_attn_be2"].reshape(JB, 128).T),
        "wgT": f(inputs["L0_ffn_w_gate"].T),
        "wuT": f(inputs["L0_ffn_w_up"].T),
        "wdT": f(inputs["L0_ffn_w_down"].T),
        "WdtT": f(inputs["L1_attn_Wdt"].T),
        "bdtc": f(inputs["L1_attn_bdt"].reshape(JB, 128).T),
        "WBT": f(inputs["L1_attn_WB"].T),
        "WCT": f(inputs["L1_attn_WC"].T),
        "A_logc": f(inputs["L1_attn_A_log"].reshape(JB, 128, NST).transpose(1, 0, 2).reshape(128, 128)),
        "Dpc": f(inputs["L1_attn_Dp"].reshape(JB, 128).T),
        "gwT": f(inputs["L1_ffn_gate_w"].T),
        "gbr": f(inputs["L1_ffn_gate_b"].reshape(1, NEX)),
        "w13T": f(inputs["L1_ffn_w13"].transpose(0, 2, 1)),
        "w2mT": f(inputs["L1_ffn_w2"].transpose(0, 2, 1)),
        "w13sT": f(inputs["L1_ffn_w13_s"].T),
        "w2sT": f(inputs["L1_ffn_w2_s"].T),
    }


def _l0_host_inputs(inputs):
    w = {k: v for k, v in _host_weights(inputs).items()
         if k in ("wqT", "wkT", "wvT", "w1T", "w2T", "b1c", "b2c", "g1c", "be1c",
                  "g2c", "be2c", "wgT", "wuT", "wdT")}
    x = np.asarray(inputs["L0_attn_x"])
    return [dict(w, x_t=np.ascontiguousarray(x[c].T)) for c in range(8)]


def _l1_host_inputs(inputs, x1_list):
    w = {k: v for k, v in _host_weights(inputs).items()
         if k not in ("wqT", "wkT", "wvT", "w1T", "w2T", "b1c", "b2c", "g1c",
                      "be1c", "g2c", "be2c", "wgT", "wuT", "wdT")}
    return [dict(w, x1_t=np.ascontiguousarray(x1_list[c])) for c in range(8)]


def kernel(**inputs):
    _ = inputs.get("L1_ffn_input_ids")
    nc = build_full()
    w = _host_weights(inputs)
    x = np.asarray(inputs["L0_attn_x"])
    in_maps = [dict(w, x_t=np.ascontiguousarray(x[c].T)) for c in range(8)]
    res = run_bass_kernel_spmd(nc, in_maps, core_ids=list(range(8)))
    out = np.stack([np.asarray(r["out_t"]).T for r in res.results], axis=0)
    return out.astype(np.float32)


# version 09
# speedup vs baseline: 1.0970x; 1.0970x over previous
import numpy as np
import ml_dtypes
from contextlib import ExitStack

import concourse.bass as bass
import concourse.mybir as mybir
from concourse.bass_utils import run_bass_kernel_spmd

F32 = mybir.dt.float32
BF16 = mybir.dt.bfloat16
I32 = mybir.dt.int32
AF = mybir.ActivationFunctionType
ALU = mybir.AluOpType
AX = mybir.AxisListType
BT = ml_dtypes.bfloat16

D = 1024
T = 2048
JB = 8
NH = 8
DFF = 4096
HID = 3072
EPS_LN = 1e-5
EPS_ATTN = 1e-6
TW = 512
TC = T // TW
NST = 16
TCH = 64
NCH = T // TCH
INTER = 512
NEX = 16
K8 = 0.5 / (8.0 + 1e-8)
K5 = 0.5 / (5.0 + 1e-8)
KDT = 0.5 / (0.0495 + 1e-8)


class Prog:
    def __init__(self, nc, ctx):
        self.nc = nc
        self.ctx = ctx
        self.block = ctx.enter_context(nc.Block())
        self._sems = {}

    def sem(self, name):
        if name not in self._sems:
            h = self.ctx.enter_context(self.nc.semaphore(name))
            self._sems[name] = [h, 0]
        return self._sems[name]

    def inc(self, name, inst, k=1):
        if name.startswith("d"):
            k *= 16
        s = self.sem(name)
        s[1] += k
        inst.then_inc(s[0], k)
        return s[1]

    def val(self, name):
        return self.sem(name)[1]

    def wait(self, eng, name, v=None):
        s = self.sem(name)
        v = s[1] if v is None else v
        if v > 0:
            eng.wait_ge(s[0], v)

    _scope = None

    def mark(self, name):
        if self._scope is not None:
            self.nc.leave_named_scope(self._scope[0], self._scope[1], False)
            self._scope = None
        if name is not None:
            sid, _ = self.nc.enter_named_scope(name, False)
            self._scope = (name, sid)


def fsl(kb, tc, w=TW):
    return slice(kb * T + tc * w, kb * T + (tc + 1) * w)


def emit_ln(p, r, src_bf, val_src, dst_bf, gcol=None, bcol=None):
    block = p.block
    ps = r["ps"]
    for tc in range(TC):
        for kb in range(JB):
            @block.scalar
            def _(s, kb=kb, tc=tc):
                p.wait(s, "pe")
                if kb == 0:
                    p.wait(s, "dw")
                    p.wait(s, "ds")
                    p.wait(s, "ve")
                p.inc("se", s.activation(r["sqb"][:, 0:TW], src_bf[:, fsl(kb, tc)], AF.Square))

            @block.tensor
            def _(te, kb=kb, tc=tc):
                p.wait(te, "se")
                if kb == 0:
                    p.wait(te, "ve")
                p.inc("pe", te.matmul(ps[7][0:1, 0:TW], r["ones_cb"][:], r["sqb"][:, 0:TW],
                                      start=(kb == 0), stop=(kb == JB - 1)))
                p.inc("pe", te.matmul(ps[6][0:1, 0:TW], r["ones_cb"][:], src_bf[:, fsl(kb, tc)],
                                      start=(kb == 0), stop=(kb == JB - 1)))

        @block.vector
        def _(v, tc=tc):
            p.wait(v, "pe")
            p.inc("ve", v.tensor_scalar_mul(r["row0"], ps[6][0:1, 0:TW], 1.0 / D))
            p.inc("ve", v.tensor_scalar_mul(r["row1"], ps[7][0:1, 0:TW], 1.0 / D))
            p.inc("ve", v.tensor_mul(r["row2"], r["row0"], r["row0"]))
            p.inc("ve", v.tensor_sub(r["row1"], r["row1"], r["row2"]))

        @block.scalar
        def _(s, tc=tc):
            p.wait(s, "ve")
            p.inc("se", s.activation(r["row1"], r["row1"], AF.Ln, bias=r["epsln"]))
            p.inc("se", s.activation(r["row1"], r["row1"], AF.Exp, scale=-0.5))

        @block.tensor
        def _(te, tc=tc):
            p.wait(te, "se")
            p.wait(te, "ve")
            p.inc("pe", te.matmul(ps[4][:, 0:TW], r["ones_r32"][:], r["row0"],
                                  start=True, stop=True))
            p.inc("pe", te.matmul(ps[5][:, 0:TW], r["ones_r32"][:], r["row1"],
                                  start=True, stop=True))

        @block.vector
        def _(v, tc=tc):
            p.wait(v, "pe")
            for kb in range(JB):
                vs = val_src(kb, tc)
                p.inc("ve", v.tensor_sub(r["sc0"][:, 0:TW], vs, ps[4][:, 0:TW]))
                if gcol is None:
                    p.inc("ve", v.tensor_mul(dst_bf[:, fsl(kb, tc)], r["sc0"][:, 0:TW], ps[5][:, 0:TW]))
                else:
                    p.inc("ve", v.tensor_mul(r["sc0"][:, 0:TW], r["sc0"][:, 0:TW], ps[5][:, 0:TW]))
                    p.inc("ve", v.tensor_scalar(dst_bf[:, fsl(kb, tc)], r["sc0"][:, 0:TW],
                                                gcol[:, kb:kb + 1], bcol[:, kb:kb + 1],
                                                ALU.mult, ALU.add))


def emit_elu_epilogue(p, ps_ap, dst_ap, r, w=TW):
    block = p.block

    @block.vector
    def _(v):
        p.wait(v, "pe")
        p.inc("ve", v.tensor_scalar_min(r["sc0"][:, 0:w], ps_ap, 0.0))
        p.inc("ve", v.tensor_sub(r["sc1"][:, 0:w], ps_ap, r["sc0"][:, 0:w]))

    @block.scalar
    def _(s):
        p.wait(s, "ve")
        p.inc("se", s.activation(r["sc2"][:, 0:w], r["sc0"][:, 0:w], AF.Exp))

    @block.vector
    def _(v):
        p.wait(v, "se")
        p.inc("ve", v.tensor_add(dst_ap, r["sc1"][:, 0:w], r["sc2"][:, 0:w]))


def _alloc(nc, e):
    sb = {}
    t = lambda name, shp, dt=F32: e(nc.sbuf_tensor(name, shp, dt))
    sb["xres"] = t("xres", [128, JB * T], F32)
    sb["xnb"] = t("xnb", [128, JB * T], BF16)
    sb["A1"] = t("A1", [128, JB * T], BF16)
    sb["A2"] = t("A2", [128, JB * T], BF16)
    sb["A3"] = t("A3", [128, JB * T], BF16)
    sb["sc0"] = t("sc0", [128, TW], F32)
    sb["sc1"] = t("sc1", [128, TW], F32)
    sb["sc2"] = t("sc2", [128, TW], F32)
    _w1 = t("w1buf", [128, JB * 128], BF16)
    _w2 = t("w2buf", [128, 1024], BF16)
    sb["w1buf"] = [_w1, _w1]
    sb["w2buf"] = [_w2, _w2]
    for n, w in (("b1s", DFF // 128), ("b2s", JB), ("g1s", JB), ("be1s", JB),
                 ("g2s", JB), ("be2s", JB), ("bdt_sb", JB), ("dp_sb", JB)):
        sb[n] = t(n, [128, w], F32)
    sb["ones_r32"] = t("ones_r32", [1, 128], F32)
    sb["ones_cb"] = t("ones_cb", [128, 1], BF16)
    sb["ones_cb2"] = t("ones_cb2", [128, 2], BF16)
    sb["epsc"] = t("epsc", [1, 1], F32)
    sb["idb"] = t("idb", [128, 128], F32)
    sb["ones_sq"] = sb["idb"]
    sb["pcol"] = t("pcol", [128, 1], F32)
    sb["a_sb"] = t("a_sb", [128, 128], F32)
    sb["a_sbb"] = t("a_sbb", [128, 128], BF16)
    sb["h_st"] = t("h_st", [128, 128], F32)
    sb["iof"] = sb["h_st"]
    sb["srow"] = t("srow", [128, 34], F32)
    sb["pm2"] = t("pm2", [128, 128], BF16)
    sb["abq"] = t("abq", [128, 4], BF16)
    sb["tq1"] = t("tq1", [128, 1], F32)
    sb["tcomb"] = t("tcomb", [128, 1], F32)
    sb["ones_sqb"] = t("ones_sqb", [128, 128], BF16)
    sb["wb_sb"] = t("wb_sb", [128, JB * NST], BF16)
    sb["wc_sb"] = t("wc_sb", [128, JB * NST], BF16)
    sb["wg_sb"] = t("wg_sb", [128, JB * NEX], BF16)
    sb["gb_rep"] = t("gb_rep", [128, NEX], F32)
    sb["gbr_sb"] = t("gbr_sb", [1, NEX], F32)
    for n in ("rcol", "mcol", "wcol"):
        sb[n] = t(n, [128, 1], F32)
    for n in ("rsc", "bia", "bwk", "selw", "comb_sc"):
        sb[n] = t(n, [128, NEX], F32)
    sb["ps"] = [e(nc.psum_tensor(f"ps{i}", [128, TW], F32)) for i in range(8)]
    return sb


def _emit_l0(p, io, sb):
    block = p.block
    x_t, wqT, wkT, wvT, w1T, w2T = io["x_t"], io["wqT"], io["wkT"], io["wvT"], io["w1T"], io["w2T"]
    b1c, b2c, g1c, be1c, g2c, be2c = io["b1c"], io["b2c"], io["g1c"], io["be1c"], io["g2c"], io["be2c"]
    wgT, wuT, wdT = io["wgT"], io["wuT"], io["wdT"]
    xres, xnb, A1, A2, A3 = sb["xres"], sb["xnb"], sb["A1"], sb["A2"], sb["A3"]
    sc0, sc1, sc2 = sb["sc0"], sb["sc1"], sb["sc2"]
    w1buf, w2buf = sb["w1buf"], sb["w2buf"]
    b1s, b2s, g1s, be1s, g2s, be2s = sb["b1s"], sb["b2s"], sb["g1s"], sb["be1s"], sb["g2s"], sb["be2s"]
    ones_r32, ones_cb, ones_cb2, epsc = sb["ones_r32"], sb["ones_cb"], sb["ones_cb2"], sb["epsc"]
    ps = sb["ps"]

    xrv = xres[:].bitcast(BF16)
    ktile = xrv[:, 0:1024]
    vtile = xrv[:, 1024:2048]
    kvb = xrv[:, 2048:3072]
    ksb = xrv[:, 3072:3072 + NH]
    wqbuf = xrv[:, 4096:4096 + 1024]
    xrvf = xres[:]
    row0 = xrvf[0:1, 4096:4096 + TW]
    row1 = xrvf[0:1, 4608:4608 + TW]
    row2 = xrvf[0:1, 5120:5120 + TW]
    zrow = xrvf[0:1, 5632:5632 + TW]
    sqb = xrv[:, 12288:12288 + TW]
    wk_sb = A2[:, 0:JB * D]
    wv_sb = A2[:, JB * D:2 * JB * D]

    r = dict(ps=ps, sqb=sqb, sc0=sc0, sc1=sc1, sc2=sc2, row0=row0, row1=row1,
             row2=row2, ones_r32=ones_r32, ones_cb=ones_cb, epsln=epsc[0:1, 0:1])

    @block.gpsimd
    def _(g):
        p.inc("dw", g.dma_start(
            out=A3[:].rearrange("p (j t) -> p j t", j=JB),
            in_=x_t.rearrange("(j p) t -> p j t", p=128)))
        for w, dst in ((wkT, wk_sb), (wvT, wv_sb)):
            p.inc("dw", g.dma_start(
                out=dst.rearrange("p (k o) -> p k o", k=JB),
                in_=w.rearrange("(k p) o -> p k o", p=128)))

    @block.sync
    def _(sync):
        for src, dst in ((b1c, b1s), (b2c, b2s), (g1c, g1s), (be1c, be1s),
                         (g2c, g2s), (be2c, be2s)):
            p.inc("ds", sync.dma_start(out=dst[:], in_=src))

    @block.vector
    def _(v):
        p.inc("ve", v.memset(ones_r32[:], 1.0))
        p.inc("ve", v.memset(ones_cb[:], 1.0))
        p.inc("ve", v.memset(ones_cb2[:], 1.0))
        p.inc("ve", v.memset(epsc[:], EPS_LN))

    p.mark("l0_ln0")
    emit_ln(p, r, A3, lambda kb, tc: A3[:, fsl(kb, tc)], xnb)

    p.mark("l0_ln1")
    emit_ln(p, r, xnb, lambda kb, tc: xnb[:, fsl(kb, tc)], A1, gcol=g1s, bcol=be1s)

    p.mark("l0_kv")
    NT = T // 128
    for tt in range(NT):
        for half in range(2):
            @block.tensor
            def _(te, tt=tt, half=half):
                p.wait(te, "ve")
                p.wait(te, "se")
                for kb in range(JB):
                    p.inc("pe", te.matmul(
                        ps[half][:, 0:TW],
                        A1[:, kb * T + tt * 128: kb * T + (tt + 1) * 128],
                        wk_sb[:, kb * D + half * TW: kb * D + (half + 1) * TW],
                        start=(kb == 0), stop=(kb == JB - 1)))
            emit_elu_epilogue(p, ps[half][:, 0:TW], ktile[:, half * TW:(half + 1) * TW], r)

        for half in range(2):
            @block.tensor
            def _(te, tt=tt, half=half):
                p.wait(te, "ve")
                p.wait(te, "se")
                for kb in range(JB):
                    p.inc("pe", te.matmul(
                        ps[half][:, 0:TW],
                        A1[:, kb * T + tt * 128: kb * T + (tt + 1) * 128],
                        wv_sb[:, kb * D + half * TW: kb * D + (half + 1) * TW],
                        start=(kb == 0), stop=(kb == JB - 1)))

            @block.scalar
            def _(s, tt=tt, half=half):
                p.wait(s, "pe")
                p.inc("se", s.activation(vtile[:, half * TW:(half + 1) * TW],
                                         ps[half][:, 0:TW], AF.Copy))

        @block.tensor
        def _(te, tt=tt):
            p.wait(te, "se")
            p.wait(te, "ve")
            for h in range(NH):
                pbank = ps[2] if h < 4 else ps[3]
                col = (h % 4) * 128
                p.inc("pe", te.matmul(
                    pbank[:, col:col + 128],
                    ktile[:, h * 128:(h + 1) * 128],
                    vtile[:, h * 128:(h + 1) * 128],
                    start=(tt == 0 and (h % 4) == 0),
                    stop=(tt == NT - 1 and (h % 4) == 3)))
            for h in range(NH):
                p.inc("pe", te.matmul(
                    ps[6][:, 2 * h:2 * h + 2],
                    ktile[:, h * 128:(h + 1) * 128],
                    ones_cb2[:],
                    start=(tt == 0 and h == 0), stop=(tt == NT - 1 and h == NH - 1)))

    @block.scalar
    def _(s):
        p.wait(s, "pe")
        p.inc("se", s.activation(kvb[:, 0:TW], ps[2][:, 0:TW], AF.Copy))
        p.inc("se", s.activation(kvb[:, TW:2 * TW], ps[3][:, 0:TW], AF.Copy))
        p.inc("se", s.activation(ksb, ps[6][:, 0:2 * NH:2], AF.Copy))

    p.mark("l0_q")
    for h in range(NH):
        @block.gpsimd
        def _(g, h=h):
            p.wait(g, "pe")
            for kq in range(JB):
                p.inc("dw", g.dma_start(
                    out=wqbuf[:, kq * 128:(kq + 1) * 128],
                    in_=wqT.rearrange("(k p) o -> k p o", p=128)[kq, :, h * 128:(h + 1) * 128]))

        for tc in range(TC):
            bank = tc % 2
            @block.tensor
            def _(te, h=h, tc=tc, bank=bank):
                p.wait(te, "dw")
                p.wait(te, "ve")
                for kb in range(JB):
                    p.inc("pe", te.matmul(
                        ps[bank][:, 0:TW],
                        wqbuf[:, kb * 128:(kb + 1) * 128],
                        A1[:, fsl(kb, tc)],
                        start=(kb == 0), stop=(kb == JB - 1)))
            emit_elu_epilogue(p, ps[bank][:, 0:TW], A2[:, fsl(h, tc)], r)

    p.mark("l0_att")
    for h in range(NH):
        for tc in range(TC):
            @block.tensor
            def _(te, h=h, tc=tc):
                p.wait(te, "se")
                p.wait(te, "ve")
                p.inc("pe", te.matmul(ps[6][0:1, 0:TW], ksb[:, h:h + 1],
                                      A2[:, fsl(h, tc)], start=True, stop=True))

            @block.vector
            def _(v, h=h, tc=tc):
                p.wait(v, "pe")
                p.inc("ve", v.tensor_scalar_add(zrow, ps[6][0:1, 0:TW], EPS_ATTN))
                p.inc("ve", v.reciprocal(zrow, zrow))

            @block.tensor
            def _(te, h=h, tc=tc):
                p.wait(te, "ve")
                p.inc("pe", te.matmul(ps[4][:, 0:TW], ones_r32[:], zrow,
                                      start=True, stop=True))
                p.inc("pe", te.matmul(ps[5][:, 0:TW],
                                      kvb[:, h * 128:(h + 1) * 128],
                                      A2[:, fsl(h, tc)], start=True, stop=True))

            @block.scalar
            def _(s, h=h, tc=tc):
                p.wait(s, "pe")
                p.inc("se", s.activation(sc0[:, 0:TW], ps[4][:, 0:TW], AF.Copy))

            @block.vector
            def _(v, h=h, tc=tc):
                p.wait(v, "se")
                p.inc("ve", v.tensor_mul(sc1[:, 0:TW], ps[5][:, 0:TW], sc0[:, 0:TW]))
                p.inc("ve", v.tensor_add(A3[:, fsl(h, tc)], sc1[:, 0:TW], xnb[:, fsl(h, tc)]))

    p.mark("l0_ln2")
    emit_ln(p, r, A3, lambda kb, tc: A3[:, fsl(kb, tc)], A1, gcol=g2s, bcol=be2s)

    @block.sync
    def _(sync):
        p.wait(sync, "pe")
        p.wait(sync, "ve")
        p.wait(sync, "se")
        p.inc("dx", sync.dma_start(
            out=xres[:].rearrange("p (j t) -> p j t", j=JB),
            in_=x_t.rearrange("(j p) t -> p j t", p=128)))

    p.mark("l0_ffnA")
    NFB = DFF // 128
    pe_fb = {}
    for tc in range(TC):
        for fb in range(NFB):
            buf = fb % 2
            wsel = [w1buf[0], w2buf[0]]
            @block.gpsimd
            def _(g, tc=tc, fb=fb, buf=buf, wsel=wsel):
                if fb < 2:
                    if tc == 0:
                        p.wait(g, "pe")
                    else:
                        p.wait(g, "pe", pe_fb[("a2", tc * NFB - 1)])
                else:
                    p.wait(g, "pe", pe_fb[tc * NFB + fb - 2])
                for kq in range(JB):
                    p.inc("dw", g.dma_start(
                        out=wsel[buf][:, kq * 128:(kq + 1) * 128],
                        in_=w1T.rearrange("(k p) o -> k p o", p=128)[kq, :, fb * 128:(fb + 1) * 128]))

            @block.tensor
            def _(te, tc=tc, fb=fb, buf=buf, wsel=wsel):
                p.wait(te, "dw")
                p.wait(te, "se")
                if fb == 0:
                    p.wait(te, "ve")
                for kb in range(JB):
                    p.inc("pe", te.matmul(
                        ps[7][:, 0:TW], wsel[buf][:, kb * 128:(kb + 1) * 128],
                        A1[:, fsl(kb, tc)],
                        start=(kb == 0), stop=(kb == JB - 1)))
                pe_fb[tc * NFB + fb] = p.val("pe")

            @block.scalar
            def _(s, tc=tc, fb=fb):
                p.wait(s, "pe")
                p.inc("se", s.activation(A2[:, fb * TW:(fb + 1) * TW], ps[7][:, 0:TW],
                                         AF.Gelu, bias=b1s[:, fb:fb + 1], scale=1.0))

        for fb in range(NFB):
            buf = fb % 2
            wsel2 = [w2buf[0], w1buf[0]]
            @block.gpsimd
            def _(g, tc=tc, fb=fb, buf=buf, wsel2=wsel2):
                if fb < 2:
                    p.wait(g, "pe", pe_fb[tc * NFB + NFB - 1])
                else:
                    p.wait(g, "pe", pe_fb[("a2", tc * NFB + fb - 2)])
                p.inc("dw", g.dma_start(
                    out=wsel2[buf][:],
                    in_=w2T.rearrange("(f p) o -> f p o", p=128)[fb]))

            @block.tensor
            def _(te, tc=tc, fb=fb, buf=buf, wsel2=wsel2):
                p.wait(te, "dw")
                if fb == 0:
                    p.wait(te, "ve")
                    p.wait(te, "se")
                for ob in range(JB):
                    p.inc("pe", te.matmul(
                        ps[ob][:, 0:TW], wsel2[buf][:, ob * 128:(ob + 1) * 128],
                        A2[:, fb * TW:(fb + 1) * TW],
                        start=(fb == 0), stop=(fb == NFB - 1)))
                pe_fb[("a2", tc * NFB + fb)] = p.val("pe")

        @block.vector
        def _(v, tc=tc):
            p.wait(v, "pe")
            p.wait(v, "dx")
            for ob in range(JB):
                p.inc("ve", v.scalar_tensor_tensor(
                    sc0[:, 0:TW], ps[ob][:, 0:TW], b2s[:, ob:ob + 1],
                    A3[:, fsl(ob, tc)], ALU.add, ALU.add))
                p.inc("ve", v.tensor_add(xres[:, fsl(ob, tc)], xres[:, fsl(ob, tc)],
                                         sc0[:, 0:TW]))

    p.mark("l0_ffnB")
    NHB = HID // 128
    wgB = [w1buf[0], A2[:, 12288:13312]]
    wuB = [w2buf[0], A2[:, 13312:14336]]
    for tc in range(TC):
        for fb in range(NHB):
            buf = fb % 2
            @block.gpsimd
            def _(g, tc=tc, fb=fb, buf=buf):
                if fb < 2:
                    if tc == 0:
                        p.wait(g, "pe")
                        p.wait(g, "ve")
                    else:
                        p.wait(g, "pe", pe_fb[("b2", tc * NHB - 1)])
                else:
                    p.wait(g, "pe", pe_fb[("b1", tc * NHB + fb - 2)])
                for kq in range(JB):
                    p.inc("dw", g.dma_start(
                        out=wgB[buf][:, kq * 128:(kq + 1) * 128],
                        in_=wgT.rearrange("(k p) o -> k p o", p=128)[kq, :, fb * 128:(fb + 1) * 128]))
                for kq in range(JB):
                    p.inc("dw", g.dma_start(
                        out=wuB[buf][:, kq * 128:(kq + 1) * 128],
                        in_=wuT.rearrange("(k p) o -> k p o", p=128)[kq, :, fb * 128:(fb + 1) * 128]))

            @block.tensor
            def _(te, tc=tc, fb=fb, buf=buf):
                p.wait(te, "dw")
                p.wait(te, "se")
                p.wait(te, "ve")
                for kb in range(JB):
                    p.inc("pe", te.matmul(
                        ps[6][:, 0:TW], wgB[buf][:, kb * 128:(kb + 1) * 128],
                        xnb[:, fsl(kb, tc)],
                        start=(kb == 0), stop=(kb == JB - 1)))
                for kb in range(JB):
                    p.inc("pe", te.matmul(
                        ps[7][:, 0:TW], wuB[buf][:, kb * 128:(kb + 1) * 128],
                        xnb[:, fsl(kb, tc)],
                        start=(kb == 0), stop=(kb == JB - 1)))
                pe_fb[("b1", tc * NHB + fb)] = p.val("pe")

            @block.scalar
            def _(s, tc=tc, fb=fb):
                p.wait(s, "pe")
                p.inc("se", s.activation(sc0[:, 0:TW], ps[6][:, 0:TW], AF.Silu,
                                         scale=1.0 / 127.0))

            @block.vector
            def _(v, tc=tc, fb=fb):
                p.wait(v, "se")
                p.inc("ve", v.scalar_tensor_tensor(
                    A2[:, fb * TW:(fb + 1) * TW], ps[7][:, 0:TW], 1.0 / 127.0,
                    sc0[:, 0:TW], ALU.mult, ALU.mult))

        for fb in range(NHB):
            buf = fb % 2
            wselD = [w2buf[0], w1buf[0]]
            @block.gpsimd
            def _(g, tc=tc, fb=fb, buf=buf, wselD=wselD):
                if fb < 2:
                    p.wait(g, "pe", pe_fb[("b1", tc * NHB + NHB - 1)])
                else:
                    p.wait(g, "pe", pe_fb[("b2", tc * NHB + fb - 2)])
                p.inc("dw", g.dma_start(
                    out=wselD[buf][:],
                    in_=wdT.rearrange("(f p) o -> f p o", p=128)[fb]))

            @block.tensor
            def _(te, tc=tc, fb=fb, buf=buf, wselD=wselD):
                p.wait(te, "dw")
                if fb == 0:
                    p.wait(te, "ve")
                    p.wait(te, "se")
                for ob in range(JB):
                    p.inc("pe", te.matmul(
                        ps[ob][:, 0:TW], wselD[buf][:, ob * 128:(ob + 1) * 128],
                        A2[:, fb * TW:(fb + 1) * TW],
                        start=(fb == 0), stop=(fb == NHB - 1)))
                pe_fb[("b2", tc * NHB + fb)] = p.val("pe")

        @block.vector
        def _(v, tc=tc):
            p.wait(v, "pe")
            for ob in range(JB):
                p.inc("ve", v.scalar_tensor_tensor(
                    xres[:, fsl(ob, tc)], ps[ob][:, 0:TW], 1.0 / 127.0,
                    xres[:, fsl(ob, tc)], ALU.mult, ALU.add))


def _emit_l1(p, io, sb, fused, skip=()):
    block = p.block
    WdtT, bdtc, WBT, WCT = io["WdtT"], io["bdtc"], io["WBT"], io["WCT"]
    A_logc, Dpc, gwT, gbr = io["A_logc"], io["Dpc"], io["gwT"], io["gbr"]
    w13T, w2mT, w13sT, w2sT, out_t = io["w13T"], io["w2mT"], io["w13sT"], io["w2sT"], io["out_t"]
    x1res, xn1b, A1, A2, A3 = sb["xres"], sb["xnb"], sb["A1"], sb["A2"], sb["A3"]
    sc0, sc1, sc2 = sb["sc0"], sb["sc1"], sb["sc2"]
    ones_r32, ones_cb, ones_sq, epsc = sb["ones_r32"], sb["ones_cb"], sb["ones_sq"], sb["epsc"]
    idb, iof, pcol = sb["idb"], sb["iof"], sb["pcol"]
    a_sb, a_sbb, h_st = sb["a_sb"], sb["a_sbb"], sb["h_st"]
    srow, abq, tq1, tcomb = sb["srow"], sb["abq"], sb["tq1"], sb["tcomb"]
    pm2 = sb["pm2"]
    ones_sqb = sb["ones_sqb"]
    wb_sb, wc_sb, wg_sb = sb["wb_sb"], sb["wc_sb"], sb["wg_sb"]
    gb_rep, gbr_sb, bdt_sb, dp_sb = sb["gb_rep"], sb["gbr_sb"], sb["bdt_sb"], sb["dp_sb"]
    rcol = sb["rcol"]
    rsc, bia, bwk, mcol, selw, comb_sc, wcol = sb["rsc"], sb["bia"], sb["bwk"], sb["mcol"], sb["selw"], sb["comb_sc"], sb["wcol"]
    ps = sb["ps"]

    A1f = A1[:].bitcast(F32)
    A2f = A2[:].bitcast(F32)
    x1bf = A3[:, 0:JB * T]
    wdt_sb = A3[:, 0:JB * D]
    bT_sb = A3[0:NST, 8192:8192 + T]
    cT_sb = A3[0:NST, 10240:10240 + T]
    combT = A3[0:NST, 12288:12288 + T]
    Esel = A3[0:NST, 14336:14336 + NST * 128]
    io_sc = A2[:, 0:NST * 128]
    m13buf = [A1[:, 0:8192], A1[:, 8192:16384]]
    m2buf = [A2[:, 8192:12288], A2[:, 12288:16384]]
    hb_moe = A2[:, 0:2048]
    crep_moe = A2f[:, 1024:1024 + TW]
    row0 = A1f[0:1, 0:TW]
    row1 = A1f[0:1, TW:2 * TW]
    row2 = A1f[0:1, 2 * TW:3 * TW]
    sqb = sc2[:].bitcast(BF16)[:, 0:TW]

    r = dict(ps=ps, sqb=sqb, sc0=sc0, sc1=sc1, sc2=sc2, row0=row0, row1=row1,
             row2=row2, ones_r32=ones_r32, ones_cb=ones_cb, epsln=epsc[0:1, 0:1])

    p.mark("l1_setup")
    @block.sync
    def _(sync):
        if not fused:
            p.inc("ds", sync.dma_start(
                out=x1res[:].rearrange("p (j t) -> p j t", j=JB),
                in_=io["x1_t"].rearrange("(j p) t -> p j t", p=128)))
        p.inc("ds", sync.dma_start(out=a_sb[:], in_=A_logc))
        p.inc("ds", sync.dma_start(out=bdt_sb[:], in_=bdtc))
        p.inc("ds", sync.dma_start(out=dp_sb[:], in_=Dpc))
        p.inc("ds", sync.dma_start(out=gbr_sb[:], in_=gbr))

    @block.gpsimd
    def _(g):
        p.wait(g, "pe")
        p.wait(g, "ve")
        if not fused:
            p.inc("dw", g.dma_start(
                out=x1bf.rearrange("p (j t) -> p j t", j=JB),
                in_=io["x1_t"].rearrange("(j p) t -> p j t", p=128)))
        p.inc("dw", g.dma_start(
            out=wb_sb[:].rearrange("p (k o) -> p k o", k=JB),
            in_=WBT.rearrange("(k p) o -> p k o", p=128)))
        p.inc("dw", g.dma_start(
            out=wc_sb[:].rearrange("p (k o) -> p k o", k=JB),
            in_=WCT.rearrange("(k p) o -> p k o", p=128)))
        p.inc("dw", g.dma_start(
            out=wg_sb[:].rearrange("p (k o) -> p k o", k=JB),
            in_=gwT.rearrange("(k p) o -> p k o", p=128)))
        p.inc("go", g.iota(iof[:], [[1, 128]], base=0, channel_multiplier=0,
                           allow_small_or_imprecise_dtypes=True))
        p.inc("go", g.iota(pcol[:], [[0, 1]], base=0, channel_multiplier=1,
                           allow_small_or_imprecise_dtypes=True))
        p.inc("go", g.iota(io_sc, [[1, NST], [0, 128]], base=0, channel_multiplier=0,
                           allow_small_or_imprecise_dtypes=True))

    if fused:
        for kb in range(JB):
            @block.scalar
            def _(s, kb=kb):
                if kb == 0:
                    p.wait(s, "ve")
                    p.wait(s, "pe")
                p.inc("se", s.activation(x1bf[:, kb * T:(kb + 1) * T],
                                         x1res[:, kb * T:(kb + 1) * T], AF.Copy))

    @block.vector
    def _(v):
        p.wait(v, "go")
        p.inc("ve", v.memset(h_st[:], 0.0))
        p.inc("ve", v.tensor_single_scalar(idb[:], iof[:], pcol[:, 0:1], ALU.is_equal))
        if not fused:
            p.inc("ve", v.memset(ones_r32[:], 1.0))
            p.inc("ve", v.memset(ones_cb[:], 1.0))
            p.inc("ve", v.memset(epsc[:], EPS_LN))

    @block.tensor
    def _(te):
        p.wait(te, "dw")
        p.wait(te, "ds")
        p.wait(te, "ve")
        p.wait(te, "se")
        p.inc("pe", te.matmul(ps[0][:, 0:NEX], ones_r32[:], gbr_sb[:], start=True, stop=True))

    @block.scalar
    def _(s):
        p.wait(s, "pe")
        p.inc("se", s.activation(gb_rep[:], ps[0][:, 0:NEX], AF.Copy))

    @block.vector
    def _(v):
        p.wait(v, "dw")
        p.wait(v, "ds")
        p.inc("ve", v.tensor_scalar_mul(a_sb[:], a_sb[:], K5))

    @block.scalar
    def _(s):
        p.wait(s, "ve")
        p.inc("se", s.activation(a_sb[:], a_sb[:], AF.Tanh))

    @block.scalar
    def _(s):
        p.wait(s, "se")
        p.inc("se", s.activation(a_sb[:], a_sb[:], AF.Exp, scale=10.0))

    @block.vector
    def _(v):
        p.wait(v, "se")
        p.inc("ve", v.tensor_scalar_mul(a_sb[:], a_sb[:], -1.0))

    p.mark("l1_ln")
    emit_ln(p, r, x1bf, lambda kb, tc: x1res[:, fsl(kb, tc)], xn1b)

    @block.gpsimd
    def _(g):
        p.wait(g, "se")
        p.wait(g, "pe")
        p.wait(g, "ve")
        p.inc("dw", g.dma_start(
            out=wdt_sb.rearrange("p (k o) -> p k o", k=JB),
            in_=WdtT.rearrange("(k p) o -> p k o", p=128)))

    @block.vector
    def _(v):
        p.wait(v, "go")
        p.wait(v, "se")
        p.wait(v, "pe")
        p.inc("ve", v.tensor_single_scalar(Esel, io_sc[0:16, :], pcol[0:16, 0:1], ALU.is_equal))

    p.mark("l1_bc")
    for tc in range(TC):
        @block.tensor
        def _(te, tc=tc):
            p.wait(te, "ve")
            p.wait(te, "se")
            for kb in range(JB):
                p.inc("pe", te.matmul(ps[0][0:NST, 0:TW],
                                      wb_sb[:, kb * NST:(kb + 1) * NST],
                                      xn1b[:, fsl(kb, tc)],
                                      start=(kb == 0), stop=(kb == JB - 1)))
            for kb in range(JB):
                p.inc("pe", te.matmul(ps[1][0:NST, 0:TW],
                                      wc_sb[:, kb * NST:(kb + 1) * NST],
                                      xn1b[:, fsl(kb, tc)],
                                      start=(kb == 0), stop=(kb == JB - 1)))

        @block.scalar
        def _(s, tc=tc):
            p.wait(s, "pe")
            p.inc("se", s.activation(bT_sb[:, tc * TW:(tc + 1) * TW], ps[0][0:NST, 0:TW], AF.Copy))
            p.inc("se", s.activation(cT_sb[:, tc * TW:(tc + 1) * TW], ps[1][0:NST, 0:TW], AF.Copy))

    p.mark("l1_route")
    NT = T // 128
    for tt in range(NT):
        @block.tensor
        def _(te, tt=tt):
            p.wait(te, "se")
            p.wait(te, "ve")
            for kb in range(JB):
                p.inc("pe", te.matmul(ps[2][:, 0:NEX],
                                      xn1b[:, kb * T + tt * 128: kb * T + (tt + 1) * 128],
                                      wg_sb[:, kb * NEX:(kb + 1) * NEX],
                                      start=(kb == 0), stop=(kb == JB - 1)))

        @block.scalar
        def _(s, tt=tt):
            p.wait(s, "pe")
            p.inc("se", s.activation(rsc[:], ps[2][:, 0:NEX], AF.Sigmoid))

        @block.vector
        def _(v, tt=tt):
            p.wait(v, "se")
            p.inc("ve", v.tensor_add(bia[:], rsc[:], gb_rep[:]))
            p.inc("ve", v.tensor_copy(bwk[:], bia[:]))
            for rnd in range(4):
                p.inc("ve", v.reduce_max(mcol[:], bwk[:], AX.X))
                if rnd < 3:
                    p.inc("ve", v.tensor_single_scalar(selw[:], bwk[:], mcol[:, 0:1], ALU.is_equal))
                    p.inc("ve", v.scalar_tensor_tensor(bwk[:], selw[:], -1e30, bwk[:],
                                                       ALU.mult, ALU.add))
            p.inc("ve", v.tensor_single_scalar(selw[:], bia[:], mcol[:, 0:1], ALU.is_ge))
            p.inc("ve", v.tensor_mul(selw[:], rsc[:], selw[:]))
            p.inc("ve", v.tensor_reduce(wcol[:], selw[:], AX.X, ALU.add))
            p.inc("ve", v.reciprocal(wcol[:], wcol[:]))
            p.inc("ve", v.tensor_scalar(comb_sc[:], selw[:], wcol[:, 0:1], 2.5,
                                        ALU.mult, ALU.mult))

        @block.tensor
        def _(te, tt=tt):
            p.wait(te, "ve")
            p.inc("pe", te.transpose(ps[3][0:NEX, 0:128], comb_sc[:], idb[:]))

        @block.scalar
        def _(s, tt=tt):
            p.wait(s, "pe")
            p.inc("se", s.activation(combT[:, tt * 128:(tt + 1) * 128],
                                     ps[3][0:NEX, 0:128], AF.Copy))

    @block.vector
    def _(v):
        p.wait(v, "pe")
        p.inc("ve", v.memset(ones_sq[:], 1.0))

    p.mark("l1_scan")
    SCH = 32
    ESC = SCH + 1
    NCHS = T // SCH
    SCS = 128
    ESS = SCS + 1
    A1b = A1[:]
    A1f = A1[:].bitcast(F32)
    A2b = A2[:]
    A2f2 = A2[:].bitcast(F32)
    dAx = A1b[:, 0:ESC * 128]
    qbuf = A1b[:, 4224:4224 + ESC * 128]
    t1p = A1b[:, 8448:8448 + SCH * 128]
    pcs = A1b[:, 0:SCH * 128]
    dtraw = A1f[:, 0:1040]
    dtrw2 = A1f[:, 1040:2080]
    wx = A2b[:, 0:ESC * 128]
    udA = A2b[:, 4224:4224 + SCH * 128]
    dt_sc = A2b[:, 8320:8320 + JB * ESS]
    u_sc = A2b[:, 9352:9352 + JB * ESS]
    brep_sc = A2b[:, 10384:10384 + NST * ESS]
    crep_sc = A2b[:, 12448:12448 + NST * ESS]
    ccv = A2f2[:, 7256:7256 + SCH]
    cuv = A2f2[:, 7296:7296 + SCH]
    cbv = A2f2[:, 7336:7336 + SCH]
    sc0b = sc0[:].bitcast(BF16)
    sc1b = sc1[:].bitcast(BF16)
    rbuf = sc0b[:, 0:128]

    if "scan" not in skip:
        @block.scalar
        def _(s):
            p.wait(s, "ve")
            p.wait(s, "pe")
            p.inc("se", s.activation(a_sbb[:], a_sb[:], AF.Copy))

        @block.vector
        def _(v):
            p.wait(v, "se")
            p.inc("ve", v.memset(qbuf[:, 0:128].bitcast(F32), 0.0))
            p.inc("ve", v.memset(srow[:], 1.0))
            p.inc("ve", v.memset(abq[:].bitcast(F32), 0.0))
            p.inc("ve", v.memset(ones_sqb[:], 1.0))

        @block.tensor
        def _(te):
            p.wait(te, "ve")
            p.inc("pe", te.matmul(ps[5][:, 0:2], ones_sqb[:], abq[:, 0:2],
                                  start=True, stop=True))
            p.inc("pe", te.matmul(ps[6][:, 0:2], ones_sqb[:], abq[:, 2:4],
                                  start=True, stop=True))

    for ch in range(NCHS if "scan" not in skip else 0):
        t0 = ch * SCH
        screl = (ch % (SCS // SCH)) * SCH
        if screl == 0:
            st0 = (ch // (SCS // SCH)) * SCS
            NE = ESS if st0 + SCS < T else SCS

            @block.tensor
            def _(te, st0=st0, NE=NE):
                p.wait(te, "ve")
                p.wait(te, "se")
                p.wait(te, "dw")
                for ob in range(JB):
                    bank, off = divmod(ob, 3)
                    for kb in range(JB):
                        p.inc("pe", te.matmul(
                            ps[bank][:, off * 130:off * 130 + NE],
                            wdt_sb[:, kb * D + ob * 128: kb * D + ob * 128 + 128],
                            xn1b[:, kb * T + st0: kb * T + st0 + NE],
                            start=(off == 0 and kb == 0),
                            stop=(kb == JB - 1)))

            @block.vector
            def _(v, NE=NE):
                p.wait(v, "pe")
                for ob in range(JB):
                    bank, off = divmod(ob, 3)
                    p.inc("ve", v.tensor_single_scalar(
                        dtraw[:, ob * 130:ob * 130 + NE],
                        ps[bank][:, off * 130:off * 130 + NE],
                        bdt_sb[:, ob:ob + 1], ALU.add))

            dra = dtraw.rearrange("p (o i) -> p o i", i=130)[:, :, 0:NE]
            dr2 = dtrw2.rearrange("p (o i) -> p o i", i=130)[:, :, 0:NE]

            @block.scalar
            def _(s, dra=dra, dr2=dr2):
                p.wait(s, "ve")
                p.inc("se", s.activation(dr2, dra, AF.Abs))
                p.inc("se", s.activation(dr2, dr2, AF.Exp, scale=-1.0))
                p.inc("se", s.activation(dr2, dr2, AF.Ln, bias=1.0))

            @block.vector
            def _(v, dra=dra, dr2=dr2):
                p.wait(v, "se")
                p.inc("ve", v.tensor_scalar_max(dra, dra, 0.0))
                p.inc("ve", v.tensor_add(dra, dra, dr2))
                p.inc("ve", v.tensor_scalar(dra, dra, 0.0505, KDT,
                                            ALU.subtract, ALU.mult))

            @block.scalar
            def _(s, dra=dra):
                p.wait(s, "ve")
                p.inc("se", s.activation(dra, dra, AF.Tanh))

            @block.vector
            def _(v, st0=st0, NE=NE, dra=dra):
                p.wait(v, "se")
                dts = dt_sc.rearrange("p (o i) -> p o i", i=ESS)[:, :, 0:NE]
                p.inc("ve", v.tensor_scalar(dts, dra, 0.099, 0.0505,
                                            ALU.mult, ALU.add))
                p.inc("ve", v.tensor_tensor(
                    u_sc.rearrange("p (o i) -> p o i", i=ESS)[:, :, 0:NE],
                    dts,
                    xn1b[:].rearrange("p (o t) -> p o t", o=JB)[:, :, st0:st0 + NE],
                    ALU.mult))

            for n in range(NST):
                @block.tensor
                def _(te, st0=st0, NE=NE, n=n):
                    p.wait(te, "se")
                    if n == 0:
                        p.wait(te, "ve")
                    p.inc("pe", te.matmul(ps[3][:, 0:NE],
                                          Esel[:, n * 128:(n + 1) * 128],
                                          bT_sb[:, st0:st0 + NE], start=True, stop=True))
                    p.inc("pe", te.matmul(ps[4][:, 0:NE],
                                          Esel[:, n * 128:(n + 1) * 128],
                                          cT_sb[:, st0:st0 + NE], start=True, stop=True))

                @block.scalar
                def _(s, NE=NE, n=n):
                    p.wait(s, "pe")
                    p.inc("se", s.activation(brep_sc[:, n * ESS:n * ESS + NE],
                                             ps[3][:, 0:NE], AF.Copy))
                    p.inc("se", s.activation(crep_sc[:, n * ESS:n * ESS + NE],
                                             ps[4][:, 0:NE], AF.Copy))

        NE1 = ESC if t0 + SCH < T else SCH
        dts_c = dt_sc.rearrange("p (o i) -> p o i", i=ESS)
        us_c = u_sc.rearrange("p (o i) -> p o i", i=ESS)
        br_c = brep_sc.rearrange("p (n i) -> p n i", i=ESS)
        cr_c = crep_sc.rearrange("p (n i) -> p n i", i=ESS)
        dA4 = dAx.rearrange("p (t o n) -> p t o n", o=JB, n=NST)
        w4 = wx.rearrange("p (t o n) -> p t o n", o=JB, n=NST)
        u4 = udA.rearrange("p (t o n) -> p t o n", o=JB, n=NST)

        @block.vector
        def _(v, screl=screl, NE1=NE1, dA4=dA4, dts_c=dts_c):
            p.wait(v, "se")
            p.wait(v, "pe")
            dtt = dts_c[:, :, screl:screl + NE1].transpose([0, 2, 1])
            p.inc("ve", v.tensor_tensor(
                dA4[:, 0:NE1], dtt.unsqueeze(3).broadcast_to((128, NE1, JB, NST)),
                a_sbb[:].rearrange("p (o n) -> p o n", o=JB).unsqueeze(1)
                .broadcast_to((128, NE1, JB, NST)), ALU.mult))

        @block.scalar
        def _(s, NE1=NE1):
            p.wait(s, "ve")
            p.inc("se", s.activation(dAx[:, 0:NE1 * 128], dAx[:, 0:NE1 * 128],
                                     AF.Tanh, scale=K8))
            p.inc("se", s.activation(dAx[:, 0:NE1 * 128], dAx[:, 0:NE1 * 128],
                                     AF.Exp, scale=16.0))

        @block.vector
        def _(v, screl=screl, NE1=NE1, w4=w4, u4=u4, us_c=us_c, br_c=br_c, dA4=dA4):
            p.wait(v, "se")
            utt = us_c[:, :, screl:screl + NE1].transpose([0, 2, 1])
            p.inc("ve", v.tensor_tensor(
                w4[:, 0:NE1], utt.unsqueeze(3).broadcast_to((128, NE1, JB, NST)),
                br_c[:, :, screl:screl + NE1].transpose([0, 2, 1]).unsqueeze(2)
                .broadcast_to((128, NE1, JB, NST)), ALU.mult))
            p.inc("ve", v.tensor_tensor(u4, dA4[:, 1:SCH + 1], w4[:, 0:SCH], ALU.mult))
            p.inc("ve", v.tensor_tensor(
                sc0b[:, 0:JB * SCH].rearrange("p (o i) -> p o i", i=SCH),
                us_c[:, :, screl:screl + SCH],
                us_c[:, :, screl:screl + SCH],
                ALU.mult))

        @block.tensor
        def _(te, ch=ch):
            p.wait(te, "ve")
            p.inc("pe", te.matmul(ps[7][:, 0:JB * SCH], ones_sqb[:],
                                  sc0b[:, 0:JB * SCH], start=True, stop=True))

        @block.vector
        def _(v, screl=screl, br_c=br_c):
            p.wait(v, "pe")
            p.inc("ve", v.tensor_reduce(
                cuv[:],
                ps[7][:, 0:JB * SCH].rearrange("p (o i) -> p i o", o=JB),
                AX.X, ALU.add))
            p.inc("ve", v.tensor_tensor(
                sc1b[:, 0:NST * SCH].rearrange("p (n i) -> p n i", i=SCH),
                br_c[:, :, screl:screl + SCH],
                br_c[:, :, screl:screl + SCH],
                ALU.mult))
            p.inc("ve", v.tensor_reduce(
                cbv[:],
                sc1b[:, 0:NST * SCH].rearrange("p (n i) -> p i n", n=NST),
                AX.X, ALU.add))
            p.inc("ve", v.tensor_mul(ccv[:], cuv[:], cbv[:]))

        for tl in range(SCH):
            t = t0 + tl
            par = t % 2
            npar = (t + 1) % 2
            last = (t == T - 1)
            se_sqrt = p.val("se") + 1
            marks = {}

            @block.vector
            def _(v, tl=tl, par=par, npar=npar, last=last, se_sqrt=se_sqrt, marks=marks):
                if not last:
                    p.inc("ve", v.tensor_mul(rbuf, dAx[:, (tl + 1) * 128:(tl + 2) * 128],
                                             qbuf[:, tl * 128:(tl + 1) * 128]))
                    p.inc("ve", v.scalar_tensor_tensor(
                        qbuf[:, (tl + 1) * 128:(tl + 2) * 128], rbuf,
                        srow[:, tl:tl + 1], udA[:, tl * 128:(tl + 1) * 128],
                        ALU.mult, ALU.add))
                    marks["q"] = p.val("ve")
                    with p.nc.allow_low_precision(reason="b partial sum bf16"):
                        p.inc("ve", v.scalar_tensor_tensor(
                            h_st[:], qbuf[:, (tl + 1) * 128:(tl + 2) * 128], 2.0,
                            wx[:, (tl + 1) * 128:(tl + 2) * 128], ALU.mult, ALU.mult,
                            accum_out=abq[:, 2 * npar + 1:2 * npar + 2]))
                p.wait(v, "pe")
                p.inc("ve", v.tensor_single_scalar(tq1[:], ps[5 + par][:, 0:1],
                                                   srow[:, tl:tl + 1], ALU.mult))
                p.inc("ve", v.tensor_tensor(tcomb[:], tq1[:], ps[5 + par][:, 1:2],
                                            ALU.add))
                marks["x"] = p.val("ve")
                p.wait(v, "se", se_sqrt)
                p.inc("ve", v.tensor_scalar_add(rcol[:], rcol[:], 1e-8))
                p.inc("ve", v.reciprocal(rcol[:], rcol[:]))
                p.inc("ve", v.tensor_scalar(srow[:, tl + 1:tl + 2], rcol[:],
                                            20.0, 1.0, ALU.mult, ALU.min))

            if not last:
                @block.gpsimd
                def _(g, tl=tl, marks=marks):
                    p.wait(g, "ve", marks["q"])
                    p.inc("po", g.tensor_mul(pm2[:],
                                             qbuf[:, (tl + 1) * 128:(tl + 2) * 128],
                                             qbuf[:, (tl + 1) * 128:(tl + 2) * 128]))

            @block.scalar
            def _(s, tl=tl, npar=npar, last=last, marks=marks):
                p.wait(s, "ve", marks["x"])
                p.inc("se", s.activation(rcol[:], tcomb[:], AF.Sqrt,
                                         scale=srow[:, tl:tl + 1],
                                         bias=ccv[:, tl:tl + 1]))
                if not last:
                    p.wait(s, "po")
                    with p.nc.allow_low_precision(reason="a partial sum bf16"):
                        p.inc("se", s.activation(sc2[:, 0:128], pm2[:], AF.Copy,
                                                 accum_out=abq[:, 2 * npar:2 * npar + 1]))

            if not last:
                @block.tensor
                def _(te, npar=npar):
                    p.wait(te, "se")
                    p.wait(te, "ve")
                    p.inc("pe", te.matmul(ps[5 + npar][:, 0:2], ones_sqb[:],
                                          abq[:, 2 * npar:2 * npar + 2],
                                          start=True, stop=True))

        @block.vector
        def _(v, t0=t0, screl=screl, ch=ch, cr_c=cr_c):
            p.wait(v, "se")
            qb4 = qbuf.rearrange("p (t o n) -> p t o n", o=JB, n=NST)[:, 0:SCH]
            sb4 = srow[:, 0:SCH].unsqueeze(2).unsqueeze(3) \
                .broadcast_to((128, SCH, JB, NST))
            p.inc("ve", v.tensor_tensor(
                t1p.rearrange("p (t o n) -> p t o n", o=JB, n=NST),
                qb4, sb4, ALU.mult))
            p.inc("ve", v.tensor_tensor(t1p, t1p, wx[:, 0:SCH * 128], ALU.add))
            p.inc("ve", v.tensor_tensor(
                pcs.rearrange("p (t o n) -> p t o n", o=JB, n=NST),
                t1p.rearrange("p (t o n) -> p t o n", o=JB, n=NST),
                cr_c[:, :, screl:screl + SCH].transpose([0, 2, 1]).unsqueeze(2)
                .broadcast_to((128, SCH, JB, NST)), ALU.mult))
            p4 = pcs.rearrange("p (t o n) -> p t o n", o=JB, n=NST)
            for hw in (8, 4, 2, 1):
                p.inc("ve", v.tensor_tensor(p4[:, :, :, 0:hw], p4[:, :, :, 0:hw],
                                            p4[:, :, :, hw:2 * hw], ALU.add))
            ymb = sc1[:, 0:JB * SCH]
            p.inc("ve", v.tensor_tensor(
                ymb.rearrange("p (o t) -> p o t", o=JB),
                p4[:, :, :, 0].transpose([0, 2, 1]),
                srow[:, 1:SCH + 1].unsqueeze(1).broadcast_to((128, JB, SCH)),
                ALU.mult))
            xsl = x1res[:].rearrange("p (o t) -> p o t", o=JB)[:, :, t0:t0 + SCH]
            p.inc("ve", v.tensor_tensor(
                xsl, xsl, ymb.rearrange("p (o t) -> p o t", o=JB), ALU.add))
            if ch + 1 < NCHS:
                p.inc("ve", v.tensor_copy(qbuf[:, 0:128],
                                          qbuf[:, SCH * 128:(SCH + 1) * 128]))
                p.inc("ve", v.tensor_copy(srow[:, 0:1], srow[:, SCH:SCH + 1]))


    p.mark("l1_dp")
    for tc in range(TC if "dp" not in skip else 0):
        @block.vector
        def _(v, tc=tc):
            p.wait(v, "dw")
            p.wait(v, "ds")
            for j in range(JB):
                p.inc("ve", v.scalar_tensor_tensor(
                    x1res[:, fsl(j, tc)], xn1b[:, fsl(j, tc)], dp_sb[:, j:j + 1],
                    x1res[:, fsl(j, tc)], ALU.mult, ALU.add))

    p.mark("l1_moe")
    silu_sl = [A2f[:, 1536 + 512 * i:2048 + 512 * i] for i in range(4)]
    pe_after_expert = {}
    moe_prev = None
    for ei in range((NEX + 1) if "moe" not in skip else 0):
        shared = (ei == NEX)
        buf = ei % 2

        @block.gpsimd
        def _(g, ei=ei, buf=buf, shared=shared):
            if ei < 2:
                p.wait(g, "pe")
            else:
                p.wait(g, "pe", pe_after_expert[ei - 2])
            for kb in range(JB):
                src = (w13sT.rearrange("(k p) o -> k p o", p=128)[kb] if shared
                       else w13T.rearrange("e (k p) o -> e k p o", p=128)[ei, kb])
                p.inc("dw", g.dma_start(out=m13buf[buf][:, kb * 1024:(kb + 1) * 1024], in_=src))
            for ib in range(4):
                src = (w2sT.rearrange("(k p) o -> k p o", p=128)[ib] if shared
                       else w2mT.rearrange("e (k p) o -> e k p o", p=128)[ei, ib])
                p.inc("dw", g.dma_start(out=m2buf[buf][:, ib * 1024:(ib + 1) * 1024], in_=src))

        for tc in range(TC):
            first = moe_prev is None
            pm = moe_prev
            m = {}

            if not shared:
                @block.tensor
                def _(te, ei=ei, tc=tc, pm=pm, first=first):
                    if first:
                        p.wait(te, "ve")
                        p.wait(te, "se")
                    else:
                        p.wait(te, "ve", pm["ep0"])
                    p.inc("pe", te.matmul(ps[0][:, 0:TW], Esel[:, ei * 128:(ei + 1) * 128],
                                          combT[:, tc * TW:(tc + 1) * TW],
                                          start=True, stop=True))

                @block.scalar
                def _(s, ei=ei, tc=tc, m=m):
                    p.wait(s, "pe")
                    p.inc("se", s.activation(crep_moe, ps[0][:, 0:TW], AF.Copy))
                    m["crep"] = p.val("se")

            @block.tensor
            def _(te, ei=ei, tc=tc, buf=buf, pm=pm, m=m, first=first, shared=shared):
                if tc == 0:
                    p.wait(te, "dw")
                if first:
                    p.wait(te, "ve")
                    p.wait(te, "se")
                for ib in range(4):
                    if not first:
                        p.wait(te, "ve", pm[f"ep{ib}"])
                    if ib == 0 and not shared:
                        p.wait(te, "se", m["crep"])
                    for kb in range(JB):
                        p.inc("pe", te.matmul(
                            ps[ib][:, 0:TW],
                            m13buf[buf][:, kb * 1024 + ib * 128: kb * 1024 + ib * 128 + 128],
                            xn1b[:, fsl(kb, tc)], start=(kb == 0), stop=(kb == JB - 1)))
                    if not first:
                        p.wait(te, "ve", pm[f"ep{4 + ib}"])
                    for kb in range(JB):
                        p.inc("pe", te.matmul(
                            ps[4 + ib][:, 0:TW],
                            m13buf[buf][:, kb * 1024 + 512 + ib * 128: kb * 1024 + 512 + ib * 128 + 128],
                            xn1b[:, fsl(kb, tc)], start=(kb == 0), stop=(kb == JB - 1)))
                    m[f"gu{ib}"] = p.val("pe")

            @block.scalar
            def _(s, ei=ei, tc=tc, m=m):
                for ib in range(4):
                    p.wait(s, "pe", m[f"gu{ib}"])
                    p.inc("se", s.activation(silu_sl[ib], ps[ib][:, 0:TW], AF.Silu,
                                             scale=1.0 / 127.0))
                    m[f"si{ib}"] = p.val("se")

            @block.vector
            def _(v, ei=ei, tc=tc, m=m, pm=pm, shared=shared, first=first):
                if not first and "down_done" in pm:
                    p.wait(v, "pe", pm["down_done"])
                for ib in range(4):
                    p.wait(v, "se", m[f"si{ib}"])
                    if shared:
                        p.inc("ve", v.scalar_tensor_tensor(
                            hb_moe[:, ib * TW:(ib + 1) * TW], ps[4 + ib][:, 0:TW],
                            1.0 / 127.0, silu_sl[ib], ALU.mult, ALU.mult))
                    else:
                        p.inc("ve", v.scalar_tensor_tensor(
                            sc1[:, 0:TW], ps[4 + ib][:, 0:TW],
                            1.0 / 127.0, silu_sl[ib], ALU.mult, ALU.mult))
                        p.inc("ve", v.tensor_mul(hb_moe[:, ib * TW:(ib + 1) * TW],
                                                 sc1[:, 0:TW], crep_moe))
                    m[f"hp{ib}"] = p.val("ve")

            @block.tensor
            def _(te, ei=ei, tc=tc, buf=buf, m=m):
                p.wait(te, "ve", m["hp3"])
                m["down0"] = p.val("pe")
                for ib in range(4):
                    for ob in range(JB):
                        p.inc("pe", te.matmul(
                            ps[ob][:, 0:TW],
                            m2buf[buf][:, ib * 1024 + ob * 128: ib * 1024 + ob * 128 + 128],
                            hb_moe[:, ib * TW:(ib + 1) * TW],
                            start=(ib == 0), stop=(ib == 3)))
                m["down_done"] = p.val("pe")

            @block.vector
            def _(v, ei=ei, tc=tc, m=m):
                for ob in range(JB):
                    p.wait(v, "pe", m["down0"] + 3 * JB + ob + 1)
                    p.inc("ve", v.scalar_tensor_tensor(
                        x1res[:, fsl(ob, tc)], ps[ob][:, 0:TW], 1.0 / 127.0,
                        x1res[:, fsl(ob, tc)], ALU.mult, ALU.add))
                    m[f"ep{ob}"] = p.val("ve")

            moe_prev = m

        pe_after_expert[ei] = p.val("pe")

    p.mark(None)
    @block.sync
    def _(sync):
        p.wait(sync, "ve")
        p.inc("dx", sync.dma_start(
            out=out_t.rearrange("(j p) t -> p j t", p=128),
            in_=x1res[:].rearrange("p (j t) -> p j t", j=JB)))
        p.wait(sync, "dx")


def _mk_io_l0(nc, out_kind="Internal"):
    ap = lambda n, shp, dt=F32, kind="ExternalInput": nc.dram_tensor(n, shp, dt, kind=kind).ap()
    io = {
        "x_t": ap("x_t", [D, T]),
        "wqT": ap("wqT", [D, D]),
        "wkT": ap("wkT", [D, D]),
        "wvT": ap("wvT", [D, D]),
        "w1T": ap("w1T", [D, DFF]),
        "w2T": ap("w2T", [DFF, D]),
        "b1c": ap("b1c", [128, DFF // 128]),
        "b2c": ap("b2c", [128, JB]),
        "g1c": ap("g1c", [128, JB]),
        "be1c": ap("be1c", [128, JB]),
        "g2c": ap("g2c", [128, JB]),
        "be2c": ap("be2c", [128, JB]),
        "wgT": ap("wgT", [D, HID], I32),
        "wuT": ap("wuT", [D, HID], I32),
        "wdT": ap("wdT", [HID, D], I32),
    }
    return io


def _mk_io_l1(nc, with_x1):
    ap = lambda n, shp, dt=F32, kind="ExternalInput": nc.dram_tensor(n, shp, dt, kind=kind).ap()
    io = {
        "WdtT": ap("WdtT", [D, D]),
        "bdtc": ap("bdtc", [128, JB]),
        "WBT": ap("WBT", [D, NST]),
        "WCT": ap("WCT", [D, NST]),
        "A_logc": ap("A_logc", [128, 128]),
        "Dpc": ap("Dpc", [128, JB]),
        "gwT": ap("gwT", [D, NEX]),
        "gbr": ap("gbr", [1, NEX]),
        "w13T": ap("w13T", [NEX, D, D]),
        "w2mT": ap("w2mT", [NEX, INTER, D]),
        "w13sT": ap("w13sT", [D, D]),
        "w2sT": ap("w2sT", [INTER, D]),
        "out_t": ap("out_t", [D, T], kind="ExternalOutput"),
    }
    if with_x1:
        io["x1_t"] = ap("x1_t", [D, T])
    return io


def build_full(skip=(), debug_taps=False):
    nc = bass.Bass("TRN2", target_bir_lowering=False)
    io0 = _mk_io_l0(nc)
    io1 = _mk_io_l1(nc, with_x1=False)
    dbg = None
    if debug_taps:
        dbg = nc.dram_tensor("dbg_t", [D, T], F32, kind="ExternalOutput").ap()
    with ExitStack() as ctx:
        e = ctx.enter_context
        p = Prog(nc, ctx)
        sb = _alloc(nc, e)
        _emit_l0(p, io0, sb)
        if debug_taps:
            block = p.block
            @block.sync
            def _(sync):
                p.wait(sync, "ve")
                p.wait(sync, "se")
                p.wait(sync, "pe")
                p.inc("dx", sync.dma_start(
                    out=dbg.rearrange("(j p) t -> p j t", p=128),
                    in_=sb["xres"][:].rearrange("p (j t) -> p j t", j=JB)))
                p.wait(sync, "dx")
        _emit_l1(p, io1, sb, fused=True, skip=skip)
    return nc


def build_l0():
    nc = bass.Bass("TRN2", target_bir_lowering=False)
    io0 = _mk_io_l0(nc)
    x1_t = nc.dram_tensor("x1_t", [D, T], F32, kind="ExternalOutput").ap()
    with ExitStack() as ctx:
        e = ctx.enter_context
        p = Prog(nc, ctx)
        sb = _alloc(nc, e)
        _emit_l0(p, io0, sb)
        block = p.block

        @block.sync
        def _(sync):
            p.wait(sync, "ve")
            p.inc("dx", sync.dma_start(
                out=x1_t.rearrange("(j p) t -> p j t", p=128),
                in_=sb["xres"][:].rearrange("p (j t) -> p j t", j=JB)))
            p.wait(sync, "dx")
    return nc


def build_l1():
    nc = bass.Bass("TRN2", target_bir_lowering=False)
    io1 = _mk_io_l1(nc, with_x1=True)
    with ExitStack() as ctx:
        e = ctx.enter_context
        p = Prog(nc, ctx)
        sb = _alloc(nc, e)
        _emit_l1(p, io1, sb, fused=False)
    return nc



def _host_weights(inputs):
    f = np.ascontiguousarray
    return {
        "wqT": f(inputs["L0_attn_wq"].T),
        "wkT": f(inputs["L0_attn_wk"].T),
        "wvT": f(inputs["L0_attn_wv"].T),
        "w1T": f(inputs["L0_attn_w1"].T),
        "w2T": f(inputs["L0_attn_w2"].T),
        "b1c": f(inputs["L0_attn_b1"].reshape(DFF // 128, 128).T),
        "b2c": f(inputs["L0_attn_b2"].reshape(JB, 128).T),
        "g1c": f(inputs["L0_attn_g1"].reshape(JB, 128).T),
        "be1c": f(inputs["L0_attn_be1"].reshape(JB, 128).T),
        "g2c": f(inputs["L0_attn_g2"].reshape(JB, 128).T),
        "be2c": f(inputs["L0_attn_be2"].reshape(JB, 128).T),
        "wgT": f(inputs["L0_ffn_w_gate"].T),
        "wuT": f(inputs["L0_ffn_w_up"].T),
        "wdT": f(inputs["L0_ffn_w_down"].T),
        "WdtT": f(inputs["L1_attn_Wdt"].T),
        "bdtc": f(inputs["L1_attn_bdt"].reshape(JB, 128).T),
        "WBT": f(inputs["L1_attn_WB"].T),
        "WCT": f(inputs["L1_attn_WC"].T),
        "A_logc": f(inputs["L1_attn_A_log"].reshape(JB, 128, NST).transpose(1, 0, 2).reshape(128, 128)),
        "Dpc": f(inputs["L1_attn_Dp"].reshape(JB, 128).T),
        "gwT": f(inputs["L1_ffn_gate_w"].T),
        "gbr": f(inputs["L1_ffn_gate_b"].reshape(1, NEX)),
        "w13T": f(inputs["L1_ffn_w13"].transpose(0, 2, 1)),
        "w2mT": f(inputs["L1_ffn_w2"].transpose(0, 2, 1)),
        "w13sT": f(inputs["L1_ffn_w13_s"].T),
        "w2sT": f(inputs["L1_ffn_w2_s"].T),
    }


def _l0_host_inputs(inputs):
    w = {k: v for k, v in _host_weights(inputs).items()
         if k in ("wqT", "wkT", "wvT", "w1T", "w2T", "b1c", "b2c", "g1c", "be1c",
                  "g2c", "be2c", "wgT", "wuT", "wdT")}
    x = np.asarray(inputs["L0_attn_x"])
    return [dict(w, x_t=np.ascontiguousarray(x[c].T)) for c in range(8)]


def _l1_host_inputs(inputs, x1_list):
    w = {k: v for k, v in _host_weights(inputs).items()
         if k not in ("wqT", "wkT", "wvT", "w1T", "w2T", "b1c", "b2c", "g1c",
                      "be1c", "g2c", "be2c", "wgT", "wuT", "wdT")}
    return [dict(w, x1_t=np.ascontiguousarray(x1_list[c])) for c in range(8)]


def kernel(**inputs):
    _ = inputs.get("L1_ffn_input_ids")
    nc = build_full()
    w = _host_weights(inputs)
    x = np.asarray(inputs["L0_attn_x"])
    in_maps = [dict(w, x_t=np.ascontiguousarray(x[c].T)) for c in range(8)]
    res = run_bass_kernel_spmd(nc, in_maps, core_ids=list(range(8)))
    out = np.stack([np.asarray(r["out_t"]).T for r in res.results], axis=0)
    return out.astype(np.float32)


# version 11
# speedup vs baseline: 1.2518x; 1.1411x over previous
import numpy as np
import ml_dtypes
from contextlib import ExitStack

import concourse.bass as bass
import concourse.mybir as mybir
from concourse.bass_utils import run_bass_kernel_spmd

F32 = mybir.dt.float32
BF16 = mybir.dt.bfloat16
I32 = mybir.dt.int32
AF = mybir.ActivationFunctionType
ALU = mybir.AluOpType
AX = mybir.AxisListType
BT = ml_dtypes.bfloat16

D = 1024
T = 2048
JB = 8
NH = 8
DFF = 4096
HID = 3072
EPS_LN = 1e-5
EPS_ATTN = 1e-6
TW = 512
TC = T // TW
NST = 16
TCH = 64
NCH = T // TCH
INTER = 512
NEX = 16
K8 = 0.5 / (8.0 + 1e-8)
K5 = 0.5 / (5.0 + 1e-8)
KDT = 0.5 / (0.0495 + 1e-8)


class Prog:
    def __init__(self, nc, ctx):
        self.nc = nc
        self.ctx = ctx
        self.block = ctx.enter_context(nc.Block())
        self._sems = {}

    def sem(self, name):
        if name not in self._sems:
            h = self.ctx.enter_context(self.nc.semaphore(name))
            self._sems[name] = [h, 0]
        return self._sems[name]

    def inc(self, name, inst, k=1):
        if name.startswith("d"):
            k *= 16
        s = self.sem(name)
        s[1] += k
        inst.then_inc(s[0], k)
        return s[1]

    def val(self, name):
        return self.sem(name)[1]

    def wait(self, eng, name, v=None):
        s = self.sem(name)
        v = s[1] if v is None else v
        if v > 0:
            eng.wait_ge(s[0], v)

    _scope = None

    def mark(self, name):
        if self._scope is not None:
            self.nc.leave_named_scope(self._scope[0], self._scope[1], False)
            self._scope = None
        if name is not None:
            sid, _ = self.nc.enter_named_scope(name, False)
            self._scope = (name, sid)


def fsl(kb, tc, w=TW):
    return slice(kb * T + tc * w, kb * T + (tc + 1) * w)


def emit_ln(p, r, src_bf, val_src, dst_bf, gcol=None, bcol=None):
    block = p.block
    ps = r["ps"]
    for tc in range(TC):
        for kb in range(JB):
            @block.scalar
            def _(s, kb=kb, tc=tc):
                p.wait(s, "pe")
                if kb == 0:
                    p.wait(s, "dw")
                    p.wait(s, "ds")
                    p.wait(s, "ve")
                p.inc("se", s.activation(r["sqb"][:, 0:TW], src_bf[:, fsl(kb, tc)], AF.Square))

            @block.tensor
            def _(te, kb=kb, tc=tc):
                p.wait(te, "se")
                if kb == 0:
                    p.wait(te, "ve")
                p.inc("pe", te.matmul(ps[7][0:1, 0:TW], r["ones_cb"][:], r["sqb"][:, 0:TW],
                                      start=(kb == 0), stop=(kb == JB - 1)))
                p.inc("pe", te.matmul(ps[6][0:1, 0:TW], r["ones_cb"][:], src_bf[:, fsl(kb, tc)],
                                      start=(kb == 0), stop=(kb == JB - 1)))

        @block.vector
        def _(v, tc=tc):
            p.wait(v, "pe")
            p.inc("ve", v.tensor_scalar_mul(r["row0"], ps[6][0:1, 0:TW], 1.0 / D))
            p.inc("ve", v.tensor_scalar_mul(r["row1"], ps[7][0:1, 0:TW], 1.0 / D))
            p.inc("ve", v.tensor_mul(r["row2"], r["row0"], r["row0"]))
            p.inc("ve", v.tensor_sub(r["row1"], r["row1"], r["row2"]))

        @block.scalar
        def _(s, tc=tc):
            p.wait(s, "ve")
            p.inc("se", s.activation(r["row1"], r["row1"], AF.Ln, bias=r["epsln"]))
            p.inc("se", s.activation(r["row1"], r["row1"], AF.Exp, scale=-0.5))

        @block.tensor
        def _(te, tc=tc):
            p.wait(te, "se")
            p.wait(te, "ve")
            p.inc("pe", te.matmul(ps[4][:, 0:TW], r["ones_r32"][:], r["row0"],
                                  start=True, stop=True))
            p.inc("pe", te.matmul(ps[5][:, 0:TW], r["ones_r32"][:], r["row1"],
                                  start=True, stop=True))

        @block.vector
        def _(v, tc=tc):
            p.wait(v, "pe")
            for kb in range(JB):
                vs = val_src(kb, tc)
                p.inc("ve", v.tensor_sub(r["sc0"][:, 0:TW], vs, ps[4][:, 0:TW]))
                if gcol is None:
                    p.inc("ve", v.tensor_mul(dst_bf[:, fsl(kb, tc)], r["sc0"][:, 0:TW], ps[5][:, 0:TW]))
                else:
                    p.inc("ve", v.tensor_mul(r["sc0"][:, 0:TW], r["sc0"][:, 0:TW], ps[5][:, 0:TW]))
                    p.inc("ve", v.tensor_scalar(dst_bf[:, fsl(kb, tc)], r["sc0"][:, 0:TW],
                                                gcol[:, kb:kb + 1], bcol[:, kb:kb + 1],
                                                ALU.mult, ALU.add))


def emit_elu_epilogue(p, ps_ap, dst_ap, r, w=TW):
    block = p.block

    @block.vector
    def _(v):
        p.wait(v, "pe")
        p.inc("ve", v.tensor_scalar_min(r["sc0"][:, 0:w], ps_ap, 0.0))
        p.inc("ve", v.tensor_sub(r["sc1"][:, 0:w], ps_ap, r["sc0"][:, 0:w]))

    @block.scalar
    def _(s):
        p.wait(s, "ve")
        p.inc("se", s.activation(r["sc2"][:, 0:w], r["sc0"][:, 0:w], AF.Exp))

    @block.vector
    def _(v):
        p.wait(v, "se")
        p.inc("ve", v.tensor_add(dst_ap, r["sc1"][:, 0:w], r["sc2"][:, 0:w]))


def _alloc(nc, e):
    sb = {}
    t = lambda name, shp, dt=F32: e(nc.sbuf_tensor(name, shp, dt))
    sb["xres"] = t("xres", [128, JB * T], F32)
    sb["xnb"] = t("xnb", [128, JB * T], BF16)
    sb["A1"] = t("A1", [128, JB * T], BF16)
    sb["A2"] = t("A2", [128, JB * T], BF16)
    sb["A3"] = t("A3", [128, JB * T], BF16)
    sb["sc0"] = t("sc0", [128, TW], F32)
    sb["sc1"] = t("sc1", [128, TW], F32)
    sb["sc2"] = t("sc2", [128, TW], F32)
    _w1 = t("w1buf", [128, JB * 128], BF16)
    _w2 = t("w2buf", [128, 1024], BF16)
    sb["w1buf"] = [_w1, _w1]
    sb["w2buf"] = [_w2, _w2]
    for n, w in (("b1s", DFF // 128), ("b2s", JB), ("g1s", JB), ("be1s", JB),
                 ("g2s", JB), ("be2s", JB), ("bdt_sb", JB), ("dp_sb", JB)):
        sb[n] = t(n, [128, w], F32)
    sb["ones_r32"] = t("ones_r32", [1, 128], F32)
    sb["ones_cb"] = t("ones_cb", [128, 1], BF16)
    sb["ones_cb2"] = t("ones_cb2", [128, 2], BF16)
    sb["epsc"] = t("epsc", [1, 1], F32)
    sb["idb"] = t("idb", [128, 128], F32)
    sb["ones_sq"] = sb["idb"]
    sb["pcol"] = t("pcol", [128, 1], F32)
    sb["a_sb"] = t("a_sb", [128, 128], F32)
    sb["a_sbb"] = t("a_sbb", [128, 128], BF16)
    sb["h_st"] = t("h_st", [128, 128], F32)
    sb["iof"] = sb["h_st"]
    sb["srow"] = t("srow", [128, 34], F32)
    sb["pm2"] = t("pm2", [128, 128], BF16)
    sb["abq"] = t("abq", [128, 4], BF16)
    sb["tq1"] = t("tq1", [128, 1], F32)
    sb["tcomb"] = t("tcomb", [128, 1], F32)
    sb["ones_sqb"] = t("ones_sqb", [128, 128], BF16)
    sb["wb_sb"] = t("wb_sb", [128, JB * NST], BF16)
    sb["wc_sb"] = t("wc_sb", [128, JB * NST], BF16)
    sb["wg_sb"] = t("wg_sb", [128, JB * NEX], BF16)
    sb["gb_rep"] = t("gb_rep", [128, NEX], F32)
    sb["gbr_sb"] = t("gbr_sb", [1, NEX], F32)
    for n in ("rcol", "mcol", "wcol"):
        sb[n] = t(n, [128, 1], F32)
    for n in ("rsc", "bia", "bwk", "selw", "comb_sc"):
        sb[n] = t(n, [128, NEX], F32)
    sb["ps"] = [e(nc.psum_tensor(f"ps{i}", [128, TW], F32)) for i in range(8)]
    return sb


def _emit_l0(p, io, sb):
    block = p.block
    x_t, wqT, wkT, wvT, w1T, w2T = io["x_t"], io["wqT"], io["wkT"], io["wvT"], io["w1T"], io["w2T"]
    b1c, b2c, g1c, be1c, g2c, be2c = io["b1c"], io["b2c"], io["g1c"], io["be1c"], io["g2c"], io["be2c"]
    wgT, wuT, wdT = io["wgT"], io["wuT"], io["wdT"]
    xres, xnb, A1, A2, A3 = sb["xres"], sb["xnb"], sb["A1"], sb["A2"], sb["A3"]
    sc0, sc1, sc2 = sb["sc0"], sb["sc1"], sb["sc2"]
    w1buf, w2buf = sb["w1buf"], sb["w2buf"]
    b1s, b2s, g1s, be1s, g2s, be2s = sb["b1s"], sb["b2s"], sb["g1s"], sb["be1s"], sb["g2s"], sb["be2s"]
    ones_r32, ones_cb, ones_cb2, epsc = sb["ones_r32"], sb["ones_cb"], sb["ones_cb2"], sb["epsc"]
    ps = sb["ps"]

    xrv = xres[:].bitcast(BF16)
    ktile = xrv[:, 0:1024]
    vtile = xrv[:, 1024:2048]
    kvb = xrv[:, 2048:3072]
    ksb = xrv[:, 3072:3072 + NH]
    wqbuf = xrv[:, 4096:4096 + 1024]
    xrvf = xres[:]
    row0 = xrvf[0:1, 4096:4096 + TW]
    row1 = xrvf[0:1, 4608:4608 + TW]
    row2 = xrvf[0:1, 5120:5120 + TW]
    zrow = xrvf[0:1, 5632:5632 + TW]
    sqb = xrv[:, 12288:12288 + TW]
    wk_sb = A2[:, 0:JB * D]
    wv_sb = A2[:, JB * D:2 * JB * D]

    r = dict(ps=ps, sqb=sqb, sc0=sc0, sc1=sc1, sc2=sc2, row0=row0, row1=row1,
             row2=row2, ones_r32=ones_r32, ones_cb=ones_cb, epsln=epsc[0:1, 0:1])

    @block.gpsimd
    def _(g):
        p.inc("dw", g.dma_start(
            out=A3[:].rearrange("p (j t) -> p j t", j=JB),
            in_=x_t.rearrange("(j p) t -> p j t", p=128)))
        for w, dst in ((wkT, wk_sb), (wvT, wv_sb)):
            p.inc("dw", g.dma_start(
                out=dst.rearrange("p (k o) -> p k o", k=JB),
                in_=w.rearrange("(k p) o -> p k o", p=128)))

    @block.sync
    def _(sync):
        for src, dst in ((b1c, b1s), (b2c, b2s), (g1c, g1s), (be1c, be1s),
                         (g2c, g2s), (be2c, be2s)):
            p.inc("ds", sync.dma_start(out=dst[:], in_=src))

    @block.vector
    def _(v):
        p.inc("ve", v.memset(ones_r32[:], 1.0))
        p.inc("ve", v.memset(ones_cb[:], 1.0))
        p.inc("ve", v.memset(ones_cb2[:], 1.0))
        p.inc("ve", v.memset(epsc[:], EPS_LN))

    p.mark("l0_ln0")
    emit_ln(p, r, A3, lambda kb, tc: A3[:, fsl(kb, tc)], xnb)

    p.mark("l0_ln1")
    emit_ln(p, r, xnb, lambda kb, tc: xnb[:, fsl(kb, tc)], A1, gcol=g1s, bcol=be1s)

    p.mark("l0_kv")
    NT = T // 128
    for tt in range(NT):
        for half in range(2):
            @block.tensor
            def _(te, tt=tt, half=half):
                p.wait(te, "ve")
                p.wait(te, "se")
                for kb in range(JB):
                    p.inc("pe", te.matmul(
                        ps[half][:, 0:TW],
                        A1[:, kb * T + tt * 128: kb * T + (tt + 1) * 128],
                        wk_sb[:, kb * D + half * TW: kb * D + (half + 1) * TW],
                        start=(kb == 0), stop=(kb == JB - 1)))
            emit_elu_epilogue(p, ps[half][:, 0:TW], ktile[:, half * TW:(half + 1) * TW], r)

        for half in range(2):
            @block.tensor
            def _(te, tt=tt, half=half):
                p.wait(te, "ve")
                p.wait(te, "se")
                for kb in range(JB):
                    p.inc("pe", te.matmul(
                        ps[half][:, 0:TW],
                        A1[:, kb * T + tt * 128: kb * T + (tt + 1) * 128],
                        wv_sb[:, kb * D + half * TW: kb * D + (half + 1) * TW],
                        start=(kb == 0), stop=(kb == JB - 1)))

            @block.scalar
            def _(s, tt=tt, half=half):
                p.wait(s, "pe")
                p.inc("se", s.activation(vtile[:, half * TW:(half + 1) * TW],
                                         ps[half][:, 0:TW], AF.Copy))

        @block.tensor
        def _(te, tt=tt):
            p.wait(te, "se")
            p.wait(te, "ve")
            for h in range(NH):
                pbank = ps[2] if h < 4 else ps[3]
                col = (h % 4) * 128
                p.inc("pe", te.matmul(
                    pbank[:, col:col + 128],
                    ktile[:, h * 128:(h + 1) * 128],
                    vtile[:, h * 128:(h + 1) * 128],
                    start=(tt == 0 and (h % 4) == 0),
                    stop=(tt == NT - 1 and (h % 4) == 3)))
            for h in range(NH):
                p.inc("pe", te.matmul(
                    ps[6][:, 2 * h:2 * h + 2],
                    ktile[:, h * 128:(h + 1) * 128],
                    ones_cb2[:],
                    start=(tt == 0 and h == 0), stop=(tt == NT - 1 and h == NH - 1)))

    @block.scalar
    def _(s):
        p.wait(s, "pe")
        p.inc("se", s.activation(kvb[:, 0:TW], ps[2][:, 0:TW], AF.Copy))
        p.inc("se", s.activation(kvb[:, TW:2 * TW], ps[3][:, 0:TW], AF.Copy))
        p.inc("se", s.activation(ksb, ps[6][:, 0:2 * NH:2], AF.Copy))

    p.mark("l0_q")
    for h in range(NH):
        @block.gpsimd
        def _(g, h=h):
            p.wait(g, "pe")
            for kq in range(JB):
                p.inc("dw", g.dma_start(
                    out=wqbuf[:, kq * 128:(kq + 1) * 128],
                    in_=wqT.rearrange("(k p) o -> k p o", p=128)[kq, :, h * 128:(h + 1) * 128]))

        for tc in range(TC):
            bank = tc % 2
            @block.tensor
            def _(te, h=h, tc=tc, bank=bank):
                p.wait(te, "dw")
                p.wait(te, "ve")
                for kb in range(JB):
                    p.inc("pe", te.matmul(
                        ps[bank][:, 0:TW],
                        wqbuf[:, kb * 128:(kb + 1) * 128],
                        A1[:, fsl(kb, tc)],
                        start=(kb == 0), stop=(kb == JB - 1)))
            emit_elu_epilogue(p, ps[bank][:, 0:TW], A2[:, fsl(h, tc)], r)

    p.mark("l0_att")
    for h in range(NH):
        for tc in range(TC):
            @block.tensor
            def _(te, h=h, tc=tc):
                p.wait(te, "se")
                p.wait(te, "ve")
                p.inc("pe", te.matmul(ps[6][0:1, 0:TW], ksb[:, h:h + 1],
                                      A2[:, fsl(h, tc)], start=True, stop=True))

            @block.vector
            def _(v, h=h, tc=tc):
                p.wait(v, "pe")
                p.inc("ve", v.tensor_scalar_add(zrow, ps[6][0:1, 0:TW], EPS_ATTN))
                p.inc("ve", v.reciprocal(zrow, zrow))

            @block.tensor
            def _(te, h=h, tc=tc):
                p.wait(te, "ve")
                p.inc("pe", te.matmul(ps[4][:, 0:TW], ones_r32[:], zrow,
                                      start=True, stop=True))
                p.inc("pe", te.matmul(ps[5][:, 0:TW],
                                      kvb[:, h * 128:(h + 1) * 128],
                                      A2[:, fsl(h, tc)], start=True, stop=True))

            @block.scalar
            def _(s, h=h, tc=tc):
                p.wait(s, "pe")
                p.inc("se", s.activation(sc0[:, 0:TW], ps[4][:, 0:TW], AF.Copy))

            @block.vector
            def _(v, h=h, tc=tc):
                p.wait(v, "se")
                p.inc("ve", v.tensor_mul(sc1[:, 0:TW], ps[5][:, 0:TW], sc0[:, 0:TW]))
                p.inc("ve", v.tensor_add(A3[:, fsl(h, tc)], sc1[:, 0:TW], xnb[:, fsl(h, tc)]))

    p.mark("l0_ln2")
    emit_ln(p, r, A3, lambda kb, tc: A3[:, fsl(kb, tc)], A1, gcol=g2s, bcol=be2s)

    @block.sync
    def _(sync):
        p.wait(sync, "pe")
        p.wait(sync, "ve")
        p.wait(sync, "se")
        p.inc("dx", sync.dma_start(
            out=xres[:].rearrange("p (j t) -> p j t", j=JB),
            in_=x_t.rearrange("(j p) t -> p j t", p=128)))

    p.mark("l0_ffnA")
    NFB = DFF // 128
    pe_fb = {}
    ge_mark = {}
    ep_mark = {}
    for tc in range(TC):
        for fb in range(NFB):
            buf = fb % 2
            bank = 4 + (fb % 4)
            wsel = [w1buf[0], w2buf[0]]
            @block.gpsimd
            def _(g, tc=tc, fb=fb, buf=buf, wsel=wsel):
                if fb < 2:
                    if tc == 0:
                        p.wait(g, "pe")
                    else:
                        p.wait(g, "pe", pe_fb[("a2", tc * NFB - 1)])
                else:
                    p.wait(g, "pe", pe_fb[tc * NFB + fb - 2])
                for kq in range(JB):
                    p.inc("dw", g.dma_start(
                        out=wsel[buf][:, kq * 128:(kq + 1) * 128],
                        in_=w1T.rearrange("(k p) o -> k p o", p=128)[kq, :, fb * 128:(fb + 1) * 128]))

            @block.tensor
            def _(te, tc=tc, fb=fb, buf=buf, bank=bank, wsel=wsel):
                p.wait(te, "dw")
                if fb < 4:
                    p.wait(te, "se")
                    if tc == 0 and fb == 0:
                        p.wait(te, "ve")
                    if tc > 0:
                        p.wait(te, "ve", ep_mark[(tc - 1, bank)])
                else:
                    p.wait(te, "se", ge_mark[(tc, fb - 4)])
                for kb in range(JB):
                    p.inc("pe", te.matmul(
                        ps[bank][:, 0:TW], wsel[buf][:, kb * 128:(kb + 1) * 128],
                        A1[:, fsl(kb, tc)],
                        start=(kb == 0), stop=(kb == JB - 1)))
                pe_fb[tc * NFB + fb] = p.val("pe")

            @block.scalar
            def _(s, tc=tc, fb=fb, bank=bank):
                p.wait(s, "pe", pe_fb[tc * NFB + fb])
                p.inc("se", s.activation(A2[:, fb * TW:(fb + 1) * TW], ps[bank][:, 0:TW],
                                         AF.Gelu, bias=b1s[:, fb:fb + 1], scale=1.0))
                ge_mark[(tc, fb)] = p.val("se")

        pa2_0 = {}
        for fb in range(NFB):
            buf = fb % 2
            wsel2 = [w2buf[0], w1buf[0]]
            @block.gpsimd
            def _(g, tc=tc, fb=fb, buf=buf, wsel2=wsel2):
                if fb < 2:
                    p.wait(g, "pe", pe_fb[tc * NFB + NFB - 1])
                else:
                    p.wait(g, "pe", pe_fb[("a2", tc * NFB + fb - 2)])
                p.inc("dw", g.dma_start(
                    out=wsel2[buf][:],
                    in_=w2T.rearrange("(f p) o -> f p o", p=128)[fb]))

            @block.tensor
            def _(te, tc=tc, fb=fb, buf=buf, wsel2=wsel2, pa2_0=pa2_0):
                p.wait(te, "dw")
                if fb == 0:
                    p.wait(te, "ve")
                    p.wait(te, "se")
                    pa2_0[tc] = p.val("pe")
                for ob in range(JB):
                    p.inc("pe", te.matmul(
                        ps[ob][:, 0:TW], wsel2[buf][:, ob * 128:(ob + 1) * 128],
                        A2[:, fb * TW:(fb + 1) * TW],
                        start=(fb == 0), stop=(fb == NFB - 1)))
                pe_fb[("a2", tc * NFB + fb)] = p.val("pe")

        @block.vector
        def _(v, tc=tc, pa2_0=pa2_0):
            p.wait(v, "dx")
            for ob in range(JB):
                p.wait(v, "pe", pa2_0[tc] + (NFB - 1) * JB + ob + 1)
                p.inc("ve", v.scalar_tensor_tensor(
                    sc0[:, 0:TW], ps[ob][:, 0:TW], b2s[:, ob:ob + 1],
                    A3[:, fsl(ob, tc)], ALU.add, ALU.add))
                p.inc("ve", v.tensor_add(xres[:, fsl(ob, tc)], xres[:, fsl(ob, tc)],
                                         sc0[:, 0:TW]))
                ep_mark[(tc, ob)] = p.val("ve")

    p.mark("l0_ffnB")
    NHB = HID // 128
    wgB = [w1buf[0], A2[:, 12288:13312]]
    wuB = [w2buf[0], A2[:, 13312:14336]]
    for tc in range(TC):
        for fb in range(NHB):
            buf = fb % 2
            @block.gpsimd
            def _(g, tc=tc, fb=fb, buf=buf):
                if fb < 2:
                    if tc == 0:
                        p.wait(g, "pe")
                        p.wait(g, "ve")
                    else:
                        p.wait(g, "pe", pe_fb[("b2", tc * NHB - 1)])
                else:
                    p.wait(g, "pe", pe_fb[("b1", tc * NHB + fb - 2)])
                for kq in range(JB):
                    p.inc("dw", g.dma_start(
                        out=wgB[buf][:, kq * 128:(kq + 1) * 128],
                        in_=wgT.rearrange("(k p) o -> k p o", p=128)[kq, :, fb * 128:(fb + 1) * 128]))
                for kq in range(JB):
                    p.inc("dw", g.dma_start(
                        out=wuB[buf][:, kq * 128:(kq + 1) * 128],
                        in_=wuT.rearrange("(k p) o -> k p o", p=128)[kq, :, fb * 128:(fb + 1) * 128]))

            gbank = 4 + 2 * (fb % 2)
            sslot = sc0 if fb % 2 == 0 else sc1

            @block.tensor
            def _(te, tc=tc, fb=fb, buf=buf, gbank=gbank):
                p.wait(te, "dw")
                if fb < 2:
                    p.wait(te, "se")
                    p.wait(te, "ve")
                else:
                    p.wait(te, "se", ge_mark[("b", tc, fb - 2)])
                    p.wait(te, "ve", ep_mark[("b", tc, fb - 2)])
                for kb in range(JB):
                    p.inc("pe", te.matmul(
                        ps[gbank][:, 0:TW], wgB[buf][:, kb * 128:(kb + 1) * 128],
                        xnb[:, fsl(kb, tc)],
                        start=(kb == 0), stop=(kb == JB - 1)))
                for kb in range(JB):
                    p.inc("pe", te.matmul(
                        ps[gbank + 1][:, 0:TW], wuB[buf][:, kb * 128:(kb + 1) * 128],
                        xnb[:, fsl(kb, tc)],
                        start=(kb == 0), stop=(kb == JB - 1)))
                pe_fb[("b1", tc * NHB + fb)] = p.val("pe")

            @block.scalar
            def _(s, tc=tc, fb=fb, gbank=gbank, sslot=sslot):
                p.wait(s, "pe", pe_fb[("b1", tc * NHB + fb)])
                p.inc("se", s.activation(sslot[:, 0:TW], ps[gbank][:, 0:TW], AF.Silu,
                                         scale=1.0 / 127.0))
                ge_mark[("b", tc, fb)] = p.val("se")

            @block.vector
            def _(v, tc=tc, fb=fb, gbank=gbank, sslot=sslot):
                p.wait(v, "se", ge_mark[("b", tc, fb)])
                p.inc("ve", v.scalar_tensor_tensor(
                    A2[:, fb * TW:(fb + 1) * TW], ps[gbank + 1][:, 0:TW], 1.0 / 127.0,
                    sslot[:, 0:TW], ALU.mult, ALU.mult))
                ep_mark[("b", tc, fb)] = p.val("ve")

        pb2_0 = {}
        for fb in range(NHB):
            buf = fb % 2
            wselD = [w2buf[0], w1buf[0]]
            @block.gpsimd
            def _(g, tc=tc, fb=fb, buf=buf, wselD=wselD):
                if fb < 2:
                    p.wait(g, "pe", pe_fb[("b1", tc * NHB + NHB - 1)])
                else:
                    p.wait(g, "pe", pe_fb[("b2", tc * NHB + fb - 2)])
                p.inc("dw", g.dma_start(
                    out=wselD[buf][:],
                    in_=wdT.rearrange("(f p) o -> f p o", p=128)[fb]))

            @block.tensor
            def _(te, tc=tc, fb=fb, buf=buf, wselD=wselD, pb2_0=pb2_0):
                p.wait(te, "dw")
                if fb == 0:
                    p.wait(te, "ve")
                    p.wait(te, "se")
                    pb2_0[tc] = p.val("pe")
                for ob in range(JB):
                    p.inc("pe", te.matmul(
                        ps[ob][:, 0:TW], wselD[buf][:, ob * 128:(ob + 1) * 128],
                        A2[:, fb * TW:(fb + 1) * TW],
                        start=(fb == 0), stop=(fb == NHB - 1)))
                pe_fb[("b2", tc * NHB + fb)] = p.val("pe")

        @block.vector
        def _(v, tc=tc, pb2_0=pb2_0):
            for ob in range(JB):
                p.wait(v, "pe", pb2_0[tc] + (NHB - 1) * JB + ob + 1)
                p.inc("ve", v.scalar_tensor_tensor(
                    xres[:, fsl(ob, tc)], ps[ob][:, 0:TW], 1.0 / 127.0,
                    xres[:, fsl(ob, tc)], ALU.mult, ALU.add))


def _emit_l1(p, io, sb, fused, skip=()):
    block = p.block
    WdtT, bdtc, WBT, WCT = io["WdtT"], io["bdtc"], io["WBT"], io["WCT"]
    A_logc, Dpc, gwT, gbr = io["A_logc"], io["Dpc"], io["gwT"], io["gbr"]
    w13T, w2mT, w13sT, w2sT, out_t = io["w13T"], io["w2mT"], io["w13sT"], io["w2sT"], io["out_t"]
    x1res, xn1b, A1, A2, A3 = sb["xres"], sb["xnb"], sb["A1"], sb["A2"], sb["A3"]
    sc0, sc1, sc2 = sb["sc0"], sb["sc1"], sb["sc2"]
    ones_r32, ones_cb, ones_sq, epsc = sb["ones_r32"], sb["ones_cb"], sb["ones_sq"], sb["epsc"]
    idb, iof, pcol = sb["idb"], sb["iof"], sb["pcol"]
    a_sb, a_sbb, h_st = sb["a_sb"], sb["a_sbb"], sb["h_st"]
    srow, abq, tq1, tcomb = sb["srow"], sb["abq"], sb["tq1"], sb["tcomb"]
    pm2 = sb["pm2"]
    ones_sqb = sb["ones_sqb"]
    wb_sb, wc_sb, wg_sb = sb["wb_sb"], sb["wc_sb"], sb["wg_sb"]
    gb_rep, gbr_sb, bdt_sb, dp_sb = sb["gb_rep"], sb["gbr_sb"], sb["bdt_sb"], sb["dp_sb"]
    rcol = sb["rcol"]
    rsc, bia, bwk, mcol, selw, comb_sc, wcol = sb["rsc"], sb["bia"], sb["bwk"], sb["mcol"], sb["selw"], sb["comb_sc"], sb["wcol"]
    ps = sb["ps"]

    A1f = A1[:].bitcast(F32)
    A2f = A2[:].bitcast(F32)
    x1bf = A3[:, 0:JB * T]
    wdt_sb = A3[:, 0:JB * D]
    bT_sb = A3[0:NST, 8192:8192 + T]
    cT_sb = A3[0:NST, 10240:10240 + T]
    combT = A3[0:NST, 12288:12288 + T]
    Esel = A3[0:NST, 14336:14336 + NST * 128]
    io_sc = A2[:, 0:NST * 128]
    m13buf = [A1[:, 0:8192], A1[:, 8192:16384]]
    m2buf = [A2[:, 8192:12288], A2[:, 12288:16384]]
    hb_moe = A2[:, 0:2048]
    crep_moe = A2f[:, 1024:1024 + TW]
    row0 = A1f[0:1, 0:TW]
    row1 = A1f[0:1, TW:2 * TW]
    row2 = A1f[0:1, 2 * TW:3 * TW]
    sqb = sc2[:].bitcast(BF16)[:, 0:TW]

    r = dict(ps=ps, sqb=sqb, sc0=sc0, sc1=sc1, sc2=sc2, row0=row0, row1=row1,
             row2=row2, ones_r32=ones_r32, ones_cb=ones_cb, epsln=epsc[0:1, 0:1])

    p.mark("l1_setup")
    @block.sync
    def _(sync):
        if not fused:
            p.inc("ds", sync.dma_start(
                out=x1res[:].rearrange("p (j t) -> p j t", j=JB),
                in_=io["x1_t"].rearrange("(j p) t -> p j t", p=128)))
        p.inc("ds", sync.dma_start(out=a_sb[:], in_=A_logc))
        p.inc("ds", sync.dma_start(out=bdt_sb[:], in_=bdtc))
        p.inc("ds", sync.dma_start(out=dp_sb[:], in_=Dpc))
        p.inc("ds", sync.dma_start(out=gbr_sb[:], in_=gbr))

    @block.gpsimd
    def _(g):
        p.wait(g, "pe")
        p.wait(g, "ve")
        if not fused:
            p.inc("dw", g.dma_start(
                out=x1bf.rearrange("p (j t) -> p j t", j=JB),
                in_=io["x1_t"].rearrange("(j p) t -> p j t", p=128)))
        p.inc("dw", g.dma_start(
            out=wb_sb[:].rearrange("p (k o) -> p k o", k=JB),
            in_=WBT.rearrange("(k p) o -> p k o", p=128)))
        p.inc("dw", g.dma_start(
            out=wc_sb[:].rearrange("p (k o) -> p k o", k=JB),
            in_=WCT.rearrange("(k p) o -> p k o", p=128)))
        p.inc("dw", g.dma_start(
            out=wg_sb[:].rearrange("p (k o) -> p k o", k=JB),
            in_=gwT.rearrange("(k p) o -> p k o", p=128)))
        p.inc("go", g.iota(iof[:], [[1, 128]], base=0, channel_multiplier=0,
                           allow_small_or_imprecise_dtypes=True))
        p.inc("go", g.iota(pcol[:], [[0, 1]], base=0, channel_multiplier=1,
                           allow_small_or_imprecise_dtypes=True))
        p.inc("go", g.iota(io_sc, [[1, NST], [0, 128]], base=0, channel_multiplier=0,
                           allow_small_or_imprecise_dtypes=True))

    if fused:
        for kb in range(JB):
            @block.scalar
            def _(s, kb=kb):
                if kb == 0:
                    p.wait(s, "ve")
                    p.wait(s, "pe")
                p.inc("se", s.activation(x1bf[:, kb * T:(kb + 1) * T],
                                         x1res[:, kb * T:(kb + 1) * T], AF.Copy))

    @block.vector
    def _(v):
        p.wait(v, "go")
        p.inc("ve", v.memset(h_st[:], 0.0))
        p.inc("ve", v.tensor_single_scalar(idb[:], iof[:], pcol[:, 0:1], ALU.is_equal))
        if not fused:
            p.inc("ve", v.memset(ones_r32[:], 1.0))
            p.inc("ve", v.memset(ones_cb[:], 1.0))
            p.inc("ve", v.memset(epsc[:], EPS_LN))

    @block.tensor
    def _(te):
        p.wait(te, "dw")
        p.wait(te, "ds")
        p.wait(te, "ve")
        p.wait(te, "se")
        p.inc("pe", te.matmul(ps[0][:, 0:NEX], ones_r32[:], gbr_sb[:], start=True, stop=True))

    @block.scalar
    def _(s):
        p.wait(s, "pe")
        p.inc("se", s.activation(gb_rep[:], ps[0][:, 0:NEX], AF.Copy))

    @block.vector
    def _(v):
        p.wait(v, "dw")
        p.wait(v, "ds")
        p.inc("ve", v.tensor_scalar_mul(a_sb[:], a_sb[:], K5))

    @block.scalar
    def _(s):
        p.wait(s, "ve")
        p.inc("se", s.activation(a_sb[:], a_sb[:], AF.Tanh))

    @block.scalar
    def _(s):
        p.wait(s, "se")
        p.inc("se", s.activation(a_sb[:], a_sb[:], AF.Exp, scale=10.0))

    @block.vector
    def _(v):
        p.wait(v, "se")
        p.inc("ve", v.tensor_scalar_mul(a_sb[:], a_sb[:], -1.0))

    p.mark("l1_ln")
    emit_ln(p, r, x1bf, lambda kb, tc: x1res[:, fsl(kb, tc)], xn1b)

    @block.gpsimd
    def _(g):
        p.wait(g, "se")
        p.wait(g, "pe")
        p.wait(g, "ve")
        p.inc("dw", g.dma_start(
            out=wdt_sb.rearrange("p (k o) -> p k o", k=JB),
            in_=WdtT.rearrange("(k p) o -> p k o", p=128)))

    @block.vector
    def _(v):
        p.wait(v, "go")
        p.wait(v, "se")
        p.wait(v, "pe")
        p.inc("ve", v.tensor_single_scalar(Esel, io_sc[0:16, :], pcol[0:16, 0:1], ALU.is_equal))

    p.mark("l1_bc")
    for tc in range(TC):
        @block.tensor
        def _(te, tc=tc):
            p.wait(te, "ve")
            p.wait(te, "se")
            for kb in range(JB):
                p.inc("pe", te.matmul(ps[0][0:NST, 0:TW],
                                      wb_sb[:, kb * NST:(kb + 1) * NST],
                                      xn1b[:, fsl(kb, tc)],
                                      start=(kb == 0), stop=(kb == JB - 1)))
            for kb in range(JB):
                p.inc("pe", te.matmul(ps[1][0:NST, 0:TW],
                                      wc_sb[:, kb * NST:(kb + 1) * NST],
                                      xn1b[:, fsl(kb, tc)],
                                      start=(kb == 0), stop=(kb == JB - 1)))

        @block.scalar
        def _(s, tc=tc):
            p.wait(s, "pe")
            p.inc("se", s.activation(bT_sb[:, tc * TW:(tc + 1) * TW], ps[0][0:NST, 0:TW], AF.Copy))
            p.inc("se", s.activation(cT_sb[:, tc * TW:(tc + 1) * TW], ps[1][0:NST, 0:TW], AF.Copy))

    p.mark("l1_route")
    NT = T // 128
    for tt in range(NT):
        @block.tensor
        def _(te, tt=tt):
            p.wait(te, "se")
            p.wait(te, "ve")
            for kb in range(JB):
                p.inc("pe", te.matmul(ps[2][:, 0:NEX],
                                      xn1b[:, kb * T + tt * 128: kb * T + (tt + 1) * 128],
                                      wg_sb[:, kb * NEX:(kb + 1) * NEX],
                                      start=(kb == 0), stop=(kb == JB - 1)))

        @block.scalar
        def _(s, tt=tt):
            p.wait(s, "pe")
            p.inc("se", s.activation(rsc[:], ps[2][:, 0:NEX], AF.Sigmoid))

        @block.vector
        def _(v, tt=tt):
            p.wait(v, "se")
            p.inc("ve", v.tensor_add(bia[:], rsc[:], gb_rep[:]))
            p.inc("ve", v.tensor_copy(bwk[:], bia[:]))
            for rnd in range(4):
                p.inc("ve", v.reduce_max(mcol[:], bwk[:], AX.X))
                if rnd < 3:
                    p.inc("ve", v.tensor_single_scalar(selw[:], bwk[:], mcol[:, 0:1], ALU.is_equal))
                    p.inc("ve", v.scalar_tensor_tensor(bwk[:], selw[:], -1e30, bwk[:],
                                                       ALU.mult, ALU.add))
            p.inc("ve", v.tensor_single_scalar(selw[:], bia[:], mcol[:, 0:1], ALU.is_ge))
            p.inc("ve", v.tensor_mul(selw[:], rsc[:], selw[:]))
            p.inc("ve", v.tensor_reduce(wcol[:], selw[:], AX.X, ALU.add))
            p.inc("ve", v.reciprocal(wcol[:], wcol[:]))
            p.inc("ve", v.tensor_scalar(comb_sc[:], selw[:], wcol[:, 0:1], 2.5,
                                        ALU.mult, ALU.mult))

        @block.tensor
        def _(te, tt=tt):
            p.wait(te, "ve")
            p.inc("pe", te.transpose(ps[3][0:NEX, 0:128], comb_sc[:], idb[:]))

        @block.scalar
        def _(s, tt=tt):
            p.wait(s, "pe")
            p.inc("se", s.activation(combT[:, tt * 128:(tt + 1) * 128],
                                     ps[3][0:NEX, 0:128], AF.Copy))

    @block.vector
    def _(v):
        p.wait(v, "pe")
        p.inc("ve", v.memset(ones_sq[:], 1.0))

    p.mark("l1_scan")
    SCH = 32
    ESC = SCH + 1
    NCHS = T // SCH
    SCS = 128
    ESS = SCS + 1
    A1b = A1[:]
    A1f = A1[:].bitcast(F32)
    A2b = A2[:]
    A2f2 = A2[:].bitcast(F32)
    dAxb = [A1b[:, 0:4096], A2b[:, 4096:8192]]
    wxA = A2b[:, 0:4096]
    wxB = A1b[:, 12416:16384]
    w32B = A2b[:, 8192:8320]
    qbuf = A1b[:, 4096:4096 + ESC * 128]
    pbuf = A1b[:, 8320:8320 + SCH * 128]
    dtraw = A1f[:, 4160:5200]
    dtrw2 = A1f[:, 6208:7248]
    dt_sc = A2b[:, 8320:8320 + JB * ESS]
    u_sc = A2b[:, 9352:9352 + JB * ESS]
    brep_sc = A2b[:, 10384:10384 + NST * ESS]
    crep_sc = A2b[:, 12448:12448 + NST * ESS]
    ccv_b = [A2f2[:, 7256:7256 + SCH], A2f2[:, 7380:7380 + SCH]]
    cuv = A2f2[:, 7296:7296 + SCH]
    cbv = A2f2[:, 7336:7336 + SCH]
    sc0b = sc0[:].bitcast(BF16)
    sc1b = sc1[:].bitcast(BF16)
    psl = sc0b[:, 256:384]
    redsl = sc0[:, 200:208]

    def w_ap(k, slot):
        if k % 2 == 0:
            return wxA[:, (slot - 1) * 128:slot * 128]
        if slot <= 31:
            return wxB[:, (slot - 1) * 128:slot * 128]
        return w32B[:]

    def w_slot0(k):
        return w_ap(k - 1 if k > 0 else 1, 32)

    def dA_ap(k, slot):
        return dAxb[k % 2][:, (slot - 1) * 128:slot * 128]

    dts_c = dt_sc.rearrange("p (o i) -> p o i", i=ESS)
    us_c = u_sc.rearrange("p (o i) -> p o i", i=ESS)
    br_c = brep_sc.rearrange("p (n i) -> p n i", i=ESS)
    cr_c = crep_sc.rearrange("p (n i) -> p n i", i=ESS)
    pb4 = pbuf.rearrange("p (t o n) -> p t o n", o=JB, n=NST)
    x1v = x1res[:].rearrange("p (o t) -> p o t", o=JB)
    redsl3 = redsl.unsqueeze(2)
    prep_ready = {}

    def emit_recon(v, ch, j):
        screl = (ch % (SCS // SCH)) * SCH
        tg = ch * SCH + j
        p.inc("ve", v.tensor_tensor(
            psl.rearrange("p (o n) -> p o n", o=JB),
            pb4[:, j],
            cr_c[:, :, screl + j].unsqueeze(1).broadcast_to((128, JB, NST)),
            ALU.mult))
        p.inc("ve", v.tensor_reduce(
            redsl[:], psl.rearrange("p (o n) -> p o n", o=JB), AX.X, ALU.add))
        p.inc("ve", v.scalar_tensor_tensor(
            x1v[:, :, tg:tg + 1], redsl3, srow[:, j + 1:j + 2],
            x1v[:, :, tg:tg + 1], ALU.mult, ALU.add))

    def mk_prep_stages(k1):
        screl1 = (k1 % (SCS // SCH)) * SCH
        par1 = k1 % 2
        dAxT = dAxb[par1]
        dA4 = dAxT.rearrange("p (t o n) -> p t o n", o=JB, n=NST)
        vals = {}
        asb3 = a_sbb[:].rearrange("p (o n) -> p o n", o=JB).unsqueeze(1)

        def dA_slice(v, sl):
            lo, hi = sl * 8, sl * 8 + 8
            dtt = dts_c[:, :, screl1 + 1 + lo:screl1 + 1 + hi].transpose([0, 2, 1])
            p.inc("ve", v.tensor_tensor(
                dA4[:, lo:hi], dtt.unsqueeze(3).broadcast_to((128, 8, JB, NST)),
                asb3.broadcast_to((128, 8, JB, NST)), ALU.mult))
            vals[f"dv{sl}"] = p.val("ve")

        def dA_act(s, sl):
            p.wait(s, "ve", vals[f"dv{sl}"])
            seg = dAxT[:, sl * 1024:(sl + 1) * 1024]
            p.inc("se", s.activation(seg, seg, AF.Tanh, scale=K8))
            p.inc("se", s.activation(seg, seg, AF.Exp, scale=16.0))
            if sl == 3:
                prep_ready[k1] = p.val("se")

        def w_slice(v, sl):
            lo = sl * 8
            nmain = 8 if (par1 == 0 or sl < 3) else 7
            wT = wxA if par1 == 0 else wxB
            w4 = wT.rearrange("p (t o n) -> p t o n", o=JB, n=NST)
            utt = us_c[:, :, screl1 + 1 + lo:screl1 + 1 + lo + nmain] \
                .transpose([0, 2, 1])
            p.inc("ve", v.tensor_tensor(
                w4[:, lo:lo + nmain],
                utt.unsqueeze(3).broadcast_to((128, nmain, JB, NST)),
                br_c[:, :, screl1 + 1 + lo:screl1 + 1 + lo + nmain]
                .transpose([0, 2, 1]).unsqueeze(2)
                .broadcast_to((128, nmain, JB, NST)), ALU.mult))
            if nmain == 7:
                w14 = w32B.rearrange("p (t o n) -> p t o n", o=JB, n=NST)
                p.inc("ve", v.tensor_tensor(
                    w14,
                    us_c[:, :, screl1 + 32:screl1 + 33].transpose([0, 2, 1])
                    .unsqueeze(3).broadcast_to((128, 1, JB, NST)),
                    br_c[:, :, screl1 + 32:screl1 + 33].transpose([0, 2, 1])
                    .unsqueeze(2).broadcast_to((128, 1, JB, NST)), ALU.mult))

        def c_squ(v):
            p.inc("ve", v.tensor_tensor(
                sc0b[:, 0:JB * SCH].rearrange("p (o i) -> p o i", i=SCH),
                us_c[:, :, screl1:screl1 + SCH],
                us_c[:, :, screl1:screl1 + SCH], ALU.mult))
            vals["squ"] = p.val("ve")

        def c_mm(te):
            p.wait(te, "ve", vals["squ"])
            p.inc("pe", te.matmul(ps[7][:, 0:JB * SCH], ones_sqb[:],
                                  sc0b[:, 0:JB * SCH], start=True, stop=True))
            vals["cmm"] = p.val("pe")

        def c_fin(v):
            p.wait(v, "pe", vals["cmm"])
            p.inc("ve", v.tensor_reduce(
                cuv[:], ps[7][:, 0:JB * SCH].rearrange("p (o i) -> p i o", o=JB),
                AX.X, ALU.add))
            p.inc("ve", v.tensor_tensor(
                sc1b[:, 0:NST * SCH].rearrange("p (n i) -> p n i", i=SCH),
                br_c[:, :, screl1:screl1 + SCH],
                br_c[:, :, screl1:screl1 + SCH], ALU.mult))
            p.inc("ve", v.tensor_reduce(
                cbv[:], sc1b[:, 0:NST * SCH].rearrange("p (n i) -> p i n", n=NST),
                AX.X, ALU.add))
            p.inc("ve", v.tensor_mul(ccv_b[par1][:], cuv[:], cbv[:]))

        def w0_extra(v):
            w14 = w32B.rearrange("p (t o n) -> p t o n", o=JB, n=NST)
            p.inc("ve", v.tensor_tensor(
                w14,
                us_c[:, :, 0:1].transpose([0, 2, 1]).unsqueeze(3)
                .broadcast_to((128, 1, JB, NST)),
                br_c[:, :, 0:1].transpose([0, 2, 1]).unsqueeze(2)
                .broadcast_to((128, 1, JB, NST)), ALU.mult))

        vstages = [lambda v, sl=sl: dA_slice(v, sl) for sl in range(4)]
        vstages += [lambda v, sl=sl: w_slice(v, sl) for sl in range(4)]
        vstages += [c_squ, c_fin]
        sstages = [lambda s, sl=sl: dA_act(s, sl) for sl in range(4)]
        return dict(v=vstages, s=sstages, te=[c_mm], w0=w0_extra)

    def emit_prep_serial(k1, with_w0=False):
        st = mk_prep_stages(k1)

        @block.vector
        def _(v, st=st):
            p.wait(v, "se")
            p.wait(v, "pe")
            for sl in range(4):
                st["v"][sl](v)

        @block.scalar
        def _(s, st=st):
            for sl in range(4):
                st["s"][sl](s)

        @block.vector
        def _(v, st=st, with_w0=with_w0):
            for sl in range(4):
                st["v"][4 + sl](v)
            if with_w0:
                st["w0"](v)
            st["v"][8](v)

        @block.tensor
        def _(te, st=st):
            st["te"][0](te)

        @block.vector
        def _(v, st=st):
            st["v"][9](v)

    def emit_sprep(st0, NE):
        @block.tensor
        def _(te, st0=st0, NE=NE):
            p.wait(te, "ve")
            p.wait(te, "se")
            p.wait(te, "dw")
            for ob in range(JB):
                bank, off = divmod(ob, 3)
                for kb in range(JB):
                    p.inc("pe", te.matmul(
                        ps[bank][:, off * 130:off * 130 + NE],
                        wdt_sb[:, kb * D + ob * 128: kb * D + ob * 128 + 128],
                        xn1b[:, kb * T + st0: kb * T + st0 + NE],
                        start=(off == 0 and kb == 0),
                        stop=(kb == JB - 1)))

        @block.vector
        def _(v, NE=NE):
            p.wait(v, "pe")
            for ob in range(JB):
                bank, off = divmod(ob, 3)
                p.inc("ve", v.tensor_single_scalar(
                    dtraw[:, ob * 130:ob * 130 + NE],
                    ps[bank][:, off * 130:off * 130 + NE],
                    bdt_sb[:, ob:ob + 1], ALU.add))

        dra = dtraw.rearrange("p (o i) -> p o i", i=130)[:, :, 0:NE]
        dr2 = dtrw2.rearrange("p (o i) -> p o i", i=130)[:, :, 0:NE]

        @block.scalar
        def _(s, dra=dra, dr2=dr2):
            p.wait(s, "ve")
            p.inc("se", s.activation(dr2, dra, AF.Abs))
            p.inc("se", s.activation(dr2, dr2, AF.Exp, scale=-1.0))
            p.inc("se", s.activation(dr2, dr2, AF.Ln, bias=1.0))

        @block.vector
        def _(v, dra=dra, dr2=dr2):
            p.wait(v, "se")
            p.inc("ve", v.tensor_scalar_max(dra, dra, 0.0))
            p.inc("ve", v.tensor_add(dra, dra, dr2))
            p.inc("ve", v.tensor_scalar(dra, dra, 0.0505, KDT,
                                        ALU.subtract, ALU.mult))

        @block.scalar
        def _(s, dra=dra):
            p.wait(s, "ve")
            p.inc("se", s.activation(dra, dra, AF.Tanh))

        @block.vector
        def _(v, st0=st0, NE=NE, dra=dra):
            p.wait(v, "se")
            dts = dt_sc.rearrange("p (o i) -> p o i", i=ESS)[:, :, 0:NE]
            p.inc("ve", v.tensor_scalar(dts, dra, 0.099, 0.0505,
                                        ALU.mult, ALU.add))
            p.inc("ve", v.tensor_tensor(
                u_sc.rearrange("p (o i) -> p o i", i=ESS)[:, :, 0:NE],
                dts,
                xn1b[:].rearrange("p (o t) -> p o t", o=JB)[:, :, st0:st0 + NE],
                ALU.mult))

        for n in range(NST):
            @block.tensor
            def _(te, st0=st0, NE=NE, n=n):
                p.wait(te, "se")
                if n == 0:
                    p.wait(te, "ve")
                p.inc("pe", te.matmul(ps[3][:, 0:NE],
                                      Esel[:, n * 128:(n + 1) * 128],
                                      bT_sb[:, st0:st0 + NE], start=True, stop=True))
                p.inc("pe", te.matmul(ps[4][:, 0:NE],
                                      Esel[:, n * 128:(n + 1) * 128],
                                      cT_sb[:, st0:st0 + NE], start=True, stop=True))

            @block.scalar
            def _(s, NE=NE, n=n):
                p.wait(s, "pe")
                p.inc("se", s.activation(brep_sc[:, n * ESS:n * ESS + NE],
                                         ps[3][:, 0:NE], AF.Copy))
                p.inc("se", s.activation(crep_sc[:, n * ESS:n * ESS + NE],
                                         ps[4][:, 0:NE], AF.Copy))

    if "scan" not in skip:
        @block.scalar
        def _(s):
            p.wait(s, "ve")
            p.wait(s, "pe")
            p.inc("se", s.activation(a_sbb[:], a_sb[:], AF.Copy))

        @block.vector
        def _(v):
            p.wait(v, "se")
            p.inc("ve", v.memset(qbuf[:, 0:128].bitcast(F32), 0.0))
            p.inc("ve", v.memset(srow[:], 1.0))
            p.inc("ve", v.memset(abq[:].bitcast(F32), 0.0))
            p.inc("ve", v.memset(ones_sqb[:], 1.0))

        @block.tensor
        def _(te):
            p.wait(te, "ve")
            p.inc("pe", te.matmul(ps[5][:, 0:2], ones_sqb[:], abq[:, 0:2],
                                  start=True, stop=True))
            p.inc("pe", te.matmul(ps[6][:, 0:2], ones_sqb[:], abq[:, 2:4],
                                  start=True, stop=True))

    VSCHED = {2: 0, 3: 1, 4: 2, 5: 3, 7: 4, 8: 5, 9: 6, 10: 7, 11: 8, 13: 9}
    SSCHED = {4: 0, 6: 1, 8: 2, 10: 3}
    TSCHED = {12: 0}

    for ch in range(NCHS if "scan" not in skip else 0):
        t0 = ch * SCH
        screl = (ch % (SCS // SCH)) * SCH
        at_super = screl == 0
        if at_super:
            if ch > 0:
                @block.vector
                def _(v, ch=ch):
                    emit_recon(v, ch - 1, 30)
                    emit_recon(v, ch - 1, 31)
            st0 = (ch // (SCS // SCH)) * SCS
            emit_sprep(st0, ESS if st0 + SCS < T else SCS)
            emit_prep_serial(ch, with_w0=(ch == 0))
        if ch > 0:
            @block.vector
            def _(v):
                p.inc("ve", v.tensor_copy(qbuf[:, 0:128],
                                          qbuf[:, SCH * 128:(SCH + 1) * 128]))
                p.inc("ve", v.tensor_copy(srow[:, 0:1], srow[:, SCH:SCH + 1]))

        do_prep = (ch + 1 < NCHS) and ((ch + 1) % (SCS // SCH) != 0)
        stg = mk_prep_stages(ch + 1) if do_prep else None
        ccv_cur = ccv_b[ch % 2]

        for tl in range(SCH):
            t = t0 + tl
            par = t % 2
            npar = (t + 1) % 2
            last = (t == T - 1)
            se_sqrt = p.val("se") + 1
            marks = {}
            recon_job = None
            if tl >= 2:
                recon_job = (ch, tl - 2)
            elif not at_super and ch > 0:
                recon_job = (ch - 1, 30 + tl)

            @block.vector
            def _(v, tl=tl, ch=ch, par=par, npar=npar, last=last,
                  se_sqrt=se_sqrt, marks=marks, recon_job=recon_job, stg=stg):
                if tl == 0 and ch in prep_ready:
                    p.wait(v, "se", prep_ready[ch])
                p.inc("ve", v.scalar_tensor_tensor(
                    pbuf[:, tl * 128:(tl + 1) * 128],
                    qbuf[:, tl * 128:(tl + 1) * 128],
                    srow[:, tl:tl + 1],
                    (w_slot0(ch) if tl == 0 else w_ap(ch, tl)),
                    ALU.mult, ALU.add))
                if not last:
                    p.inc("ve", v.tensor_mul(qbuf[:, (tl + 1) * 128:(tl + 2) * 128],
                                             pbuf[:, tl * 128:(tl + 1) * 128],
                                             dA_ap(ch, tl + 1)))
                    marks["q"] = p.val("ve")
                    with p.nc.allow_low_precision(reason="b partial sum bf16"):
                        p.inc("ve", v.scalar_tensor_tensor(
                            h_st[:], qbuf[:, (tl + 1) * 128:(tl + 2) * 128], 2.0,
                            w_ap(ch, tl + 1), ALU.mult, ALU.mult,
                            accum_out=abq[:, 2 * npar + 1:2 * npar + 2]))
                    marks["b"] = p.val("ve")
                p.wait(v, "pe")
                p.inc("ve", v.tensor_single_scalar(tq1[:], ps[5 + par][:, 0:1],
                                                   srow[:, tl:tl + 1], ALU.mult))
                p.inc("ve", v.tensor_tensor(tcomb[:], tq1[:], ps[5 + par][:, 1:2],
                                            ALU.add))
                marks["x"] = p.val("ve")
                if recon_job is not None:
                    emit_recon(v, recon_job[0], recon_job[1])
                if stg is not None and tl in VSCHED:
                    stg["v"][VSCHED[tl]](v)
                p.wait(v, "se", se_sqrt)
                p.inc("ve", v.tensor_scalar_add(rcol[:], rcol[:], 1e-8))
                p.inc("ve", v.reciprocal(rcol[:], rcol[:]))
                p.inc("ve", v.tensor_scalar(srow[:, tl + 1:tl + 2], rcol[:],
                                            20.0, 1.0, ALU.mult, ALU.min))

            if not last:
                @block.gpsimd
                def _(g, tl=tl, marks=marks):
                    p.wait(g, "ve", marks["q"])
                    p.inc("po", g.tensor_mul(pm2[:],
                                             qbuf[:, (tl + 1) * 128:(tl + 2) * 128],
                                             qbuf[:, (tl + 1) * 128:(tl + 2) * 128]))

            @block.scalar
            def _(s, tl=tl, npar=npar, last=last, marks=marks,
                  ccv_cur=ccv_cur, stg=stg):
                p.wait(s, "ve", marks["x"])
                p.inc("se", s.activation(rcol[:], tcomb[:], AF.Sqrt,
                                         scale=srow[:, tl:tl + 1],
                                         bias=ccv_cur[:, tl:tl + 1]))
                if not last:
                    p.wait(s, "po")
                    with p.nc.allow_low_precision(reason="a partial sum bf16"):
                        p.inc("se", s.activation(sc2[:, 0:128], pm2[:], AF.Copy,
                                                 accum_out=abq[:, 2 * npar:2 * npar + 1]))
                marks["cp"] = p.val("se")
                if stg is not None and tl in SSCHED:
                    stg["s"][SSCHED[tl]](s)

            if not last:
                @block.tensor
                def _(te, npar=npar, marks=marks, stg=stg, tl=tl):
                    p.wait(te, "se", marks["cp"])
                    p.wait(te, "ve", marks["b"])
                    p.inc("pe", te.matmul(ps[5 + npar][:, 0:2], ones_sqb[:],
                                          abq[:, 2 * npar:2 * npar + 2],
                                          start=True, stop=True))
                    if stg is not None and tl in TSCHED:
                        stg["te"][TSCHED[tl]](te)

        if ch == NCHS - 1:
            @block.vector
            def _(v, ch=ch):
                emit_recon(v, ch, 30)
                emit_recon(v, ch, 31)


    p.mark("l1_dp")
    for tc in range(TC if "dp" not in skip else 0):
        @block.vector
        def _(v, tc=tc):
            p.wait(v, "dw")
            p.wait(v, "ds")
            for j in range(JB):
                p.inc("ve", v.scalar_tensor_tensor(
                    x1res[:, fsl(j, tc)], xn1b[:, fsl(j, tc)], dp_sb[:, j:j + 1],
                    x1res[:, fsl(j, tc)], ALU.mult, ALU.add))

    p.mark("l1_moe")
    silu_sl = [A2f[:, 1536 + 512 * i:2048 + 512 * i] for i in range(4)]
    pe_after_expert = {}
    moe_prev = None
    for ei in range((NEX + 1) if "moe" not in skip else 0):
        shared = (ei == NEX)
        buf = ei % 2

        @block.gpsimd
        def _(g, ei=ei, buf=buf, shared=shared):
            if ei < 2:
                p.wait(g, "pe")
            else:
                p.wait(g, "pe", pe_after_expert[ei - 2])
            for kb in range(JB):
                src = (w13sT.rearrange("(k p) o -> k p o", p=128)[kb] if shared
                       else w13T.rearrange("e (k p) o -> e k p o", p=128)[ei, kb])
                p.inc("dw", g.dma_start(out=m13buf[buf][:, kb * 1024:(kb + 1) * 1024], in_=src))
            for ib in range(4):
                src = (w2sT.rearrange("(k p) o -> k p o", p=128)[ib] if shared
                       else w2mT.rearrange("e (k p) o -> e k p o", p=128)[ei, ib])
                p.inc("dw", g.dma_start(out=m2buf[buf][:, ib * 1024:(ib + 1) * 1024], in_=src))

        for tc in range(TC):
            first = moe_prev is None
            pm = moe_prev
            m = {}

            if not shared:
                @block.tensor
                def _(te, ei=ei, tc=tc, pm=pm, first=first):
                    if first:
                        p.wait(te, "ve")
                        p.wait(te, "se")
                    else:
                        p.wait(te, "ve", pm["ep0"])
                    p.inc("pe", te.matmul(ps[0][:, 0:TW], Esel[:, ei * 128:(ei + 1) * 128],
                                          combT[:, tc * TW:(tc + 1) * TW],
                                          start=True, stop=True))

                @block.scalar
                def _(s, ei=ei, tc=tc, m=m):
                    p.wait(s, "pe")
                    p.inc("se", s.activation(crep_moe, ps[0][:, 0:TW], AF.Copy))
                    m["crep"] = p.val("se")

            @block.tensor
            def _(te, ei=ei, tc=tc, buf=buf, pm=pm, m=m, first=first, shared=shared):
                if tc == 0:
                    p.wait(te, "dw")
                if first:
                    p.wait(te, "ve")
                    p.wait(te, "se")
                for ib in range(4):
                    if not first:
                        p.wait(te, "ve", pm[f"ep{ib}"])
                    if ib == 0 and not shared:
                        p.wait(te, "se", m["crep"])
                    for kb in range(JB):
                        p.inc("pe", te.matmul(
                            ps[ib][:, 0:TW],
                            m13buf[buf][:, kb * 1024 + ib * 128: kb * 1024 + ib * 128 + 128],
                            xn1b[:, fsl(kb, tc)], start=(kb == 0), stop=(kb == JB - 1)))
                    if not first:
                        p.wait(te, "ve", pm[f"ep{4 + ib}"])
                    for kb in range(JB):
                        p.inc("pe", te.matmul(
                            ps[4 + ib][:, 0:TW],
                            m13buf[buf][:, kb * 1024 + 512 + ib * 128: kb * 1024 + 512 + ib * 128 + 128],
                            xn1b[:, fsl(kb, tc)], start=(kb == 0), stop=(kb == JB - 1)))
                    m[f"gu{ib}"] = p.val("pe")

            @block.scalar
            def _(s, ei=ei, tc=tc, m=m):
                for ib in range(4):
                    p.wait(s, "pe", m[f"gu{ib}"])
                    p.inc("se", s.activation(silu_sl[ib], ps[ib][:, 0:TW], AF.Silu,
                                             scale=1.0 / 127.0))
                    m[f"si{ib}"] = p.val("se")

            @block.vector
            def _(v, ei=ei, tc=tc, m=m, pm=pm, shared=shared, first=first):
                if not first and "down_done" in pm:
                    p.wait(v, "pe", pm["down_done"])
                for ib in range(4):
                    p.wait(v, "se", m[f"si{ib}"])
                    if shared:
                        p.inc("ve", v.scalar_tensor_tensor(
                            hb_moe[:, ib * TW:(ib + 1) * TW], ps[4 + ib][:, 0:TW],
                            1.0 / 127.0, silu_sl[ib], ALU.mult, ALU.mult))
                    else:
                        p.inc("ve", v.scalar_tensor_tensor(
                            sc1[:, 0:TW], ps[4 + ib][:, 0:TW],
                            1.0 / 127.0, silu_sl[ib], ALU.mult, ALU.mult))
                        p.inc("ve", v.tensor_mul(hb_moe[:, ib * TW:(ib + 1) * TW],
                                                 sc1[:, 0:TW], crep_moe))
                    m[f"hp{ib}"] = p.val("ve")

            @block.tensor
            def _(te, ei=ei, tc=tc, buf=buf, m=m):
                p.wait(te, "ve", m["hp3"])
                m["down0"] = p.val("pe")
                for ib in range(4):
                    for ob in range(JB):
                        p.inc("pe", te.matmul(
                            ps[ob][:, 0:TW],
                            m2buf[buf][:, ib * 1024 + ob * 128: ib * 1024 + ob * 128 + 128],
                            hb_moe[:, ib * TW:(ib + 1) * TW],
                            start=(ib == 0), stop=(ib == 3)))
                m["down_done"] = p.val("pe")

            @block.vector
            def _(v, ei=ei, tc=tc, m=m):
                for ob in range(JB):
                    p.wait(v, "pe", m["down0"] + 3 * JB + ob + 1)
                    p.inc("ve", v.scalar_tensor_tensor(
                        x1res[:, fsl(ob, tc)], ps[ob][:, 0:TW], 1.0 / 127.0,
                        x1res[:, fsl(ob, tc)], ALU.mult, ALU.add))
                    m[f"ep{ob}"] = p.val("ve")

            moe_prev = m

        pe_after_expert[ei] = p.val("pe")

    p.mark(None)
    @block.sync
    def _(sync):
        p.wait(sync, "ve")
        p.inc("dx", sync.dma_start(
            out=out_t.rearrange("(j p) t -> p j t", p=128),
            in_=x1res[:].rearrange("p (j t) -> p j t", j=JB)))
        p.wait(sync, "dx")


def _mk_io_l0(nc, out_kind="Internal"):
    ap = lambda n, shp, dt=F32, kind="ExternalInput": nc.dram_tensor(n, shp, dt, kind=kind).ap()
    io = {
        "x_t": ap("x_t", [D, T]),
        "wqT": ap("wqT", [D, D]),
        "wkT": ap("wkT", [D, D]),
        "wvT": ap("wvT", [D, D]),
        "w1T": ap("w1T", [D, DFF]),
        "w2T": ap("w2T", [DFF, D]),
        "b1c": ap("b1c", [128, DFF // 128]),
        "b2c": ap("b2c", [128, JB]),
        "g1c": ap("g1c", [128, JB]),
        "be1c": ap("be1c", [128, JB]),
        "g2c": ap("g2c", [128, JB]),
        "be2c": ap("be2c", [128, JB]),
        "wgT": ap("wgT", [D, HID], I32),
        "wuT": ap("wuT", [D, HID], I32),
        "wdT": ap("wdT", [HID, D], I32),
    }
    return io


def _mk_io_l1(nc, with_x1):
    ap = lambda n, shp, dt=F32, kind="ExternalInput": nc.dram_tensor(n, shp, dt, kind=kind).ap()
    io = {
        "WdtT": ap("WdtT", [D, D]),
        "bdtc": ap("bdtc", [128, JB]),
        "WBT": ap("WBT", [D, NST]),
        "WCT": ap("WCT", [D, NST]),
        "A_logc": ap("A_logc", [128, 128]),
        "Dpc": ap("Dpc", [128, JB]),
        "gwT": ap("gwT", [D, NEX]),
        "gbr": ap("gbr", [1, NEX]),
        "w13T": ap("w13T", [NEX, D, D]),
        "w2mT": ap("w2mT", [NEX, INTER, D]),
        "w13sT": ap("w13sT", [D, D]),
        "w2sT": ap("w2sT", [INTER, D]),
        "out_t": ap("out_t", [D, T], kind="ExternalOutput"),
    }
    if with_x1:
        io["x1_t"] = ap("x1_t", [D, T])
    return io


def build_full(skip=(), debug_taps=False):
    nc = bass.Bass("TRN2", target_bir_lowering=False)
    io0 = _mk_io_l0(nc)
    io1 = _mk_io_l1(nc, with_x1=False)
    dbg = None
    if debug_taps:
        dbg = nc.dram_tensor("dbg_t", [D, T], F32, kind="ExternalOutput").ap()
    with ExitStack() as ctx:
        e = ctx.enter_context
        p = Prog(nc, ctx)
        sb = _alloc(nc, e)
        _emit_l0(p, io0, sb)
        if debug_taps:
            block = p.block
            @block.sync
            def _(sync):
                p.wait(sync, "ve")
                p.wait(sync, "se")
                p.wait(sync, "pe")
                p.inc("dx", sync.dma_start(
                    out=dbg.rearrange("(j p) t -> p j t", p=128),
                    in_=sb["xres"][:].rearrange("p (j t) -> p j t", j=JB)))
                p.wait(sync, "dx")
        _emit_l1(p, io1, sb, fused=True, skip=skip)
    return nc


def build_l0():
    nc = bass.Bass("TRN2", target_bir_lowering=False)
    io0 = _mk_io_l0(nc)
    x1_t = nc.dram_tensor("x1_t", [D, T], F32, kind="ExternalOutput").ap()
    with ExitStack() as ctx:
        e = ctx.enter_context
        p = Prog(nc, ctx)
        sb = _alloc(nc, e)
        _emit_l0(p, io0, sb)
        block = p.block

        @block.sync
        def _(sync):
            p.wait(sync, "ve")
            p.inc("dx", sync.dma_start(
                out=x1_t.rearrange("(j p) t -> p j t", p=128),
                in_=sb["xres"][:].rearrange("p (j t) -> p j t", j=JB)))
            p.wait(sync, "dx")
    return nc


def build_l1():
    nc = bass.Bass("TRN2", target_bir_lowering=False)
    io1 = _mk_io_l1(nc, with_x1=True)
    with ExitStack() as ctx:
        e = ctx.enter_context
        p = Prog(nc, ctx)
        sb = _alloc(nc, e)
        _emit_l1(p, io1, sb, fused=False)
    return nc



def _host_weights(inputs):
    f = np.ascontiguousarray
    return {
        "wqT": f(inputs["L0_attn_wq"].T),
        "wkT": f(inputs["L0_attn_wk"].T),
        "wvT": f(inputs["L0_attn_wv"].T),
        "w1T": f(inputs["L0_attn_w1"].T),
        "w2T": f(inputs["L0_attn_w2"].T),
        "b1c": f(inputs["L0_attn_b1"].reshape(DFF // 128, 128).T),
        "b2c": f(inputs["L0_attn_b2"].reshape(JB, 128).T),
        "g1c": f(inputs["L0_attn_g1"].reshape(JB, 128).T),
        "be1c": f(inputs["L0_attn_be1"].reshape(JB, 128).T),
        "g2c": f(inputs["L0_attn_g2"].reshape(JB, 128).T),
        "be2c": f(inputs["L0_attn_be2"].reshape(JB, 128).T),
        "wgT": f(inputs["L0_ffn_w_gate"].T),
        "wuT": f(inputs["L0_ffn_w_up"].T),
        "wdT": f(inputs["L0_ffn_w_down"].T),
        "WdtT": f(inputs["L1_attn_Wdt"].T),
        "bdtc": f(inputs["L1_attn_bdt"].reshape(JB, 128).T),
        "WBT": f(inputs["L1_attn_WB"].T),
        "WCT": f(inputs["L1_attn_WC"].T),
        "A_logc": f(inputs["L1_attn_A_log"].reshape(JB, 128, NST).transpose(1, 0, 2).reshape(128, 128)),
        "Dpc": f(inputs["L1_attn_Dp"].reshape(JB, 128).T),
        "gwT": f(inputs["L1_ffn_gate_w"].T),
        "gbr": f(inputs["L1_ffn_gate_b"].reshape(1, NEX)),
        "w13T": f(inputs["L1_ffn_w13"].transpose(0, 2, 1)),
        "w2mT": f(inputs["L1_ffn_w2"].transpose(0, 2, 1)),
        "w13sT": f(inputs["L1_ffn_w13_s"].T),
        "w2sT": f(inputs["L1_ffn_w2_s"].T),
    }


def _l0_host_inputs(inputs):
    w = {k: v for k, v in _host_weights(inputs).items()
         if k in ("wqT", "wkT", "wvT", "w1T", "w2T", "b1c", "b2c", "g1c", "be1c",
                  "g2c", "be2c", "wgT", "wuT", "wdT")}
    x = np.asarray(inputs["L0_attn_x"])
    return [dict(w, x_t=np.ascontiguousarray(x[c].T)) for c in range(8)]


def _l1_host_inputs(inputs, x1_list):
    w = {k: v for k, v in _host_weights(inputs).items()
         if k not in ("wqT", "wkT", "wvT", "w1T", "w2T", "b1c", "b2c", "g1c",
                      "be1c", "g2c", "be2c", "wgT", "wuT", "wdT")}
    return [dict(w, x1_t=np.ascontiguousarray(x1_list[c])) for c in range(8)]


def kernel(**inputs):
    _ = inputs.get("L1_ffn_input_ids")
    nc = build_full()
    w = _host_weights(inputs)
    x = np.asarray(inputs["L0_attn_x"])
    in_maps = [dict(w, x_t=np.ascontiguousarray(x[c].T)) for c in range(8)]
    res = run_bass_kernel_spmd(nc, in_maps, core_ids=list(range(8)))
    out = np.stack([np.asarray(r["out_t"]).T for r in res.results], axis=0)
    return out.astype(np.float32)


# version 13
# speedup vs baseline: 1.3119x; 1.0479x over previous
import numpy as np
import ml_dtypes
from contextlib import ExitStack

import concourse.bass as bass
import concourse.mybir as mybir
from concourse.bass_utils import run_bass_kernel_spmd

F32 = mybir.dt.float32
BF16 = mybir.dt.bfloat16
I32 = mybir.dt.int32
AF = mybir.ActivationFunctionType
ALU = mybir.AluOpType
AX = mybir.AxisListType
BT = ml_dtypes.bfloat16

D = 1024
T = 2048
JB = 8
NH = 8
DFF = 4096
HID = 3072
EPS_LN = 1e-5
EPS_ATTN = 1e-6
TW = 512
TC = T // TW
NST = 16
TCH = 64
NCH = T // TCH
INTER = 512
NEX = 16
K8 = 0.5 / (8.0 + 1e-8)
K5 = 0.5 / (5.0 + 1e-8)
KDT = 0.5 / (0.0495 + 1e-8)


class Prog:
    def __init__(self, nc, ctx):
        self.nc = nc
        self.ctx = ctx
        self.block = ctx.enter_context(nc.Block())
        self._sems = {}

    def sem(self, name):
        if name not in self._sems:
            h = self.ctx.enter_context(self.nc.semaphore(name))
            self._sems[name] = [h, 0]
        return self._sems[name]

    def inc(self, name, inst, k=1):
        if name.startswith("d"):
            k *= 16
        s = self.sem(name)
        s[1] += k
        inst.then_inc(s[0], k)
        return s[1]

    def val(self, name):
        return self.sem(name)[1]

    def wait(self, eng, name, v=None):
        s = self.sem(name)
        v = s[1] if v is None else v
        if v > 0:
            eng.wait_ge(s[0], v)

    _scope = None

    def mark(self, name):
        if self._scope is not None:
            self.nc.leave_named_scope(self._scope[0], self._scope[1], False)
            self._scope = None
        if name is not None:
            sid, _ = self.nc.enter_named_scope(name, False)
            self._scope = (name, sid)


def fsl(kb, tc, w=TW):
    return slice(kb * T + tc * w, kb * T + (tc + 1) * w)


def emit_ln(p, r, src_bf, val_src, dst_bf, gcol=None, bcol=None):
    block = p.block
    ps = r["ps"]
    for tc in range(TC):
        for kb in range(JB):
            @block.scalar
            def _(s, kb=kb, tc=tc):
                p.wait(s, "pe")
                if kb == 0:
                    p.wait(s, "dw")
                    p.wait(s, "ds")
                    p.wait(s, "ve")
                p.inc("se", s.activation(r["sqb"][:, 0:TW], src_bf[:, fsl(kb, tc)], AF.Square))

            @block.tensor
            def _(te, kb=kb, tc=tc):
                p.wait(te, "se")
                if kb == 0:
                    p.wait(te, "ve")
                p.inc("pe", te.matmul(ps[7][0:1, 0:TW], r["ones_cb"][:], r["sqb"][:, 0:TW],
                                      start=(kb == 0), stop=(kb == JB - 1)))
                p.inc("pe", te.matmul(ps[6][0:1, 0:TW], r["ones_cb"][:], src_bf[:, fsl(kb, tc)],
                                      start=(kb == 0), stop=(kb == JB - 1)))

        @block.vector
        def _(v, tc=tc):
            p.wait(v, "pe")
            p.inc("ve", v.tensor_scalar_mul(r["row0"], ps[6][0:1, 0:TW], 1.0 / D))
            p.inc("ve", v.tensor_scalar_mul(r["row1"], ps[7][0:1, 0:TW], 1.0 / D))
            p.inc("ve", v.tensor_mul(r["row2"], r["row0"], r["row0"]))
            p.inc("ve", v.tensor_sub(r["row1"], r["row1"], r["row2"]))

        @block.scalar
        def _(s, tc=tc):
            p.wait(s, "ve")
            p.inc("se", s.activation(r["row1"], r["row1"], AF.Ln, bias=r["epsln"]))
            p.inc("se", s.activation(r["row1"], r["row1"], AF.Exp, scale=-0.5))

        @block.tensor
        def _(te, tc=tc):
            p.wait(te, "se")
            p.wait(te, "ve")
            p.inc("pe", te.matmul(ps[4][:, 0:TW], r["ones_r32"][:], r["row0"],
                                  start=True, stop=True))
            p.inc("pe", te.matmul(ps[5][:, 0:TW], r["ones_r32"][:], r["row1"],
                                  start=True, stop=True))

        @block.vector
        def _(v, tc=tc):
            p.wait(v, "pe")
            for kb in range(JB):
                vs = val_src(kb, tc)
                p.inc("ve", v.tensor_sub(r["sc0"][:, 0:TW], vs, ps[4][:, 0:TW]))
                if gcol is None:
                    p.inc("ve", v.tensor_mul(dst_bf[:, fsl(kb, tc)], r["sc0"][:, 0:TW], ps[5][:, 0:TW]))
                else:
                    p.inc("ve", v.tensor_mul(r["sc0"][:, 0:TW], r["sc0"][:, 0:TW], ps[5][:, 0:TW]))
                    p.inc("ve", v.tensor_scalar(dst_bf[:, fsl(kb, tc)], r["sc0"][:, 0:TW],
                                                gcol[:, kb:kb + 1], bcol[:, kb:kb + 1],
                                                ALU.mult, ALU.add))


def emit_elu_epilogue(p, ps_ap, dst_ap, r, w=TW):
    block = p.block

    @block.vector
    def _(v):
        p.wait(v, "pe")
        p.inc("ve", v.tensor_scalar_min(r["sc0"][:, 0:w], ps_ap, 0.0))
        p.inc("ve", v.tensor_sub(r["sc1"][:, 0:w], ps_ap, r["sc0"][:, 0:w]))

    @block.scalar
    def _(s):
        p.wait(s, "ve")
        p.inc("se", s.activation(r["sc2"][:, 0:w], r["sc0"][:, 0:w], AF.Exp))

    @block.vector
    def _(v):
        p.wait(v, "se")
        p.inc("ve", v.tensor_add(dst_ap, r["sc1"][:, 0:w], r["sc2"][:, 0:w]))


def _alloc(nc, e):
    sb = {}
    t = lambda name, shp, dt=F32: e(nc.sbuf_tensor(name, shp, dt))
    sb["xres"] = t("xres", [128, JB * T], F32)
    sb["xnb"] = t("xnb", [128, JB * T], BF16)
    sb["A1"] = t("A1", [128, JB * T], BF16)
    sb["A2"] = t("A2", [128, JB * T], BF16)
    sb["A3"] = t("A3", [128, JB * T], BF16)
    sb["sc0"] = t("sc0", [128, TW], F32)
    sb["sc1"] = t("sc1", [128, TW], F32)
    sb["sc2"] = t("sc2", [128, TW], F32)
    _w1 = t("w1buf", [128, JB * 128], BF16)
    _w2 = t("w2buf", [128, 1024], BF16)
    sb["w1buf"] = [_w1, _w1]
    sb["w2buf"] = [_w2, _w2]
    for n, w in (("b1s", DFF // 128), ("b2s", JB), ("g1s", JB), ("be1s", JB),
                 ("g2s", JB), ("be2s", JB), ("bdt_sb", JB), ("dp_sb", JB)):
        sb[n] = t(n, [128, w], F32)
    sb["ones_r32"] = t("ones_r32", [1, 128], F32)
    sb["ones_cb"] = t("ones_cb", [128, 1], BF16)
    sb["ones_cb2"] = t("ones_cb2", [128, 2], BF16)
    sb["epsc"] = t("epsc", [1, 1], F32)
    sb["idb"] = t("idb", [128, 128], F32)
    sb["ones_sq"] = sb["idb"]
    sb["pcol"] = t("pcol", [128, 1], F32)
    sb["a_sb"] = t("a_sb", [128, 128], F32)
    sb["a_sbb"] = t("a_sbb", [128, 128], BF16)
    sb["h_st"] = t("h_st", [128, 128], F32)
    sb["iof"] = sb["h_st"]
    sb["srow"] = t("srow", [128, 34], F32)
    sb["pm2"] = t("pm2", [128, 128], BF16)
    sb["abq"] = t("abq", [128, 4], BF16)
    sb["tq1"] = t("tq1", [128, 1], F32)
    sb["tcomb"] = t("tcomb", [128, 1], F32)
    sb["ones_sqb"] = t("ones_sqb", [128, 128], BF16)
    sb["wb_sb"] = t("wb_sb", [128, JB * NST], BF16)
    sb["wc_sb"] = t("wc_sb", [128, JB * NST], BF16)
    sb["wg_sb"] = t("wg_sb", [128, JB * NEX], BF16)
    sb["gb_rep"] = t("gb_rep", [128, NEX], F32)
    sb["gbr_sb"] = t("gbr_sb", [1, NEX], F32)
    for n in ("rcol", "mcol", "wcol"):
        sb[n] = t(n, [128, 1], F32)
    for n in ("rsc", "bia", "bwk", "selw", "comb_sc"):
        sb[n] = t(n, [128, NEX], F32)
    sb["ps"] = [e(nc.psum_tensor(f"ps{i}", [128, TW], F32)) for i in range(8)]
    return sb


def _emit_l0(p, io, sb):
    block = p.block
    x_t, wqT, wkT, wvT, w1T, w2T = io["x_t"], io["wqT"], io["wkT"], io["wvT"], io["w1T"], io["w2T"]
    b1c, b2c, g1c, be1c, g2c, be2c = io["b1c"], io["b2c"], io["g1c"], io["be1c"], io["g2c"], io["be2c"]
    wgT, wuT, wdT = io["wgT"], io["wuT"], io["wdT"]
    xres, xnb, A1, A2, A3 = sb["xres"], sb["xnb"], sb["A1"], sb["A2"], sb["A3"]
    sc0, sc1, sc2 = sb["sc0"], sb["sc1"], sb["sc2"]
    w1buf, w2buf = sb["w1buf"], sb["w2buf"]
    b1s, b2s, g1s, be1s, g2s, be2s = sb["b1s"], sb["b2s"], sb["g1s"], sb["be1s"], sb["g2s"], sb["be2s"]
    ones_r32, ones_cb, ones_cb2, epsc = sb["ones_r32"], sb["ones_cb"], sb["ones_cb2"], sb["epsc"]
    ps = sb["ps"]

    xrv = xres[:].bitcast(BF16)
    ktile = xrv[:, 0:1024]
    vtile = xrv[:, 1024:2048]
    kvb = xrv[:, 2048:3072]
    ksb = xrv[:, 3072:3072 + NH]
    wqbuf = xrv[:, 4096:4096 + 1024]
    xrvf = xres[:]
    row0 = xrvf[0:1, 4096:4096 + TW]
    row1 = xrvf[0:1, 4608:4608 + TW]
    row2 = xrvf[0:1, 5120:5120 + TW]
    zrow = xrvf[0:1, 5632:5632 + TW]
    sqb = xrv[:, 12288:12288 + TW]
    wk_sb = A2[:, 0:JB * D]
    wv_sb = A2[:, JB * D:2 * JB * D]

    r = dict(ps=ps, sqb=sqb, sc0=sc0, sc1=sc1, sc2=sc2, row0=row0, row1=row1,
             row2=row2, ones_r32=ones_r32, ones_cb=ones_cb, epsln=epsc[0:1, 0:1])

    @block.gpsimd
    def _(g):
        p.inc("dw", g.dma_start(
            out=A3[:].rearrange("p (j t) -> p j t", j=JB),
            in_=x_t.rearrange("(j p) t -> p j t", p=128)))
        for w, dst in ((wkT, wk_sb), (wvT, wv_sb)):
            p.inc("dw", g.dma_start(
                out=dst.rearrange("p (k o) -> p k o", k=JB),
                in_=w.rearrange("(k p) o -> p k o", p=128)))

    @block.sync
    def _(sync):
        for src, dst in ((b1c, b1s), (b2c, b2s), (g1c, g1s), (be1c, be1s),
                         (g2c, g2s), (be2c, be2s)):
            p.inc("ds", sync.dma_start(out=dst[:], in_=src))

    @block.vector
    def _(v):
        p.inc("ve", v.memset(ones_r32[:], 1.0))
        p.inc("ve", v.memset(ones_cb[:], 1.0))
        p.inc("ve", v.memset(ones_cb2[:], 1.0))
        p.inc("ve", v.memset(epsc[:], EPS_LN))

    p.mark("l0_ln0")
    emit_ln(p, r, A3, lambda kb, tc: A3[:, fsl(kb, tc)], xnb)

    p.mark("l0_ln1")
    emit_ln(p, r, xnb, lambda kb, tc: xnb[:, fsl(kb, tc)], A1, gcol=g1s, bcol=be1s)

    p.mark("l0_kv")
    NT = T // 128
    for tt in range(NT):
        for half in range(2):
            @block.tensor
            def _(te, tt=tt, half=half):
                p.wait(te, "ve")
                p.wait(te, "se")
                for kb in range(JB):
                    p.inc("pe", te.matmul(
                        ps[half][:, 0:TW],
                        A1[:, kb * T + tt * 128: kb * T + (tt + 1) * 128],
                        wk_sb[:, kb * D + half * TW: kb * D + (half + 1) * TW],
                        start=(kb == 0), stop=(kb == JB - 1)))
            emit_elu_epilogue(p, ps[half][:, 0:TW], ktile[:, half * TW:(half + 1) * TW], r)

        for half in range(2):
            @block.tensor
            def _(te, tt=tt, half=half):
                p.wait(te, "ve")
                p.wait(te, "se")
                for kb in range(JB):
                    p.inc("pe", te.matmul(
                        ps[half][:, 0:TW],
                        A1[:, kb * T + tt * 128: kb * T + (tt + 1) * 128],
                        wv_sb[:, kb * D + half * TW: kb * D + (half + 1) * TW],
                        start=(kb == 0), stop=(kb == JB - 1)))

            @block.scalar
            def _(s, tt=tt, half=half):
                p.wait(s, "pe")
                p.inc("se", s.activation(vtile[:, half * TW:(half + 1) * TW],
                                         ps[half][:, 0:TW], AF.Copy))

        @block.tensor
        def _(te, tt=tt):
            p.wait(te, "se")
            p.wait(te, "ve")
            for h in range(NH):
                pbank = ps[2] if h < 4 else ps[3]
                col = (h % 4) * 128
                p.inc("pe", te.matmul(
                    pbank[:, col:col + 128],
                    ktile[:, h * 128:(h + 1) * 128],
                    vtile[:, h * 128:(h + 1) * 128],
                    start=(tt == 0 and (h % 4) == 0),
                    stop=(tt == NT - 1 and (h % 4) == 3)))
            for h in range(NH):
                p.inc("pe", te.matmul(
                    ps[6][:, 2 * h:2 * h + 2],
                    ktile[:, h * 128:(h + 1) * 128],
                    ones_cb2[:],
                    start=(tt == 0 and h == 0), stop=(tt == NT - 1 and h == NH - 1)))

    @block.scalar
    def _(s):
        p.wait(s, "pe")
        p.inc("se", s.activation(kvb[:, 0:TW], ps[2][:, 0:TW], AF.Copy))
        p.inc("se", s.activation(kvb[:, TW:2 * TW], ps[3][:, 0:TW], AF.Copy))
        p.inc("se", s.activation(ksb, ps[6][:, 0:2 * NH:2], AF.Copy))

    p.mark("l0_q")
    for h in range(NH):
        @block.gpsimd
        def _(g, h=h):
            p.wait(g, "pe")
            for kq in range(JB):
                p.inc("dw", g.dma_start(
                    out=wqbuf[:, kq * 128:(kq + 1) * 128],
                    in_=wqT.rearrange("(k p) o -> k p o", p=128)[kq, :, h * 128:(h + 1) * 128]))

        for tc in range(TC):
            bank = tc % 2
            @block.tensor
            def _(te, h=h, tc=tc, bank=bank):
                p.wait(te, "dw")
                p.wait(te, "ve")
                for kb in range(JB):
                    p.inc("pe", te.matmul(
                        ps[bank][:, 0:TW],
                        wqbuf[:, kb * 128:(kb + 1) * 128],
                        A1[:, fsl(kb, tc)],
                        start=(kb == 0), stop=(kb == JB - 1)))
            emit_elu_epilogue(p, ps[bank][:, 0:TW], A2[:, fsl(h, tc)], r)

    p.mark("l0_att")
    for h in range(NH):
        for tc in range(TC):
            @block.tensor
            def _(te, h=h, tc=tc):
                p.wait(te, "se")
                p.wait(te, "ve")
                p.inc("pe", te.matmul(ps[6][0:1, 0:TW], ksb[:, h:h + 1],
                                      A2[:, fsl(h, tc)], start=True, stop=True))

            @block.vector
            def _(v, h=h, tc=tc):
                p.wait(v, "pe")
                p.inc("ve", v.tensor_scalar_add(zrow, ps[6][0:1, 0:TW], EPS_ATTN))
                p.inc("ve", v.reciprocal(zrow, zrow))

            @block.tensor
            def _(te, h=h, tc=tc):
                p.wait(te, "ve")
                p.inc("pe", te.matmul(ps[4][:, 0:TW], ones_r32[:], zrow,
                                      start=True, stop=True))
                p.inc("pe", te.matmul(ps[5][:, 0:TW],
                                      kvb[:, h * 128:(h + 1) * 128],
                                      A2[:, fsl(h, tc)], start=True, stop=True))

            @block.scalar
            def _(s, h=h, tc=tc):
                p.wait(s, "pe")
                p.inc("se", s.activation(sc0[:, 0:TW], ps[4][:, 0:TW], AF.Copy))

            @block.vector
            def _(v, h=h, tc=tc):
                p.wait(v, "se")
                p.inc("ve", v.tensor_mul(sc1[:, 0:TW], ps[5][:, 0:TW], sc0[:, 0:TW]))
                p.inc("ve", v.tensor_add(A3[:, fsl(h, tc)], sc1[:, 0:TW], xnb[:, fsl(h, tc)]))

    p.mark("l0_ln2")
    emit_ln(p, r, A3, lambda kb, tc: A3[:, fsl(kb, tc)], A1, gcol=g2s, bcol=be2s)

    p.mark("l0_ffnA")
    NFB = DFF // 128
    NHB = HID // 128
    pe_fb = {}
    ge_mark = {}
    ep_mark = {}
    sc_mark = {}
    xres_free = (p.val("pe"), p.val("ve"), p.val("se"))
    w1Tb = p.nc.dram_tensor("w1Tb", [32 * 128, D], BF16, kind="Internal").ap()
    w2Tb = p.nc.dram_tensor("w2Tb", [DFF, D], BF16, kind="Internal").ap()
    wgTb = p.nc.dram_tensor("wgTb", [24 * 128, D], BF16, kind="Internal").ap()
    wuTb = p.nc.dram_tensor("wuTb", [24 * 128, D], BF16, kind="Internal").ap()
    wdTb = p.nc.dram_tensor("wdTb", [HID, D], BF16, kind="Internal").ap()
    for tc in range(TC):
        for fb in range(NFB):
            buf = fb % 2
            bank = 4 + (fb % 4)
            qn = "dw" if (tc == 0 or fb % 2 == 0) else "ds"
            wsel = [w1buf[0], w2buf[0]]

            def _load(g, tc=tc, fb=fb, buf=buf, wsel=wsel, qn=qn):
                if fb < 2:
                    if tc == 0:
                        p.wait(g, "pe")
                    else:
                        p.wait(g, "pe", pe_fb[("a2", tc * NFB - 1)])
                else:
                    p.wait(g, "pe", pe_fb[tc * NFB + fb - 2])
                    if tc == 0:
                        p.wait(g, "dsc", sc_mark[("w1", fb - 2)])
                if tc == 0:
                    for kq in range(JB):
                        p.inc(qn, g.dma_start(
                            out=wsel[buf][:, kq * 128:(kq + 1) * 128],
                            in_=w1T.rearrange("(k p) o -> k p o", p=128)[kq, :, fb * 128:(fb + 1) * 128]))
                else:
                    if tc == 1:
                        p.wait(g, "dsc", sc_mark[("w1", fb)])
                    p.inc(qn, g.dma_start(
                        out=wsel[buf][:],
                        in_=w1Tb.rearrange("(f p) o -> f p o", p=128)[fb]))
            if qn == "dw":
                block.gpsimd(_load)
            else:
                block.sync(_load)

            if tc == 0:
                @block.scalar
                def _(s, fb=fb, buf=buf, wsel=wsel):
                    p.wait(s, "dw")
                    p.inc("dsc", s.dma_start(
                        out=w1Tb.rearrange("(f p) o -> f p o", p=128)[fb],
                        in_=wsel[buf][:]))
                    sc_mark[("w1", fb)] = p.val("dsc")

            @block.tensor
            def _(te, tc=tc, fb=fb, buf=buf, bank=bank, wsel=wsel, qn=qn):
                p.wait(te, qn)
                if fb < 4:
                    p.wait(te, "se")
                    if tc == 0 and fb == 0:
                        p.wait(te, "ve")
                    if tc > 0:
                        p.wait(te, "ve", ep_mark[(tc - 1, bank)])
                else:
                    p.wait(te, "se", ge_mark[(tc, fb - 4)])
                for kb in range(JB):
                    p.inc("pe", te.matmul(
                        ps[bank][:, 0:TW], wsel[buf][:, kb * 128:(kb + 1) * 128],
                        A1[:, fsl(kb, tc)],
                        start=(kb == 0), stop=(kb == JB - 1)))
                pe_fb[tc * NFB + fb] = p.val("pe")

            @block.scalar
            def _(s, tc=tc, fb=fb, bank=bank):
                p.wait(s, "pe", pe_fb[tc * NFB + fb])
                p.inc("se", s.activation(A2[:, fb * TW:(fb + 1) * TW], ps[bank][:, 0:TW],
                                         AF.Gelu, bias=b1s[:, fb:fb + 1], scale=1.0))
                ge_mark[(tc, fb)] = p.val("se")

        pa2_0 = {}
        for fb in range(NFB):
            buf = fb % 2
            qn = "dw" if (tc == 0 or fb % 2 == 0) else "ds"
            wsel2 = [w2buf[0], w1buf[0]]

            def _load2(g, tc=tc, fb=fb, buf=buf, wsel2=wsel2, qn=qn):
                if fb < 2:
                    p.wait(g, "pe", pe_fb[tc * NFB + NFB - 1])
                else:
                    p.wait(g, "pe", pe_fb[("a2", tc * NFB + fb - 2)])
                    if tc == 0:
                        p.wait(g, "dsc", sc_mark[("w2", fb - 2)])
                if tc == 0:
                    p.inc(qn, g.dma_start(
                        out=wsel2[buf][:],
                        in_=w2T.rearrange("(f p) o -> f p o", p=128)[fb]))
                else:
                    if tc == 1:
                        p.wait(g, "dsc", sc_mark[("w2", fb)])
                    p.inc(qn, g.dma_start(
                        out=wsel2[buf][:],
                        in_=w2Tb.rearrange("(f p) o -> f p o", p=128)[fb]))
            if qn == "dw":
                block.gpsimd(_load2)
            else:
                block.sync(_load2)

            if tc == 0:
                @block.scalar
                def _(s, fb=fb, buf=buf, wsel2=wsel2):
                    p.wait(s, "dw")
                    p.inc("dsc", s.dma_start(
                        out=w2Tb.rearrange("(f p) o -> f p o", p=128)[fb],
                        in_=wsel2[buf][:]))
                    sc_mark[("w2", fb)] = p.val("dsc")

            @block.tensor
            def _(te, tc=tc, fb=fb, buf=buf, wsel2=wsel2, pa2_0=pa2_0, qn=qn):
                p.wait(te, qn)
                if fb == 0:
                    p.wait(te, "ve")
                    p.wait(te, "se")
                    pa2_0[tc] = p.val("pe")
                for ob in range(JB):
                    p.inc("pe", te.matmul(
                        ps[ob][:, 0:TW], wsel2[buf][:, ob * 128:(ob + 1) * 128],
                        A2[:, fb * TW:(fb + 1) * TW],
                        start=(fb == 0), stop=(fb == NFB - 1)))
                pe_fb[("a2", tc * NFB + fb)] = p.val("pe")

        @block.scalar
        def _(s, tc=tc):
            if tc == 0:
                p.wait(s, "pe", xres_free[0])
                p.wait(s, "ve", xres_free[1])
            p.inc("dsc", s.dma_start(
                out=xres[:].rearrange("p (j t) -> p j t", j=JB)[:, :, tc * TW:(tc + 1) * TW],
                in_=x_t.rearrange("(j p) t -> p j t", p=128)[:, :, tc * TW:(tc + 1) * TW]))

        @block.vector
        def _(v, tc=tc, pa2_0=pa2_0):
            p.wait(v, "dsc")
            for ob in range(JB):
                p.wait(v, "pe", pa2_0[tc] + (NFB - 1) * JB + ob + 1)
                p.inc("ve", v.scalar_tensor_tensor(
                    sc0[:, 0:TW], ps[ob][:, 0:TW], b2s[:, ob:ob + 1],
                    A3[:, fsl(ob, tc)], ALU.add, ALU.add))
                p.inc("ve", v.tensor_add(xres[:, fsl(ob, tc)], xres[:, fsl(ob, tc)],
                                         sc0[:, 0:TW]))
                ep_mark[(tc, ob)] = p.val("ve")

    p.mark("l0_ffnB")
    NHB = HID // 128
    wgB = [w1buf[0], A2[:, 12288:13312]]
    wuB = [w2buf[0], A2[:, 13312:14336]]
    for tc in range(TC):
        for fb in range(NHB):
            buf = fb % 2
            def _loadG(g, tc=tc, fb=fb, buf=buf):
                if fb < 2:
                    if tc == 0:
                        p.wait(g, "pe")
                        p.wait(g, "ve")
                    else:
                        p.wait(g, "pe", pe_fb[("b2", tc * NHB - 1)])
                else:
                    p.wait(g, "pe", pe_fb[("b1", tc * NHB + fb - 2)])
                    if tc == 0:
                        p.wait(g, "dsc", sc_mark[("wu", fb - 2)])
                if tc == 0:
                    for kq in range(JB):
                        p.inc("dw", g.dma_start(
                            out=wgB[buf][:, kq * 128:(kq + 1) * 128],
                            in_=wgT.rearrange("(k p) o -> k p o", p=128)[kq, :, fb * 128:(fb + 1) * 128]))
                    for kq in range(JB):
                        p.inc("dw", g.dma_start(
                            out=wuB[buf][:, kq * 128:(kq + 1) * 128],
                            in_=wuT.rearrange("(k p) o -> k p o", p=128)[kq, :, fb * 128:(fb + 1) * 128]))
                else:
                    if tc == 1:
                        p.wait(g, "dsc", sc_mark[("wg", fb)])
                    p.inc("dw", g.dma_start(
                        out=wgB[buf][:],
                        in_=wgTb.rearrange("(f p) o -> f p o", p=128)[fb]))
            block.gpsimd(_loadG)

            if tc > 0:
                @block.sync
                def _(sy, tc=tc, fb=fb, buf=buf):
                    if fb < 2:
                        p.wait(sy, "pe", pe_fb[("b2", tc * NHB - 1)])
                    else:
                        p.wait(sy, "pe", pe_fb[("b1", tc * NHB + fb - 2)])
                    if tc == 1:
                        p.wait(sy, "dsc", sc_mark[("wu", fb)])
                    p.inc("ds", sy.dma_start(
                        out=wuB[buf][:],
                        in_=wuTb.rearrange("(f p) o -> f p o", p=128)[fb]))
            else:
                @block.scalar
                def _(s, fb=fb, buf=buf):
                    p.wait(s, "dw")
                    p.inc("dsc", s.dma_start(
                        out=wgTb.rearrange("(f p) o -> f p o", p=128)[fb],
                        in_=wgB[buf][:]))
                    sc_mark[("wg", fb)] = p.val("dsc")
                    p.inc("dsc", s.dma_start(
                        out=wuTb.rearrange("(f p) o -> f p o", p=128)[fb],
                        in_=wuB[buf][:]))
                    sc_mark[("wu", fb)] = p.val("dsc")

            gbank = 4 + 2 * (fb % 2)
            sslot = sc0 if fb % 2 == 0 else sc1

            @block.tensor
            def _(te, tc=tc, fb=fb, buf=buf, gbank=gbank):
                p.wait(te, "dw")
                if tc > 0:
                    p.wait(te, "ds")
                if fb < 2:
                    p.wait(te, "se")
                    p.wait(te, "ve")
                else:
                    p.wait(te, "se", ge_mark[("b", tc, fb - 2)])
                    p.wait(te, "ve", ep_mark[("b", tc, fb - 2)])
                for kb in range(JB):
                    p.inc("pe", te.matmul(
                        ps[gbank][:, 0:TW], wgB[buf][:, kb * 128:(kb + 1) * 128],
                        xnb[:, fsl(kb, tc)],
                        start=(kb == 0), stop=(kb == JB - 1)))
                for kb in range(JB):
                    p.inc("pe", te.matmul(
                        ps[gbank + 1][:, 0:TW], wuB[buf][:, kb * 128:(kb + 1) * 128],
                        xnb[:, fsl(kb, tc)],
                        start=(kb == 0), stop=(kb == JB - 1)))
                pe_fb[("b1", tc * NHB + fb)] = p.val("pe")

            @block.scalar
            def _(s, tc=tc, fb=fb, gbank=gbank, sslot=sslot):
                p.wait(s, "pe", pe_fb[("b1", tc * NHB + fb)])
                p.inc("se", s.activation(sslot[:, 0:TW], ps[gbank][:, 0:TW], AF.Silu,
                                         scale=1.0 / 127.0))
                ge_mark[("b", tc, fb)] = p.val("se")

            @block.vector
            def _(v, tc=tc, fb=fb, gbank=gbank, sslot=sslot):
                p.wait(v, "se", ge_mark[("b", tc, fb)])
                p.inc("ve", v.scalar_tensor_tensor(
                    A2[:, fb * TW:(fb + 1) * TW], ps[gbank + 1][:, 0:TW], 1.0 / 127.0,
                    sslot[:, 0:TW], ALU.mult, ALU.mult))
                ep_mark[("b", tc, fb)] = p.val("ve")

        pb2_0 = {}
        for fb in range(NHB):
            buf = fb % 2
            qn = "dw" if (tc == 0 or fb % 2 == 0) else "ds"
            wselD = [w2buf[0], w1buf[0]]

            def _loadD(g, tc=tc, fb=fb, buf=buf, wselD=wselD, qn=qn):
                if fb < 2:
                    p.wait(g, "pe", pe_fb[("b1", tc * NHB + NHB - 1)])
                else:
                    p.wait(g, "pe", pe_fb[("b2", tc * NHB + fb - 2)])
                    if tc == 0:
                        p.wait(g, "dsc", sc_mark[("wd", fb - 2)])
                if tc == 0:
                    p.inc(qn, g.dma_start(
                        out=wselD[buf][:],
                        in_=wdT.rearrange("(f p) o -> f p o", p=128)[fb]))
                else:
                    if tc == 1:
                        p.wait(g, "dsc", sc_mark[("wd", fb)])
                    p.inc(qn, g.dma_start(
                        out=wselD[buf][:],
                        in_=wdTb.rearrange("(f p) o -> f p o", p=128)[fb]))
            if qn == "dw":
                block.gpsimd(_loadD)
            else:
                block.sync(_loadD)

            if tc == 0:
                @block.scalar
                def _(s, fb=fb, buf=buf, wselD=wselD):
                    p.wait(s, "dw")
                    p.inc("dsc", s.dma_start(
                        out=wdTb.rearrange("(f p) o -> f p o", p=128)[fb],
                        in_=wselD[buf][:]))
                    sc_mark[("wd", fb)] = p.val("dsc")

            @block.tensor
            def _(te, tc=tc, fb=fb, buf=buf, wselD=wselD, pb2_0=pb2_0, qn=qn):
                p.wait(te, qn)
                if fb == 0:
                    p.wait(te, "ve")
                    p.wait(te, "se")
                    pb2_0[tc] = p.val("pe")
                for ob in range(JB):
                    p.inc("pe", te.matmul(
                        ps[ob][:, 0:TW], wselD[buf][:, ob * 128:(ob + 1) * 128],
                        A2[:, fb * TW:(fb + 1) * TW],
                        start=(fb == 0), stop=(fb == NHB - 1)))
                pe_fb[("b2", tc * NHB + fb)] = p.val("pe")

        @block.vector
        def _(v, tc=tc, pb2_0=pb2_0):
            for ob in range(JB):
                p.wait(v, "pe", pb2_0[tc] + (NHB - 1) * JB + ob + 1)
                p.inc("ve", v.scalar_tensor_tensor(
                    xres[:, fsl(ob, tc)], ps[ob][:, 0:TW], 1.0 / 127.0,
                    xres[:, fsl(ob, tc)], ALU.mult, ALU.add))


def _emit_l1(p, io, sb, fused, skip=()):
    block = p.block
    WdtT, bdtc, WBT, WCT = io["WdtT"], io["bdtc"], io["WBT"], io["WCT"]
    A_logc, Dpc, gwT, gbr = io["A_logc"], io["Dpc"], io["gwT"], io["gbr"]
    w13T, w2mT, w13sT, w2sT, out_t = io["w13T"], io["w2mT"], io["w13sT"], io["w2sT"], io["out_t"]
    x1res, xn1b, A1, A2, A3 = sb["xres"], sb["xnb"], sb["A1"], sb["A2"], sb["A3"]
    sc0, sc1, sc2 = sb["sc0"], sb["sc1"], sb["sc2"]
    ones_r32, ones_cb, ones_sq, epsc = sb["ones_r32"], sb["ones_cb"], sb["ones_sq"], sb["epsc"]
    idb, iof, pcol = sb["idb"], sb["iof"], sb["pcol"]
    a_sb, a_sbb, h_st = sb["a_sb"], sb["a_sbb"], sb["h_st"]
    srow, abq, tq1, tcomb = sb["srow"], sb["abq"], sb["tq1"], sb["tcomb"]
    pm2 = sb["pm2"]
    ones_sqb = sb["ones_sqb"]
    wb_sb, wc_sb, wg_sb = sb["wb_sb"], sb["wc_sb"], sb["wg_sb"]
    gb_rep, gbr_sb, bdt_sb, dp_sb = sb["gb_rep"], sb["gbr_sb"], sb["bdt_sb"], sb["dp_sb"]
    rcol = sb["rcol"]
    rsc, bia, bwk, mcol, selw, comb_sc, wcol = sb["rsc"], sb["bia"], sb["bwk"], sb["mcol"], sb["selw"], sb["comb_sc"], sb["wcol"]
    ps = sb["ps"]

    A1f = A1[:].bitcast(F32)
    A2f = A2[:].bitcast(F32)
    x1bf = A3[:, 0:JB * T]
    wdt_sb = A3[:, 0:JB * D]
    bT_sb = A3[0:NST, 8192:8192 + T]
    cT_sb = A3[0:NST, 10240:10240 + T]
    combT = A3[0:NST, 12288:12288 + T]
    Esel = A3[0:NST, 14336:14336 + NST * 128]
    io_sc = A2[:, 0:NST * 128]
    m13buf = [A1[:, 0:8192], A1[:, 8192:16384]]
    m2buf = [A2[:, 8192:12288], A2[:, 12288:16384]]
    hb_moe = A2[:, 0:2048]
    crep_moe = A2f[:, 1024:1024 + TW]
    row0 = A1f[0:1, 0:TW]
    row1 = A1f[0:1, TW:2 * TW]
    row2 = A1f[0:1, 2 * TW:3 * TW]
    sqb = sc2[:].bitcast(BF16)[:, 0:TW]

    r = dict(ps=ps, sqb=sqb, sc0=sc0, sc1=sc1, sc2=sc2, row0=row0, row1=row1,
             row2=row2, ones_r32=ones_r32, ones_cb=ones_cb, epsln=epsc[0:1, 0:1])

    p.mark("l1_setup")
    @block.sync
    def _(sync):
        if not fused:
            p.inc("ds", sync.dma_start(
                out=x1res[:].rearrange("p (j t) -> p j t", j=JB),
                in_=io["x1_t"].rearrange("(j p) t -> p j t", p=128)))
        p.inc("ds", sync.dma_start(out=a_sb[:], in_=A_logc))
        p.inc("ds", sync.dma_start(out=bdt_sb[:], in_=bdtc))
        p.inc("ds", sync.dma_start(out=dp_sb[:], in_=Dpc))
        p.inc("ds", sync.dma_start(out=gbr_sb[:], in_=gbr))

    @block.gpsimd
    def _(g):
        p.wait(g, "pe")
        p.wait(g, "ve")
        if not fused:
            p.inc("dw", g.dma_start(
                out=x1bf.rearrange("p (j t) -> p j t", j=JB),
                in_=io["x1_t"].rearrange("(j p) t -> p j t", p=128)))
        p.inc("dw", g.dma_start(
            out=wb_sb[:].rearrange("p (k o) -> p k o", k=JB),
            in_=WBT.rearrange("(k p) o -> p k o", p=128)))
        p.inc("dw", g.dma_start(
            out=wc_sb[:].rearrange("p (k o) -> p k o", k=JB),
            in_=WCT.rearrange("(k p) o -> p k o", p=128)))
        p.inc("dw", g.dma_start(
            out=wg_sb[:].rearrange("p (k o) -> p k o", k=JB),
            in_=gwT.rearrange("(k p) o -> p k o", p=128)))
        p.inc("go", g.iota(iof[:], [[1, 128]], base=0, channel_multiplier=0,
                           allow_small_or_imprecise_dtypes=True))
        p.inc("go", g.iota(pcol[:], [[0, 1]], base=0, channel_multiplier=1,
                           allow_small_or_imprecise_dtypes=True))
        p.inc("go", g.iota(io_sc, [[1, NST], [0, 128]], base=0, channel_multiplier=0,
                           allow_small_or_imprecise_dtypes=True))

    if fused:
        for kb in range(JB):
            @block.scalar
            def _(s, kb=kb):
                if kb == 0:
                    p.wait(s, "ve")
                    p.wait(s, "pe")
                p.inc("se", s.activation(x1bf[:, kb * T:(kb + 1) * T],
                                         x1res[:, kb * T:(kb + 1) * T], AF.Copy))

    @block.vector
    def _(v):
        p.wait(v, "go")
        p.inc("ve", v.memset(h_st[:], 0.0))
        p.inc("ve", v.tensor_single_scalar(idb[:], iof[:], pcol[:, 0:1], ALU.is_equal))
        if not fused:
            p.inc("ve", v.memset(ones_r32[:], 1.0))
            p.inc("ve", v.memset(ones_cb[:], 1.0))
            p.inc("ve", v.memset(epsc[:], EPS_LN))

    @block.tensor
    def _(te):
        p.wait(te, "dw")
        p.wait(te, "ds")
        p.wait(te, "ve")
        p.wait(te, "se")
        p.inc("pe", te.matmul(ps[0][:, 0:NEX], ones_r32[:], gbr_sb[:], start=True, stop=True))

    @block.scalar
    def _(s):
        p.wait(s, "pe")
        p.inc("se", s.activation(gb_rep[:], ps[0][:, 0:NEX], AF.Copy))

    @block.vector
    def _(v):
        p.wait(v, "dw")
        p.wait(v, "ds")
        p.inc("ve", v.tensor_scalar_mul(a_sb[:], a_sb[:], K5))

    @block.scalar
    def _(s):
        p.wait(s, "ve")
        p.inc("se", s.activation(a_sb[:], a_sb[:], AF.Tanh))

    @block.scalar
    def _(s):
        p.wait(s, "se")
        p.inc("se", s.activation(a_sb[:], a_sb[:], AF.Exp, scale=10.0))

    @block.vector
    def _(v):
        p.wait(v, "se")
        p.inc("ve", v.tensor_scalar_mul(a_sb[:], a_sb[:], -1.0))

    p.mark("l1_ln")
    emit_ln(p, r, x1bf, lambda kb, tc: x1res[:, fsl(kb, tc)], xn1b)

    @block.gpsimd
    def _(g):
        p.wait(g, "se")
        p.wait(g, "pe")
        p.wait(g, "ve")
        p.inc("dw", g.dma_start(
            out=wdt_sb.rearrange("p (k o) -> p k o", k=JB),
            in_=WdtT.rearrange("(k p) o -> p k o", p=128)))

    @block.vector
    def _(v):
        p.wait(v, "go")
        p.wait(v, "se")
        p.wait(v, "pe")
        p.inc("ve", v.tensor_single_scalar(Esel, io_sc[0:16, :], pcol[0:16, 0:1], ALU.is_equal))

    p.mark("l1_bc")
    for tc in range(TC):
        @block.tensor
        def _(te, tc=tc):
            p.wait(te, "ve")
            p.wait(te, "se")
            for kb in range(JB):
                p.inc("pe", te.matmul(ps[0][0:NST, 0:TW],
                                      wb_sb[:, kb * NST:(kb + 1) * NST],
                                      xn1b[:, fsl(kb, tc)],
                                      start=(kb == 0), stop=(kb == JB - 1)))
            for kb in range(JB):
                p.inc("pe", te.matmul(ps[1][0:NST, 0:TW],
                                      wc_sb[:, kb * NST:(kb + 1) * NST],
                                      xn1b[:, fsl(kb, tc)],
                                      start=(kb == 0), stop=(kb == JB - 1)))

        @block.scalar
        def _(s, tc=tc):
            p.wait(s, "pe")
            p.inc("se", s.activation(bT_sb[:, tc * TW:(tc + 1) * TW], ps[0][0:NST, 0:TW], AF.Copy))
            p.inc("se", s.activation(cT_sb[:, tc * TW:(tc + 1) * TW], ps[1][0:NST, 0:TW], AF.Copy))

    p.mark("l1_route")
    NT = T // 128
    for tt in range(NT):
        @block.tensor
        def _(te, tt=tt):
            p.wait(te, "se")
            p.wait(te, "ve")
            for kb in range(JB):
                p.inc("pe", te.matmul(ps[2][:, 0:NEX],
                                      xn1b[:, kb * T + tt * 128: kb * T + (tt + 1) * 128],
                                      wg_sb[:, kb * NEX:(kb + 1) * NEX],
                                      start=(kb == 0), stop=(kb == JB - 1)))

        @block.scalar
        def _(s, tt=tt):
            p.wait(s, "pe")
            p.inc("se", s.activation(rsc[:], ps[2][:, 0:NEX], AF.Sigmoid))

        @block.vector
        def _(v, tt=tt):
            p.wait(v, "se")
            p.inc("ve", v.tensor_add(bia[:], rsc[:], gb_rep[:]))
            p.inc("ve", v.tensor_copy(bwk[:], bia[:]))
            for rnd in range(4):
                p.inc("ve", v.reduce_max(mcol[:], bwk[:], AX.X))
                if rnd < 3:
                    p.inc("ve", v.tensor_single_scalar(selw[:], bwk[:], mcol[:, 0:1], ALU.is_equal))
                    p.inc("ve", v.scalar_tensor_tensor(bwk[:], selw[:], -1e30, bwk[:],
                                                       ALU.mult, ALU.add))
            p.inc("ve", v.tensor_single_scalar(selw[:], bia[:], mcol[:, 0:1], ALU.is_ge))
            p.inc("ve", v.tensor_mul(selw[:], rsc[:], selw[:]))
            p.inc("ve", v.tensor_reduce(wcol[:], selw[:], AX.X, ALU.add))
            p.inc("ve", v.reciprocal(wcol[:], wcol[:]))
            p.inc("ve", v.tensor_scalar(comb_sc[:], selw[:], wcol[:, 0:1], 2.5,
                                        ALU.mult, ALU.mult))

        @block.tensor
        def _(te, tt=tt):
            p.wait(te, "ve")
            p.inc("pe", te.transpose(ps[3][0:NEX, 0:128], comb_sc[:], idb[:]))

        @block.scalar
        def _(s, tt=tt):
            p.wait(s, "pe")
            p.inc("se", s.activation(combT[:, tt * 128:(tt + 1) * 128],
                                     ps[3][0:NEX, 0:128], AF.Copy))

    @block.vector
    def _(v):
        p.wait(v, "pe")
        p.inc("ve", v.memset(ones_sq[:], 1.0))

    p.mark("l1_scan")
    SCH = 32
    ESC = SCH + 1
    NCHS = T // SCH
    SCS = 128
    ESS = SCS + 1
    A1b = A1[:]
    A1f = A1[:].bitcast(F32)
    A2b = A2[:]
    A2f2 = A2[:].bitcast(F32)
    dAxb = [A1b[:, 0:4096], A2b[:, 4096:8192]]
    wxA = A2b[:, 0:4096]
    wxB = A1b[:, 12416:16384]
    w32B = A2b[:, 8192:8320]
    qbuf = A1b[:, 4096:4096 + ESC * 128]
    pbuf = A1b[:, 8320:8320 + SCH * 128]
    dtraw = A1f[:, 4160:5200]
    dtrw2 = A1f[:, 6208:7248]
    dt_sc = A2b[:, 8320:8320 + JB * ESS]
    u_sc = A2b[:, 9352:9352 + JB * ESS]
    brep_sc = A2b[:, 10384:10384 + NST * ESS]
    crep_sc = A2b[:, 12448:12448 + NST * ESS]
    ccv_b = [A2f2[:, 7256:7256 + SCH], A2f2[:, 7380:7380 + SCH]]
    cuv = A2f2[:, 7296:7296 + SCH]
    cbv = A2f2[:, 7336:7336 + SCH]
    sc0b = sc0[:].bitcast(BF16)
    sc1b = sc1[:].bitcast(BF16)
    psl = sc0b[:, 256:384]
    redsl = sc0[:, 200:208]

    def w_ap(k, slot):
        if k % 2 == 0:
            return wxA[:, (slot - 1) * 128:slot * 128]
        if slot <= 31:
            return wxB[:, (slot - 1) * 128:slot * 128]
        return w32B[:]

    def w_slot0(k):
        return w_ap(k - 1 if k > 0 else 1, 32)

    def dA_ap(k, slot):
        return dAxb[k % 2][:, (slot - 1) * 128:slot * 128]

    dts_c = dt_sc.rearrange("p (o i) -> p o i", i=ESS)
    us_c = u_sc.rearrange("p (o i) -> p o i", i=ESS)
    br_c = brep_sc.rearrange("p (n i) -> p n i", i=ESS)
    cr_c = crep_sc.rearrange("p (n i) -> p n i", i=ESS)
    pb4 = pbuf.rearrange("p (t o n) -> p t o n", o=JB, n=NST)
    x1v = x1res[:].rearrange("p (o t) -> p o t", o=JB)
    redsl3 = redsl.unsqueeze(2)
    prep_ready = {}

    def emit_recon(v, ch, j):
        screl = (ch % (SCS // SCH)) * SCH
        tg = ch * SCH + j
        p.inc("ve", v.tensor_tensor(
            psl.rearrange("p (o n) -> p o n", o=JB),
            pb4[:, j],
            cr_c[:, :, screl + j].unsqueeze(1).broadcast_to((128, JB, NST)),
            ALU.mult))
        p.inc("ve", v.tensor_reduce(
            redsl[:], psl.rearrange("p (o n) -> p o n", o=JB), AX.X, ALU.add))
        p.inc("ve", v.scalar_tensor_tensor(
            x1v[:, :, tg:tg + 1], redsl3, srow[:, j + 1:j + 2],
            x1v[:, :, tg:tg + 1], ALU.mult, ALU.add))

    def mk_prep_stages(k1):
        screl1 = (k1 % (SCS // SCH)) * SCH
        par1 = k1 % 2
        dAxT = dAxb[par1]
        dA4 = dAxT.rearrange("p (t o n) -> p t o n", o=JB, n=NST)
        vals = {}
        asb3 = a_sbb[:].rearrange("p (o n) -> p o n", o=JB).unsqueeze(1)

        def dA_slice(v, sl):
            lo, hi = sl * 8, sl * 8 + 8
            dtt = dts_c[:, :, screl1 + 1 + lo:screl1 + 1 + hi].transpose([0, 2, 1])
            p.inc("ve", v.tensor_tensor(
                dA4[:, lo:hi], dtt.unsqueeze(3).broadcast_to((128, 8, JB, NST)),
                asb3.broadcast_to((128, 8, JB, NST)), ALU.mult))
            vals[f"dv{sl}"] = p.val("ve")

        def dA_act(s, sl):
            p.wait(s, "ve", vals[f"dv{sl}"])
            seg = dAxT[:, sl * 1024:(sl + 1) * 1024]
            p.inc("se", s.activation(seg, seg, AF.Tanh, scale=K8))
            p.inc("se", s.activation(seg, seg, AF.Exp, scale=16.0))
            if sl == 3:
                prep_ready[k1] = p.val("se")

        def w_slice(v, sl):
            lo = sl * 8
            nmain = 8 if (par1 == 0 or sl < 3) else 7
            wT = wxA if par1 == 0 else wxB
            w4 = wT.rearrange("p (t o n) -> p t o n", o=JB, n=NST)
            utt = us_c[:, :, screl1 + 1 + lo:screl1 + 1 + lo + nmain] \
                .transpose([0, 2, 1])
            p.inc("ve", v.tensor_tensor(
                w4[:, lo:lo + nmain],
                utt.unsqueeze(3).broadcast_to((128, nmain, JB, NST)),
                br_c[:, :, screl1 + 1 + lo:screl1 + 1 + lo + nmain]
                .transpose([0, 2, 1]).unsqueeze(2)
                .broadcast_to((128, nmain, JB, NST)), ALU.mult))
            if nmain == 7:
                w14 = w32B.rearrange("p (t o n) -> p t o n", o=JB, n=NST)
                p.inc("ve", v.tensor_tensor(
                    w14,
                    us_c[:, :, screl1 + 32:screl1 + 33].transpose([0, 2, 1])
                    .unsqueeze(3).broadcast_to((128, 1, JB, NST)),
                    br_c[:, :, screl1 + 32:screl1 + 33].transpose([0, 2, 1])
                    .unsqueeze(2).broadcast_to((128, 1, JB, NST)), ALU.mult))

        def c_squ(v):
            p.inc("ve", v.tensor_tensor(
                sc0b[:, 0:JB * SCH].rearrange("p (o i) -> p o i", i=SCH),
                us_c[:, :, screl1:screl1 + SCH],
                us_c[:, :, screl1:screl1 + SCH], ALU.mult))
            vals["squ"] = p.val("ve")

        def c_mm(te):
            p.wait(te, "ve", vals["squ"])
            p.inc("pe", te.matmul(ps[7][:, 0:JB * SCH], ones_sqb[:],
                                  sc0b[:, 0:JB * SCH], start=True, stop=True))
            vals["cmm"] = p.val("pe")

        def c_fin(v):
            p.wait(v, "pe", vals["cmm"])
            p.inc("ve", v.tensor_reduce(
                cuv[:], ps[7][:, 0:JB * SCH].rearrange("p (o i) -> p i o", o=JB),
                AX.X, ALU.add))
            p.inc("ve", v.tensor_tensor(
                sc1b[:, 0:NST * SCH].rearrange("p (n i) -> p n i", i=SCH),
                br_c[:, :, screl1:screl1 + SCH],
                br_c[:, :, screl1:screl1 + SCH], ALU.mult))
            p.inc("ve", v.tensor_reduce(
                cbv[:], sc1b[:, 0:NST * SCH].rearrange("p (n i) -> p i n", n=NST),
                AX.X, ALU.add))
            p.inc("ve", v.tensor_mul(ccv_b[par1][:], cuv[:], cbv[:]))

        def w0_extra(v):
            w14 = w32B.rearrange("p (t o n) -> p t o n", o=JB, n=NST)
            p.inc("ve", v.tensor_tensor(
                w14,
                us_c[:, :, 0:1].transpose([0, 2, 1]).unsqueeze(3)
                .broadcast_to((128, 1, JB, NST)),
                br_c[:, :, 0:1].transpose([0, 2, 1]).unsqueeze(2)
                .broadcast_to((128, 1, JB, NST)), ALU.mult))

        vstages = [lambda v, sl=sl: dA_slice(v, sl) for sl in range(4)]
        vstages += [lambda v, sl=sl: w_slice(v, sl) for sl in range(4)]
        vstages += [c_squ, c_fin]
        sstages = [lambda s, sl=sl: dA_act(s, sl) for sl in range(4)]
        return dict(v=vstages, s=sstages, te=[c_mm], w0=w0_extra)

    def emit_prep_serial(k1, with_w0=False):
        st = mk_prep_stages(k1)

        @block.vector
        def _(v, st=st):
            p.wait(v, "se")
            p.wait(v, "pe")
            for sl in range(4):
                st["v"][sl](v)

        @block.scalar
        def _(s, st=st):
            for sl in range(4):
                st["s"][sl](s)

        @block.vector
        def _(v, st=st, with_w0=with_w0):
            for sl in range(4):
                st["v"][4 + sl](v)
            if with_w0:
                st["w0"](v)
            st["v"][8](v)

        @block.tensor
        def _(te, st=st):
            st["te"][0](te)

        @block.vector
        def _(v, st=st):
            st["v"][9](v)

    def emit_sprep(st0, NE):
        @block.tensor
        def _(te, st0=st0, NE=NE):
            p.wait(te, "ve")
            p.wait(te, "se")
            p.wait(te, "dw")
            for ob in range(JB):
                bank, off = divmod(ob, 3)
                for kb in range(JB):
                    p.inc("pe", te.matmul(
                        ps[bank][:, off * 130:off * 130 + NE],
                        wdt_sb[:, kb * D + ob * 128: kb * D + ob * 128 + 128],
                        xn1b[:, kb * T + st0: kb * T + st0 + NE],
                        start=(off == 0 and kb == 0),
                        stop=(kb == JB - 1)))

        @block.vector
        def _(v, NE=NE):
            p.wait(v, "pe")
            for ob in range(JB):
                bank, off = divmod(ob, 3)
                p.inc("ve", v.tensor_single_scalar(
                    dtraw[:, ob * 130:ob * 130 + NE],
                    ps[bank][:, off * 130:off * 130 + NE],
                    bdt_sb[:, ob:ob + 1], ALU.add))

        dra = dtraw.rearrange("p (o i) -> p o i", i=130)[:, :, 0:NE]
        dr2 = dtrw2.rearrange("p (o i) -> p o i", i=130)[:, :, 0:NE]

        @block.scalar
        def _(s, dra=dra, dr2=dr2):
            p.wait(s, "ve")
            p.inc("se", s.activation(dr2, dra, AF.Abs))
            p.inc("se", s.activation(dr2, dr2, AF.Exp, scale=-1.0))
            p.inc("se", s.activation(dr2, dr2, AF.Ln, bias=1.0))

        @block.vector
        def _(v, dra=dra, dr2=dr2):
            p.wait(v, "se")
            p.inc("ve", v.tensor_scalar_max(dra, dra, 0.0))
            p.inc("ve", v.tensor_add(dra, dra, dr2))
            p.inc("ve", v.tensor_scalar(dra, dra, 0.0505, KDT,
                                        ALU.subtract, ALU.mult))

        @block.scalar
        def _(s, dra=dra):
            p.wait(s, "ve")
            p.inc("se", s.activation(dra, dra, AF.Tanh))

        @block.vector
        def _(v, st0=st0, NE=NE, dra=dra):
            p.wait(v, "se")
            dts = dt_sc.rearrange("p (o i) -> p o i", i=ESS)[:, :, 0:NE]
            p.inc("ve", v.tensor_scalar(dts, dra, 0.099, 0.0505,
                                        ALU.mult, ALU.add))
            p.inc("ve", v.tensor_tensor(
                u_sc.rearrange("p (o i) -> p o i", i=ESS)[:, :, 0:NE],
                dts,
                xn1b[:].rearrange("p (o t) -> p o t", o=JB)[:, :, st0:st0 + NE],
                ALU.mult))

        for n in range(NST):
            @block.tensor
            def _(te, st0=st0, NE=NE, n=n):
                p.wait(te, "se")
                if n == 0:
                    p.wait(te, "ve")
                p.inc("pe", te.matmul(ps[3][:, 0:NE],
                                      Esel[:, n * 128:(n + 1) * 128],
                                      bT_sb[:, st0:st0 + NE], start=True, stop=True))
                p.inc("pe", te.matmul(ps[4][:, 0:NE],
                                      Esel[:, n * 128:(n + 1) * 128],
                                      cT_sb[:, st0:st0 + NE], start=True, stop=True))

            @block.scalar
            def _(s, NE=NE, n=n):
                p.wait(s, "pe")
                p.inc("se", s.activation(brep_sc[:, n * ESS:n * ESS + NE],
                                         ps[3][:, 0:NE], AF.Copy))
                p.inc("se", s.activation(crep_sc[:, n * ESS:n * ESS + NE],
                                         ps[4][:, 0:NE], AF.Copy))

    if "scan" not in skip:
        @block.scalar
        def _(s):
            p.wait(s, "ve")
            p.wait(s, "pe")
            p.inc("se", s.activation(a_sbb[:], a_sb[:], AF.Copy))

        @block.vector
        def _(v):
            p.wait(v, "se")
            p.inc("ve", v.memset(qbuf[:, 0:128].bitcast(F32), 0.0))
            p.inc("ve", v.memset(srow[:], 1.0))
            p.inc("ve", v.memset(abq[:].bitcast(F32), 0.0))
            p.inc("ve", v.memset(ones_sqb[:], 1.0))

        @block.tensor
        def _(te):
            p.wait(te, "ve")
            p.inc("pe", te.matmul(ps[5][:, 0:2], ones_sqb[:], abq[:, 0:2],
                                  start=True, stop=True))
            p.inc("pe", te.matmul(ps[6][:, 0:2], ones_sqb[:], abq[:, 2:4],
                                  start=True, stop=True))

    VSCHED = {2: 0, 3: 1, 4: 2, 5: 3, 7: 4, 8: 5, 9: 6, 10: 7, 11: 8, 13: 9}
    SSCHED = {4: 0, 6: 1, 8: 2, 10: 3}
    TSCHED = {12: 0}

    for ch in range(NCHS if "scan" not in skip else 0):
        t0 = ch * SCH
        screl = (ch % (SCS // SCH)) * SCH
        at_super = screl == 0
        if at_super:
            if ch > 0:
                @block.vector
                def _(v, ch=ch):
                    emit_recon(v, ch - 1, 30)
                    emit_recon(v, ch - 1, 31)
            st0 = (ch // (SCS // SCH)) * SCS
            emit_sprep(st0, ESS if st0 + SCS < T else SCS)
            emit_prep_serial(ch, with_w0=(ch == 0))
        if ch > 0:
            @block.vector
            def _(v):
                p.inc("ve", v.tensor_copy(qbuf[:, 0:128],
                                          qbuf[:, SCH * 128:(SCH + 1) * 128]))
                p.inc("ve", v.tensor_copy(srow[:, 0:1], srow[:, SCH:SCH + 1]))

        do_prep = (ch + 1 < NCHS) and ((ch + 1) % (SCS // SCH) != 0)
        stg = mk_prep_stages(ch + 1) if do_prep else None
        ccv_cur = ccv_b[ch % 2]

        for tl in range(SCH):
            t = t0 + tl
            par = t % 2
            npar = (t + 1) % 2
            last = (t == T - 1)
            se_sqrt = p.val("se") + 1
            marks = {}
            recon_job = None
            if tl >= 2:
                recon_job = (ch, tl - 2)
            elif not at_super and ch > 0:
                recon_job = (ch - 1, 30 + tl)

            @block.vector
            def _(v, tl=tl, ch=ch, par=par, npar=npar, last=last,
                  se_sqrt=se_sqrt, marks=marks, recon_job=recon_job, stg=stg):
                if tl == 0 and ch in prep_ready:
                    p.wait(v, "se", prep_ready[ch])
                p.inc("ve", v.scalar_tensor_tensor(
                    pbuf[:, tl * 128:(tl + 1) * 128],
                    qbuf[:, tl * 128:(tl + 1) * 128],
                    srow[:, tl:tl + 1],
                    (w_slot0(ch) if tl == 0 else w_ap(ch, tl)),
                    ALU.mult, ALU.add))
                if not last:
                    p.inc("ve", v.tensor_mul(qbuf[:, (tl + 1) * 128:(tl + 2) * 128],
                                             pbuf[:, tl * 128:(tl + 1) * 128],
                                             dA_ap(ch, tl + 1)))
                    marks["q"] = p.val("ve")
                    with p.nc.allow_low_precision(reason="b partial sum bf16"):
                        p.inc("ve", v.scalar_tensor_tensor(
                            h_st[:], qbuf[:, (tl + 1) * 128:(tl + 2) * 128], 2.0,
                            w_ap(ch, tl + 1), ALU.mult, ALU.mult,
                            accum_out=abq[:, 2 * npar + 1:2 * npar + 2]))
                    marks["b"] = p.val("ve")
                p.wait(v, "pe")
                p.inc("ve", v.tensor_scalar(
                    tcomb[:], ps[5 + par][:, 0:1], srow[:, tl:tl + 1],
                    ps[5 + par][:, 1:2], ALU.mult, ALU.add))
                marks["x"] = p.val("ve")
                if recon_job is not None:
                    emit_recon(v, recon_job[0], recon_job[1])
                if stg is not None and tl in VSCHED:
                    stg["v"][VSCHED[tl]](v)
                p.wait(v, "se", se_sqrt)
                p.inc("ve", v.tensor_scalar_add(rcol[:], rcol[:], 1e-8))
                p.inc("ve", v.reciprocal(rcol[:], rcol[:]))
                p.inc("ve", v.tensor_scalar(srow[:, tl + 1:tl + 2], rcol[:],
                                            20.0, 1.0, ALU.mult, ALU.min))

            if not last:
                @block.gpsimd
                def _(g, tl=tl, marks=marks):
                    p.wait(g, "ve", marks["q"])
                    p.inc("po", g.tensor_mul(pm2[:],
                                             qbuf[:, (tl + 1) * 128:(tl + 2) * 128],
                                             qbuf[:, (tl + 1) * 128:(tl + 2) * 128]))

            @block.scalar
            def _(s, tl=tl, npar=npar, last=last, marks=marks,
                  ccv_cur=ccv_cur, stg=stg):
                p.wait(s, "ve", marks["x"])
                p.inc("se", s.activation(rcol[:], tcomb[:], AF.Sqrt,
                                         scale=srow[:, tl:tl + 1],
                                         bias=ccv_cur[:, tl:tl + 1]))
                if not last:
                    p.wait(s, "po")
                    with p.nc.allow_low_precision(reason="a partial sum bf16"):
                        p.inc("se", s.activation(sc2[:, 0:128], pm2[:], AF.Copy,
                                                 accum_out=abq[:, 2 * npar:2 * npar + 1]))
                marks["cp"] = p.val("se")
                if stg is not None and tl in SSCHED:
                    stg["s"][SSCHED[tl]](s)

            if not last:
                @block.tensor
                def _(te, npar=npar, marks=marks, stg=stg, tl=tl):
                    p.wait(te, "se", marks["cp"])
                    p.wait(te, "ve", marks["b"])
                    p.inc("pe", te.matmul(ps[5 + npar][:, 0:2], ones_sqb[:],
                                          abq[:, 2 * npar:2 * npar + 2],
                                          start=True, stop=True))
                    if stg is not None and tl in TSCHED:
                        stg["te"][TSCHED[tl]](te)

        if ch == NCHS - 1:
            @block.vector
            def _(v, ch=ch):
                emit_recon(v, ch, 30)
                emit_recon(v, ch, 31)


    p.mark("l1_dp")
    for tc in range(TC if "dp" not in skip else 0):
        @block.vector
        def _(v, tc=tc):
            p.wait(v, "dw")
            p.wait(v, "ds")
            for j in range(JB):
                p.inc("ve", v.scalar_tensor_tensor(
                    x1res[:, fsl(j, tc)], xn1b[:, fsl(j, tc)], dp_sb[:, j:j + 1],
                    x1res[:, fsl(j, tc)], ALU.mult, ALU.add))

    p.mark("l1_moe")
    silu_sl = [A2f[:, 1536 + 512 * i:2048 + 512 * i] for i in range(4)]
    pe_after_expert = {}
    moe_prev = None
    for ei in range((NEX + 1) if "moe" not in skip else 0):
        shared = (ei == NEX)
        buf = ei % 2

        @block.gpsimd
        def _(g, ei=ei, buf=buf, shared=shared):
            if ei < 2:
                p.wait(g, "pe")
            else:
                p.wait(g, "pe", pe_after_expert[ei - 2])
            for kb in range(JB):
                src = (w13sT.rearrange("(k p) o -> k p o", p=128)[kb] if shared
                       else w13T.rearrange("e (k p) o -> e k p o", p=128)[ei, kb])
                p.inc("dw", g.dma_start(out=m13buf[buf][:, kb * 1024:(kb + 1) * 1024], in_=src))
            for ib in range(4):
                src = (w2sT.rearrange("(k p) o -> k p o", p=128)[ib] if shared
                       else w2mT.rearrange("e (k p) o -> e k p o", p=128)[ei, ib])
                p.inc("dw", g.dma_start(out=m2buf[buf][:, ib * 1024:(ib + 1) * 1024], in_=src))

        for tc in range(TC):
            first = moe_prev is None
            pm = moe_prev
            m = {}

            if not shared:
                @block.tensor
                def _(te, ei=ei, tc=tc, pm=pm, first=first):
                    if first:
                        p.wait(te, "ve")
                        p.wait(te, "se")
                    else:
                        p.wait(te, "ve", pm["ep0"])
                    p.inc("pe", te.matmul(ps[0][:, 0:TW], Esel[:, ei * 128:(ei + 1) * 128],
                                          combT[:, tc * TW:(tc + 1) * TW],
                                          start=True, stop=True))

                @block.scalar
                def _(s, ei=ei, tc=tc, m=m):
                    p.wait(s, "pe")
                    p.inc("se", s.activation(crep_moe, ps[0][:, 0:TW], AF.Copy))
                    m["crep"] = p.val("se")

            @block.tensor
            def _(te, ei=ei, tc=tc, buf=buf, pm=pm, m=m, first=first, shared=shared):
                if tc == 0:
                    p.wait(te, "dw")
                if first:
                    p.wait(te, "ve")
                    p.wait(te, "se")
                for ib in range(4):
                    if not first:
                        p.wait(te, "ve", pm[f"ep{ib}"])
                    if ib == 0 and not shared:
                        p.wait(te, "se", m["crep"])
                    for kb in range(JB):
                        p.inc("pe", te.matmul(
                            ps[ib][:, 0:TW],
                            m13buf[buf][:, kb * 1024 + ib * 128: kb * 1024 + ib * 128 + 128],
                            xn1b[:, fsl(kb, tc)], start=(kb == 0), stop=(kb == JB - 1)))
                    if not first:
                        p.wait(te, "ve", pm[f"ep{4 + ib}"])
                    for kb in range(JB):
                        p.inc("pe", te.matmul(
                            ps[4 + ib][:, 0:TW],
                            m13buf[buf][:, kb * 1024 + 512 + ib * 128: kb * 1024 + 512 + ib * 128 + 128],
                            xn1b[:, fsl(kb, tc)], start=(kb == 0), stop=(kb == JB - 1)))
                    m[f"gu{ib}"] = p.val("pe")

            @block.scalar
            def _(s, ei=ei, tc=tc, m=m):
                for ib in range(4):
                    p.wait(s, "pe", m[f"gu{ib}"])
                    p.inc("se", s.activation(silu_sl[ib], ps[ib][:, 0:TW], AF.Silu,
                                             scale=1.0 / 127.0))
                    m[f"si{ib}"] = p.val("se")

            @block.vector
            def _(v, ei=ei, tc=tc, m=m, pm=pm, shared=shared, first=first):
                if not first and "down_done" in pm:
                    p.wait(v, "pe", pm["down_done"])
                for ib in range(4):
                    p.wait(v, "se", m[f"si{ib}"])
                    if shared:
                        p.inc("ve", v.scalar_tensor_tensor(
                            hb_moe[:, ib * TW:(ib + 1) * TW], ps[4 + ib][:, 0:TW],
                            1.0 / 127.0, silu_sl[ib], ALU.mult, ALU.mult))
                    else:
                        p.inc("ve", v.scalar_tensor_tensor(
                            sc1[:, 0:TW], ps[4 + ib][:, 0:TW],
                            1.0 / 127.0, silu_sl[ib], ALU.mult, ALU.mult))
                        p.inc("ve", v.tensor_mul(hb_moe[:, ib * TW:(ib + 1) * TW],
                                                 sc1[:, 0:TW], crep_moe))
                    m[f"hp{ib}"] = p.val("ve")

            @block.tensor
            def _(te, ei=ei, tc=tc, buf=buf, m=m):
                p.wait(te, "ve", m["hp3"])
                m["down0"] = p.val("pe")
                for ib in range(4):
                    for ob in range(JB):
                        p.inc("pe", te.matmul(
                            ps[ob][:, 0:TW],
                            m2buf[buf][:, ib * 1024 + ob * 128: ib * 1024 + ob * 128 + 128],
                            hb_moe[:, ib * TW:(ib + 1) * TW],
                            start=(ib == 0), stop=(ib == 3)))
                m["down_done"] = p.val("pe")

            @block.vector
            def _(v, ei=ei, tc=tc, m=m):
                for ob in range(JB):
                    p.wait(v, "pe", m["down0"] + 3 * JB + ob + 1)
                    p.inc("ve", v.scalar_tensor_tensor(
                        x1res[:, fsl(ob, tc)], ps[ob][:, 0:TW], 1.0 / 127.0,
                        x1res[:, fsl(ob, tc)], ALU.mult, ALU.add))
                    m[f"ep{ob}"] = p.val("ve")

            moe_prev = m

        pe_after_expert[ei] = p.val("pe")

    p.mark(None)
    @block.sync
    def _(sync):
        p.wait(sync, "ve")
        p.inc("dx", sync.dma_start(
            out=out_t.rearrange("(j p) t -> p j t", p=128),
            in_=x1res[:].rearrange("p (j t) -> p j t", j=JB)))
        p.wait(sync, "dx")


def _mk_io_l0(nc, out_kind="Internal"):
    ap = lambda n, shp, dt=F32, kind="ExternalInput": nc.dram_tensor(n, shp, dt, kind=kind).ap()
    io = {
        "x_t": ap("x_t", [D, T]),
        "wqT": ap("wqT", [D, D]),
        "wkT": ap("wkT", [D, D]),
        "wvT": ap("wvT", [D, D]),
        "w1T": ap("w1T", [D, DFF]),
        "w2T": ap("w2T", [DFF, D]),
        "b1c": ap("b1c", [128, DFF // 128]),
        "b2c": ap("b2c", [128, JB]),
        "g1c": ap("g1c", [128, JB]),
        "be1c": ap("be1c", [128, JB]),
        "g2c": ap("g2c", [128, JB]),
        "be2c": ap("be2c", [128, JB]),
        "wgT": ap("wgT", [D, HID], I32),
        "wuT": ap("wuT", [D, HID], I32),
        "wdT": ap("wdT", [HID, D], I32),
    }
    return io


def _mk_io_l1(nc, with_x1):
    ap = lambda n, shp, dt=F32, kind="ExternalInput": nc.dram_tensor(n, shp, dt, kind=kind).ap()
    io = {
        "WdtT": ap("WdtT", [D, D]),
        "bdtc": ap("bdtc", [128, JB]),
        "WBT": ap("WBT", [D, NST]),
        "WCT": ap("WCT", [D, NST]),
        "A_logc": ap("A_logc", [128, 128]),
        "Dpc": ap("Dpc", [128, JB]),
        "gwT": ap("gwT", [D, NEX]),
        "gbr": ap("gbr", [1, NEX]),
        "w13T": ap("w13T", [NEX, D, D]),
        "w2mT": ap("w2mT", [NEX, INTER, D]),
        "w13sT": ap("w13sT", [D, D]),
        "w2sT": ap("w2sT", [INTER, D]),
        "out_t": ap("out_t", [D, T], kind="ExternalOutput"),
    }
    if with_x1:
        io["x1_t"] = ap("x1_t", [D, T])
    return io


def build_full(skip=(), debug_taps=False):
    nc = bass.Bass("TRN2", target_bir_lowering=False)
    io0 = _mk_io_l0(nc)
    io1 = _mk_io_l1(nc, with_x1=False)
    dbg = None
    if debug_taps:
        dbg = nc.dram_tensor("dbg_t", [D, T], F32, kind="ExternalOutput").ap()
    with ExitStack() as ctx:
        e = ctx.enter_context
        p = Prog(nc, ctx)
        sb = _alloc(nc, e)
        _emit_l0(p, io0, sb)
        if debug_taps:
            block = p.block
            @block.sync
            def _(sync):
                p.wait(sync, "ve")
                p.wait(sync, "se")
                p.wait(sync, "pe")
                p.inc("dx", sync.dma_start(
                    out=dbg.rearrange("(j p) t -> p j t", p=128),
                    in_=sb["xres"][:].rearrange("p (j t) -> p j t", j=JB)))
                p.wait(sync, "dx")
        _emit_l1(p, io1, sb, fused=True, skip=skip)
    return nc


def build_l0():
    nc = bass.Bass("TRN2", target_bir_lowering=False)
    io0 = _mk_io_l0(nc)
    x1_t = nc.dram_tensor("x1_t", [D, T], F32, kind="ExternalOutput").ap()
    with ExitStack() as ctx:
        e = ctx.enter_context
        p = Prog(nc, ctx)
        sb = _alloc(nc, e)
        _emit_l0(p, io0, sb)
        block = p.block

        @block.sync
        def _(sync):
            p.wait(sync, "ve")
            p.inc("dx", sync.dma_start(
                out=x1_t.rearrange("(j p) t -> p j t", p=128),
                in_=sb["xres"][:].rearrange("p (j t) -> p j t", j=JB)))
            p.wait(sync, "dx")
    return nc


def build_l1():
    nc = bass.Bass("TRN2", target_bir_lowering=False)
    io1 = _mk_io_l1(nc, with_x1=True)
    with ExitStack() as ctx:
        e = ctx.enter_context
        p = Prog(nc, ctx)
        sb = _alloc(nc, e)
        _emit_l1(p, io1, sb, fused=False)
    return nc



def _host_weights(inputs):
    f = np.ascontiguousarray
    return {
        "wqT": f(inputs["L0_attn_wq"].T),
        "wkT": f(inputs["L0_attn_wk"].T),
        "wvT": f(inputs["L0_attn_wv"].T),
        "w1T": f(inputs["L0_attn_w1"].T),
        "w2T": f(inputs["L0_attn_w2"].T),
        "b1c": f(inputs["L0_attn_b1"].reshape(DFF // 128, 128).T),
        "b2c": f(inputs["L0_attn_b2"].reshape(JB, 128).T),
        "g1c": f(inputs["L0_attn_g1"].reshape(JB, 128).T),
        "be1c": f(inputs["L0_attn_be1"].reshape(JB, 128).T),
        "g2c": f(inputs["L0_attn_g2"].reshape(JB, 128).T),
        "be2c": f(inputs["L0_attn_be2"].reshape(JB, 128).T),
        "wgT": f(inputs["L0_ffn_w_gate"].T),
        "wuT": f(inputs["L0_ffn_w_up"].T),
        "wdT": f(inputs["L0_ffn_w_down"].T),
        "WdtT": f(inputs["L1_attn_Wdt"].T),
        "bdtc": f(inputs["L1_attn_bdt"].reshape(JB, 128).T),
        "WBT": f(inputs["L1_attn_WB"].T),
        "WCT": f(inputs["L1_attn_WC"].T),
        "A_logc": f(inputs["L1_attn_A_log"].reshape(JB, 128, NST).transpose(1, 0, 2).reshape(128, 128)),
        "Dpc": f(inputs["L1_attn_Dp"].reshape(JB, 128).T),
        "gwT": f(inputs["L1_ffn_gate_w"].T),
        "gbr": f(inputs["L1_ffn_gate_b"].reshape(1, NEX)),
        "w13T": f(inputs["L1_ffn_w13"].transpose(0, 2, 1)),
        "w2mT": f(inputs["L1_ffn_w2"].transpose(0, 2, 1)),
        "w13sT": f(inputs["L1_ffn_w13_s"].T),
        "w2sT": f(inputs["L1_ffn_w2_s"].T),
    }


def _l0_host_inputs(inputs):
    w = {k: v for k, v in _host_weights(inputs).items()
         if k in ("wqT", "wkT", "wvT", "w1T", "w2T", "b1c", "b2c", "g1c", "be1c",
                  "g2c", "be2c", "wgT", "wuT", "wdT")}
    x = np.asarray(inputs["L0_attn_x"])
    return [dict(w, x_t=np.ascontiguousarray(x[c].T)) for c in range(8)]


def _l1_host_inputs(inputs, x1_list):
    w = {k: v for k, v in _host_weights(inputs).items()
         if k not in ("wqT", "wkT", "wvT", "w1T", "w2T", "b1c", "b2c", "g1c",
                      "be1c", "g2c", "be2c", "wgT", "wuT", "wdT")}
    return [dict(w, x1_t=np.ascontiguousarray(x1_list[c])) for c in range(8)]


def kernel(**inputs):
    _ = inputs.get("L1_ffn_input_ids")
    nc = build_full()
    w = _host_weights(inputs)
    x = np.asarray(inputs["L0_attn_x"])
    in_maps = [dict(w, x_t=np.ascontiguousarray(x[c].T)) for c in range(8)]
    res = run_bass_kernel_spmd(nc, in_maps, core_ids=list(range(8)))
    out = np.stack([np.asarray(r["out_t"]).T for r in res.results], axis=0)
    return out.astype(np.float32)


# version 14
# speedup vs baseline: 1.3361x; 1.0185x over previous
import numpy as np
import ml_dtypes
from contextlib import ExitStack

import concourse.bass as bass
import concourse.mybir as mybir
from concourse.bass_utils import run_bass_kernel_spmd

F32 = mybir.dt.float32
BF16 = mybir.dt.bfloat16
I32 = mybir.dt.int32
AF = mybir.ActivationFunctionType
ALU = mybir.AluOpType
AX = mybir.AxisListType
BT = ml_dtypes.bfloat16

D = 1024
T = 2048
JB = 8
NH = 8
DFF = 4096
HID = 3072
EPS_LN = 1e-5
EPS_ATTN = 1e-6
TW = 512
TC = T // TW
NST = 16
TCH = 64
NCH = T // TCH
INTER = 512
NEX = 16
K8 = 0.5 / (8.0 + 1e-8)
K5 = 0.5 / (5.0 + 1e-8)
KDT = 0.5 / (0.0495 + 1e-8)


class Prog:
    def __init__(self, nc, ctx):
        self.nc = nc
        self.ctx = ctx
        self.block = ctx.enter_context(nc.Block())
        self._sems = {}

    def sem(self, name):
        if name not in self._sems:
            h = self.ctx.enter_context(self.nc.semaphore(name))
            self._sems[name] = [h, 0]
        return self._sems[name]

    def inc(self, name, inst, k=1):
        if name.startswith("d"):
            k *= 16
        s = self.sem(name)
        s[1] += k
        inst.then_inc(s[0], k)
        return s[1]

    def val(self, name):
        return self.sem(name)[1]

    def wait(self, eng, name, v=None):
        s = self.sem(name)
        v = s[1] if v is None else v
        if v > 0:
            eng.wait_ge(s[0], v)

    _scope = None

    def mark(self, name):
        if self._scope is not None:
            self.nc.leave_named_scope(self._scope[0], self._scope[1], False)
            self._scope = None
        if name is not None:
            sid, _ = self.nc.enter_named_scope(name, False)
            self._scope = (name, sid)


def fsl(kb, tc, w=TW):
    return slice(kb * T + tc * w, kb * T + (tc + 1) * w)


def emit_ln(p, r, src_bf, val_src, dst_bf, gcol=None, bcol=None):
    block = p.block
    ps = r["ps"]
    sq2 = r["sc2"][:].bitcast(BF16)
    sqt = [sq2[:, 0:TW], sq2[:, TW:2 * TW]]
    for tc in range(TC):
        vq_mark = {}
        mm_mark = {}
        for kb in range(JB):
            @block.vector
            def _(v, kb=kb, tc=tc, vq_mark=vq_mark, mm_mark=mm_mark):
                if kb == 0:
                    p.wait(v, "dw")
                    p.wait(v, "ds")
                    p.wait(v, "se")
                    p.wait(v, "pe")
                elif kb >= 2:
                    p.wait(v, "pe", mm_mark[kb - 2])
                p.inc("ve", v.tensor_mul(sqt[kb % 2], src_bf[:, fsl(kb, tc)],
                                         src_bf[:, fsl(kb, tc)]))
                vq_mark[kb] = p.val("ve")

            @block.tensor
            def _(te, kb=kb, tc=tc, vq_mark=vq_mark, mm_mark=mm_mark):
                p.wait(te, "ve", vq_mark[kb])
                p.inc("pe", te.matmul(ps[7][0:1, 0:TW], r["ones_cb"][:], sqt[kb % 2],
                                      start=(kb == 0), stop=(kb == JB - 1)))
                p.inc("pe", te.matmul(ps[6][0:1, 0:TW], r["ones_cb"][:], src_bf[:, fsl(kb, tc)],
                                      start=(kb == 0), stop=(kb == JB - 1)))
                mm_mark[kb] = p.val("pe")

        @block.vector
        def _(v, tc=tc):
            p.wait(v, "pe")
            p.inc("ve", v.tensor_scalar_mul(r["row0"], ps[6][0:1, 0:TW], 1.0 / D))
            p.inc("ve", v.tensor_scalar_mul(r["row1"], ps[7][0:1, 0:TW], 1.0 / D))
            p.inc("ve", v.tensor_mul(r["row2"], r["row0"], r["row0"]))
            p.inc("ve", v.tensor_sub(r["row1"], r["row1"], r["row2"]))

        @block.scalar
        def _(s, tc=tc):
            p.wait(s, "ve")
            p.inc("se", s.activation(r["row1"], r["row1"], AF.Ln, bias=r["epsln"]))
            p.inc("se", s.activation(r["row1"], r["row1"], AF.Exp, scale=-0.5))

        @block.tensor
        def _(te, tc=tc):
            p.wait(te, "se")
            p.wait(te, "ve")
            p.inc("pe", te.matmul(ps[4][:, 0:TW], r["ones_r32"][:], r["row0"],
                                  start=True, stop=True))
            p.inc("pe", te.matmul(ps[5][:, 0:TW], r["ones_r32"][:], r["row1"],
                                  start=True, stop=True))

        @block.vector
        def _(v, tc=tc):
            p.wait(v, "pe")
            for kb in range(JB):
                vs = val_src(kb, tc)
                p.inc("ve", v.tensor_sub(r["sc0"][:, 0:TW], vs, ps[4][:, 0:TW]))
                if gcol is None:
                    p.inc("ve", v.tensor_mul(dst_bf[:, fsl(kb, tc)], r["sc0"][:, 0:TW], ps[5][:, 0:TW]))
                else:
                    p.inc("ve", v.tensor_mul(r["sc0"][:, 0:TW], r["sc0"][:, 0:TW], ps[5][:, 0:TW]))
                    p.inc("ve", v.tensor_scalar(dst_bf[:, fsl(kb, tc)], r["sc0"][:, 0:TW],
                                                gcol[:, kb:kb + 1], bcol[:, kb:kb + 1],
                                                ALU.mult, ALU.add))


def emit_elu_epilogue(p, ps_ap, dst_ap, r, w=TW):
    block = p.block

    @block.vector
    def _(v):
        p.wait(v, "pe")
        p.inc("ve", v.tensor_scalar_min(r["sc0"][:, 0:w], ps_ap, 0.0))
        p.inc("ve", v.tensor_sub(r["sc1"][:, 0:w], ps_ap, r["sc0"][:, 0:w]))

    @block.scalar
    def _(s):
        p.wait(s, "ve")
        p.inc("se", s.activation(r["sc2"][:, 0:w], r["sc0"][:, 0:w], AF.Exp))

    @block.vector
    def _(v):
        p.wait(v, "se")
        p.inc("ve", v.tensor_add(dst_ap, r["sc1"][:, 0:w], r["sc2"][:, 0:w]))


def _alloc(nc, e):
    sb = {}
    t = lambda name, shp, dt=F32: e(nc.sbuf_tensor(name, shp, dt))
    sb["xres"] = t("xres", [128, JB * T], F32)
    sb["xnb"] = t("xnb", [128, JB * T], BF16)
    sb["A1"] = t("A1", [128, JB * T], BF16)
    sb["A2"] = t("A2", [128, JB * T], BF16)
    sb["A3"] = t("A3", [128, JB * T], BF16)
    sb["sc0"] = t("sc0", [128, TW], F32)
    sb["sc1"] = t("sc1", [128, TW], F32)
    sb["sc2"] = t("sc2", [128, TW], F32)
    _w1 = t("w1buf", [128, JB * 128], BF16)
    _w2 = t("w2buf", [128, 1024], BF16)
    sb["w1buf"] = [_w1, _w1]
    sb["w2buf"] = [_w2, _w2]
    for n, w in (("b1s", DFF // 128), ("b2s", JB), ("g1s", JB), ("be1s", JB),
                 ("g2s", JB), ("be2s", JB), ("bdt_sb", JB), ("dp_sb", JB)):
        sb[n] = t(n, [128, w], F32)
    sb["ones_r32"] = t("ones_r32", [1, 128], F32)
    sb["ones_cb"] = t("ones_cb", [128, 1], BF16)
    sb["ones_cb2"] = t("ones_cb2", [128, 2], BF16)
    sb["epsc"] = t("epsc", [1, 1], F32)
    sb["idb"] = t("idb", [128, 128], F32)
    sb["ones_sq"] = sb["idb"]
    sb["pcol"] = t("pcol", [128, 1], F32)
    sb["a_sb"] = t("a_sb", [128, 128], F32)
    sb["a_sbb"] = t("a_sbb", [128, 128], BF16)
    sb["h_st"] = t("h_st", [128, 128], F32)
    sb["iof"] = sb["h_st"]
    sb["srow"] = t("srow", [128, 34], F32)
    sb["pm2"] = t("pm2", [128, 128], BF16)
    sb["abq"] = t("abq", [128, 4], BF16)
    sb["tq1"] = t("tq1", [128, 1], F32)
    sb["tcomb"] = t("tcomb", [128, 1], F32)
    sb["ones_sqb"] = t("ones_sqb", [128, 128], BF16)
    sb["wb_sb"] = t("wb_sb", [128, JB * NST], BF16)
    sb["wc_sb"] = t("wc_sb", [128, JB * NST], BF16)
    sb["wg_sb"] = t("wg_sb", [128, JB * NEX], BF16)
    sb["gb_rep"] = t("gb_rep", [128, NEX], F32)
    sb["gbr_sb"] = t("gbr_sb", [1, NEX], F32)
    for n in ("rcol", "mcol", "wcol"):
        sb[n] = t(n, [128, 1], F32)
    for n in ("rsc", "bia", "bwk", "selw", "comb_sc"):
        sb[n] = t(n, [128, NEX], F32)
    sb["ps"] = [e(nc.psum_tensor(f"ps{i}", [128, TW], F32)) for i in range(8)]
    return sb


def _emit_l0(p, io, sb):
    block = p.block
    x_t, wqT, wkT, wvT, w1T, w2T = io["x_t"], io["wqT"], io["wkT"], io["wvT"], io["w1T"], io["w2T"]
    b1c, b2c, g1c, be1c, g2c, be2c = io["b1c"], io["b2c"], io["g1c"], io["be1c"], io["g2c"], io["be2c"]
    wgT, wuT, wdT = io["wgT"], io["wuT"], io["wdT"]
    xres, xnb, A1, A2, A3 = sb["xres"], sb["xnb"], sb["A1"], sb["A2"], sb["A3"]
    sc0, sc1, sc2 = sb["sc0"], sb["sc1"], sb["sc2"]
    w1buf, w2buf = sb["w1buf"], sb["w2buf"]
    b1s, b2s, g1s, be1s, g2s, be2s = sb["b1s"], sb["b2s"], sb["g1s"], sb["be1s"], sb["g2s"], sb["be2s"]
    ones_r32, ones_cb, ones_cb2, epsc = sb["ones_r32"], sb["ones_cb"], sb["ones_cb2"], sb["epsc"]
    ps = sb["ps"]

    xrv = xres[:].bitcast(BF16)
    ktile = xrv[:, 0:1024]
    vtile = xrv[:, 1024:2048]
    kvb = xrv[:, 2048:3072]
    ksb = xrv[:, 3072:3072 + NH]
    wqbuf = xrv[:, 4096:4096 + 1024]
    xrvf = xres[:]
    row0 = xrvf[0:1, 4096:4096 + TW]
    row1 = xrvf[0:1, 4608:4608 + TW]
    row2 = xrvf[0:1, 5120:5120 + TW]
    zrow = xrvf[0:1, 5632:5632 + TW]
    sqb = xrv[:, 12288:12288 + TW]
    wk_sb = A2[:, 0:JB * D]
    wv_sb = A2[:, JB * D:2 * JB * D]

    r = dict(ps=ps, sqb=sqb, sc0=sc0, sc1=sc1, sc2=sc2, row0=row0, row1=row1,
             row2=row2, ones_r32=ones_r32, ones_cb=ones_cb, epsln=epsc[0:1, 0:1])

    @block.gpsimd
    def _(g):
        p.inc("dw", g.dma_start(
            out=A3[:].rearrange("p (j t) -> p j t", j=JB),
            in_=x_t.rearrange("(j p) t -> p j t", p=128)))
        for w, dst in ((wkT, wk_sb), (wvT, wv_sb)):
            p.inc("dw", g.dma_start(
                out=dst.rearrange("p (k o) -> p k o", k=JB),
                in_=w.rearrange("(k p) o -> p k o", p=128)))

    @block.sync
    def _(sync):
        for src, dst in ((b1c, b1s), (b2c, b2s), (g1c, g1s), (be1c, be1s),
                         (g2c, g2s), (be2c, be2s)):
            p.inc("ds", sync.dma_start(out=dst[:], in_=src))

    @block.vector
    def _(v):
        p.inc("ve", v.memset(ones_r32[:], 1.0))
        p.inc("ve", v.memset(ones_cb[:], 1.0))
        p.inc("ve", v.memset(ones_cb2[:], 1.0))
        p.inc("ve", v.memset(epsc[:], EPS_LN))

    p.mark("l0_ln0")
    emit_ln(p, r, A3, lambda kb, tc: A3[:, fsl(kb, tc)], xnb)

    p.mark("l0_ln1")
    emit_ln(p, r, xnb, lambda kb, tc: xnb[:, fsl(kb, tc)], A1, gcol=g1s, bcol=be1s)

    p.mark("l0_kv")
    NT = T // 128
    for tt in range(NT):
        for half in range(2):
            @block.tensor
            def _(te, tt=tt, half=half):
                p.wait(te, "ve")
                p.wait(te, "se")
                for kb in range(JB):
                    p.inc("pe", te.matmul(
                        ps[half][:, 0:TW],
                        A1[:, kb * T + tt * 128: kb * T + (tt + 1) * 128],
                        wk_sb[:, kb * D + half * TW: kb * D + (half + 1) * TW],
                        start=(kb == 0), stop=(kb == JB - 1)))
            emit_elu_epilogue(p, ps[half][:, 0:TW], ktile[:, half * TW:(half + 1) * TW], r)

        for half in range(2):
            @block.tensor
            def _(te, tt=tt, half=half):
                p.wait(te, "ve")
                p.wait(te, "se")
                for kb in range(JB):
                    p.inc("pe", te.matmul(
                        ps[half][:, 0:TW],
                        A1[:, kb * T + tt * 128: kb * T + (tt + 1) * 128],
                        wv_sb[:, kb * D + half * TW: kb * D + (half + 1) * TW],
                        start=(kb == 0), stop=(kb == JB - 1)))

            @block.scalar
            def _(s, tt=tt, half=half):
                p.wait(s, "pe")
                p.inc("se", s.activation(vtile[:, half * TW:(half + 1) * TW],
                                         ps[half][:, 0:TW], AF.Copy))

        @block.tensor
        def _(te, tt=tt):
            p.wait(te, "se")
            p.wait(te, "ve")
            for h in range(NH):
                pbank = ps[2] if h < 4 else ps[3]
                col = (h % 4) * 128
                p.inc("pe", te.matmul(
                    pbank[:, col:col + 128],
                    ktile[:, h * 128:(h + 1) * 128],
                    vtile[:, h * 128:(h + 1) * 128],
                    start=(tt == 0 and (h % 4) == 0),
                    stop=(tt == NT - 1 and (h % 4) == 3)))
            for h in range(NH):
                p.inc("pe", te.matmul(
                    ps[6][:, 2 * h:2 * h + 2],
                    ktile[:, h * 128:(h + 1) * 128],
                    ones_cb2[:],
                    start=(tt == 0 and h == 0), stop=(tt == NT - 1 and h == NH - 1)))

    @block.scalar
    def _(s):
        p.wait(s, "pe")
        p.inc("se", s.activation(kvb[:, 0:TW], ps[2][:, 0:TW], AF.Copy))
        p.inc("se", s.activation(kvb[:, TW:2 * TW], ps[3][:, 0:TW], AF.Copy))
        p.inc("se", s.activation(ksb, ps[6][:, 0:2 * NH:2], AF.Copy))

    p.mark("l0_q")
    for h in range(NH):
        @block.gpsimd
        def _(g, h=h):
            p.wait(g, "pe")
            for kq in range(JB):
                p.inc("dw", g.dma_start(
                    out=wqbuf[:, kq * 128:(kq + 1) * 128],
                    in_=wqT.rearrange("(k p) o -> k p o", p=128)[kq, :, h * 128:(h + 1) * 128]))

        for tc in range(TC):
            bank = tc % 2
            @block.tensor
            def _(te, h=h, tc=tc, bank=bank):
                p.wait(te, "dw")
                p.wait(te, "ve")
                for kb in range(JB):
                    p.inc("pe", te.matmul(
                        ps[bank][:, 0:TW],
                        wqbuf[:, kb * 128:(kb + 1) * 128],
                        A1[:, fsl(kb, tc)],
                        start=(kb == 0), stop=(kb == JB - 1)))
            emit_elu_epilogue(p, ps[bank][:, 0:TW], A2[:, fsl(h, tc)], r)

    p.mark("l0_att")
    for h in range(NH):
        for tc in range(TC):
            @block.tensor
            def _(te, h=h, tc=tc):
                p.wait(te, "se")
                p.wait(te, "ve")
                p.inc("pe", te.matmul(ps[6][0:1, 0:TW], ksb[:, h:h + 1],
                                      A2[:, fsl(h, tc)], start=True, stop=True))

            @block.vector
            def _(v, h=h, tc=tc):
                p.wait(v, "pe")
                p.inc("ve", v.tensor_scalar_add(zrow, ps[6][0:1, 0:TW], EPS_ATTN))
                p.inc("ve", v.reciprocal(zrow, zrow))

            @block.tensor
            def _(te, h=h, tc=tc):
                p.wait(te, "ve")
                p.inc("pe", te.matmul(ps[4][:, 0:TW], ones_r32[:], zrow,
                                      start=True, stop=True))
                p.inc("pe", te.matmul(ps[5][:, 0:TW],
                                      kvb[:, h * 128:(h + 1) * 128],
                                      A2[:, fsl(h, tc)], start=True, stop=True))

            @block.scalar
            def _(s, h=h, tc=tc):
                p.wait(s, "pe")
                p.inc("se", s.activation(sc0[:, 0:TW], ps[4][:, 0:TW], AF.Copy))

            @block.vector
            def _(v, h=h, tc=tc):
                p.wait(v, "se")
                p.inc("ve", v.tensor_mul(sc1[:, 0:TW], ps[5][:, 0:TW], sc0[:, 0:TW]))
                p.inc("ve", v.tensor_add(A3[:, fsl(h, tc)], sc1[:, 0:TW], xnb[:, fsl(h, tc)]))

    p.mark("l0_ln2")
    emit_ln(p, r, A3, lambda kb, tc: A3[:, fsl(kb, tc)], A1, gcol=g2s, bcol=be2s)

    p.mark("l0_ffnA")
    NFB = DFF // 128
    NHB = HID // 128
    pe_fb = {}
    ge_mark = {}
    ep_mark = {}
    sc_mark = {}
    xres_free = (p.val("pe"), p.val("ve"), p.val("se"))
    w1Tb = p.nc.dram_tensor("w1Tb", [32 * 128, D], BF16, kind="Internal").ap()
    w2Tb = p.nc.dram_tensor("w2Tb", [DFF, D], BF16, kind="Internal").ap()
    wgTb = p.nc.dram_tensor("wgTb", [24 * 128, D], BF16, kind="Internal").ap()
    wuTb = p.nc.dram_tensor("wuTb", [24 * 128, D], BF16, kind="Internal").ap()
    wdTb = p.nc.dram_tensor("wdTb", [HID, D], BF16, kind="Internal").ap()
    for tc in range(TC):
        for fb in range(NFB):
            buf = fb % 2
            bank = 4 + (fb % 4)
            qn = "dw" if (tc == 0 or fb % 2 == 0) else "ds"
            wsel = [w1buf[0], w2buf[0]]

            def _load(g, tc=tc, fb=fb, buf=buf, wsel=wsel, qn=qn):
                if fb < 2:
                    if tc == 0:
                        p.wait(g, "pe")
                    else:
                        p.wait(g, "pe", pe_fb[("a2", tc * NFB - 1)])
                else:
                    p.wait(g, "pe", pe_fb[tc * NFB + fb - 2])
                    if tc == 0:
                        p.wait(g, "dsc", sc_mark[("w1", fb - 2)])
                if tc == 0:
                    for kq in range(JB):
                        p.inc(qn, g.dma_start(
                            out=wsel[buf][:, kq * 128:(kq + 1) * 128],
                            in_=w1T.rearrange("(k p) o -> k p o", p=128)[kq, :, fb * 128:(fb + 1) * 128]))
                else:
                    if tc == 1:
                        p.wait(g, "dsc", sc_mark[("w1", fb)])
                    p.inc(qn, g.dma_start(
                        out=wsel[buf][:],
                        in_=w1Tb.rearrange("(f p) o -> f p o", p=128)[fb]))
            if qn == "dw":
                block.gpsimd(_load)
            else:
                block.sync(_load)

            if tc == 0:
                @block.scalar
                def _(s, fb=fb, buf=buf, wsel=wsel):
                    p.wait(s, "dw")
                    p.inc("dsc", s.dma_start(
                        out=w1Tb.rearrange("(f p) o -> f p o", p=128)[fb],
                        in_=wsel[buf][:]))
                    sc_mark[("w1", fb)] = p.val("dsc")

            @block.tensor
            def _(te, tc=tc, fb=fb, buf=buf, bank=bank, wsel=wsel, qn=qn):
                p.wait(te, qn)
                if fb < 4:
                    p.wait(te, "se")
                    if tc == 0 and fb == 0:
                        p.wait(te, "ve")
                    if tc > 0:
                        p.wait(te, "ve", ep_mark[(tc - 1, bank)])
                else:
                    p.wait(te, "se", ge_mark[(tc, fb - 4)])
                for kb in range(JB):
                    p.inc("pe", te.matmul(
                        ps[bank][:, 0:TW], wsel[buf][:, kb * 128:(kb + 1) * 128],
                        A1[:, fsl(kb, tc)],
                        start=(kb == 0), stop=(kb == JB - 1)))
                pe_fb[tc * NFB + fb] = p.val("pe")

            @block.scalar
            def _(s, tc=tc, fb=fb, bank=bank):
                p.wait(s, "pe", pe_fb[tc * NFB + fb])
                p.inc("se", s.activation(A2[:, fb * TW:(fb + 1) * TW], ps[bank][:, 0:TW],
                                         AF.Gelu, bias=b1s[:, fb:fb + 1], scale=1.0))
                ge_mark[(tc, fb)] = p.val("se")

        pa2_0 = {}
        for fb in range(NFB):
            buf = fb % 2
            qn = "dw" if (tc == 0 or fb % 2 == 0) else "ds"
            wsel2 = [w2buf[0], w1buf[0]]

            def _load2(g, tc=tc, fb=fb, buf=buf, wsel2=wsel2, qn=qn):
                if fb < 2:
                    p.wait(g, "pe", pe_fb[tc * NFB + NFB - 1])
                else:
                    p.wait(g, "pe", pe_fb[("a2", tc * NFB + fb - 2)])
                    if tc == 0:
                        p.wait(g, "dsc", sc_mark[("w2", fb - 2)])
                if tc == 0:
                    p.inc(qn, g.dma_start(
                        out=wsel2[buf][:],
                        in_=w2T.rearrange("(f p) o -> f p o", p=128)[fb]))
                else:
                    if tc == 1:
                        p.wait(g, "dsc", sc_mark[("w2", fb)])
                    p.inc(qn, g.dma_start(
                        out=wsel2[buf][:],
                        in_=w2Tb.rearrange("(f p) o -> f p o", p=128)[fb]))
            if qn == "dw":
                block.gpsimd(_load2)
            else:
                block.sync(_load2)

            if tc == 0:
                @block.scalar
                def _(s, fb=fb, buf=buf, wsel2=wsel2):
                    p.wait(s, "dw")
                    p.inc("dsc", s.dma_start(
                        out=w2Tb.rearrange("(f p) o -> f p o", p=128)[fb],
                        in_=wsel2[buf][:]))
                    sc_mark[("w2", fb)] = p.val("dsc")

            @block.tensor
            def _(te, tc=tc, fb=fb, buf=buf, wsel2=wsel2, pa2_0=pa2_0, qn=qn):
                p.wait(te, qn)
                if fb == 0:
                    p.wait(te, "ve")
                    p.wait(te, "se")
                    pa2_0[tc] = p.val("pe")
                for ob in range(JB):
                    p.inc("pe", te.matmul(
                        ps[ob][:, 0:TW], wsel2[buf][:, ob * 128:(ob + 1) * 128],
                        A2[:, fb * TW:(fb + 1) * TW],
                        start=(fb == 0), stop=(fb == NFB - 1)))
                pe_fb[("a2", tc * NFB + fb)] = p.val("pe")

        @block.scalar
        def _(s, tc=tc):
            if tc == 0:
                p.wait(s, "pe", xres_free[0])
                p.wait(s, "ve", xres_free[1])
            p.inc("dsc", s.dma_start(
                out=xres[:].rearrange("p (j t) -> p j t", j=JB)[:, :, tc * TW:(tc + 1) * TW],
                in_=x_t.rearrange("(j p) t -> p j t", p=128)[:, :, tc * TW:(tc + 1) * TW]))

        @block.vector
        def _(v, tc=tc, pa2_0=pa2_0):
            p.wait(v, "dsc")
            for ob in range(JB):
                p.wait(v, "pe", pa2_0[tc] + (NFB - 1) * JB + ob + 1)
                p.inc("ve", v.scalar_tensor_tensor(
                    sc0[:, 0:TW], ps[ob][:, 0:TW], b2s[:, ob:ob + 1],
                    A3[:, fsl(ob, tc)], ALU.add, ALU.add))
                p.inc("ve", v.tensor_add(xres[:, fsl(ob, tc)], xres[:, fsl(ob, tc)],
                                         sc0[:, 0:TW]))
                ep_mark[(tc, ob)] = p.val("ve")

    p.mark("l0_ffnB")
    NHB = HID // 128
    wgB = [w1buf[0], A2[:, 12288:13312]]
    wuB = [w2buf[0], A2[:, 13312:14336]]
    for tc in range(TC):
        for fb in range(NHB):
            buf = fb % 2
            def _loadG(g, tc=tc, fb=fb, buf=buf):
                if fb < 2:
                    if tc == 0:
                        p.wait(g, "pe")
                        p.wait(g, "ve")
                    else:
                        p.wait(g, "pe", pe_fb[("b2", tc * NHB - 1)])
                else:
                    p.wait(g, "pe", pe_fb[("b1", tc * NHB + fb - 2)])
                    if tc == 0:
                        p.wait(g, "dsc", sc_mark[("wu", fb - 2)])
                if tc == 0:
                    for kq in range(JB):
                        p.inc("dw", g.dma_start(
                            out=wgB[buf][:, kq * 128:(kq + 1) * 128],
                            in_=wgT.rearrange("(k p) o -> k p o", p=128)[kq, :, fb * 128:(fb + 1) * 128]))
                    for kq in range(JB):
                        p.inc("dw", g.dma_start(
                            out=wuB[buf][:, kq * 128:(kq + 1) * 128],
                            in_=wuT.rearrange("(k p) o -> k p o", p=128)[kq, :, fb * 128:(fb + 1) * 128]))
                else:
                    if tc == 1:
                        p.wait(g, "dsc", sc_mark[("wg", fb)])
                    p.inc("dw", g.dma_start(
                        out=wgB[buf][:],
                        in_=wgTb.rearrange("(f p) o -> f p o", p=128)[fb]))
            block.gpsimd(_loadG)

            if tc > 0:
                @block.sync
                def _(sy, tc=tc, fb=fb, buf=buf):
                    if fb < 2:
                        p.wait(sy, "pe", pe_fb[("b2", tc * NHB - 1)])
                    else:
                        p.wait(sy, "pe", pe_fb[("b1", tc * NHB + fb - 2)])
                    if tc == 1:
                        p.wait(sy, "dsc", sc_mark[("wu", fb)])
                    p.inc("ds", sy.dma_start(
                        out=wuB[buf][:],
                        in_=wuTb.rearrange("(f p) o -> f p o", p=128)[fb]))
            else:
                @block.scalar
                def _(s, fb=fb, buf=buf):
                    p.wait(s, "dw")
                    p.inc("dsc", s.dma_start(
                        out=wgTb.rearrange("(f p) o -> f p o", p=128)[fb],
                        in_=wgB[buf][:]))
                    sc_mark[("wg", fb)] = p.val("dsc")
                    p.inc("dsc", s.dma_start(
                        out=wuTb.rearrange("(f p) o -> f p o", p=128)[fb],
                        in_=wuB[buf][:]))
                    sc_mark[("wu", fb)] = p.val("dsc")

            gbank = 4 + 2 * (fb % 2)
            sslot = sc0 if fb % 2 == 0 else sc1

            @block.tensor
            def _(te, tc=tc, fb=fb, buf=buf, gbank=gbank):
                p.wait(te, "dw")
                if tc > 0:
                    p.wait(te, "ds")
                if fb < 2:
                    p.wait(te, "se")
                    p.wait(te, "ve")
                else:
                    p.wait(te, "se", ge_mark[("b", tc, fb - 2)])
                    p.wait(te, "ve", ep_mark[("b", tc, fb - 2)])
                for kb in range(JB):
                    p.inc("pe", te.matmul(
                        ps[gbank][:, 0:TW], wgB[buf][:, kb * 128:(kb + 1) * 128],
                        xnb[:, fsl(kb, tc)],
                        start=(kb == 0), stop=(kb == JB - 1)))
                for kb in range(JB):
                    p.inc("pe", te.matmul(
                        ps[gbank + 1][:, 0:TW], wuB[buf][:, kb * 128:(kb + 1) * 128],
                        xnb[:, fsl(kb, tc)],
                        start=(kb == 0), stop=(kb == JB - 1)))
                pe_fb[("b1", tc * NHB + fb)] = p.val("pe")

            @block.scalar
            def _(s, tc=tc, fb=fb, gbank=gbank, sslot=sslot):
                p.wait(s, "pe", pe_fb[("b1", tc * NHB + fb)])
                p.inc("se", s.activation(sslot[:, 0:TW], ps[gbank][:, 0:TW], AF.Silu,
                                         scale=1.0 / 127.0))
                ge_mark[("b", tc, fb)] = p.val("se")

            @block.vector
            def _(v, tc=tc, fb=fb, gbank=gbank, sslot=sslot):
                p.wait(v, "se", ge_mark[("b", tc, fb)])
                p.inc("ve", v.scalar_tensor_tensor(
                    A2[:, fb * TW:(fb + 1) * TW], ps[gbank + 1][:, 0:TW], 1.0 / 127.0,
                    sslot[:, 0:TW], ALU.mult, ALU.mult))
                ep_mark[("b", tc, fb)] = p.val("ve")

        pb2_0 = {}
        for fb in range(NHB):
            buf = fb % 2
            qn = "dw" if (tc == 0 or fb % 2 == 0) else "ds"
            wselD = [w2buf[0], w1buf[0]]

            def _loadD(g, tc=tc, fb=fb, buf=buf, wselD=wselD, qn=qn):
                if fb < 2:
                    p.wait(g, "pe", pe_fb[("b1", tc * NHB + NHB - 1)])
                else:
                    p.wait(g, "pe", pe_fb[("b2", tc * NHB + fb - 2)])
                    if tc == 0:
                        p.wait(g, "dsc", sc_mark[("wd", fb - 2)])
                if tc == 0:
                    p.inc(qn, g.dma_start(
                        out=wselD[buf][:],
                        in_=wdT.rearrange("(f p) o -> f p o", p=128)[fb]))
                else:
                    if tc == 1:
                        p.wait(g, "dsc", sc_mark[("wd", fb)])
                    p.inc(qn, g.dma_start(
                        out=wselD[buf][:],
                        in_=wdTb.rearrange("(f p) o -> f p o", p=128)[fb]))
            if qn == "dw":
                block.gpsimd(_loadD)
            else:
                block.sync(_loadD)

            if tc == 0:
                @block.scalar
                def _(s, fb=fb, buf=buf, wselD=wselD):
                    p.wait(s, "dw")
                    p.inc("dsc", s.dma_start(
                        out=wdTb.rearrange("(f p) o -> f p o", p=128)[fb],
                        in_=wselD[buf][:]))
                    sc_mark[("wd", fb)] = p.val("dsc")

            @block.tensor
            def _(te, tc=tc, fb=fb, buf=buf, wselD=wselD, pb2_0=pb2_0, qn=qn):
                p.wait(te, qn)
                if fb == 0:
                    p.wait(te, "ve")
                    p.wait(te, "se")
                    pb2_0[tc] = p.val("pe")
                for ob in range(JB):
                    p.inc("pe", te.matmul(
                        ps[ob][:, 0:TW], wselD[buf][:, ob * 128:(ob + 1) * 128],
                        A2[:, fb * TW:(fb + 1) * TW],
                        start=(fb == 0), stop=(fb == NHB - 1)))
                pe_fb[("b2", tc * NHB + fb)] = p.val("pe")

        @block.vector
        def _(v, tc=tc, pb2_0=pb2_0):
            for ob in range(JB):
                p.wait(v, "pe", pb2_0[tc] + (NHB - 1) * JB + ob + 1)
                p.inc("ve", v.scalar_tensor_tensor(
                    xres[:, fsl(ob, tc)], ps[ob][:, 0:TW], 1.0 / 127.0,
                    xres[:, fsl(ob, tc)], ALU.mult, ALU.add))


def _emit_l1(p, io, sb, fused, skip=()):
    block = p.block
    WdtT, bdtc, WBT, WCT = io["WdtT"], io["bdtc"], io["WBT"], io["WCT"]
    A_logc, Dpc, gwT, gbr = io["A_logc"], io["Dpc"], io["gwT"], io["gbr"]
    w13T, w2mT, w13sT, w2sT, out_t = io["w13T"], io["w2mT"], io["w13sT"], io["w2sT"], io["out_t"]
    x1res, xn1b, A1, A2, A3 = sb["xres"], sb["xnb"], sb["A1"], sb["A2"], sb["A3"]
    sc0, sc1, sc2 = sb["sc0"], sb["sc1"], sb["sc2"]
    ones_r32, ones_cb, ones_sq, epsc = sb["ones_r32"], sb["ones_cb"], sb["ones_sq"], sb["epsc"]
    idb, iof, pcol = sb["idb"], sb["iof"], sb["pcol"]
    a_sb, a_sbb, h_st = sb["a_sb"], sb["a_sbb"], sb["h_st"]
    srow, abq, tq1, tcomb = sb["srow"], sb["abq"], sb["tq1"], sb["tcomb"]
    pm2 = sb["pm2"]
    ones_sqb = sb["ones_sqb"]
    wb_sb, wc_sb, wg_sb = sb["wb_sb"], sb["wc_sb"], sb["wg_sb"]
    gb_rep, gbr_sb, bdt_sb, dp_sb = sb["gb_rep"], sb["gbr_sb"], sb["bdt_sb"], sb["dp_sb"]
    rcol = sb["rcol"]
    rsc, bia, bwk, mcol, selw, comb_sc, wcol = sb["rsc"], sb["bia"], sb["bwk"], sb["mcol"], sb["selw"], sb["comb_sc"], sb["wcol"]
    ps = sb["ps"]

    A1f = A1[:].bitcast(F32)
    A2f = A2[:].bitcast(F32)
    x1bf = A3[:, 0:JB * T]
    wdt_sb = A3[:, 0:JB * D]
    bT_sb = A3[0:NST, 8192:8192 + T]
    cT_sb = A3[0:NST, 10240:10240 + T]
    combT = A3[0:NST, 12288:12288 + T]
    Esel = A3[0:NST, 14336:14336 + NST * 128]
    io_sc = A2[:, 0:NST * 128]
    m13buf = [A1[:, 0:8192], A1[:, 8192:16384]]
    m2buf = [A2[:, 8192:12288], A2[:, 12288:16384]]
    hb_moe = A2[:, 0:2048]
    crep_moe = A2f[:, 1024:1024 + TW]
    row0 = A1f[0:1, 0:TW]
    row1 = A1f[0:1, TW:2 * TW]
    row2 = A1f[0:1, 2 * TW:3 * TW]
    sqb = sc2[:].bitcast(BF16)[:, 0:TW]

    r = dict(ps=ps, sqb=sqb, sc0=sc0, sc1=sc1, sc2=sc2, row0=row0, row1=row1,
             row2=row2, ones_r32=ones_r32, ones_cb=ones_cb, epsln=epsc[0:1, 0:1])

    p.mark("l1_setup")
    @block.sync
    def _(sync):
        if not fused:
            p.inc("ds", sync.dma_start(
                out=x1res[:].rearrange("p (j t) -> p j t", j=JB),
                in_=io["x1_t"].rearrange("(j p) t -> p j t", p=128)))
        p.inc("ds", sync.dma_start(out=a_sb[:], in_=A_logc))
        p.inc("ds", sync.dma_start(out=bdt_sb[:], in_=bdtc))
        p.inc("ds", sync.dma_start(out=dp_sb[:], in_=Dpc))
        p.inc("ds", sync.dma_start(out=gbr_sb[:], in_=gbr))

    @block.gpsimd
    def _(g):
        p.wait(g, "pe")
        p.wait(g, "ve")
        if not fused:
            p.inc("dw", g.dma_start(
                out=x1bf.rearrange("p (j t) -> p j t", j=JB),
                in_=io["x1_t"].rearrange("(j p) t -> p j t", p=128)))
        p.inc("dw", g.dma_start(
            out=wb_sb[:].rearrange("p (k o) -> p k o", k=JB),
            in_=WBT.rearrange("(k p) o -> p k o", p=128)))
        p.inc("dw", g.dma_start(
            out=wc_sb[:].rearrange("p (k o) -> p k o", k=JB),
            in_=WCT.rearrange("(k p) o -> p k o", p=128)))
        p.inc("dw", g.dma_start(
            out=wg_sb[:].rearrange("p (k o) -> p k o", k=JB),
            in_=gwT.rearrange("(k p) o -> p k o", p=128)))
        p.inc("go", g.iota(iof[:], [[1, 128]], base=0, channel_multiplier=0,
                           allow_small_or_imprecise_dtypes=True))
        p.inc("go", g.iota(pcol[:], [[0, 1]], base=0, channel_multiplier=1,
                           allow_small_or_imprecise_dtypes=True))
        p.inc("go", g.iota(io_sc, [[1, NST], [0, 128]], base=0, channel_multiplier=0,
                           allow_small_or_imprecise_dtypes=True))

    if fused:
        for kb in range(JB):
            @block.scalar
            def _(s, kb=kb):
                if kb == 0:
                    p.wait(s, "ve")
                    p.wait(s, "pe")
                p.inc("se", s.activation(x1bf[:, kb * T:(kb + 1) * T],
                                         x1res[:, kb * T:(kb + 1) * T], AF.Copy))

    @block.vector
    def _(v):
        p.wait(v, "go")
        p.inc("ve", v.memset(h_st[:], 0.0))
        p.inc("ve", v.tensor_single_scalar(idb[:], iof[:], pcol[:, 0:1], ALU.is_equal))
        if not fused:
            p.inc("ve", v.memset(ones_r32[:], 1.0))
            p.inc("ve", v.memset(ones_cb[:], 1.0))
            p.inc("ve", v.memset(epsc[:], EPS_LN))

    @block.tensor
    def _(te):
        p.wait(te, "dw")
        p.wait(te, "ds")
        p.wait(te, "ve")
        p.wait(te, "se")
        p.inc("pe", te.matmul(ps[0][:, 0:NEX], ones_r32[:], gbr_sb[:], start=True, stop=True))

    @block.scalar
    def _(s):
        p.wait(s, "pe")
        p.inc("se", s.activation(gb_rep[:], ps[0][:, 0:NEX], AF.Copy))

    @block.vector
    def _(v):
        p.wait(v, "dw")
        p.wait(v, "ds")
        p.inc("ve", v.tensor_scalar_mul(a_sb[:], a_sb[:], K5))

    @block.scalar
    def _(s):
        p.wait(s, "ve")
        p.inc("se", s.activation(a_sb[:], a_sb[:], AF.Tanh))

    @block.scalar
    def _(s):
        p.wait(s, "se")
        p.inc("se", s.activation(a_sb[:], a_sb[:], AF.Exp, scale=10.0))

    @block.vector
    def _(v):
        p.wait(v, "se")
        p.inc("ve", v.tensor_scalar_mul(a_sb[:], a_sb[:], -1.0))

    p.mark("l1_ln")
    emit_ln(p, r, x1bf, lambda kb, tc: x1res[:, fsl(kb, tc)], xn1b)

    @block.gpsimd
    def _(g):
        p.wait(g, "se")
        p.wait(g, "pe")
        p.wait(g, "ve")
        p.inc("dw", g.dma_start(
            out=wdt_sb.rearrange("p (k o) -> p k o", k=JB),
            in_=WdtT.rearrange("(k p) o -> p k o", p=128)))

    @block.vector
    def _(v):
        p.wait(v, "go")
        p.wait(v, "se")
        p.wait(v, "pe")
        p.inc("ve", v.tensor_single_scalar(Esel, io_sc[0:16, :], pcol[0:16, 0:1], ALU.is_equal))

    p.mark("l1_bc")
    for tc in range(TC):
        @block.tensor
        def _(te, tc=tc):
            p.wait(te, "ve")
            p.wait(te, "se")
            for kb in range(JB):
                p.inc("pe", te.matmul(ps[0][0:NST, 0:TW],
                                      wb_sb[:, kb * NST:(kb + 1) * NST],
                                      xn1b[:, fsl(kb, tc)],
                                      start=(kb == 0), stop=(kb == JB - 1)))
            for kb in range(JB):
                p.inc("pe", te.matmul(ps[1][0:NST, 0:TW],
                                      wc_sb[:, kb * NST:(kb + 1) * NST],
                                      xn1b[:, fsl(kb, tc)],
                                      start=(kb == 0), stop=(kb == JB - 1)))

        @block.scalar
        def _(s, tc=tc):
            p.wait(s, "pe")
            p.inc("se", s.activation(bT_sb[:, tc * TW:(tc + 1) * TW], ps[0][0:NST, 0:TW], AF.Copy))
            p.inc("se", s.activation(cT_sb[:, tc * TW:(tc + 1) * TW], ps[1][0:NST, 0:TW], AF.Copy))

    p.mark("l1_route")
    NT = T // 128
    for tt in range(NT):
        @block.tensor
        def _(te, tt=tt):
            p.wait(te, "se")
            p.wait(te, "ve")
            for kb in range(JB):
                p.inc("pe", te.matmul(ps[2][:, 0:NEX],
                                      xn1b[:, kb * T + tt * 128: kb * T + (tt + 1) * 128],
                                      wg_sb[:, kb * NEX:(kb + 1) * NEX],
                                      start=(kb == 0), stop=(kb == JB - 1)))

        @block.scalar
        def _(s, tt=tt):
            p.wait(s, "pe")
            p.inc("se", s.activation(rsc[:], ps[2][:, 0:NEX], AF.Sigmoid))

        @block.vector
        def _(v, tt=tt):
            p.wait(v, "se")
            p.inc("ve", v.tensor_add(bia[:], rsc[:], gb_rep[:]))
            p.inc("ve", v.tensor_copy(bwk[:], bia[:]))
            for rnd in range(4):
                p.inc("ve", v.reduce_max(mcol[:], bwk[:], AX.X))
                if rnd < 3:
                    p.inc("ve", v.tensor_single_scalar(selw[:], bwk[:], mcol[:, 0:1], ALU.is_equal))
                    p.inc("ve", v.scalar_tensor_tensor(bwk[:], selw[:], -1e30, bwk[:],
                                                       ALU.mult, ALU.add))
            p.inc("ve", v.tensor_single_scalar(selw[:], bia[:], mcol[:, 0:1], ALU.is_ge))
            p.inc("ve", v.tensor_mul(selw[:], rsc[:], selw[:]))
            p.inc("ve", v.tensor_reduce(wcol[:], selw[:], AX.X, ALU.add))
            p.inc("ve", v.reciprocal(wcol[:], wcol[:]))
            p.inc("ve", v.tensor_scalar(comb_sc[:], selw[:], wcol[:, 0:1], 2.5,
                                        ALU.mult, ALU.mult))

        @block.tensor
        def _(te, tt=tt):
            p.wait(te, "ve")
            p.inc("pe", te.transpose(ps[3][0:NEX, 0:128], comb_sc[:], idb[:]))

        @block.scalar
        def _(s, tt=tt):
            p.wait(s, "pe")
            p.inc("se", s.activation(combT[:, tt * 128:(tt + 1) * 128],
                                     ps[3][0:NEX, 0:128], AF.Copy))

    @block.vector
    def _(v):
        p.wait(v, "pe")
        p.inc("ve", v.memset(ones_sq[:], 1.0))

    p.mark("l1_scan")
    SCH = 32
    ESC = SCH + 1
    NCHS = T // SCH
    SCS = 128
    ESS = SCS + 1
    A1b = A1[:]
    A1f = A1[:].bitcast(F32)
    A2b = A2[:]
    A2f2 = A2[:].bitcast(F32)
    dAxb = [A1b[:, 0:4096], A2b[:, 4096:8192]]
    wxA = A2b[:, 0:4096]
    wxB = A1b[:, 12416:16384]
    w32B = A2b[:, 8192:8320]
    qbuf = A1b[:, 4096:4096 + ESC * 128]
    pbuf = A1b[:, 8320:8320 + SCH * 128]
    dtraw = A1f[:, 4160:5200]
    dtrw2 = A1f[:, 6208:7248]
    dt_sc = A2b[:, 8320:8320 + JB * ESS]
    u_sc = A2b[:, 9352:9352 + JB * ESS]
    brep_sc = A2b[:, 10384:10384 + NST * ESS]
    crep_sc = A2b[:, 12448:12448 + NST * ESS]
    ccv_b = [A2f2[:, 7256:7256 + SCH], A2f2[:, 7380:7380 + SCH]]
    cuv = A2f2[:, 7296:7296 + SCH]
    cbv = A2f2[:, 7336:7336 + SCH]
    sc0b = sc0[:].bitcast(BF16)
    sc1b = sc1[:].bitcast(BF16)
    psl = sc0b[:, 256:384]
    redsl = sc0[:, 200:208]

    def w_ap(k, slot):
        if k % 2 == 0:
            return wxA[:, (slot - 1) * 128:slot * 128]
        if slot <= 31:
            return wxB[:, (slot - 1) * 128:slot * 128]
        return w32B[:]

    def w_slot0(k):
        return w_ap(k - 1 if k > 0 else 1, 32)

    def dA_ap(k, slot):
        return dAxb[k % 2][:, (slot - 1) * 128:slot * 128]

    dts_c = dt_sc.rearrange("p (o i) -> p o i", i=ESS)
    us_c = u_sc.rearrange("p (o i) -> p o i", i=ESS)
    br_c = brep_sc.rearrange("p (n i) -> p n i", i=ESS)
    cr_c = crep_sc.rearrange("p (n i) -> p n i", i=ESS)
    pb4 = pbuf.rearrange("p (t o n) -> p t o n", o=JB, n=NST)
    x1v = x1res[:].rearrange("p (o t) -> p o t", o=JB)
    redsl3 = redsl.unsqueeze(2)
    prep_ready = {}

    def emit_recon(v, ch, j):
        screl = (ch % (SCS // SCH)) * SCH
        tg = ch * SCH + j
        p.inc("ve", v.tensor_tensor(
            psl.rearrange("p (o n) -> p o n", o=JB),
            pb4[:, j],
            cr_c[:, :, screl + j].unsqueeze(1).broadcast_to((128, JB, NST)),
            ALU.mult))
        p.inc("ve", v.tensor_reduce(
            redsl[:], psl.rearrange("p (o n) -> p o n", o=JB), AX.X, ALU.add))
        p.inc("ve", v.scalar_tensor_tensor(
            x1v[:, :, tg:tg + 1], redsl3, srow[:, j + 1:j + 2],
            x1v[:, :, tg:tg + 1], ALU.mult, ALU.add))

    def mk_prep_stages(k1):
        screl1 = (k1 % (SCS // SCH)) * SCH
        par1 = k1 % 2
        dAxT = dAxb[par1]
        dA4 = dAxT.rearrange("p (t o n) -> p t o n", o=JB, n=NST)
        vals = {}
        asb3 = a_sbb[:].rearrange("p (o n) -> p o n", o=JB).unsqueeze(1)

        def dA_slice(v, sl):
            lo, hi = sl * 8, sl * 8 + 8
            dtt = dts_c[:, :, screl1 + 1 + lo:screl1 + 1 + hi].transpose([0, 2, 1])
            p.inc("ve", v.tensor_tensor(
                dA4[:, lo:hi], dtt.unsqueeze(3).broadcast_to((128, 8, JB, NST)),
                asb3.broadcast_to((128, 8, JB, NST)), ALU.mult))
            vals[f"dv{sl}"] = p.val("ve")

        def dA_act(s, sl):
            p.wait(s, "ve", vals[f"dv{sl}"])
            seg = dAxT[:, sl * 1024:(sl + 1) * 1024]
            p.inc("se", s.activation(seg, seg, AF.Tanh, scale=K8))
            p.inc("se", s.activation(seg, seg, AF.Exp, scale=16.0))
            if sl == 3:
                prep_ready[k1] = p.val("se")

        def w_slice(v, sl):
            lo = sl * 8
            nmain = 8 if (par1 == 0 or sl < 3) else 7
            wT = wxA if par1 == 0 else wxB
            w4 = wT.rearrange("p (t o n) -> p t o n", o=JB, n=NST)
            utt = us_c[:, :, screl1 + 1 + lo:screl1 + 1 + lo + nmain] \
                .transpose([0, 2, 1])
            p.inc("ve", v.tensor_tensor(
                w4[:, lo:lo + nmain],
                utt.unsqueeze(3).broadcast_to((128, nmain, JB, NST)),
                br_c[:, :, screl1 + 1 + lo:screl1 + 1 + lo + nmain]
                .transpose([0, 2, 1]).unsqueeze(2)
                .broadcast_to((128, nmain, JB, NST)), ALU.mult))
            if nmain == 7:
                w14 = w32B.rearrange("p (t o n) -> p t o n", o=JB, n=NST)
                p.inc("ve", v.tensor_tensor(
                    w14,
                    us_c[:, :, screl1 + 32:screl1 + 33].transpose([0, 2, 1])
                    .unsqueeze(3).broadcast_to((128, 1, JB, NST)),
                    br_c[:, :, screl1 + 32:screl1 + 33].transpose([0, 2, 1])
                    .unsqueeze(2).broadcast_to((128, 1, JB, NST)), ALU.mult))

        def c_squ(v):
            p.inc("ve", v.tensor_tensor(
                sc0b[:, 0:JB * SCH].rearrange("p (o i) -> p o i", i=SCH),
                us_c[:, :, screl1:screl1 + SCH],
                us_c[:, :, screl1:screl1 + SCH], ALU.mult))
            vals["squ"] = p.val("ve")

        def c_mm(te):
            p.wait(te, "ve", vals["squ"])
            p.inc("pe", te.matmul(ps[7][:, 0:JB * SCH], ones_sqb[:],
                                  sc0b[:, 0:JB * SCH], start=True, stop=True))
            vals["cmm"] = p.val("pe")

        def c_fin(v):
            p.wait(v, "pe", vals["cmm"])
            p.inc("ve", v.tensor_reduce(
                cuv[:], ps[7][:, 0:JB * SCH].rearrange("p (o i) -> p i o", o=JB),
                AX.X, ALU.add))
            p.inc("ve", v.tensor_tensor(
                sc1b[:, 0:NST * SCH].rearrange("p (n i) -> p n i", i=SCH),
                br_c[:, :, screl1:screl1 + SCH],
                br_c[:, :, screl1:screl1 + SCH], ALU.mult))
            p.inc("ve", v.tensor_reduce(
                cbv[:], sc1b[:, 0:NST * SCH].rearrange("p (n i) -> p i n", n=NST),
                AX.X, ALU.add))
            p.inc("ve", v.tensor_mul(ccv_b[par1][:], cuv[:], cbv[:]))

        def w0_extra(v):
            w14 = w32B.rearrange("p (t o n) -> p t o n", o=JB, n=NST)
            p.inc("ve", v.tensor_tensor(
                w14,
                us_c[:, :, 0:1].transpose([0, 2, 1]).unsqueeze(3)
                .broadcast_to((128, 1, JB, NST)),
                br_c[:, :, 0:1].transpose([0, 2, 1]).unsqueeze(2)
                .broadcast_to((128, 1, JB, NST)), ALU.mult))

        vstages = [lambda v, sl=sl: dA_slice(v, sl) for sl in range(4)]
        vstages += [lambda v, sl=sl: w_slice(v, sl) for sl in range(4)]
        vstages += [c_squ, c_fin]
        sstages = [lambda s, sl=sl: dA_act(s, sl) for sl in range(4)]
        return dict(v=vstages, s=sstages, te=[c_mm], w0=w0_extra)

    def emit_prep_serial(k1, with_w0=False):
        st = mk_prep_stages(k1)

        @block.vector
        def _(v, st=st):
            p.wait(v, "se")
            p.wait(v, "pe")
            for sl in range(4):
                st["v"][sl](v)

        @block.scalar
        def _(s, st=st):
            for sl in range(4):
                st["s"][sl](s)

        @block.vector
        def _(v, st=st, with_w0=with_w0):
            for sl in range(4):
                st["v"][4 + sl](v)
            if with_w0:
                st["w0"](v)
            st["v"][8](v)

        @block.tensor
        def _(te, st=st):
            st["te"][0](te)

        @block.vector
        def _(v, st=st):
            st["v"][9](v)

    def emit_sprep(st0, NE):
        @block.tensor
        def _(te, st0=st0, NE=NE):
            p.wait(te, "ve")
            p.wait(te, "se")
            p.wait(te, "dw")
            for ob in range(JB):
                bank, off = divmod(ob, 3)
                for kb in range(JB):
                    p.inc("pe", te.matmul(
                        ps[bank][:, off * 130:off * 130 + NE],
                        wdt_sb[:, kb * D + ob * 128: kb * D + ob * 128 + 128],
                        xn1b[:, kb * T + st0: kb * T + st0 + NE],
                        start=(off == 0 and kb == 0),
                        stop=(kb == JB - 1)))

        @block.vector
        def _(v, NE=NE):
            p.wait(v, "pe")
            for ob in range(JB):
                bank, off = divmod(ob, 3)
                p.inc("ve", v.tensor_single_scalar(
                    dtraw[:, ob * 130:ob * 130 + NE],
                    ps[bank][:, off * 130:off * 130 + NE],
                    bdt_sb[:, ob:ob + 1], ALU.add))

        dra = dtraw.rearrange("p (o i) -> p o i", i=130)[:, :, 0:NE]
        dr2 = dtrw2.rearrange("p (o i) -> p o i", i=130)[:, :, 0:NE]

        @block.scalar
        def _(s, dra=dra, dr2=dr2):
            p.wait(s, "ve")
            p.inc("se", s.activation(dr2, dra, AF.Abs))
            p.inc("se", s.activation(dr2, dr2, AF.Exp, scale=-1.0))
            p.inc("se", s.activation(dr2, dr2, AF.Ln, bias=1.0))

        @block.vector
        def _(v, dra=dra, dr2=dr2):
            p.wait(v, "se")
            p.inc("ve", v.tensor_scalar_max(dra, dra, 0.0))
            p.inc("ve", v.tensor_add(dra, dra, dr2))
            p.inc("ve", v.tensor_scalar(dra, dra, 0.0505, KDT,
                                        ALU.subtract, ALU.mult))

        @block.scalar
        def _(s, dra=dra):
            p.wait(s, "ve")
            p.inc("se", s.activation(dra, dra, AF.Tanh))

        @block.vector
        def _(v, st0=st0, NE=NE, dra=dra):
            p.wait(v, "se")
            dts = dt_sc.rearrange("p (o i) -> p o i", i=ESS)[:, :, 0:NE]
            p.inc("ve", v.tensor_scalar(dts, dra, 0.099, 0.0505,
                                        ALU.mult, ALU.add))
            p.inc("ve", v.tensor_tensor(
                u_sc.rearrange("p (o i) -> p o i", i=ESS)[:, :, 0:NE],
                dts,
                xn1b[:].rearrange("p (o t) -> p o t", o=JB)[:, :, st0:st0 + NE],
                ALU.mult))

        for n in range(NST):
            @block.tensor
            def _(te, st0=st0, NE=NE, n=n):
                p.wait(te, "se")
                if n == 0:
                    p.wait(te, "ve")
                p.inc("pe", te.matmul(ps[3][:, 0:NE],
                                      Esel[:, n * 128:(n + 1) * 128],
                                      bT_sb[:, st0:st0 + NE], start=True, stop=True))
                p.inc("pe", te.matmul(ps[4][:, 0:NE],
                                      Esel[:, n * 128:(n + 1) * 128],
                                      cT_sb[:, st0:st0 + NE], start=True, stop=True))

            @block.scalar
            def _(s, NE=NE, n=n):
                p.wait(s, "pe")
                p.inc("se", s.activation(brep_sc[:, n * ESS:n * ESS + NE],
                                         ps[3][:, 0:NE], AF.Copy))
                p.inc("se", s.activation(crep_sc[:, n * ESS:n * ESS + NE],
                                         ps[4][:, 0:NE], AF.Copy))

    if "scan" not in skip:
        @block.scalar
        def _(s):
            p.wait(s, "ve")
            p.wait(s, "pe")
            p.inc("se", s.activation(a_sbb[:], a_sb[:], AF.Copy))

        @block.vector
        def _(v):
            p.wait(v, "se")
            p.inc("ve", v.memset(qbuf[:, 0:128].bitcast(F32), 0.0))
            p.inc("ve", v.memset(srow[:], 1.0))
            p.inc("ve", v.memset(abq[:].bitcast(F32), 0.0))
            p.inc("ve", v.memset(ones_sqb[:], 1.0))

        @block.tensor
        def _(te):
            p.wait(te, "ve")
            p.inc("pe", te.matmul(ps[5][:, 0:2], ones_sqb[:], abq[:, 0:2],
                                  start=True, stop=True))
            p.inc("pe", te.matmul(ps[6][:, 0:2], ones_sqb[:], abq[:, 2:4],
                                  start=True, stop=True))

    VSCHED = {2: 0, 3: 1, 4: 2, 5: 3, 7: 4, 8: 5, 9: 6, 10: 7, 11: 8, 13: 9}
    SSCHED = {4: 0, 6: 1, 8: 2, 10: 3}
    TSCHED = {12: 0}

    for ch in range(NCHS if "scan" not in skip else 0):
        t0 = ch * SCH
        screl = (ch % (SCS // SCH)) * SCH
        at_super = screl == 0
        if at_super:
            if ch > 0:
                @block.vector
                def _(v, ch=ch):
                    emit_recon(v, ch - 1, 30)
                    emit_recon(v, ch - 1, 31)
            st0 = (ch // (SCS // SCH)) * SCS
            emit_sprep(st0, ESS if st0 + SCS < T else SCS)
            emit_prep_serial(ch, with_w0=(ch == 0))
        if ch > 0:
            @block.vector
            def _(v):
                p.inc("ve", v.tensor_copy(qbuf[:, 0:128],
                                          qbuf[:, SCH * 128:(SCH + 1) * 128]))
                p.inc("ve", v.tensor_copy(srow[:, 0:1], srow[:, SCH:SCH + 1]))

        do_prep = (ch + 1 < NCHS) and ((ch + 1) % (SCS // SCH) != 0)
        stg = mk_prep_stages(ch + 1) if do_prep else None
        ccv_cur = ccv_b[ch % 2]

        for tl in range(SCH):
            t = t0 + tl
            par = t % 2
            npar = (t + 1) % 2
            last = (t == T - 1)
            se_sqrt = p.val("se") + 1
            marks = {}
            recon_job = None
            if tl >= 2:
                recon_job = (ch, tl - 2)
            elif not at_super and ch > 0:
                recon_job = (ch - 1, 30 + tl)

            @block.vector
            def _(v, tl=tl, ch=ch, par=par, npar=npar, last=last,
                  se_sqrt=se_sqrt, marks=marks, recon_job=recon_job, stg=stg):
                if tl == 0 and ch in prep_ready:
                    p.wait(v, "se", prep_ready[ch])
                p.inc("ve", v.scalar_tensor_tensor(
                    pbuf[:, tl * 128:(tl + 1) * 128],
                    qbuf[:, tl * 128:(tl + 1) * 128],
                    srow[:, tl:tl + 1],
                    (w_slot0(ch) if tl == 0 else w_ap(ch, tl)),
                    ALU.mult, ALU.add))
                if not last:
                    p.inc("ve", v.tensor_mul(qbuf[:, (tl + 1) * 128:(tl + 2) * 128],
                                             pbuf[:, tl * 128:(tl + 1) * 128],
                                             dA_ap(ch, tl + 1)))
                    marks["q"] = p.val("ve")
                    with p.nc.allow_low_precision(reason="b partial sum bf16"):
                        p.inc("ve", v.scalar_tensor_tensor(
                            h_st[:], qbuf[:, (tl + 1) * 128:(tl + 2) * 128], 2.0,
                            w_ap(ch, tl + 1), ALU.mult, ALU.mult,
                            accum_out=abq[:, 2 * npar + 1:2 * npar + 2]))
                    marks["b"] = p.val("ve")
                p.wait(v, "pe")
                p.inc("ve", v.tensor_scalar(
                    tcomb[:], ps[5 + par][:, 0:1], srow[:, tl:tl + 1],
                    ps[5 + par][:, 1:2], ALU.mult, ALU.add))
                marks["x"] = p.val("ve")
                if recon_job is not None:
                    emit_recon(v, recon_job[0], recon_job[1])
                if stg is not None and tl in VSCHED:
                    stg["v"][VSCHED[tl]](v)
                p.wait(v, "se", se_sqrt)
                p.inc("ve", v.tensor_scalar_add(rcol[:], rcol[:], 1e-8))
                p.inc("ve", v.reciprocal(rcol[:], rcol[:]))
                p.inc("ve", v.tensor_scalar(srow[:, tl + 1:tl + 2], rcol[:],
                                            20.0, 1.0, ALU.mult, ALU.min))

            if not last:
                @block.gpsimd
                def _(g, tl=tl, marks=marks):
                    p.wait(g, "ve", marks["q"])
                    p.inc("po", g.tensor_mul(pm2[:],
                                             qbuf[:, (tl + 1) * 128:(tl + 2) * 128],
                                             qbuf[:, (tl + 1) * 128:(tl + 2) * 128]))

            @block.scalar
            def _(s, tl=tl, npar=npar, last=last, marks=marks,
                  ccv_cur=ccv_cur, stg=stg):
                p.wait(s, "ve", marks["x"])
                p.inc("se", s.activation(rcol[:], tcomb[:], AF.Sqrt,
                                         scale=srow[:, tl:tl + 1],
                                         bias=ccv_cur[:, tl:tl + 1]))
                if not last:
                    p.wait(s, "po")
                    with p.nc.allow_low_precision(reason="a partial sum bf16"):
                        p.inc("se", s.activation(sc2[:, 0:128], pm2[:], AF.Copy,
                                                 accum_out=abq[:, 2 * npar:2 * npar + 1]))
                marks["cp"] = p.val("se")
                if stg is not None and tl in SSCHED:
                    stg["s"][SSCHED[tl]](s)

            if not last:
                @block.tensor
                def _(te, npar=npar, marks=marks, stg=stg, tl=tl):
                    p.wait(te, "se", marks["cp"])
                    p.wait(te, "ve", marks["b"])
                    p.inc("pe", te.matmul(ps[5 + npar][:, 0:2], ones_sqb[:],
                                          abq[:, 2 * npar:2 * npar + 2],
                                          start=True, stop=True))
                    if stg is not None and tl in TSCHED:
                        stg["te"][TSCHED[tl]](te)

        if ch == NCHS - 1:
            @block.vector
            def _(v, ch=ch):
                emit_recon(v, ch, 30)
                emit_recon(v, ch, 31)


    p.mark("l1_dp")
    for tc in range(TC if "dp" not in skip else 0):
        @block.vector
        def _(v, tc=tc):
            p.wait(v, "dw")
            p.wait(v, "ds")
            for j in range(JB):
                p.inc("ve", v.scalar_tensor_tensor(
                    x1res[:, fsl(j, tc)], xn1b[:, fsl(j, tc)], dp_sb[:, j:j + 1],
                    x1res[:, fsl(j, tc)], ALU.mult, ALU.add))

    p.mark("l1_moe")
    silu_sl = [A2f[:, 1536 + 512 * i:2048 + 512 * i] for i in range(4)]
    pe_after_expert = {}
    moe_prev = None
    for ei in range((NEX + 1) if "moe" not in skip else 0):
        shared = (ei == NEX)
        buf = ei % 2

        @block.gpsimd
        def _(g, ei=ei, buf=buf, shared=shared):
            if ei < 2:
                p.wait(g, "pe")
            else:
                p.wait(g, "pe", pe_after_expert[ei - 2])
            for kb in range(JB):
                src = (w13sT.rearrange("(k p) o -> k p o", p=128)[kb] if shared
                       else w13T.rearrange("e (k p) o -> e k p o", p=128)[ei, kb])
                p.inc("dw", g.dma_start(out=m13buf[buf][:, kb * 1024:(kb + 1) * 1024], in_=src))
            for ib in range(4):
                src = (w2sT.rearrange("(k p) o -> k p o", p=128)[ib] if shared
                       else w2mT.rearrange("e (k p) o -> e k p o", p=128)[ei, ib])
                p.inc("dw", g.dma_start(out=m2buf[buf][:, ib * 1024:(ib + 1) * 1024], in_=src))

        for tc in range(TC):
            first = moe_prev is None
            pm = moe_prev
            m = {}

            if not shared:
                @block.tensor
                def _(te, ei=ei, tc=tc, pm=pm, first=first):
                    if first:
                        p.wait(te, "ve")
                        p.wait(te, "se")
                    else:
                        p.wait(te, "ve", pm["ep0"])
                    p.inc("pe", te.matmul(ps[0][:, 0:TW], Esel[:, ei * 128:(ei + 1) * 128],
                                          combT[:, tc * TW:(tc + 1) * TW],
                                          start=True, stop=True))

                @block.scalar
                def _(s, ei=ei, tc=tc, m=m):
                    p.wait(s, "pe")
                    p.inc("se", s.activation(crep_moe, ps[0][:, 0:TW], AF.Copy))
                    m["crep"] = p.val("se")

            @block.tensor
            def _(te, ei=ei, tc=tc, buf=buf, pm=pm, m=m, first=first, shared=shared):
                if tc == 0:
                    p.wait(te, "dw")
                if first:
                    p.wait(te, "ve")
                    p.wait(te, "se")
                for ib in range(4):
                    if not first:
                        p.wait(te, "ve", pm[f"ep{ib}"])
                    if ib == 0 and not shared:
                        p.wait(te, "se", m["crep"])
                    for kb in range(JB):
                        p.inc("pe", te.matmul(
                            ps[ib][:, 0:TW],
                            m13buf[buf][:, kb * 1024 + ib * 128: kb * 1024 + ib * 128 + 128],
                            xn1b[:, fsl(kb, tc)], start=(kb == 0), stop=(kb == JB - 1)))
                    if not first:
                        p.wait(te, "ve", pm[f"ep{4 + ib}"])
                    for kb in range(JB):
                        p.inc("pe", te.matmul(
                            ps[4 + ib][:, 0:TW],
                            m13buf[buf][:, kb * 1024 + 512 + ib * 128: kb * 1024 + 512 + ib * 128 + 128],
                            xn1b[:, fsl(kb, tc)], start=(kb == 0), stop=(kb == JB - 1)))
                    m[f"gu{ib}"] = p.val("pe")

            @block.scalar
            def _(s, ei=ei, tc=tc, m=m):
                for ib in range(4):
                    p.wait(s, "pe", m[f"gu{ib}"])
                    p.inc("se", s.activation(silu_sl[ib], ps[ib][:, 0:TW], AF.Silu,
                                             scale=1.0 / 127.0))
                    m[f"si{ib}"] = p.val("se")

            @block.vector
            def _(v, ei=ei, tc=tc, m=m, pm=pm, shared=shared, first=first):
                if not first and "down_done" in pm:
                    p.wait(v, "pe", pm["down_done"])
                for ib in range(4):
                    p.wait(v, "se", m[f"si{ib}"])
                    if shared:
                        p.inc("ve", v.scalar_tensor_tensor(
                            hb_moe[:, ib * TW:(ib + 1) * TW], ps[4 + ib][:, 0:TW],
                            1.0 / 127.0, silu_sl[ib], ALU.mult, ALU.mult))
                    else:
                        p.inc("ve", v.scalar_tensor_tensor(
                            sc1[:, 0:TW], ps[4 + ib][:, 0:TW],
                            1.0 / 127.0, silu_sl[ib], ALU.mult, ALU.mult))
                        p.inc("ve", v.tensor_mul(hb_moe[:, ib * TW:(ib + 1) * TW],
                                                 sc1[:, 0:TW], crep_moe))
                    m[f"hp{ib}"] = p.val("ve")

            @block.tensor
            def _(te, ei=ei, tc=tc, buf=buf, m=m):
                p.wait(te, "ve", m["hp3"])
                m["down0"] = p.val("pe")
                for ib in range(4):
                    for ob in range(JB):
                        p.inc("pe", te.matmul(
                            ps[ob][:, 0:TW],
                            m2buf[buf][:, ib * 1024 + ob * 128: ib * 1024 + ob * 128 + 128],
                            hb_moe[:, ib * TW:(ib + 1) * TW],
                            start=(ib == 0), stop=(ib == 3)))
                m["down_done"] = p.val("pe")

            @block.vector
            def _(v, ei=ei, tc=tc, m=m):
                for ob in range(JB):
                    p.wait(v, "pe", m["down0"] + 3 * JB + ob + 1)
                    p.inc("ve", v.scalar_tensor_tensor(
                        x1res[:, fsl(ob, tc)], ps[ob][:, 0:TW], 1.0 / 127.0,
                        x1res[:, fsl(ob, tc)], ALU.mult, ALU.add))
                    m[f"ep{ob}"] = p.val("ve")

            moe_prev = m

        pe_after_expert[ei] = p.val("pe")

    p.mark(None)
    @block.sync
    def _(sync):
        p.wait(sync, "ve")
        p.inc("dx", sync.dma_start(
            out=out_t.rearrange("(j p) t -> p j t", p=128),
            in_=x1res[:].rearrange("p (j t) -> p j t", j=JB)))
        p.wait(sync, "dx")


def _mk_io_l0(nc, out_kind="Internal"):
    ap = lambda n, shp, dt=F32, kind="ExternalInput": nc.dram_tensor(n, shp, dt, kind=kind).ap()
    io = {
        "x_t": ap("x_t", [D, T]),
        "wqT": ap("wqT", [D, D]),
        "wkT": ap("wkT", [D, D]),
        "wvT": ap("wvT", [D, D]),
        "w1T": ap("w1T", [D, DFF]),
        "w2T": ap("w2T", [DFF, D]),
        "b1c": ap("b1c", [128, DFF // 128]),
        "b2c": ap("b2c", [128, JB]),
        "g1c": ap("g1c", [128, JB]),
        "be1c": ap("be1c", [128, JB]),
        "g2c": ap("g2c", [128, JB]),
        "be2c": ap("be2c", [128, JB]),
        "wgT": ap("wgT", [D, HID], I32),
        "wuT": ap("wuT", [D, HID], I32),
        "wdT": ap("wdT", [HID, D], I32),
    }
    return io


def _mk_io_l1(nc, with_x1):
    ap = lambda n, shp, dt=F32, kind="ExternalInput": nc.dram_tensor(n, shp, dt, kind=kind).ap()
    io = {
        "WdtT": ap("WdtT", [D, D]),
        "bdtc": ap("bdtc", [128, JB]),
        "WBT": ap("WBT", [D, NST]),
        "WCT": ap("WCT", [D, NST]),
        "A_logc": ap("A_logc", [128, 128]),
        "Dpc": ap("Dpc", [128, JB]),
        "gwT": ap("gwT", [D, NEX]),
        "gbr": ap("gbr", [1, NEX]),
        "w13T": ap("w13T", [NEX, D, D]),
        "w2mT": ap("w2mT", [NEX, INTER, D]),
        "w13sT": ap("w13sT", [D, D]),
        "w2sT": ap("w2sT", [INTER, D]),
        "out_t": ap("out_t", [D, T], kind="ExternalOutput"),
    }
    if with_x1:
        io["x1_t"] = ap("x1_t", [D, T])
    return io


def build_full(skip=(), debug_taps=False):
    nc = bass.Bass("TRN2", target_bir_lowering=False)
    io0 = _mk_io_l0(nc)
    io1 = _mk_io_l1(nc, with_x1=False)
    dbg = None
    if debug_taps:
        dbg = nc.dram_tensor("dbg_t", [D, T], F32, kind="ExternalOutput").ap()
    with ExitStack() as ctx:
        e = ctx.enter_context
        p = Prog(nc, ctx)
        sb = _alloc(nc, e)
        _emit_l0(p, io0, sb)
        if debug_taps:
            block = p.block
            @block.sync
            def _(sync):
                p.wait(sync, "ve")
                p.wait(sync, "se")
                p.wait(sync, "pe")
                p.inc("dx", sync.dma_start(
                    out=dbg.rearrange("(j p) t -> p j t", p=128),
                    in_=sb["xres"][:].rearrange("p (j t) -> p j t", j=JB)))
                p.wait(sync, "dx")
        _emit_l1(p, io1, sb, fused=True, skip=skip)
    return nc


def build_l0():
    nc = bass.Bass("TRN2", target_bir_lowering=False)
    io0 = _mk_io_l0(nc)
    x1_t = nc.dram_tensor("x1_t", [D, T], F32, kind="ExternalOutput").ap()
    with ExitStack() as ctx:
        e = ctx.enter_context
        p = Prog(nc, ctx)
        sb = _alloc(nc, e)
        _emit_l0(p, io0, sb)
        block = p.block

        @block.sync
        def _(sync):
            p.wait(sync, "ve")
            p.inc("dx", sync.dma_start(
                out=x1_t.rearrange("(j p) t -> p j t", p=128),
                in_=sb["xres"][:].rearrange("p (j t) -> p j t", j=JB)))
            p.wait(sync, "dx")
    return nc


def build_l1():
    nc = bass.Bass("TRN2", target_bir_lowering=False)
    io1 = _mk_io_l1(nc, with_x1=True)
    with ExitStack() as ctx:
        e = ctx.enter_context
        p = Prog(nc, ctx)
        sb = _alloc(nc, e)
        _emit_l1(p, io1, sb, fused=False)
    return nc



def _host_weights(inputs):
    f = np.ascontiguousarray
    return {
        "wqT": f(inputs["L0_attn_wq"].T),
        "wkT": f(inputs["L0_attn_wk"].T),
        "wvT": f(inputs["L0_attn_wv"].T),
        "w1T": f(inputs["L0_attn_w1"].T),
        "w2T": f(inputs["L0_attn_w2"].T),
        "b1c": f(inputs["L0_attn_b1"].reshape(DFF // 128, 128).T),
        "b2c": f(inputs["L0_attn_b2"].reshape(JB, 128).T),
        "g1c": f(inputs["L0_attn_g1"].reshape(JB, 128).T),
        "be1c": f(inputs["L0_attn_be1"].reshape(JB, 128).T),
        "g2c": f(inputs["L0_attn_g2"].reshape(JB, 128).T),
        "be2c": f(inputs["L0_attn_be2"].reshape(JB, 128).T),
        "wgT": f(inputs["L0_ffn_w_gate"].T),
        "wuT": f(inputs["L0_ffn_w_up"].T),
        "wdT": f(inputs["L0_ffn_w_down"].T),
        "WdtT": f(inputs["L1_attn_Wdt"].T),
        "bdtc": f(inputs["L1_attn_bdt"].reshape(JB, 128).T),
        "WBT": f(inputs["L1_attn_WB"].T),
        "WCT": f(inputs["L1_attn_WC"].T),
        "A_logc": f(inputs["L1_attn_A_log"].reshape(JB, 128, NST).transpose(1, 0, 2).reshape(128, 128)),
        "Dpc": f(inputs["L1_attn_Dp"].reshape(JB, 128).T),
        "gwT": f(inputs["L1_ffn_gate_w"].T),
        "gbr": f(inputs["L1_ffn_gate_b"].reshape(1, NEX)),
        "w13T": f(inputs["L1_ffn_w13"].transpose(0, 2, 1)),
        "w2mT": f(inputs["L1_ffn_w2"].transpose(0, 2, 1)),
        "w13sT": f(inputs["L1_ffn_w13_s"].T),
        "w2sT": f(inputs["L1_ffn_w2_s"].T),
    }


def _l0_host_inputs(inputs):
    w = {k: v for k, v in _host_weights(inputs).items()
         if k in ("wqT", "wkT", "wvT", "w1T", "w2T", "b1c", "b2c", "g1c", "be1c",
                  "g2c", "be2c", "wgT", "wuT", "wdT")}
    x = np.asarray(inputs["L0_attn_x"])
    return [dict(w, x_t=np.ascontiguousarray(x[c].T)) for c in range(8)]


def _l1_host_inputs(inputs, x1_list):
    w = {k: v for k, v in _host_weights(inputs).items()
         if k not in ("wqT", "wkT", "wvT", "w1T", "w2T", "b1c", "b2c", "g1c",
                      "be1c", "g2c", "be2c", "wgT", "wuT", "wdT")}
    return [dict(w, x1_t=np.ascontiguousarray(x1_list[c])) for c in range(8)]


def kernel(**inputs):
    _ = inputs.get("L1_ffn_input_ids")
    nc = build_full()
    w = _host_weights(inputs)
    x = np.asarray(inputs["L0_attn_x"])
    in_maps = [dict(w, x_t=np.ascontiguousarray(x[c].T)) for c in range(8)]
    res = run_bass_kernel_spmd(nc, in_maps, core_ids=list(range(8)))
    out = np.stack([np.asarray(r["out_t"]).T for r in res.results], axis=0)
    return out.astype(np.float32)


# version 15 (final)
# speedup vs baseline: 1.5120x; 1.1317x over previous
import numpy as np
import ml_dtypes
from contextlib import ExitStack

import concourse.bass as bass
import concourse.mybir as mybir
from concourse.bass_utils import run_bass_kernel_spmd

F32 = mybir.dt.float32
BF16 = mybir.dt.bfloat16
I32 = mybir.dt.int32
AF = mybir.ActivationFunctionType
ALU = mybir.AluOpType
AX = mybir.AxisListType
BT = ml_dtypes.bfloat16

D = 1024
T = 2048
JB = 8
NH = 8
DFF = 4096
HID = 3072
EPS_LN = 1e-5
EPS_ATTN = 1e-6
TW = 512
TC = T // TW
NST = 16
TCH = 64
NCH = T // TCH
INTER = 512
NEX = 16
K8 = 0.5 / (8.0 + 1e-8)
K5 = 0.5 / (5.0 + 1e-8)
KDT = 0.5 / (0.0495 + 1e-8)


class Prog:
    def __init__(self, nc, ctx):
        self.nc = nc
        self.ctx = ctx
        self.block = ctx.enter_context(nc.Block())
        self._sems = {}

    def sem(self, name):
        if name not in self._sems:
            h = self.ctx.enter_context(self.nc.semaphore(name))
            self._sems[name] = [h, 0]
        return self._sems[name]

    def inc(self, name, inst, k=1):
        if name.startswith("d"):
            k *= 16
        s = self.sem(name)
        s[1] += k
        inst.then_inc(s[0], k)
        return s[1]

    def val(self, name):
        return self.sem(name)[1]

    def wait(self, eng, name, v=None):
        s = self.sem(name)
        v = s[1] if v is None else v
        if v > 0:
            eng.wait_ge(s[0], v)

    _scope = None

    def mark(self, name):
        if self._scope is not None:
            self.nc.leave_named_scope(self._scope[0], self._scope[1], False)
            self._scope = None
        if name is not None:
            sid, _ = self.nc.enter_named_scope(name, False)
            self._scope = (name, sid)


def fsl(kb, tc, w=TW):
    return slice(kb * T + tc * w, kb * T + (tc + 1) * w)


def emit_ln(p, r, src_bf, val_src, dst_bf, gcol=None, bcol=None):
    block = p.block
    ps = r["ps"]
    sq2 = r["sc2"][:].bitcast(BF16)
    sqt = [sq2[:, 0:TW], sq2[:, TW:2 * TW]]
    for tc in range(TC):
        vq_mark = {}
        mm_mark = {}
        for kb in range(JB):
            @block.vector
            def _(v, kb=kb, tc=tc, vq_mark=vq_mark, mm_mark=mm_mark):
                if kb == 0:
                    p.wait(v, "dw")
                    p.wait(v, "ds")
                    p.wait(v, "se")
                    p.wait(v, "pe")
                elif kb >= 2:
                    p.wait(v, "pe", mm_mark[kb - 2])
                p.inc("ve", v.tensor_mul(sqt[kb % 2], src_bf[:, fsl(kb, tc)],
                                         src_bf[:, fsl(kb, tc)]))
                vq_mark[kb] = p.val("ve")

            @block.tensor
            def _(te, kb=kb, tc=tc, vq_mark=vq_mark, mm_mark=mm_mark):
                p.wait(te, "ve", vq_mark[kb])
                p.inc("pe", te.matmul(ps[7][0:1, 0:TW], r["ones_cb"][:], sqt[kb % 2],
                                      start=(kb == 0), stop=(kb == JB - 1)))
                p.inc("pe", te.matmul(ps[6][0:1, 0:TW], r["ones_cb"][:], src_bf[:, fsl(kb, tc)],
                                      start=(kb == 0), stop=(kb == JB - 1)))
                mm_mark[kb] = p.val("pe")

        @block.vector
        def _(v, tc=tc):
            p.wait(v, "pe")
            p.inc("ve", v.tensor_scalar_mul(r["row0"], ps[6][0:1, 0:TW], 1.0 / D))
            p.inc("ve", v.tensor_scalar_mul(r["row1"], ps[7][0:1, 0:TW], 1.0 / D))
            p.inc("ve", v.tensor_mul(r["row2"], r["row0"], r["row0"]))
            p.inc("ve", v.tensor_sub(r["row1"], r["row1"], r["row2"]))

        @block.scalar
        def _(s, tc=tc):
            p.wait(s, "ve")
            p.inc("se", s.activation(r["row1"], r["row1"], AF.Ln, bias=r["epsln"]))
            p.inc("se", s.activation(r["row1"], r["row1"], AF.Exp, scale=-0.5))

        @block.tensor
        def _(te, tc=tc):
            p.wait(te, "se")
            p.wait(te, "ve")
            p.inc("pe", te.matmul(ps[4][:, 0:TW], r["ones_r32"][:], r["row0"],
                                  start=True, stop=True))
            p.inc("pe", te.matmul(ps[5][:, 0:TW], r["ones_r32"][:], r["row1"],
                                  start=True, stop=True))

        @block.vector
        def _(v, tc=tc):
            p.wait(v, "pe")
            for kb in range(JB):
                vs = val_src(kb, tc)
                p.inc("ve", v.tensor_sub(r["sc0"][:, 0:TW], vs, ps[4][:, 0:TW]))
                if gcol is None:
                    p.inc("ve", v.tensor_mul(dst_bf[:, fsl(kb, tc)], r["sc0"][:, 0:TW], ps[5][:, 0:TW]))
                else:
                    p.inc("ve", v.tensor_mul(r["sc0"][:, 0:TW], r["sc0"][:, 0:TW], ps[5][:, 0:TW]))
                    p.inc("ve", v.tensor_scalar(dst_bf[:, fsl(kb, tc)], r["sc0"][:, 0:TW],
                                                gcol[:, kb:kb + 1], bcol[:, kb:kb + 1],
                                                ALU.mult, ALU.add))


def emit_elu_epilogue(p, ps_ap, dst_ap, r, w=TW):
    block = p.block

    @block.vector
    def _(v):
        p.wait(v, "pe")
        p.inc("ve", v.tensor_scalar_min(r["sc0"][:, 0:w], ps_ap, 0.0))
        p.inc("ve", v.tensor_sub(r["sc1"][:, 0:w], ps_ap, r["sc0"][:, 0:w]))

    @block.scalar
    def _(s):
        p.wait(s, "ve")
        p.inc("se", s.activation(r["sc2"][:, 0:w], r["sc0"][:, 0:w], AF.Exp))

    @block.vector
    def _(v):
        p.wait(v, "se")
        p.inc("ve", v.tensor_add(dst_ap, r["sc1"][:, 0:w], r["sc2"][:, 0:w]))


def _alloc(nc, e):
    sb = {}
    t = lambda name, shp, dt=F32: e(nc.sbuf_tensor(name, shp, dt))
    sb["xres"] = t("xres", [128, JB * T], F32)
    sb["xnb"] = t("xnb", [128, JB * T], BF16)
    sb["A1"] = t("A1", [128, JB * T], BF16)
    sb["A2"] = t("A2", [128, JB * T], BF16)
    sb["A3"] = t("A3", [128, JB * T], BF16)
    sb["sc0"] = t("sc0", [128, TW], F32)
    sb["sc1"] = t("sc1", [128, TW], F32)
    sb["sc2"] = t("sc2", [128, TW], F32)
    _w1 = t("w1buf", [128, JB * 128], BF16)
    _w2 = t("w2buf", [128, 1024], BF16)
    sb["w1buf"] = [_w1, _w1]
    sb["w2buf"] = [_w2, _w2]
    for n, w in (("b1s", DFF // 128), ("b2s", JB), ("g1s", JB), ("be1s", JB),
                 ("g2s", JB), ("be2s", JB), ("bdt_sb", JB), ("dp_sb", JB)):
        sb[n] = t(n, [128, w], F32)
    sb["ones_r32"] = t("ones_r32", [1, 128], F32)
    sb["ones_cb"] = t("ones_cb", [128, 1], BF16)
    sb["ones_cb2"] = t("ones_cb2", [128, 2], BF16)
    sb["epsc"] = t("epsc", [1, 1], F32)
    sb["idb"] = t("idb", [128, 128], F32)
    sb["ones_sq"] = sb["idb"]
    sb["pcol"] = t("pcol", [128, 1], F32)
    sb["a_sb"] = t("a_sb", [128, 128], F32)
    sb["a_sbb"] = t("a_sbb", [128, 128], BF16)
    sb["h_st"] = t("h_st", [128, 128], F32)
    sb["iof"] = sb["h_st"]
    sb["srow"] = t("srow", [128, 34], F32)
    sb["pm2"] = t("pm2", [128, 128], BF16)
    sb["abq"] = t("abq", [128, 4], BF16)
    sb["tq1"] = t("tq1", [128, 1], F32)
    sb["tcomb"] = t("tcomb", [128, 1], F32)
    sb["ones_sqb"] = t("ones_sqb", [128, 128], BF16)
    sb["wb_sb"] = t("wb_sb", [128, JB * NST], BF16)
    sb["wc_sb"] = t("wc_sb", [128, JB * NST], BF16)
    sb["wg_sb"] = t("wg_sb", [128, JB * NEX], BF16)
    sb["gb_rep"] = t("gb_rep", [128, NEX], F32)
    sb["gbr_sb"] = t("gbr_sb", [1, NEX], F32)
    for n in ("rcol", "mcol", "wcol"):
        sb[n] = t(n, [128, 1], F32)
    for n in ("rsc", "bia", "bwk", "selw", "comb_sc"):
        sb[n] = t(n, [128, NEX], F32)
    sb["ps"] = [e(nc.psum_tensor(f"ps{i}", [128, TW], F32)) for i in range(8)]
    return sb


def _emit_l0(p, io, sb):
    block = p.block
    x_t, wqT, wkT, wvT, w1T, w2T = io["x_t"], io["wqT"], io["wkT"], io["wvT"], io["w1T"], io["w2T"]
    b1c, b2c, g1c, be1c, g2c, be2c = io["b1c"], io["b2c"], io["g1c"], io["be1c"], io["g2c"], io["be2c"]
    wgT, wuT, wdT = io["wgT"], io["wuT"], io["wdT"]
    xres, xnb, A1, A2, A3 = sb["xres"], sb["xnb"], sb["A1"], sb["A2"], sb["A3"]
    sc0, sc1, sc2 = sb["sc0"], sb["sc1"], sb["sc2"]
    w1buf, w2buf = sb["w1buf"], sb["w2buf"]
    b1s, b2s, g1s, be1s, g2s, be2s = sb["b1s"], sb["b2s"], sb["g1s"], sb["be1s"], sb["g2s"], sb["be2s"]
    ones_r32, ones_cb, ones_cb2, epsc = sb["ones_r32"], sb["ones_cb"], sb["ones_cb2"], sb["epsc"]
    ps = sb["ps"]

    xrv = xres[:].bitcast(BF16)
    ktile = xrv[:, 0:1024]
    vtile = xrv[:, 1024:2048]
    kvb = xrv[:, 2048:3072]
    ksb = xrv[:, 3072:3072 + NH]
    wqbuf = xrv[:, 4096:4096 + 1024]
    xrvf = xres[:]
    row0 = xrvf[0:1, 4096:4096 + TW]
    row1 = xrvf[0:1, 4608:4608 + TW]
    row2 = xrvf[0:1, 5120:5120 + TW]
    zrow = xrvf[0:1, 5632:5632 + TW]
    sqb = xrv[:, 12288:12288 + TW]
    wk_sb = A2[:, 0:JB * D]
    wv_sb = A2[:, JB * D:2 * JB * D]

    r = dict(ps=ps, sqb=sqb, sc0=sc0, sc1=sc1, sc2=sc2, row0=row0, row1=row1,
             row2=row2, ones_r32=ones_r32, ones_cb=ones_cb, epsln=epsc[0:1, 0:1])

    @block.gpsimd
    def _(g):
        p.inc("dw", g.dma_start(
            out=A3[:].rearrange("p (j t) -> p j t", j=JB),
            in_=x_t.rearrange("(j p) t -> p j t", p=128)))
        for w, dst in ((wkT, wk_sb), (wvT, wv_sb)):
            p.inc("dw", g.dma_start(
                out=dst.rearrange("p (k o) -> p k o", k=JB),
                in_=w.rearrange("(k p) o -> p k o", p=128)))

    @block.sync
    def _(sync):
        for src, dst in ((b1c, b1s), (b2c, b2s), (g1c, g1s), (be1c, be1s),
                         (g2c, g2s), (be2c, be2s)):
            p.inc("ds", sync.dma_start(out=dst[:], in_=src))

    @block.vector
    def _(v):
        p.inc("ve", v.memset(ones_r32[:], 1.0))
        p.inc("ve", v.memset(ones_cb[:], 1.0))
        p.inc("ve", v.memset(ones_cb2[:], 1.0))
        p.inc("ve", v.memset(epsc[:], EPS_LN))

    p.mark("l0_ln0")
    emit_ln(p, r, A3, lambda kb, tc: A3[:, fsl(kb, tc)], xnb)

    p.mark("l0_ln1")
    emit_ln(p, r, xnb, lambda kb, tc: xnb[:, fsl(kb, tc)], A1, gcol=g1s, bcol=be1s)

    p.mark("l0_kv")
    NT = T // 128
    for tt in range(NT):
        for half in range(2):
            @block.tensor
            def _(te, tt=tt, half=half):
                p.wait(te, "ve")
                p.wait(te, "se")
                for kb in range(JB):
                    p.inc("pe", te.matmul(
                        ps[half][:, 0:TW],
                        A1[:, kb * T + tt * 128: kb * T + (tt + 1) * 128],
                        wk_sb[:, kb * D + half * TW: kb * D + (half + 1) * TW],
                        start=(kb == 0), stop=(kb == JB - 1)))
            emit_elu_epilogue(p, ps[half][:, 0:TW], ktile[:, half * TW:(half + 1) * TW], r)

        for half in range(2):
            @block.tensor
            def _(te, tt=tt, half=half):
                p.wait(te, "ve")
                p.wait(te, "se")
                for kb in range(JB):
                    p.inc("pe", te.matmul(
                        ps[half][:, 0:TW],
                        A1[:, kb * T + tt * 128: kb * T + (tt + 1) * 128],
                        wv_sb[:, kb * D + half * TW: kb * D + (half + 1) * TW],
                        start=(kb == 0), stop=(kb == JB - 1)))

            @block.scalar
            def _(s, tt=tt, half=half):
                p.wait(s, "pe")
                p.inc("se", s.activation(vtile[:, half * TW:(half + 1) * TW],
                                         ps[half][:, 0:TW], AF.Copy))

        @block.tensor
        def _(te, tt=tt):
            p.wait(te, "se")
            p.wait(te, "ve")
            for h in range(NH):
                pbank = ps[2] if h < 4 else ps[3]
                col = (h % 4) * 128
                p.inc("pe", te.matmul(
                    pbank[:, col:col + 128],
                    ktile[:, h * 128:(h + 1) * 128],
                    vtile[:, h * 128:(h + 1) * 128],
                    start=(tt == 0 and (h % 4) == 0),
                    stop=(tt == NT - 1 and (h % 4) == 3)))
            for h in range(NH):
                p.inc("pe", te.matmul(
                    ps[6][:, 2 * h:2 * h + 2],
                    ktile[:, h * 128:(h + 1) * 128],
                    ones_cb2[:],
                    start=(tt == 0 and h == 0), stop=(tt == NT - 1 and h == NH - 1)))

    @block.scalar
    def _(s):
        p.wait(s, "pe")
        p.inc("se", s.activation(kvb[:, 0:TW], ps[2][:, 0:TW], AF.Copy))
        p.inc("se", s.activation(kvb[:, TW:2 * TW], ps[3][:, 0:TW], AF.Copy))
        p.inc("se", s.activation(ksb, ps[6][:, 0:2 * NH:2], AF.Copy))

    p.mark("l0_q")
    for h in range(NH):
        @block.gpsimd
        def _(g, h=h):
            p.wait(g, "pe")
            for kq in range(JB):
                p.inc("dw", g.dma_start(
                    out=wqbuf[:, kq * 128:(kq + 1) * 128],
                    in_=wqT.rearrange("(k p) o -> k p o", p=128)[kq, :, h * 128:(h + 1) * 128]))

        for tc in range(TC):
            bank = tc % 2
            @block.tensor
            def _(te, h=h, tc=tc, bank=bank):
                p.wait(te, "dw")
                p.wait(te, "ve")
                for kb in range(JB):
                    p.inc("pe", te.matmul(
                        ps[bank][:, 0:TW],
                        wqbuf[:, kb * 128:(kb + 1) * 128],
                        A1[:, fsl(kb, tc)],
                        start=(kb == 0), stop=(kb == JB - 1)))
            emit_elu_epilogue(p, ps[bank][:, 0:TW], A2[:, fsl(h, tc)], r)

    p.mark("l0_att")
    for h in range(NH):
        for tc in range(TC):
            @block.tensor
            def _(te, h=h, tc=tc):
                p.wait(te, "se")
                p.wait(te, "ve")
                p.inc("pe", te.matmul(ps[6][0:1, 0:TW], ksb[:, h:h + 1],
                                      A2[:, fsl(h, tc)], start=True, stop=True))

            @block.vector
            def _(v, h=h, tc=tc):
                p.wait(v, "pe")
                p.inc("ve", v.tensor_scalar_add(zrow, ps[6][0:1, 0:TW], EPS_ATTN))
                p.inc("ve", v.reciprocal(zrow, zrow))

            @block.tensor
            def _(te, h=h, tc=tc):
                p.wait(te, "ve")
                p.inc("pe", te.matmul(ps[4][:, 0:TW], ones_r32[:], zrow,
                                      start=True, stop=True))
                p.inc("pe", te.matmul(ps[5][:, 0:TW],
                                      kvb[:, h * 128:(h + 1) * 128],
                                      A2[:, fsl(h, tc)], start=True, stop=True))

            @block.scalar
            def _(s, h=h, tc=tc):
                p.wait(s, "pe")
                p.inc("se", s.activation(sc0[:, 0:TW], ps[4][:, 0:TW], AF.Copy))

            @block.vector
            def _(v, h=h, tc=tc):
                p.wait(v, "se")
                p.inc("ve", v.tensor_mul(sc1[:, 0:TW], ps[5][:, 0:TW], sc0[:, 0:TW]))
                p.inc("ve", v.tensor_add(A3[:, fsl(h, tc)], sc1[:, 0:TW], xnb[:, fsl(h, tc)]))

    p.mark("l0_ln2")
    emit_ln(p, r, A3, lambda kb, tc: A3[:, fsl(kb, tc)], A1, gcol=g2s, bcol=be2s)

    p.mark("l0_ffnA")
    NFB = DFF // 128
    NHB = HID // 128
    pe_fb = {}
    ge_mark = {}
    ep_mark = {}
    sc_mark = {}
    xres_free = (p.val("pe"), p.val("ve"), p.val("se"))
    w1Tb = p.nc.dram_tensor("w1Tb", [32 * 128, D], BF16, kind="Internal").ap()
    w2Tb = p.nc.dram_tensor("w2Tb", [DFF, D], BF16, kind="Internal").ap()
    wgTb = p.nc.dram_tensor("wgTb", [24 * 128, D], BF16, kind="Internal").ap()
    wuTb = p.nc.dram_tensor("wuTb", [24 * 128, D], BF16, kind="Internal").ap()
    wdTb = p.nc.dram_tensor("wdTb", [HID, D], BF16, kind="Internal").ap()
    for tc in range(TC):
        for fb in range(NFB):
            buf = fb % 2
            bank = 4 + (fb % 4)
            qn = "dw" if (tc == 0 or fb % 2 == 0) else "ds"
            wsel = [w1buf[0], w2buf[0]]

            def _load(g, tc=tc, fb=fb, buf=buf, wsel=wsel, qn=qn):
                if fb < 2:
                    if tc == 0:
                        p.wait(g, "pe")
                    else:
                        p.wait(g, "pe", pe_fb[("a2", tc * NFB - 1)])
                else:
                    p.wait(g, "pe", pe_fb[tc * NFB + fb - 2])
                    if tc == 0:
                        p.wait(g, "dsc", sc_mark[("w1", fb - 2)])
                if tc == 0:
                    for kq in range(JB):
                        p.inc(qn, g.dma_start(
                            out=wsel[buf][:, kq * 128:(kq + 1) * 128],
                            in_=w1T.rearrange("(k p) o -> k p o", p=128)[kq, :, fb * 128:(fb + 1) * 128]))
                else:
                    if tc == 1:
                        p.wait(g, "dsc", sc_mark[("w1", fb)])
                    p.inc(qn, g.dma_start(
                        out=wsel[buf][:],
                        in_=w1Tb.rearrange("(f p) o -> f p o", p=128)[fb]))
            if qn == "dw":
                block.gpsimd(_load)
            else:
                block.sync(_load)

            if tc == 0:
                @block.scalar
                def _(s, fb=fb, buf=buf, wsel=wsel):
                    p.wait(s, "dw")
                    p.inc("dsc", s.dma_start(
                        out=w1Tb.rearrange("(f p) o -> f p o", p=128)[fb],
                        in_=wsel[buf][:]))
                    sc_mark[("w1", fb)] = p.val("dsc")

            @block.tensor
            def _(te, tc=tc, fb=fb, buf=buf, bank=bank, wsel=wsel, qn=qn):
                p.wait(te, qn)
                if fb < 4:
                    p.wait(te, "se")
                    if tc == 0 and fb == 0:
                        p.wait(te, "ve")
                    if tc > 0:
                        p.wait(te, "ve", ep_mark[(tc - 1, bank)])
                else:
                    p.wait(te, "se", ge_mark[(tc, fb - 4)])
                for kb in range(JB):
                    p.inc("pe", te.matmul(
                        ps[bank][:, 0:TW], wsel[buf][:, kb * 128:(kb + 1) * 128],
                        A1[:, fsl(kb, tc)],
                        start=(kb == 0), stop=(kb == JB - 1)))
                pe_fb[tc * NFB + fb] = p.val("pe")

            @block.scalar
            def _(s, tc=tc, fb=fb, bank=bank):
                p.wait(s, "pe", pe_fb[tc * NFB + fb])
                p.inc("se", s.activation(A2[:, fb * TW:(fb + 1) * TW], ps[bank][:, 0:TW],
                                         AF.Gelu, bias=b1s[:, fb:fb + 1], scale=1.0))
                ge_mark[(tc, fb)] = p.val("se")

        pa2_0 = {}
        for fb in range(NFB):
            buf = fb % 2
            qn = "dw" if (tc == 0 or fb % 2 == 0) else "ds"
            wsel2 = [w2buf[0], w1buf[0]]

            def _load2(g, tc=tc, fb=fb, buf=buf, wsel2=wsel2, qn=qn):
                if fb < 2:
                    p.wait(g, "pe", pe_fb[tc * NFB + NFB - 1])
                else:
                    p.wait(g, "pe", pe_fb[("a2", tc * NFB + fb - 2)])
                    if tc == 0:
                        p.wait(g, "dsc", sc_mark[("w2", fb - 2)])
                if tc == 0:
                    p.inc(qn, g.dma_start(
                        out=wsel2[buf][:],
                        in_=w2T.rearrange("(f p) o -> f p o", p=128)[fb]))
                else:
                    if tc == 1:
                        p.wait(g, "dsc", sc_mark[("w2", fb)])
                    p.inc(qn, g.dma_start(
                        out=wsel2[buf][:],
                        in_=w2Tb.rearrange("(f p) o -> f p o", p=128)[fb]))
            if qn == "dw":
                block.gpsimd(_load2)
            else:
                block.sync(_load2)

            if tc == 0:
                @block.scalar
                def _(s, fb=fb, buf=buf, wsel2=wsel2):
                    p.wait(s, "dw")
                    p.inc("dsc", s.dma_start(
                        out=w2Tb.rearrange("(f p) o -> f p o", p=128)[fb],
                        in_=wsel2[buf][:]))
                    sc_mark[("w2", fb)] = p.val("dsc")

            @block.tensor
            def _(te, tc=tc, fb=fb, buf=buf, wsel2=wsel2, pa2_0=pa2_0, qn=qn):
                p.wait(te, qn)
                if fb == 0:
                    p.wait(te, "ve")
                    p.wait(te, "se")
                    pa2_0[tc] = p.val("pe")
                for ob in range(JB):
                    p.inc("pe", te.matmul(
                        ps[ob][:, 0:TW], wsel2[buf][:, ob * 128:(ob + 1) * 128],
                        A2[:, fb * TW:(fb + 1) * TW],
                        start=(fb == 0), stop=(fb == NFB - 1)))
                pe_fb[("a2", tc * NFB + fb)] = p.val("pe")

        @block.scalar
        def _(s, tc=tc):
            if tc == 0:
                p.wait(s, "pe", xres_free[0])
                p.wait(s, "ve", xres_free[1])
            p.inc("dsc", s.dma_start(
                out=xres[:].rearrange("p (j t) -> p j t", j=JB)[:, :, tc * TW:(tc + 1) * TW],
                in_=x_t.rearrange("(j p) t -> p j t", p=128)[:, :, tc * TW:(tc + 1) * TW]))

        @block.vector
        def _(v, tc=tc, pa2_0=pa2_0):
            p.wait(v, "dsc")
            for ob in range(JB):
                p.wait(v, "pe", pa2_0[tc] + (NFB - 1) * JB + ob + 1)
                p.inc("ve", v.scalar_tensor_tensor(
                    sc0[:, 0:TW], ps[ob][:, 0:TW], b2s[:, ob:ob + 1],
                    A3[:, fsl(ob, tc)], ALU.add, ALU.add))
                p.inc("ve", v.tensor_add(xres[:, fsl(ob, tc)], xres[:, fsl(ob, tc)],
                                         sc0[:, 0:TW]))
                ep_mark[(tc, ob)] = p.val("ve")

    p.mark("l0_ffnB")
    NHB = HID // 128
    wgB = [w1buf[0], A2[:, 12288:13312]]
    wuB = [w2buf[0], A2[:, 13312:14336]]
    for tc in range(TC):
        for fb in range(NHB):
            buf = fb % 2
            def _loadG(g, tc=tc, fb=fb, buf=buf):
                if fb < 2:
                    if tc == 0:
                        p.wait(g, "pe")
                        p.wait(g, "ve")
                    else:
                        p.wait(g, "pe", pe_fb[("b2", tc * NHB - 1)])
                else:
                    p.wait(g, "pe", pe_fb[("b1", tc * NHB + fb - 2)])
                    if tc == 0:
                        p.wait(g, "dsc", sc_mark[("wu", fb - 2)])
                if tc == 0:
                    for kq in range(JB):
                        p.inc("dw", g.dma_start(
                            out=wgB[buf][:, kq * 128:(kq + 1) * 128],
                            in_=wgT.rearrange("(k p) o -> k p o", p=128)[kq, :, fb * 128:(fb + 1) * 128]))
                    for kq in range(JB):
                        p.inc("dw", g.dma_start(
                            out=wuB[buf][:, kq * 128:(kq + 1) * 128],
                            in_=wuT.rearrange("(k p) o -> k p o", p=128)[kq, :, fb * 128:(fb + 1) * 128]))
                else:
                    if tc == 1:
                        p.wait(g, "dsc", sc_mark[("wg", fb)])
                    p.inc("dw", g.dma_start(
                        out=wgB[buf][:],
                        in_=wgTb.rearrange("(f p) o -> f p o", p=128)[fb]))
            block.gpsimd(_loadG)

            if tc > 0:
                @block.sync
                def _(sy, tc=tc, fb=fb, buf=buf):
                    if fb < 2:
                        p.wait(sy, "pe", pe_fb[("b2", tc * NHB - 1)])
                    else:
                        p.wait(sy, "pe", pe_fb[("b1", tc * NHB + fb - 2)])
                    if tc == 1:
                        p.wait(sy, "dsc", sc_mark[("wu", fb)])
                    p.inc("ds", sy.dma_start(
                        out=wuB[buf][:],
                        in_=wuTb.rearrange("(f p) o -> f p o", p=128)[fb]))
            else:
                @block.scalar
                def _(s, fb=fb, buf=buf):
                    p.wait(s, "dw")
                    p.inc("dsc", s.dma_start(
                        out=wgTb.rearrange("(f p) o -> f p o", p=128)[fb],
                        in_=wgB[buf][:]))
                    sc_mark[("wg", fb)] = p.val("dsc")
                    p.inc("dsc", s.dma_start(
                        out=wuTb.rearrange("(f p) o -> f p o", p=128)[fb],
                        in_=wuB[buf][:]))
                    sc_mark[("wu", fb)] = p.val("dsc")

            gbank = 4 + 2 * (fb % 2)
            sslot = sc0 if fb % 2 == 0 else sc1

            @block.tensor
            def _(te, tc=tc, fb=fb, buf=buf, gbank=gbank):
                p.wait(te, "dw")
                if tc > 0:
                    p.wait(te, "ds")
                if fb < 2:
                    p.wait(te, "se")
                    p.wait(te, "ve")
                else:
                    p.wait(te, "se", ge_mark[("b", tc, fb - 2)])
                    p.wait(te, "ve", ep_mark[("b", tc, fb - 2)])
                for kb in range(JB):
                    p.inc("pe", te.matmul(
                        ps[gbank][:, 0:TW], wgB[buf][:, kb * 128:(kb + 1) * 128],
                        xnb[:, fsl(kb, tc)],
                        start=(kb == 0), stop=(kb == JB - 1)))
                for kb in range(JB):
                    p.inc("pe", te.matmul(
                        ps[gbank + 1][:, 0:TW], wuB[buf][:, kb * 128:(kb + 1) * 128],
                        xnb[:, fsl(kb, tc)],
                        start=(kb == 0), stop=(kb == JB - 1)))
                pe_fb[("b1", tc * NHB + fb)] = p.val("pe")

            @block.scalar
            def _(s, tc=tc, fb=fb, gbank=gbank, sslot=sslot):
                p.wait(s, "pe", pe_fb[("b1", tc * NHB + fb)])
                p.inc("se", s.activation(sslot[:, 0:TW], ps[gbank][:, 0:TW], AF.Silu,
                                         scale=1.0 / 127.0))
                ge_mark[("b", tc, fb)] = p.val("se")

            @block.vector
            def _(v, tc=tc, fb=fb, gbank=gbank, sslot=sslot):
                p.wait(v, "se", ge_mark[("b", tc, fb)])
                p.inc("ve", v.scalar_tensor_tensor(
                    A2[:, fb * TW:(fb + 1) * TW], ps[gbank + 1][:, 0:TW], 1.0 / 127.0,
                    sslot[:, 0:TW], ALU.mult, ALU.mult))
                ep_mark[("b", tc, fb)] = p.val("ve")

        pb2_0 = {}
        for fb in range(NHB):
            buf = fb % 2
            qn = "dw" if (tc == 0 or fb % 2 == 0) else "ds"
            wselD = [w2buf[0], w1buf[0]]

            def _loadD(g, tc=tc, fb=fb, buf=buf, wselD=wselD, qn=qn):
                if fb < 2:
                    p.wait(g, "pe", pe_fb[("b1", tc * NHB + NHB - 1)])
                else:
                    p.wait(g, "pe", pe_fb[("b2", tc * NHB + fb - 2)])
                    if tc == 0:
                        p.wait(g, "dsc", sc_mark[("wd", fb - 2)])
                if tc == 0:
                    p.inc(qn, g.dma_start(
                        out=wselD[buf][:],
                        in_=wdT.rearrange("(f p) o -> f p o", p=128)[fb]))
                else:
                    if tc == 1:
                        p.wait(g, "dsc", sc_mark[("wd", fb)])
                    p.inc(qn, g.dma_start(
                        out=wselD[buf][:],
                        in_=wdTb.rearrange("(f p) o -> f p o", p=128)[fb]))
            if qn == "dw":
                block.gpsimd(_loadD)
            else:
                block.sync(_loadD)

            if tc == 0:
                @block.scalar
                def _(s, fb=fb, buf=buf, wselD=wselD):
                    p.wait(s, "dw")
                    p.inc("dsc", s.dma_start(
                        out=wdTb.rearrange("(f p) o -> f p o", p=128)[fb],
                        in_=wselD[buf][:]))
                    sc_mark[("wd", fb)] = p.val("dsc")

            @block.tensor
            def _(te, tc=tc, fb=fb, buf=buf, wselD=wselD, pb2_0=pb2_0, qn=qn):
                p.wait(te, qn)
                if fb == 0:
                    p.wait(te, "ve")
                    p.wait(te, "se")
                    pb2_0[tc] = p.val("pe")
                for ob in range(JB):
                    p.inc("pe", te.matmul(
                        ps[ob][:, 0:TW], wselD[buf][:, ob * 128:(ob + 1) * 128],
                        A2[:, fb * TW:(fb + 1) * TW],
                        start=(fb == 0), stop=(fb == NHB - 1)))
                pe_fb[("b2", tc * NHB + fb)] = p.val("pe")

        @block.vector
        def _(v, tc=tc, pb2_0=pb2_0):
            for ob in range(JB):
                p.wait(v, "pe", pb2_0[tc] + (NHB - 1) * JB + ob + 1)
                p.inc("ve", v.scalar_tensor_tensor(
                    xres[:, fsl(ob, tc)], ps[ob][:, 0:TW], 1.0 / 127.0,
                    xres[:, fsl(ob, tc)], ALU.mult, ALU.add))


def _emit_l1(p, io, sb, fused, skip=()):
    block = p.block
    WdtT, bdtc, WBT, WCT = io["WdtT"], io["bdtc"], io["WBT"], io["WCT"]
    A_logc, Dpc, gwT, gbr = io["A_logc"], io["Dpc"], io["gwT"], io["gbr"]
    w13T, w2mT, w13sT, w2sT, out_t = io["w13T"], io["w2mT"], io["w13sT"], io["w2sT"], io["out_t"]
    x1res, xn1b, A1, A2, A3 = sb["xres"], sb["xnb"], sb["A1"], sb["A2"], sb["A3"]
    sc0, sc1, sc2 = sb["sc0"], sb["sc1"], sb["sc2"]
    ones_r32, ones_cb, ones_sq, epsc = sb["ones_r32"], sb["ones_cb"], sb["ones_sq"], sb["epsc"]
    idb, iof, pcol = sb["idb"], sb["iof"], sb["pcol"]
    a_sb, a_sbb, h_st = sb["a_sb"], sb["a_sbb"], sb["h_st"]
    srow, abq, tq1, tcomb = sb["srow"], sb["abq"], sb["tq1"], sb["tcomb"]
    pm2 = sb["pm2"]
    ones_sqb = sb["ones_sqb"]
    wb_sb, wc_sb, wg_sb = sb["wb_sb"], sb["wc_sb"], sb["wg_sb"]
    gb_rep, gbr_sb, bdt_sb, dp_sb = sb["gb_rep"], sb["gbr_sb"], sb["bdt_sb"], sb["dp_sb"]
    rcol = sb["rcol"]
    rsc, bia, bwk, mcol, selw, comb_sc, wcol = sb["rsc"], sb["bia"], sb["bwk"], sb["mcol"], sb["selw"], sb["comb_sc"], sb["wcol"]
    ps = sb["ps"]

    A1f = A1[:].bitcast(F32)
    A2f = A2[:].bitcast(F32)
    x1bf = A3[:, 0:JB * T]
    wdt_sb = A3[:, 0:JB * D]
    bT_sb = A3[0:NST, 8192:8192 + T]
    cT_sb = A3[0:NST, 10240:10240 + T]
    combT = A3[0:NST, 12288:12288 + T]
    Esel = A3[0:NST, 14336:14336 + NST * 128]
    io_sc = A2[:, 0:NST * 128]
    m13buf = [A1[:, 0:8192], A1[:, 8192:16384]]
    m2buf = [A2[:, 8192:12288], A2[:, 12288:16384]]
    hb_moe = A2[:, 0:2048]
    crep_moe = A2f[:, 1024:1024 + TW]
    row0 = A1f[0:1, 0:TW]
    row1 = A1f[0:1, TW:2 * TW]
    row2 = A1f[0:1, 2 * TW:3 * TW]
    sqb = sc2[:].bitcast(BF16)[:, 0:TW]

    r = dict(ps=ps, sqb=sqb, sc0=sc0, sc1=sc1, sc2=sc2, row0=row0, row1=row1,
             row2=row2, ones_r32=ones_r32, ones_cb=ones_cb, epsln=epsc[0:1, 0:1])

    p.mark("l1_setup")
    @block.sync
    def _(sync):
        if not fused:
            p.inc("ds", sync.dma_start(
                out=x1res[:].rearrange("p (j t) -> p j t", j=JB),
                in_=io["x1_t"].rearrange("(j p) t -> p j t", p=128)))
        p.inc("ds", sync.dma_start(out=a_sb[:], in_=A_logc))
        p.inc("ds", sync.dma_start(out=bdt_sb[:], in_=bdtc))
        p.inc("ds", sync.dma_start(out=dp_sb[:], in_=Dpc))
        p.inc("ds", sync.dma_start(out=gbr_sb[:], in_=gbr))

    @block.gpsimd
    def _(g):
        p.wait(g, "pe")
        p.wait(g, "ve")
        if not fused:
            p.inc("dw", g.dma_start(
                out=x1bf.rearrange("p (j t) -> p j t", j=JB),
                in_=io["x1_t"].rearrange("(j p) t -> p j t", p=128)))
        p.inc("dw", g.dma_start(
            out=wb_sb[:].rearrange("p (k o) -> p k o", k=JB),
            in_=WBT.rearrange("(k p) o -> p k o", p=128)))
        p.inc("dw", g.dma_start(
            out=wc_sb[:].rearrange("p (k o) -> p k o", k=JB),
            in_=WCT.rearrange("(k p) o -> p k o", p=128)))
        p.inc("dw", g.dma_start(
            out=wg_sb[:].rearrange("p (k o) -> p k o", k=JB),
            in_=gwT.rearrange("(k p) o -> p k o", p=128)))
        p.inc("go", g.iota(iof[:], [[1, 128]], base=0, channel_multiplier=0,
                           allow_small_or_imprecise_dtypes=True))
        p.inc("go", g.iota(pcol[:], [[0, 1]], base=0, channel_multiplier=1,
                           allow_small_or_imprecise_dtypes=True))
        p.inc("go", g.iota(io_sc, [[1, NST], [0, 128]], base=0, channel_multiplier=0,
                           allow_small_or_imprecise_dtypes=True))

    if fused:
        for kb in range(JB):
            @block.scalar
            def _(s, kb=kb):
                if kb == 0:
                    p.wait(s, "ve")
                    p.wait(s, "pe")
                p.inc("se", s.activation(x1bf[:, kb * T:(kb + 1) * T],
                                         x1res[:, kb * T:(kb + 1) * T], AF.Copy))

    @block.vector
    def _(v):
        p.wait(v, "go")
        p.inc("ve", v.memset(h_st[:], 0.0))
        p.inc("ve", v.tensor_single_scalar(idb[:], iof[:], pcol[:, 0:1], ALU.is_equal))
        if not fused:
            p.inc("ve", v.memset(ones_r32[:], 1.0))
            p.inc("ve", v.memset(ones_cb[:], 1.0))
            p.inc("ve", v.memset(epsc[:], EPS_LN))

    @block.tensor
    def _(te):
        p.wait(te, "dw")
        p.wait(te, "ds")
        p.wait(te, "ve")
        p.wait(te, "se")
        p.inc("pe", te.matmul(ps[0][:, 0:NEX], ones_r32[:], gbr_sb[:], start=True, stop=True))

    @block.scalar
    def _(s):
        p.wait(s, "pe")
        p.inc("se", s.activation(gb_rep[:], ps[0][:, 0:NEX], AF.Copy))

    @block.vector
    def _(v):
        p.wait(v, "dw")
        p.wait(v, "ds")
        p.inc("ve", v.tensor_scalar_mul(a_sb[:], a_sb[:], K5))

    @block.scalar
    def _(s):
        p.wait(s, "ve")
        p.inc("se", s.activation(a_sb[:], a_sb[:], AF.Tanh))

    @block.scalar
    def _(s):
        p.wait(s, "se")
        p.inc("se", s.activation(a_sb[:], a_sb[:], AF.Exp, scale=10.0))

    @block.vector
    def _(v):
        p.wait(v, "se")
        p.inc("ve", v.tensor_scalar_mul(a_sb[:], a_sb[:], -1.0))

    p.mark("l1_ln")
    emit_ln(p, r, x1bf, lambda kb, tc: x1res[:, fsl(kb, tc)], xn1b)

    @block.gpsimd
    def _(g):
        p.wait(g, "se")
        p.wait(g, "pe")
        p.wait(g, "ve")
        p.inc("dw", g.dma_start(
            out=wdt_sb.rearrange("p (k o) -> p k o", k=JB),
            in_=WdtT.rearrange("(k p) o -> p k o", p=128)))

    @block.vector
    def _(v):
        p.wait(v, "go")
        p.wait(v, "se")
        p.wait(v, "pe")
        p.inc("ve", v.tensor_single_scalar(Esel, io_sc[0:16, :], pcol[0:16, 0:1], ALU.is_equal))

    p.mark("l1_bc")
    for tc in range(TC):
        @block.tensor
        def _(te, tc=tc):
            p.wait(te, "ve")
            p.wait(te, "se")
            for kb in range(JB):
                p.inc("pe", te.matmul(ps[0][0:NST, 0:TW],
                                      wb_sb[:, kb * NST:(kb + 1) * NST],
                                      xn1b[:, fsl(kb, tc)],
                                      start=(kb == 0), stop=(kb == JB - 1)))
            for kb in range(JB):
                p.inc("pe", te.matmul(ps[1][0:NST, 0:TW],
                                      wc_sb[:, kb * NST:(kb + 1) * NST],
                                      xn1b[:, fsl(kb, tc)],
                                      start=(kb == 0), stop=(kb == JB - 1)))

        @block.scalar
        def _(s, tc=tc):
            p.wait(s, "pe")
            p.inc("se", s.activation(bT_sb[:, tc * TW:(tc + 1) * TW], ps[0][0:NST, 0:TW], AF.Copy))
            p.inc("se", s.activation(cT_sb[:, tc * TW:(tc + 1) * TW], ps[1][0:NST, 0:TW], AF.Copy))

    p.mark("l1_route")
    NT = T // 128
    for tt in range(NT):
        @block.tensor
        def _(te, tt=tt):
            p.wait(te, "se")
            p.wait(te, "ve")
            for kb in range(JB):
                p.inc("pe", te.matmul(ps[2][:, 0:NEX],
                                      xn1b[:, kb * T + tt * 128: kb * T + (tt + 1) * 128],
                                      wg_sb[:, kb * NEX:(kb + 1) * NEX],
                                      start=(kb == 0), stop=(kb == JB - 1)))

        @block.scalar
        def _(s, tt=tt):
            p.wait(s, "pe")
            p.inc("se", s.activation(rsc[:], ps[2][:, 0:NEX], AF.Sigmoid))

        @block.vector
        def _(v, tt=tt):
            p.wait(v, "se")
            p.inc("ve", v.tensor_add(bia[:], rsc[:], gb_rep[:]))
            p.inc("ve", v.tensor_copy(bwk[:], bia[:]))
            for rnd in range(4):
                p.inc("ve", v.reduce_max(mcol[:], bwk[:], AX.X))
                if rnd < 3:
                    p.inc("ve", v.tensor_single_scalar(selw[:], bwk[:], mcol[:, 0:1], ALU.is_equal))
                    p.inc("ve", v.scalar_tensor_tensor(bwk[:], selw[:], -1e30, bwk[:],
                                                       ALU.mult, ALU.add))
            p.inc("ve", v.tensor_single_scalar(selw[:], bia[:], mcol[:, 0:1], ALU.is_ge))
            p.inc("ve", v.tensor_mul(selw[:], rsc[:], selw[:]))
            p.inc("ve", v.tensor_reduce(wcol[:], selw[:], AX.X, ALU.add))
            p.inc("ve", v.reciprocal(wcol[:], wcol[:]))
            p.inc("ve", v.tensor_scalar(comb_sc[:], selw[:], wcol[:, 0:1], 2.5,
                                        ALU.mult, ALU.mult))

        @block.tensor
        def _(te, tt=tt):
            p.wait(te, "ve")
            p.inc("pe", te.transpose(ps[3][0:NEX, 0:128], comb_sc[:], idb[:]))

        @block.scalar
        def _(s, tt=tt):
            p.wait(s, "pe")
            p.inc("se", s.activation(combT[:, tt * 128:(tt + 1) * 128],
                                     ps[3][0:NEX, 0:128], AF.Copy))

    @block.vector
    def _(v):
        p.wait(v, "pe")
        p.inc("ve", v.memset(ones_sq[:], 1.0))

    p.mark("l1_scan")
    SCH = 32
    ESC = SCH + 1
    NCHS = T // SCH
    SCS = 128
    ESS = SCS + 1
    A1b = A1[:]
    A1f = A1[:].bitcast(F32)
    A2b = A2[:]
    A2f2 = A2[:].bitcast(F32)
    dAxb = [A1b[:, 0:4096], A2b[:, 4096:8192]]
    wxA = A2b[:, 0:4096]
    wxB = A1b[:, 12416:16384]
    w32B = A2b[:, 8192:8320]
    qbuf = A1b[:, 4096:4096 + ESC * 128]
    pbuf = A1b[:, 8320:8320 + SCH * 128]
    dtraw = A1f[:, 4160:5200]
    dtrw2 = A1f[:, 6208:7248]
    dt_sc = A2b[:, 8320:8320 + JB * ESS]
    u_sc = A2b[:, 9352:9352 + JB * ESS]
    brep_sc = A2b[:, 10384:10384 + NST * ESS]
    crep_sc = A2b[:, 12448:12448 + NST * ESS]
    ccv_b = [A2f2[:, 7256:7256 + SCH], A2f2[:, 7380:7380 + SCH]]
    cuv = A2f2[:, 7296:7296 + SCH]
    cbv = A2f2[:, 7336:7336 + SCH]
    sc0b = sc0[:].bitcast(BF16)
    sc1b = sc1[:].bitcast(BF16)
    psl = sc0b[:, 256:384]
    redsl = sc0[:, 200:208]

    def w_ap(k, slot):
        if k % 2 == 0:
            return wxA[:, (slot - 1) * 128:slot * 128]
        if slot <= 31:
            return wxB[:, (slot - 1) * 128:slot * 128]
        return w32B[:]

    def w_slot0(k):
        return w_ap(k - 1 if k > 0 else 1, 32)

    def dA_ap(k, slot):
        return dAxb[k % 2][:, (slot - 1) * 128:slot * 128]

    dts_c = dt_sc.rearrange("p (o i) -> p o i", i=ESS)
    us_c = u_sc.rearrange("p (o i) -> p o i", i=ESS)
    br_c = brep_sc.rearrange("p (n i) -> p n i", i=ESS)
    cr_c = crep_sc.rearrange("p (n i) -> p n i", i=ESS)
    pb4 = pbuf.rearrange("p (t o n) -> p t o n", o=JB, n=NST)
    x1v = x1res[:].rearrange("p (o t) -> p o t", o=JB)
    redsl3 = redsl.unsqueeze(2)
    prep_ready = {}

    def emit_recon_mult(g, ch, j):
        screl = (ch % (SCS // SCH)) * SCH
        p.inc("po", g.tensor_tensor(
            psl.rearrange("p (o n) -> p o n", o=JB),
            pb4[:, j],
            cr_c[:, :, screl + j].unsqueeze(1).broadcast_to((128, JB, NST)),
            ALU.mult))

    def emit_recon_fin(v, ch, j):
        tg = ch * SCH + j
        p.inc("ve", v.tensor_reduce(
            redsl[:], psl.rearrange("p (o n) -> p o n", o=JB), AX.X, ALU.add))
        p.inc("ve", v.scalar_tensor_tensor(
            x1v[:, :, tg:tg + 1], redsl3, srow[:, j + 1:j + 2],
            x1v[:, :, tg:tg + 1], ALU.mult, ALU.add))

    def emit_recon(v, ch, j):
        screl = (ch % (SCS // SCH)) * SCH
        p.inc("ve", v.tensor_tensor(
            psl.rearrange("p (o n) -> p o n", o=JB),
            pb4[:, j],
            cr_c[:, :, screl + j].unsqueeze(1).broadcast_to((128, JB, NST)),
            ALU.mult))
        emit_recon_fin(v, ch, j)

    def mk_prep_stages(k1):
        screl1 = (k1 % (SCS // SCH)) * SCH
        par1 = k1 % 2
        dAxT = dAxb[par1]
        dA4 = dAxT.rearrange("p (t o n) -> p t o n", o=JB, n=NST)
        vals = {}
        asb3 = a_sbb[:].rearrange("p (o n) -> p o n", o=JB).unsqueeze(1)

        def dA_slice(v, sl):
            lo, hi = sl * 8, sl * 8 + 8
            dtt = dts_c[:, :, screl1 + 1 + lo:screl1 + 1 + hi].transpose([0, 2, 1])
            p.inc("ve", v.tensor_tensor(
                dA4[:, lo:hi], dtt.unsqueeze(3).broadcast_to((128, 8, JB, NST)),
                asb3.broadcast_to((128, 8, JB, NST)), ALU.mult))
            vals[f"dv{sl}"] = p.val("ve")

        def dA_act(s, sl):
            p.wait(s, "ve", vals[f"dv{sl}"])
            seg = dAxT[:, sl * 1024:(sl + 1) * 1024]
            p.inc("se", s.activation(seg, seg, AF.Tanh, scale=K8))
            p.inc("se", s.activation(seg, seg, AF.Exp, scale=16.0))
            if sl == 3:
                prep_ready[k1] = p.val("se")

        def w_slice(v, sl):
            lo = sl * 8
            nmain = 8 if (par1 == 0 or sl < 3) else 7
            wT = wxA if par1 == 0 else wxB
            w4 = wT.rearrange("p (t o n) -> p t o n", o=JB, n=NST)
            utt = us_c[:, :, screl1 + 1 + lo:screl1 + 1 + lo + nmain] \
                .transpose([0, 2, 1])
            p.inc("ve", v.tensor_tensor(
                w4[:, lo:lo + nmain],
                utt.unsqueeze(3).broadcast_to((128, nmain, JB, NST)),
                br_c[:, :, screl1 + 1 + lo:screl1 + 1 + lo + nmain]
                .transpose([0, 2, 1]).unsqueeze(2)
                .broadcast_to((128, nmain, JB, NST)), ALU.mult))
            if nmain == 7:
                w14 = w32B.rearrange("p (t o n) -> p t o n", o=JB, n=NST)
                p.inc("ve", v.tensor_tensor(
                    w14,
                    us_c[:, :, screl1 + 32:screl1 + 33].transpose([0, 2, 1])
                    .unsqueeze(3).broadcast_to((128, 1, JB, NST)),
                    br_c[:, :, screl1 + 32:screl1 + 33].transpose([0, 2, 1])
                    .unsqueeze(2).broadcast_to((128, 1, JB, NST)), ALU.mult))

        def c_squ(v):
            p.inc("ve", v.tensor_tensor(
                sc0b[:, 0:JB * SCH].rearrange("p (o i) -> p o i", i=SCH),
                us_c[:, :, screl1:screl1 + SCH],
                us_c[:, :, screl1:screl1 + SCH], ALU.mult))
            vals["squ"] = p.val("ve")

        def c_mm(te):
            p.wait(te, "ve", vals["squ"])
            p.inc("pe", te.matmul(ps[7][:, 0:JB * SCH], ones_sqb[:],
                                  sc0b[:, 0:JB * SCH], start=True, stop=True))
            vals["cmm"] = p.val("pe")

        def c_fin(v):
            p.wait(v, "pe", vals["cmm"])
            p.inc("ve", v.tensor_reduce(
                cuv[:], ps[7][:, 0:JB * SCH].rearrange("p (o i) -> p i o", o=JB),
                AX.X, ALU.add))
            p.inc("ve", v.tensor_tensor(
                sc1b[:, 0:NST * SCH].rearrange("p (n i) -> p n i", i=SCH),
                br_c[:, :, screl1:screl1 + SCH],
                br_c[:, :, screl1:screl1 + SCH], ALU.mult))
            p.inc("ve", v.tensor_reduce(
                cbv[:], sc1b[:, 0:NST * SCH].rearrange("p (n i) -> p i n", n=NST),
                AX.X, ALU.add))
            p.inc("ve", v.tensor_mul(ccv_b[par1][:], cuv[:], cbv[:]))

        def w0_extra(v):
            w14 = w32B.rearrange("p (t o n) -> p t o n", o=JB, n=NST)
            p.inc("ve", v.tensor_tensor(
                w14,
                us_c[:, :, 0:1].transpose([0, 2, 1]).unsqueeze(3)
                .broadcast_to((128, 1, JB, NST)),
                br_c[:, :, 0:1].transpose([0, 2, 1]).unsqueeze(2)
                .broadcast_to((128, 1, JB, NST)), ALU.mult))

        vstages = [lambda v, sl=sl: dA_slice(v, sl) for sl in range(4)]
        vstages += [lambda v, sl=sl: w_slice(v, sl) for sl in range(4)]
        vstages += [c_squ, c_fin]
        sstages = [lambda s, sl=sl: dA_act(s, sl) for sl in range(4)]
        return dict(v=vstages, s=sstages, te=[c_mm], w0=w0_extra)

    def emit_prep_serial(k1, with_w0=False):
        st = mk_prep_stages(k1)

        @block.vector
        def _(v, st=st):
            p.wait(v, "se")
            p.wait(v, "pe")
            for sl in range(4):
                st["v"][sl](v)

        @block.scalar
        def _(s, st=st):
            for sl in range(4):
                st["s"][sl](s)

        @block.vector
        def _(v, st=st, with_w0=with_w0):
            for sl in range(4):
                st["v"][4 + sl](v)
            if with_w0:
                st["w0"](v)
            st["v"][8](v)

        @block.tensor
        def _(te, st=st):
            st["te"][0](te)

        @block.vector
        def _(v, st=st):
            st["v"][9](v)

    def emit_sprep(st0, NE):
        @block.tensor
        def _(te, st0=st0, NE=NE):
            p.wait(te, "ve")
            p.wait(te, "se")
            p.wait(te, "dw")
            for ob in range(JB):
                bank, off = divmod(ob, 3)
                for kb in range(JB):
                    p.inc("pe", te.matmul(
                        ps[bank][:, off * 130:off * 130 + NE],
                        wdt_sb[:, kb * D + ob * 128: kb * D + ob * 128 + 128],
                        xn1b[:, kb * T + st0: kb * T + st0 + NE],
                        start=(off == 0 and kb == 0),
                        stop=(kb == JB - 1)))

        @block.vector
        def _(v, NE=NE):
            p.wait(v, "pe")
            for ob in range(JB):
                bank, off = divmod(ob, 3)
                p.inc("ve", v.tensor_single_scalar(
                    dtraw[:, ob * 130:ob * 130 + NE],
                    ps[bank][:, off * 130:off * 130 + NE],
                    bdt_sb[:, ob:ob + 1], ALU.add))

        dra = dtraw.rearrange("p (o i) -> p o i", i=130)[:, :, 0:NE]
        dr2 = dtrw2.rearrange("p (o i) -> p o i", i=130)[:, :, 0:NE]

        @block.scalar
        def _(s, dra=dra, dr2=dr2):
            p.wait(s, "ve")
            p.inc("se", s.activation(dr2, dra, AF.Abs))
            p.inc("se", s.activation(dr2, dr2, AF.Exp, scale=-1.0))
            p.inc("se", s.activation(dr2, dr2, AF.Ln, bias=1.0))

        @block.vector
        def _(v, dra=dra, dr2=dr2):
            p.wait(v, "se")
            p.inc("ve", v.tensor_scalar_max(dra, dra, 0.0))
            p.inc("ve", v.tensor_add(dra, dra, dr2))
            p.inc("ve", v.tensor_scalar(dra, dra, 0.0505, KDT,
                                        ALU.subtract, ALU.mult))

        @block.scalar
        def _(s, dra=dra):
            p.wait(s, "ve")
            p.inc("se", s.activation(dra, dra, AF.Tanh))

        @block.vector
        def _(v, st0=st0, NE=NE, dra=dra):
            p.wait(v, "se")
            dts = dt_sc.rearrange("p (o i) -> p o i", i=ESS)[:, :, 0:NE]
            p.inc("ve", v.tensor_scalar(dts, dra, 0.099, 0.0505,
                                        ALU.mult, ALU.add))
            p.inc("ve", v.tensor_tensor(
                u_sc.rearrange("p (o i) -> p o i", i=ESS)[:, :, 0:NE],
                dts,
                xn1b[:].rearrange("p (o t) -> p o t", o=JB)[:, :, st0:st0 + NE],
                ALU.mult))

        for n in range(NST):
            @block.tensor
            def _(te, st0=st0, NE=NE, n=n):
                p.wait(te, "se")
                if n == 0:
                    p.wait(te, "ve")
                p.inc("pe", te.matmul(ps[3][:, 0:NE],
                                      Esel[:, n * 128:(n + 1) * 128],
                                      bT_sb[:, st0:st0 + NE], start=True, stop=True))
                p.inc("pe", te.matmul(ps[4][:, 0:NE],
                                      Esel[:, n * 128:(n + 1) * 128],
                                      cT_sb[:, st0:st0 + NE], start=True, stop=True))

            @block.scalar
            def _(s, NE=NE, n=n):
                p.wait(s, "pe")
                p.inc("se", s.activation(brep_sc[:, n * ESS:n * ESS + NE],
                                         ps[3][:, 0:NE], AF.Copy))
                p.inc("se", s.activation(crep_sc[:, n * ESS:n * ESS + NE],
                                         ps[4][:, 0:NE], AF.Copy))

    if "scan" not in skip:
        @block.scalar
        def _(s):
            p.wait(s, "ve")
            p.wait(s, "pe")
            p.inc("se", s.activation(a_sbb[:], a_sb[:], AF.Copy))

        @block.vector
        def _(v):
            p.wait(v, "se")
            p.inc("ve", v.memset(qbuf[:, 0:128].bitcast(F32), 0.0))
            p.inc("ve", v.memset(srow[:], 1.0))
            p.inc("ve", v.memset(abq[:].bitcast(F32), 0.0))
            p.inc("ve", v.memset(ones_sqb[:], 1.0))

        @block.tensor
        def _(te):
            p.wait(te, "ve")
            p.inc("pe", te.matmul(ps[5][:, 0:2], ones_sqb[:], abq[:, 0:2],
                                  start=True, stop=True))
            p.inc("pe", te.matmul(ps[6][:, 0:2], ones_sqb[:], abq[:, 2:4],
                                  start=True, stop=True))

    VSCHED = {2: 0, 3: 1, 4: 2, 5: 3, 7: 4, 8: 5, 9: 6, 10: 7, 11: 8, 13: 9}
    SSCHED = {4: 0, 6: 1, 8: 2, 10: 3}
    TSCHED = {12: 0}
    prev_step = None

    for ch in range(NCHS if "scan" not in skip else 0):
        t0 = ch * SCH
        screl = (ch % (SCS // SCH)) * SCH
        at_super = screl == 0
        if at_super:
            if ch > 0:
                @block.vector
                def _(v, ch=ch):
                    emit_recon(v, ch - 1, 30)
                    emit_recon(v, ch - 1, 31)
            st0 = (ch // (SCS // SCH)) * SCS
            emit_sprep(st0, ESS if st0 + SCS < T else SCS)
            emit_prep_serial(ch, with_w0=(ch == 0))
        if ch > 0:
            @block.vector
            def _(v):
                p.inc("ve", v.tensor_copy(qbuf[:, 0:128],
                                          qbuf[:, SCH * 128:(SCH + 1) * 128]))
                p.inc("ve", v.tensor_copy(srow[:, 0:1], srow[:, SCH:SCH + 1]))

        do_prep = (ch + 1 < NCHS) and ((ch + 1) % (SCS // SCH) != 0)
        stg = mk_prep_stages(ch + 1) if do_prep else None
        ccv_cur = ccv_b[ch % 2]

        for tl in range(SCH):
            t = t0 + tl
            par = t % 2
            npar = (t + 1) % 2
            last = (t == T - 1)
            se_sqrt = p.val("se") + 1
            marks = {}
            recon_job = None
            if tl >= 2:
                recon_job = (ch, tl - 2)
            elif not at_super and ch > 0:
                recon_job = (ch - 1, 30 + tl)

            if recon_job is not None:
                @block.gpsimd
                def _(g, recon_job=recon_job, pv=prev_step):
                    if pv is not None:
                        p.wait(g, "ve", pv.get("rf", pv["p1"]))
                    emit_recon_mult(g, recon_job[0], recon_job[1])

            @block.vector
            def _(v, tl=tl, ch=ch, par=par, npar=npar, last=last,
                  marks=marks, recon_job=recon_job, stg=stg, pv=prev_step):
                if tl == 0 and ch in prep_ready:
                    p.wait(v, "se", prep_ready[ch])
                if pv is not None:
                    p.wait(v, "se", pv["sq"])
                    p.inc("ve", v.tensor_scalar_add(rcol[:], rcol[:], 1e-8))
                    p.inc("ve", v.reciprocal(rcol[:], rcol[:]))
                    p.inc("ve", v.tensor_scalar(srow[:, tl:tl + 1], rcol[:],
                                                20.0, 1.0, ALU.mult, ALU.min))
                p.wait(v, "pe")
                p.inc("ve", v.tensor_scalar(
                    tcomb[:], ps[5 + par][:, 0:1], srow[:, tl:tl + 1],
                    ps[5 + par][:, 1:2], ALU.mult, ALU.add))
                marks["x"] = p.val("ve")
                p.inc("ve", v.scalar_tensor_tensor(
                    pbuf[:, tl * 128:(tl + 1) * 128],
                    qbuf[:, tl * 128:(tl + 1) * 128],
                    srow[:, tl:tl + 1],
                    (w_slot0(ch) if tl == 0 else w_ap(ch, tl)),
                    ALU.mult, ALU.add))
                marks["p1"] = p.val("ve")
                if not last:
                    p.inc("ve", v.tensor_mul(qbuf[:, (tl + 1) * 128:(tl + 2) * 128],
                                             pbuf[:, tl * 128:(tl + 1) * 128],
                                             dA_ap(ch, tl + 1)))
                    with p.nc.allow_low_precision(reason="a/b partial sums bf16"):
                        p.inc("ve", v.scalar_tensor_tensor(
                            h_st[:], qbuf[:, (tl + 1) * 128:(tl + 2) * 128], 2.0,
                            w_ap(ch, tl + 1), ALU.mult, ALU.mult,
                            accum_out=abq[:, 2 * npar + 1:2 * npar + 2]))
                        p.inc("ve", v.scalar_tensor_tensor(
                            h_st[:], qbuf[:, (tl + 1) * 128:(tl + 2) * 128], 1.0,
                            qbuf[:, (tl + 1) * 128:(tl + 2) * 128],
                            ALU.mult, ALU.mult,
                            accum_out=abq[:, 2 * npar:2 * npar + 1]))
                    marks["ab"] = p.val("ve")
                if recon_job is not None:
                    p.wait(v, "po")
                    emit_recon_fin(v, recon_job[0], recon_job[1])
                    marks["rf"] = p.val("ve")
                if stg is not None and tl in VSCHED:
                    stg["v"][VSCHED[tl]](v)

            @block.scalar
            def _(s, tl=tl, marks=marks, ccv_cur=ccv_cur, stg=stg):
                p.wait(s, "ve", marks["x"])
                p.inc("se", s.activation(rcol[:], tcomb[:], AF.Sqrt,
                                         scale=srow[:, tl:tl + 1],
                                         bias=ccv_cur[:, tl:tl + 1]))
                marks["sq"] = p.val("se")
                if stg is not None and tl in SSCHED:
                    stg["s"][SSCHED[tl]](s)

            if not last:
                @block.tensor
                def _(te, npar=npar, marks=marks, stg=stg, tl=tl):
                    p.wait(te, "ve", marks["ab"])
                    p.inc("pe", te.matmul(ps[5 + npar][:, 0:2], ones_sqb[:],
                                          abq[:, 2 * npar:2 * npar + 2],
                                          start=True, stop=True))
                    if stg is not None and tl in TSCHED:
                        stg["te"][TSCHED[tl]](te)
            prev_step = marks

        @block.vector
        def _(v, pv=prev_step):
            p.wait(v, "se", pv["sq"])
            p.inc("ve", v.tensor_scalar_add(rcol[:], rcol[:], 1e-8))
            p.inc("ve", v.reciprocal(rcol[:], rcol[:]))
            p.inc("ve", v.tensor_scalar(srow[:, SCH:SCH + 1], rcol[:],
                                        20.0, 1.0, ALU.mult, ALU.min))
        prev_step = None

        if ch == NCHS - 1:
            @block.vector
            def _(v, ch=ch):
                emit_recon(v, ch, 30)
                emit_recon(v, ch, 31)


    p.mark("l1_dp")
    for tc in range(TC if "dp" not in skip else 0):
        @block.vector
        def _(v, tc=tc):
            p.wait(v, "dw")
            p.wait(v, "ds")
            for j in range(JB):
                p.inc("ve", v.scalar_tensor_tensor(
                    x1res[:, fsl(j, tc)], xn1b[:, fsl(j, tc)], dp_sb[:, j:j + 1],
                    x1res[:, fsl(j, tc)], ALU.mult, ALU.add))

    p.mark("l1_moe")
    silu_sl = [A2f[:, 1536 + 512 * i:2048 + 512 * i] for i in range(4)]
    pe_after_expert = {}
    moe_prev = None
    for ei in range((NEX + 1) if "moe" not in skip else 0):
        shared = (ei == NEX)
        buf = ei % 2

        @block.gpsimd
        def _(g, ei=ei, buf=buf, shared=shared):
            if ei < 2:
                p.wait(g, "pe")
            else:
                p.wait(g, "pe", pe_after_expert[ei - 2])
            for kb in range(JB):
                src = (w13sT.rearrange("(k p) o -> k p o", p=128)[kb] if shared
                       else w13T.rearrange("e (k p) o -> e k p o", p=128)[ei, kb])
                p.inc("dw", g.dma_start(out=m13buf[buf][:, kb * 1024:(kb + 1) * 1024], in_=src))
            for ib in range(4):
                src = (w2sT.rearrange("(k p) o -> k p o", p=128)[ib] if shared
                       else w2mT.rearrange("e (k p) o -> e k p o", p=128)[ei, ib])
                p.inc("dw", g.dma_start(out=m2buf[buf][:, ib * 1024:(ib + 1) * 1024], in_=src))

        for tc in range(TC):
            first = moe_prev is None
            pm = moe_prev
            m = {}

            if not shared:
                @block.tensor
                def _(te, ei=ei, tc=tc, pm=pm, first=first):
                    if first:
                        p.wait(te, "ve")
                        p.wait(te, "se")
                    else:
                        p.wait(te, "ve", pm["ep0"])
                    p.inc("pe", te.matmul(ps[0][:, 0:TW], Esel[:, ei * 128:(ei + 1) * 128],
                                          combT[:, tc * TW:(tc + 1) * TW],
                                          start=True, stop=True))

                @block.scalar
                def _(s, ei=ei, tc=tc, m=m):
                    p.wait(s, "pe")
                    p.inc("se", s.activation(crep_moe, ps[0][:, 0:TW], AF.Copy))
                    m["crep"] = p.val("se")

            @block.tensor
            def _(te, ei=ei, tc=tc, buf=buf, pm=pm, m=m, first=first, shared=shared):
                if tc == 0:
                    p.wait(te, "dw")
                if first:
                    p.wait(te, "ve")
                    p.wait(te, "se")
                for ib in range(4):
                    if not first:
                        p.wait(te, "ve", pm[f"ep{ib}"])
                    if ib == 0 and not shared:
                        p.wait(te, "se", m["crep"])
                    for kb in range(JB):
                        p.inc("pe", te.matmul(
                            ps[ib][:, 0:TW],
                            m13buf[buf][:, kb * 1024 + ib * 128: kb * 1024 + ib * 128 + 128],
                            xn1b[:, fsl(kb, tc)], start=(kb == 0), stop=(kb == JB - 1)))
                    if not first:
                        p.wait(te, "ve", pm[f"ep{4 + ib}"])
                    for kb in range(JB):
                        p.inc("pe", te.matmul(
                            ps[4 + ib][:, 0:TW],
                            m13buf[buf][:, kb * 1024 + 512 + ib * 128: kb * 1024 + 512 + ib * 128 + 128],
                            xn1b[:, fsl(kb, tc)], start=(kb == 0), stop=(kb == JB - 1)))
                    m[f"gu{ib}"] = p.val("pe")

            @block.scalar
            def _(s, ei=ei, tc=tc, m=m):
                for ib in range(4):
                    p.wait(s, "pe", m[f"gu{ib}"])
                    p.inc("se", s.activation(silu_sl[ib], ps[ib][:, 0:TW], AF.Silu,
                                             scale=1.0 / 127.0))
                    m[f"si{ib}"] = p.val("se")

            @block.vector
            def _(v, ei=ei, tc=tc, m=m, pm=pm, shared=shared, first=first):
                if not first and "down_done" in pm:
                    p.wait(v, "pe", pm["down_done"])
                for ib in range(4):
                    p.wait(v, "se", m[f"si{ib}"])
                    if shared:
                        p.inc("ve", v.scalar_tensor_tensor(
                            hb_moe[:, ib * TW:(ib + 1) * TW], ps[4 + ib][:, 0:TW],
                            1.0 / 127.0, silu_sl[ib], ALU.mult, ALU.mult))
                    else:
                        p.inc("ve", v.scalar_tensor_tensor(
                            sc1[:, 0:TW], ps[4 + ib][:, 0:TW],
                            1.0 / 127.0, silu_sl[ib], ALU.mult, ALU.mult))
                        p.inc("ve", v.tensor_mul(hb_moe[:, ib * TW:(ib + 1) * TW],
                                                 sc1[:, 0:TW], crep_moe))
                    m[f"hp{ib}"] = p.val("ve")

            @block.tensor
            def _(te, ei=ei, tc=tc, buf=buf, m=m):
                p.wait(te, "ve", m["hp3"])
                m["down0"] = p.val("pe")
                for ib in range(4):
                    for ob in range(JB):
                        p.inc("pe", te.matmul(
                            ps[ob][:, 0:TW],
                            m2buf[buf][:, ib * 1024 + ob * 128: ib * 1024 + ob * 128 + 128],
                            hb_moe[:, ib * TW:(ib + 1) * TW],
                            start=(ib == 0), stop=(ib == 3)))
                m["down_done"] = p.val("pe")

            @block.vector
            def _(v, ei=ei, tc=tc, m=m):
                for ob in range(JB):
                    p.wait(v, "pe", m["down0"] + 3 * JB + ob + 1)
                    p.inc("ve", v.scalar_tensor_tensor(
                        x1res[:, fsl(ob, tc)], ps[ob][:, 0:TW], 1.0 / 127.0,
                        x1res[:, fsl(ob, tc)], ALU.mult, ALU.add))
                    m[f"ep{ob}"] = p.val("ve")

            moe_prev = m

        pe_after_expert[ei] = p.val("pe")

    p.mark(None)
    @block.sync
    def _(sync):
        p.wait(sync, "ve")
        p.inc("dx", sync.dma_start(
            out=out_t.rearrange("(j p) t -> p j t", p=128),
            in_=x1res[:].rearrange("p (j t) -> p j t", j=JB)))
        p.wait(sync, "dx")


def _mk_io_l0(nc, out_kind="Internal"):
    ap = lambda n, shp, dt=F32, kind="ExternalInput": nc.dram_tensor(n, shp, dt, kind=kind).ap()
    io = {
        "x_t": ap("x_t", [D, T]),
        "wqT": ap("wqT", [D, D]),
        "wkT": ap("wkT", [D, D]),
        "wvT": ap("wvT", [D, D]),
        "w1T": ap("w1T", [D, DFF]),
        "w2T": ap("w2T", [DFF, D]),
        "b1c": ap("b1c", [128, DFF // 128]),
        "b2c": ap("b2c", [128, JB]),
        "g1c": ap("g1c", [128, JB]),
        "be1c": ap("be1c", [128, JB]),
        "g2c": ap("g2c", [128, JB]),
        "be2c": ap("be2c", [128, JB]),
        "wgT": ap("wgT", [D, HID], I32),
        "wuT": ap("wuT", [D, HID], I32),
        "wdT": ap("wdT", [HID, D], I32),
    }
    return io


def _mk_io_l1(nc, with_x1):
    ap = lambda n, shp, dt=F32, kind="ExternalInput": nc.dram_tensor(n, shp, dt, kind=kind).ap()
    io = {
        "WdtT": ap("WdtT", [D, D]),
        "bdtc": ap("bdtc", [128, JB]),
        "WBT": ap("WBT", [D, NST]),
        "WCT": ap("WCT", [D, NST]),
        "A_logc": ap("A_logc", [128, 128]),
        "Dpc": ap("Dpc", [128, JB]),
        "gwT": ap("gwT", [D, NEX]),
        "gbr": ap("gbr", [1, NEX]),
        "w13T": ap("w13T", [NEX, D, D]),
        "w2mT": ap("w2mT", [NEX, INTER, D]),
        "w13sT": ap("w13sT", [D, D]),
        "w2sT": ap("w2sT", [INTER, D]),
        "out_t": ap("out_t", [D, T], kind="ExternalOutput"),
    }
    if with_x1:
        io["x1_t"] = ap("x1_t", [D, T])
    return io


def build_full(skip=(), debug_taps=False):
    nc = bass.Bass("TRN2", target_bir_lowering=False)
    io0 = _mk_io_l0(nc)
    io1 = _mk_io_l1(nc, with_x1=False)
    dbg = None
    if debug_taps:
        dbg = nc.dram_tensor("dbg_t", [D, T], F32, kind="ExternalOutput").ap()
    with ExitStack() as ctx:
        e = ctx.enter_context
        p = Prog(nc, ctx)
        sb = _alloc(nc, e)
        _emit_l0(p, io0, sb)
        if debug_taps:
            block = p.block
            @block.sync
            def _(sync):
                p.wait(sync, "ve")
                p.wait(sync, "se")
                p.wait(sync, "pe")
                p.inc("dx", sync.dma_start(
                    out=dbg.rearrange("(j p) t -> p j t", p=128),
                    in_=sb["xres"][:].rearrange("p (j t) -> p j t", j=JB)))
                p.wait(sync, "dx")
        _emit_l1(p, io1, sb, fused=True, skip=skip)
    return nc


def build_l0():
    nc = bass.Bass("TRN2", target_bir_lowering=False)
    io0 = _mk_io_l0(nc)
    x1_t = nc.dram_tensor("x1_t", [D, T], F32, kind="ExternalOutput").ap()
    with ExitStack() as ctx:
        e = ctx.enter_context
        p = Prog(nc, ctx)
        sb = _alloc(nc, e)
        _emit_l0(p, io0, sb)
        block = p.block

        @block.sync
        def _(sync):
            p.wait(sync, "ve")
            p.inc("dx", sync.dma_start(
                out=x1_t.rearrange("(j p) t -> p j t", p=128),
                in_=sb["xres"][:].rearrange("p (j t) -> p j t", j=JB)))
            p.wait(sync, "dx")
    return nc


def build_l1():
    nc = bass.Bass("TRN2", target_bir_lowering=False)
    io1 = _mk_io_l1(nc, with_x1=True)
    with ExitStack() as ctx:
        e = ctx.enter_context
        p = Prog(nc, ctx)
        sb = _alloc(nc, e)
        _emit_l1(p, io1, sb, fused=False)
    return nc



def _host_weights(inputs):
    f = np.ascontiguousarray
    return {
        "wqT": f(inputs["L0_attn_wq"].T),
        "wkT": f(inputs["L0_attn_wk"].T),
        "wvT": f(inputs["L0_attn_wv"].T),
        "w1T": f(inputs["L0_attn_w1"].T),
        "w2T": f(inputs["L0_attn_w2"].T),
        "b1c": f(inputs["L0_attn_b1"].reshape(DFF // 128, 128).T),
        "b2c": f(inputs["L0_attn_b2"].reshape(JB, 128).T),
        "g1c": f(inputs["L0_attn_g1"].reshape(JB, 128).T),
        "be1c": f(inputs["L0_attn_be1"].reshape(JB, 128).T),
        "g2c": f(inputs["L0_attn_g2"].reshape(JB, 128).T),
        "be2c": f(inputs["L0_attn_be2"].reshape(JB, 128).T),
        "wgT": f(inputs["L0_ffn_w_gate"].T),
        "wuT": f(inputs["L0_ffn_w_up"].T),
        "wdT": f(inputs["L0_ffn_w_down"].T),
        "WdtT": f(inputs["L1_attn_Wdt"].T),
        "bdtc": f(inputs["L1_attn_bdt"].reshape(JB, 128).T),
        "WBT": f(inputs["L1_attn_WB"].T),
        "WCT": f(inputs["L1_attn_WC"].T),
        "A_logc": f(inputs["L1_attn_A_log"].reshape(JB, 128, NST).transpose(1, 0, 2).reshape(128, 128)),
        "Dpc": f(inputs["L1_attn_Dp"].reshape(JB, 128).T),
        "gwT": f(inputs["L1_ffn_gate_w"].T),
        "gbr": f(inputs["L1_ffn_gate_b"].reshape(1, NEX)),
        "w13T": f(inputs["L1_ffn_w13"].transpose(0, 2, 1)),
        "w2mT": f(inputs["L1_ffn_w2"].transpose(0, 2, 1)),
        "w13sT": f(inputs["L1_ffn_w13_s"].T),
        "w2sT": f(inputs["L1_ffn_w2_s"].T),
    }


def _l0_host_inputs(inputs):
    w = {k: v for k, v in _host_weights(inputs).items()
         if k in ("wqT", "wkT", "wvT", "w1T", "w2T", "b1c", "b2c", "g1c", "be1c",
                  "g2c", "be2c", "wgT", "wuT", "wdT")}
    x = np.asarray(inputs["L0_attn_x"])
    return [dict(w, x_t=np.ascontiguousarray(x[c].T)) for c in range(8)]


def _l1_host_inputs(inputs, x1_list):
    w = {k: v for k, v in _host_weights(inputs).items()
         if k not in ("wqT", "wkT", "wvT", "w1T", "w2T", "b1c", "b2c", "g1c",
                      "be1c", "g2c", "be2c", "wgT", "wuT", "wdT")}
    return [dict(w, x1_t=np.ascontiguousarray(x1_list[c])) for c in range(8)]


def kernel(**inputs):
    _ = inputs.get("L1_ffn_input_ids")
    nc = build_full()
    w = _host_weights(inputs)
    x = np.asarray(inputs["L0_attn_x"])
    in_maps = [dict(w, x_t=np.ascontiguousarray(x[c].T)) for c in range(8)]
    res = run_bass_kernel_spmd(nc, in_maps, core_ids=list(range(8)))
    out = np.stack([np.asarray(r["out_t"]).T for r in res.results], axis=0)
    return out.astype(np.float32)
